# version 21
# baseline (speedup 1.0000x reference)
import numpy as np
import ml_dtypes
from contextlib import ExitStack

import concourse.bass as bass
import concourse.mybir as mybir
from concourse.bass_utils import run_bass_kernel_spmd

F32 = mybir.dt.float32
BF16 = mybir.dt.bfloat16
AF = mybir.ActivationFunctionType
ALU = mybir.AluOpType
AX = mybir.AxisListType.X
BF = ml_dtypes.bfloat16

D = 1024
S = 4096
T = 2048
H = 16
NE = 8
NG = 4
HPG = 4
NI = 4096
EPS1 = 1e-5
EPS3 = 1e-12


class Sem:
    def __init__(self, ctx, nc, name):
        self.h = ctx.enter_context(nc.semaphore(name))
        self.n = 0

    def inc(self, ins, k=1):
        ins.then_inc(self.h, k)
        self.n += k

    def dma(self, ins):
        self.inc(ins, 16)

    def w(self, eng, at=None):
        v = self.n if at is None else at
        if v > 0:
            eng.wait_ge(self.h, v)


def build():
    nc = bass.Bass("TRN2", target_bir_lowering=False, debug=False)

    x_h = nc.dram_tensor("xseq", [S, D], F32, kind="ExternalInput").ap()
    wq_h = nc.dram_tensor("wq_c", [NG, 128, 8 * 256], F32, kind="ExternalInput").ap()
    wk_h = nc.dram_tensor("wk_c", [NG, 128, 8 * 256], F32, kind="ExternalInput").ap()
    wv_h = nc.dram_tensor("wv_c", [NG, 128, 8 * 256], F32, kind="ExternalInput").ap()
    wo_h = nc.dram_tensor("wo_c", [128, 8 * 1024], F32, kind="ExternalInput").ap()
    wrt_h = nc.dram_tensor("wrt_c", [128, 64], F32, kind="ExternalInput").ap()
    w1_h = nc.dram_tensor("w1_c", [NE, 16, 128, 8 * 256], F32, kind="ExternalInput").ap()
    wo2_h = nc.dram_tensor("wo2_c", [8, 128, 32 * 128], F32, kind="ExternalInput").ap()
    bq_h = nc.dram_tensor("bq_t", [128, 8], F32, kind="ExternalInput").ap()
    bk_h = nc.dram_tensor("bk_t", [128, 8], F32, kind="ExternalInput").ap()
    bv_h = nc.dram_tensor("bv_r", [1, D], F32, kind="ExternalInput").ap()
    bo_h = nc.dram_tensor("bo_r", [1, D], F32, kind="ExternalInput").ap()
    b1_h = nc.dram_tensor("b1_t", [128, 32 * 8], F32, kind="ExternalInput").ap()
    bo2_h = nc.dram_tensor("bo2_t", [128, 8], F32, kind="ExternalInput").ap()
    ga_h = nc.dram_tensor("ga_r", [1, D], F32, kind="ExternalInput").ap()
    be_h = nc.dram_tensor("be_r", [1, D], F32, kind="ExternalInput").ap()
    y_h = nc.dram_tensor("y", [T, D], F32, kind="ExternalOutput").ap()
    attn_scr = nc.dram_tensor("attn_scr", [T, D], BF16, kind="Internal").ap()
    xn_scr = nc.dram_tensor("xn_scr", [T, D], BF16, kind="Internal").ap()
    xe_scr = [nc.dram_tensor(f"xe_scr{h}", [2049, D], BF16, kind="Internal").ap()
              for h in range(2)]
    hg_scr = nc.dram_tensor("hg_scr", [NI, 2048], BF16, kind="Internal").ap()
    fg_scr = nc.dram_tensor("fg_scr", [2048, D], BF16, kind="Internal").ap()
    w1bf = nc.dram_tensor("w1bf", [NE, 16, 128, 8 * 256], BF16, kind="Internal").ap()
    wo2bf = nc.dram_tensor("wo2bf", [8, 128, 4096], BF16, kind="Internal").ap()

    idf = nc.inline_tensor(np.eye(128, dtype=np.float32), name="idf").ap()
    idb = nc.inline_tensor(np.eye(128, dtype=BF), name="idb").ap()
    ones_r_b = nc.inline_tensor(np.ones((1, 128), dtype=BF), name="ones_r_b").ap()
    ones_r_f = nc.inline_tensor(np.ones((1, 128), dtype=np.float32), name="ones_r_f").ap()
    ones_c_b = nc.inline_tensor(np.ones((128, 1), dtype=BF), name="ones_c_b").ap()
    ones_c_f = nc.inline_tensor(np.ones((128, 1), dtype=np.float32), name="ones_c_f").ap()
    w8c = nc.inline_tensor(
        np.broadcast_to(8.0 - np.arange(8, dtype=np.float32), (128, 8)).copy(),
        name="w8c").ap()
    pi_np = np.zeros((16, 8 * 128), dtype=BF)
    for dc in range(8):
        pi_np[2 * dc, dc * 128:dc * 128 + 64] = 1
        pi_np[2 * dc + 1, dc * 128 + 64:dc * 128 + 128] = 1
    pairind = nc.inline_tensor(pi_np, name="pairind").ap()
    ecol1 = nc.inline_tensor(
        (np.arange(8, dtype=np.float32) * 256.0 - 1.0).reshape(8, 1), name="ecol1").ap()

    ctx = ExitStack()
    with ctx:
        E = ctx.enter_context

        xnT_own = E(nc.sbuf_tensor("xnT_own", [128, 8 * T], BF16))
        rowf = E(nc.sbuf_tensor("rowf", [1, D], F32))
        bv_b = E(nc.sbuf_tensor("bv_b", [1, D], BF16))
        bo_b = E(nc.sbuf_tensor("bo_b", [1, D], BF16))
        idf_s = E(nc.sbuf_tensor("idf_s", [128, 128], F32))
        idb_s = E(nc.sbuf_tensor("idb_s", [128, 128], BF16))
        or_b = E(nc.sbuf_tensor("or_b", [1, 128], BF16))
        or_f = E(nc.sbuf_tensor("or_f", [1, 128], F32))
        oc_b = E(nc.sbuf_tensor("oc_b", [128, 1], BF16))
        oc_f = E(nc.sbuf_tensor("oc_f", [128, 1], F32))
        w8_s = E(nc.sbuf_tensor("w8_s", [128, 8], F32))
        pi_s = E(nc.sbuf_tensor("pi_s", [16, 8 * 128], BF16))
        ec_s = E(nc.sbuf_tensor("ec_s", [8, 1], F32))
        wrt_s = E(nc.sbuf_tensor("wrt_s", [128, 64], F32))
        bq_s = E(nc.sbuf_tensor("bq_s", [128, 8], F32))
        bk_s = E(nc.sbuf_tensor("bk_s", [128, 8], F32))
        epsA = E(nc.sbuf_tensor("epsA", [128, 1], F32))
        epsC = E(nc.sbuf_tensor("epsC", [128, 1], F32))
        maskT = E(nc.sbuf_tensor("maskT", [8, T], BF16))
        desti2 = E(nc.sbuf_tensor("desti2", [128, 16], mybir.dt.int32))
        w1b = [E(nc.sbuf_tensor(f"w1b{j}", [128, 8 * 256], BF16)) for j in range(2)]

        qS = Sem(ctx, nc, "qS")
        qG = Sem(ctx, nc, "qG")
        xs = [Sem(ctx, nc, f"xs{j}") for j in range(2)]
        asl = [Sem(ctx, nc, f"asl{j}") for j in range(2)]
        w1s = [Sem(ctx, nc, f"w1s{j}") for j in range(2)]
        wo2s = [Sem(ctx, nc, f"wo2s{j}") for j in range(2)]
        pe = Sem(ctx, nc, "pe")
        pesc = Sem(ctx, nc, "pesc")
        peav = Sem(ctx, nc, "peav")
        ac = Sem(ctx, nc, "ac")
        dv = Sem(ctx, nc, "dv")
        ssem = Sem(ctx, nc, "ssem")
        ascr = Sem(ctx, nc, "ascr")
        gp = Sem(ctx, nc, "gp")
        xnsc = Sem(ctx, nc, "xnsc")
        isem = Sem(ctx, nc, "isem")
        gsem = Sem(ctx, nc, "gsem")
        hsem = [Sem(ctx, nc, f"hsem{j}") for j in range(2)]
        fsem = Sem(ctx, nc, "fsem")
        wcst = Sem(ctx, nc, "wcst")
        gscat = Sem(ctx, nc, "gscat")
        w1c = [Sem(ctx, nc, f"w1c{j}") for j in range(4)]
        w1g = [Sem(ctx, nc, f"w1g{j}") for j in range(4)]
        w1r = [Sem(ctx, nc, f"w1r{j}") for j in range(4)]
        st = Sem(ctx, nc, "st")
        COMPUTE = [pe, pesc, peav, ac, dv]

        block = E(nc.Block())
        TEN, VEC, SCA, GPS, SYN = (block.tensor, block.vector, block.scalar,
                                   block.gpsimd, block.sync)

        def fence(eng):
            for s in COMPUTE:
                s.w(eng)

        def chain(*steps):
            for secwrap, sem, fn in steps:
                def body(eng, fn=fn, sem=sem):
                    fence(eng)
                    sem.inc(fn(eng))
                secwrap(body)

        def pchain(steps):
            counts = []
            for secwrap, sem, fn, deps, extra in steps:
                def body(eng, fn=fn, sem=sem, deps=tuple(deps), extra=tuple(extra)):
                    for s_, v_ in extra:
                        s_.w(eng, v_)
                    for d_ in deps:
                        ds, dval = counts[d_]
                        ds.w(eng, dval)
                    sem.inc(fn(eng))
                secwrap(body)
                counts.append((sem, sem.n))
            return counts

        @SYN
        def _(sy):
            for dst, src in [(idf_s, idf), (idb_s, idb), (or_b, ones_r_b),
                             (or_f, ones_r_f), (oc_b, ones_c_b), (oc_f, ones_c_f), (w8_s, w8c),
                             (pi_s, pairind), (ec_s, ecol1), (wrt_s, wrt_h),
                             (bq_s, bq_h), (bk_s, bk_h),
                             (rowf, bv_h)]:
                qS.dma(sy.dma_start(out=dst[:, :], in_=src[:, :]))

        @VEC
        def _(ve):
            qS.w(ve)
            dv.inc(ve.memset(epsA[:, :], EPS1))
            dv.inc(ve.memset(epsC[:, :], EPS3))
            dv.inc(ve.tensor_copy(out=bv_b[:, :], in_=rowf[:, :]))

        @SYN
        def _(sy):
            dv.w(sy)
            qS.dma(sy.dma_start(out=rowf[:, :], in_=bo_h[:, :]))

        @VEC
        def _(ve):
            qS.w(ve)
            dv.inc(ve.tensor_copy(out=bo_b[:, :], in_=rowf[:, :]))

        ctxA = ctx.enter_context(ExitStack())
        EA = ctxA.enter_context
        xnT_par = EA(nc.sbuf_tensor("xnT_par", [128, 8 * T], BF16))
        scstg = [EA(nc.sbuf_tensor(f"scstg{j}", [128, D], BF16)) for j in range(2)]
        O_u = EA(nc.sbuf_tensor("O_u", [128, 8 * T], BF16))

        ctxA1 = ExitStack()
        EA1 = ctxA1.enter_context
        xtile = [EA1(nc.sbuf_tensor(f"xt1_{j}", [128, D], F32)) for j in range(2)]
        xnb = [EA1(nc.sbuf_tensor(f"xnb{j}", [128, D], BF16)) for j in range(2)]
        sqs = [EA1(nc.sbuf_tensor(f"sqs{j}", [128, D], F32)) for j in range(2)]
        xnf = [EA1(nc.sbuf_tensor(f"xnf{j}", [128, D], F32)) for j in range(2)]
        xtf = EA1(nc.sbuf_tensor("xtf", [128, 256], F32))
        st1 = [EA1(nc.sbuf_tensor(f"st1_{j}", [128, 1], F32)) for j in range(2)]
        st2 = [EA1(nc.sbuf_tensor(f"st2_{j}", [128, 1], F32)) for j in range(2)]
        st3 = [EA1(nc.sbuf_tensor(f"st3_{j}", [128, 1], F32)) for j in range(2)]
        st4 = [EA1(nc.sbuf_tensor(f"st4_{j}", [128, 1], F32)) for j in range(2)]
        lgs = EA1(nc.sbuf_tensor("lgs", [128, 8], F32))
        eqs = EA1(nc.sbuf_tensor("eqs", [128, 8], F32))
        v8s = EA1(nc.sbuf_tensor("v8s", [128, 8], F32))
        mkb = EA1(nc.sbuf_tensor("mkb", [128, 8], BF16))
        psT = [EA1(nc.psum_tensor(f"psT{j}", [128, 128], F32)) for j in range(3)]
        pidx = EA1(nc.psum_tensor("pidx", [1, 1024], F32))
        pslog = [EA1(nc.psum_tensor(f"pslog{j}", [128, 8], F32)) for j in range(2)]
        psmT = EA1(nc.psum_tensor("psmT", [8, 128], BF16))

        NT = S // 128
        acSq = {}; acRstd = {}; dvXnf = {}; dvS2 = {}; dvVar = {}; dvEvac = {}
        dvLog = {}; dvMkb = {}; dvMT = {}; peTr = {}; peLog = {}; peMT = {}

        for tt in range(NT):
            j = tt % 2
            own = tt < 16

            @SYN
            def _(sy, tt=tt, j=j):
                if tt >= 2:
                    ac.w(sy, acSq[tt - 2])
                    dv.w(sy, dvXnf[tt - 2])
                xs[j].dma(sy.dma_start(out=xtile[j][:, :],
                                       in_=x_h[tt * 128:(tt + 1) * 128, :]))

            @SCA
            def _(se, tt=tt, j=j):
                xs[j].w(se)
                if tt >= 2:
                    dv.w(se, dvS2[tt - 2])
                ac.inc(se.activation(out=sqs[j][:, :], in_=xtile[j][:, :], func=AF.Square))
                acSq[tt] = ac.n

            @VEC
            def _(ve, tt=tt, j=j):
                xs[j].w(ve)
                if tt >= 2:
                    dv.w(ve, dvXnf[tt - 2])
                    ac.w(ve, acRstd[tt - 2])
                dv.inc(ve.reduce_sum(out=st1[j][:, :], in_=xtile[j][:, :], axis=AX))
                dv.w(ve)
                dv.inc(ve.tensor_scalar_mul(out=st1[j][:, :], in0=st1[j][:, :], scalar1=1.0 / D))
                dv.w(ve)
                dv.inc(ve.tensor_mul(out=st3[j][:, :], in0=st1[j][:, :], in1=st1[j][:, :]))
                dv.w(ve)
                ac.w(ve, acSq[tt])
                dv.inc(ve.reduce_sum(out=st2[j][:, :], in_=sqs[j][:, :], axis=AX))
                dvS2[tt] = dv.n
                dv.w(ve)
                dv.inc(ve.tensor_scalar_mul(out=st2[j][:, :], in0=st2[j][:, :], scalar1=1.0 / D))
                dv.w(ve)
                dv.inc(ve.tensor_sub(out=st2[j][:, :], in0=st2[j][:, :], in1=st3[j][:, :]))
                dvVar[tt] = dv.n

            @SCA
            def _(se, tt=tt, j=j):
                dv.w(se, dvVar[tt])
                ac.inc(se.activation(out=st4[j][:, :], in_=st2[j][:, :], func=AF.Ln, bias=epsA[:, :]))
                ac.w(se)
                ac.inc(se.activation(out=st4[j][:, :], in_=st4[j][:, :], func=AF.Exp, scale=-0.5))
                acRstd[tt] = ac.n

            @VEC
            def _(ve, tt=tt, j=j, own=own):
                ac.w(ve, acRstd[tt])
                if tt >= 2:
                    pe.w(ve, peTr[(tt - 2, 7)])
                if own:
                    if tt >= 2:
                        xnsc.w(ve, 16 * (tt - 1))
                    dv.inc(ve.tensor_scalar(out=xnb[tt % 2][:, :], in0=xtile[j][:, :],
                                            scalar1=st1[j][:, :], scalar2=st4[j][:, :],
                                            op0=ALU.subtract, op1=ALU.mult))
                dv.inc(ve.tensor_scalar(out=xnf[j][:, :], in0=xtile[j][:, :],
                                        scalar1=st1[j][:, :], scalar2=st4[j][:, :],
                                        op0=ALU.subtract, op1=ALU.mult))
                dvXnf[tt] = dv.n

            if own:
                @SYN
                def _(sy, tt=tt):
                    dv.w(sy, dvXnf[tt])
                    xnsc.w(sy, 16 * tt)
                    xnsc.dma(sy.dma_start(out=xn_scr[tt * 128:(tt + 1) * 128, :],
                                          in_=xnb[tt % 2][:, :]))

            dst = xnT_own if own else xnT_par
            toff = (tt % 16) * 128
            for dc in range(8):
                pj = (tt * 8 + dc) % 3

                @TEN
                def _(te, tt=tt, dc=dc, j=j, pj=pj):
                    dv.w(te, dvXnf[tt])
                    prior = (tt * 8 + dc) - 3
                    if prior >= 0:
                        dv.w(te, dvEvac[(prior // 8, prior % 8)])
                    pe.inc(te.transpose(psT[pj][:, :],
                                        xnf[j][:, dc * 128:(dc + 1) * 128], idf_s[:, :]))
                    peTr[(tt, dc)] = pe.n

                @VEC
                def _(ve, tt=tt, dc=dc, dst=dst, toff=toff, own=own, pj=pj):
                    pe.w(ve, peTr[(tt, dc)])
                    dv.inc(ve.tensor_copy(out=dst[:, dc * T + toff:dc * T + toff + 128],
                                          in_=psT[pj][:, :]))
                    if own:
                        dv.inc(ve.tensor_copy(
                            out=xtf[:, (dc % 2) * 128:(dc % 2) * 128 + 128],
                            in_=psT[pj][:, :]))
                    dvEvac[(tt, dc)] = dv.n

                if own:
                    @TEN
                    def _(te, tt=tt, dc=dc):
                        dv.w(te, dvEvac[(tt, dc)])
                        if dc == 0 and tt >= 2:
                            dv.w(te, dvLog[tt - 2])
                        ins = te.matmul(pslog[tt % 2][:, :],
                                        xtf[:, (dc % 2) * 128:(dc % 2) * 128 + 128],
                                        wrt_s[:, dc * 8:(dc + 1) * 8],
                                        start=(dc == 0), stop=(dc == 7))
                        if dc == 7:
                            pe.inc(ins)
                            peLog[tt] = pe.n

            if own:
                @VEC
                def _(ve, tt=tt):
                    pe.w(ve, peLog[tt])
                    dv.inc(ve.tensor_copy(out=lgs[:, :], in_=pslog[tt % 2][:, :]))
                    dvLog[tt] = dv.n
                    dv.w(ve)
                    dv.inc(ve.reduce_max(out=st1[tt % 2][:, :], in_=lgs[:, :], axis=AX))
                    dv.w(ve)
                    dv.inc(ve.tensor_scalar(out=eqs[:, :], in0=lgs[:, :],
                                            scalar1=st1[tt % 2][:, :], scalar2=None,
                                            op0=ALU.is_equal))
                    dv.w(ve)
                    dv.inc(ve.tensor_mul(out=v8s[:, :], in0=eqs[:, :], in1=w8_s[:, :]))
                    dv.w(ve)
                    dv.inc(ve.reduce_max(out=st3[tt % 2][:, :], in_=v8s[:, :], axis=AX))
                    dv.w(ve)
                    dv.inc(ve.tensor_scalar(out=mkb[:, :], in0=v8s[:, :],
                                            scalar1=st3[tt % 2][:, :], scalar2=None,
                                            op0=ALU.is_equal))
                    dvMkb[tt] = dv.n

                @TEN
                def _(te, tt=tt):
                    dv.w(te, dvMkb[tt])
                    if tt >= 1:
                        dv.w(te, dvMT[tt - 1])
                    pe.inc(te.transpose(psmT[:, :], mkb[:, :], idb_s[:, :]))
                    peMT[tt] = pe.n

                @VEC
                def _(ve, tt=tt, toff=toff):
                    pe.w(ve, peMT[tt])
                    dv.inc(ve.tensor_copy(out=maskT[:, toff:toff + 128], in_=psmT[:, :]))
                    dvMT[tt] = dv.n

        prefA = EA1(nc.sbuf_tensor("prefA", [8, 1024], F32))
        prefB = EA1(nc.sbuf_tensor("prefB", [8, 1024], F32))
        zeros8 = EA1(nc.sbuf_tensor("zeros8", [8, 1024], BF16))
        drow = EA1(nc.sbuf_tensor("drow", [1, 1024], F32))
        vrow = EA1(nc.sbuf_tensor("vrow", [1, 1024], F32))
        destc = EA1(nc.sbuf_tensor("destc", [128, 16], F32))
        stgb = EA1(nc.sbuf_tensor("stgb", [128, D], BF16))

        @VEC
        def _(ve):
            dv.inc(ve.memset(zeros8[:, :], 0.0))
            dv.inc(ve.memset(stgb[:, :], 0.0))

        @SYN
        def _(sy):
            dv.w(sy)
            for h2 in range(2):
                for zb in range(16):
                    fsem.dma(sy.dma_start(out=xe_scr[h2][zb * 128:(zb + 1) * 128, :],
                                          in_=stgb[:, :]))
                fsem.dma(sy.dma_start(out=xe_scr[h2][2048:2049, :], in_=stgb[0:1, :]))

        for half in range(2):
            mTh = maskT[:, half * 1024:(half + 1) * 1024]
            chain(
                (VEC, dv, lambda ve, mTh=mTh: ve.tensor_tensor_scan(
                    out=prefA[:, :], data0=mTh, data1=zeros8[:, :],
                    initial=0.0, op0=ALU.add, op1=ALU.add)),
                (VEC, dv, lambda ve: ve.tensor_scalar(
                    out=prefB[:, :], in0=prefA[:, :], scalar1=256.0, scalar2=None,
                    op0=ALU.is_le)),
                (VEC, dv, lambda ve: ve.tensor_scalar_add(
                    out=prefA[:, :], in0=prefA[:, :], scalar1=ec_s[:, :])),
                (VEC, dv, lambda ve, mTh=mTh: ve.tensor_tensor(
                    out=prefB[:, :], in0=mTh, in1=prefB[:, :], op=ALU.mult)),
                (VEC, dv, lambda ve: ve.tensor_tensor(
                    out=prefA[:, :], in0=prefB[:, :], in1=prefA[:, :], op=ALU.mult)),
                (TEN, pe, lambda te: te.matmul(pidx[0:1, 0:512], oc_f[0:8, :],
                                               prefA[:, 0:512], start=True, stop=True)),
                (TEN, pe, lambda te: te.matmul(pidx[0:1, 512:1024], oc_f[0:8, :],
                                               prefA[:, 512:1024], start=True, stop=True)),
                (VEC, dv, lambda ve: ve.tensor_copy(out=drow[:, :], in_=pidx[0:1, :])),
                (TEN, pe, lambda te: te.matmul(pidx[0:1, 0:512], oc_f[0:8, :],
                                               prefB[:, 0:512], start=True, stop=True)),
                (TEN, pe, lambda te: te.matmul(pidx[0:1, 512:1024], oc_f[0:8, :],
                                               prefB[:, 512:1024], start=True, stop=True)),
                (VEC, dv, lambda ve: ve.tensor_copy(out=vrow[:, :], in_=pidx[0:1, :])),
                (VEC, dv, lambda ve: ve.scalar_tensor_tensor(
                    out=drow[:, :], in0=vrow[:, :], scalar=-2048.0, in1=drow[:, :],
                    op0=ALU.mult, op1=ALU.add)),
                (VEC, dv, lambda ve: ve.tensor_scalar_add(
                    out=drow[:, :], in0=drow[:, :], scalar1=2048.0)),
            )
            for tb in range(8):
                chain(
                    (TEN, pe, lambda te, tb=tb: te.transpose(
                        psT[0][:, 0:1], drow[0:1, tb * 128:(tb + 1) * 128],
                        idf_s[0:1, 0:1])),
                    (VEC, dv, lambda ve, tb=tb, half=half: ve.tensor_copy(
                        out=destc[:, half * 8 + tb:half * 8 + tb + 1],
                        in_=psT[0][0:128, 0:1])),
                )
        chain((VEC, dv, lambda ve: ve.tensor_copy(out=desti2[:, :], in_=destc[:, :])))
        dv_idx2 = dv.n
        fsem_zf = fsem.n

        ctxA1.close()

        ctxA2 = ExitStack()
        E2 = ctxA2.enter_context
        Qg = E2(nc.sbuf_tensor("Qg", [128, 4 * T], BF16))
        Kg = E2(nc.sbuf_tensor("Kg", [128, 2 * S], BF16))
        Vg = E2(nc.sbuf_tensor("Vg", [128, 32 * HPG * 65 + 64], BF16))
        sq4 = E2(nc.sbuf_tensor("sq4", [128, T], BF16))
        wqg = E2(nc.sbuf_tensor("wqg", [128, 2048], BF16))
        wkg = E2(nc.sbuf_tensor("wkg", [128, 2048], BF16))
        ring = [E2(nc.sbuf_tensor(f"ring{r}", [128, 1024], BF16)) for r in range(3)]
        sring = E2(nc.sbuf_tensor("sring", [1, 512], F32))
        S16 = E2(nc.sbuf_tensor("S16", [16, T], F32))
        negm8 = E2(nc.sbuf_tensor("negm8", [128, 1], F32))
        t11a = E2(nc.sbuf_tensor("t11a", [1, 1], F32))
        t11b = E2(nc.sbuf_tensor("t11b", [1, 1], F32))
        t11d = E2(nc.sbuf_tensor("t11d", [1, 1], F32))
        t11c = E2(nc.sbuf_tensor("t11c", [1, 1], BF16))
        sc = [E2(nc.psum_tensor(f"sc{j}", [128, 1024], F32)) for j in range(3)]
        oacc = [E2(nc.psum_tensor(f"oacc{j}", [128, 512], F32)) for j in range(2)]

        @GPS
        def _(ge):
            fence(ge)
            xnsc.w(ge)
            fsem.w(ge)
            gp.inc(ge.memset(Vg[:, :], 0.0))
            gp.w(ge)
            gp.inc(ge.memset(
                Vg[:, 0:32 * HPG * 65].rearrange("p (c s) -> p c s", s=65)[:, :, 64:65], 1.0))
            gp.inc(ge.memset(Qg[:, :], 0.0))
            qG.dma(ge.dma_start(out=wqg[:, :], in_=wq_h[0]))
            qG.dma(ge.dma_start(out=wkg[:, :], in_=wk_h[0]))

        def xnT_at(tok):
            return (xnT_own, tok) if tok < T else (xnT_par, tok - T)

        def w1_precast_batch(g):
            @GPS
            def _(ge, g=g):
                for kk in range(32):
                    k = g * 32 + kk
                    e, iq4 = divmod(k, 16)
                    if k >= 2:
                        w1c[k % 2].w(ge, 16 * (k // 2))
                    w1g[k % 2].dma(ge.dma_start(out=w1b[k % 2][:, :], in_=w1_h[e, iq4]))

            @SYN
            def _(sy, g=g):
                for kk in range(32):
                    k = g * 32 + kk
                    e, iq4 = divmod(k, 16)
                    sy.wait_ge(w1g[k % 2].h, 16 * (k // 2 + 1))
                    w1c[k % 2].dma(sy.dma_start(out=w1bf[e, iq4], in_=w1b[k % 2][:, :]))

        for g in range(NG):
            if g > 0:
                @GPS
                def _(ge, g=g):
                    fence(ge)
                    qG.dma(ge.dma_start(out=wqg[:, :], in_=wq_h[g]))
                    qG.dma(ge.dma_start(out=wkg[:, :], in_=wk_h[g]))

            peP = {}; dvP = {}
            for t in range(8):
                d2cl, qs = divmod(t, 4)

                @TEN
                def _(te, t=t, d2cl=d2cl, qs=qs):
                    qG.w(te)
                    gp.w(te)
                    if t == 0:
                        ac.w(te)
                        dv.w(te)
                    if t >= 2:
                        dv.w(te, dvP[t - 2])
                    for dc in range(8):
                        ins = te.matmul(
                            sc[t % 2][:, 0:512],
                            wqg[:, dc * 256 + d2cl * 128:dc * 256 + (d2cl + 1) * 128],
                            xnT_own[:, dc * T + qs * 512:dc * T + (qs + 1) * 512],
                            start=(dc == 0), stop=(dc == 7))
                    pe.inc(ins)
                    peP[t] = pe.n

                @VEC
                def _(ve, t=t, d2cl=d2cl, qs=qs, g=g):
                    pe.w(ve, peP[t])
                    gp.w(ve)
                    col = 2 * g + d2cl
                    for hh in range(2):
                        hl = 2 * d2cl + hh
                        dv.inc(ve.tensor_scalar_add(
                            out=Qg[hh * 64:hh * 64 + 64,
                                   hl * T + qs * 512:hl * T + (qs + 1) * 512],
                            in0=sc[t % 2][hh * 64:hh * 64 + 64, 0:512],
                            scalar1=bq_s[hh * 64:hh * 64 + 64, col:col + 1]))
                    dvP[t] = dv.n

            peK = {}; dvK = {}
            for t in range(16):
                d2cl, ks = divmod(t, 8)

                @TEN
                def _(te, t=t, d2cl=d2cl, ks=ks):
                    if t < 2:
                        dv.w(te, dvP[6 + t])

                    else:
                        dv.w(te, dvK[t - 2])
                    srcx, base = xnT_at(ks * 512)
                    for dc in range(8):
                        ins = te.matmul(
                            sc[t % 2][:, 0:512],
                            wkg[:, dc * 256 + d2cl * 128:dc * 256 + (d2cl + 1) * 128],
                            srcx[:, dc * T + base:dc * T + base + 512],
                            start=(dc == 0), stop=(dc == 7))
                    pe.inc(ins)
                    peK[t] = pe.n

                @VEC
                def _(ve, t=t, d2cl=d2cl, ks=ks, g=g):
                    pe.w(ve, peK[t])
                    dv.inc(ve.tensor_scalar_add(
                        out=Kg[:, d2cl * S + ks * 512:d2cl * S + (ks + 1) * 512],
                        in0=sc[t % 2][:, 0:512],
                        scalar1=bk_s[:, 2 * g + d2cl:2 * g + d2cl + 1]))
                    dvK[t] = dv.n

            @GPS
            def _(ge, g=g):
                pe.w(ge, peP[7])
                qG.dma(ge.dma_start(out=wqg[:, :], in_=wv_h[g]))

            w1_precast_batch(g)

            if g == 3:
                @GPS
                def _(ge):
                    ge.wait_ge(dv.h, dv_idx2)
                    ge.wait_ge(fsem.h, fsem_zf)
                    for k in range(16):
                        h2, tb = divmod(k, 8)
                        if k >= 2:
                            ge.wait_ge(isem.h, 32 * (k // 2))
                        gscat.dma(ge.dma_start(out=scstg[k % 2][:, :],
                                               in_=xn_scr[h2 * 1024 + tb * 128:
                                                          h2 * 1024 + (tb + 1) * 128, :]))
                        ge.wait_ge(gscat.h, 16 * (k + 1))
                        isem.dma(ge.indirect_dma_start(
                            out=xe_scr[h2][:, :],
                            out_offset=bass.IndirectOffsetOnAxis(
                                desti2[:, h2 * 8 + tb:h2 * 8 + tb + 1], 0),
                            in_=scstg[k % 2][:, :], in_offset=None))

            peV = {}; dvV = {}
            for kc in range(32):
                @TEN
                def _(te, kc=kc, g=g):
                    qG.w(te)
                    if kc < 2:
                        dv.w(te, dvK[14 + kc])
                    else:
                        dv.w(te, dvV[kc - 2])
                    src, base = xnT_at(kc * 128)
                    for dc in range(8):
                        te.matmul(sc[kc % 2][:, 0:256],
                                  src[:, dc * T + base:dc * T + base + 128],
                                  wqg[:, dc * 256:(dc + 1) * 256],
                                  start=(dc == 0), stop=False)
                    ins = te.matmul(sc[kc % 2][:, 0:256], or_b[:, :],
                                    bv_b[:, g * 256:(g + 1) * 256],
                                    start=False, stop=True)
                    pe.inc(ins)
                    peV[kc] = pe.n

                @VEC
                def _(ve, kc=kc):
                    pe.w(ve, peV[kc])
                    dv.inc(ve.tensor_copy(
                        out=Vg[:, kc * HPG * 65:(kc + 1) * HPG * 65]
                            .rearrange("p (h s) -> p h s", s=65)[:, :, 0:64],
                        in_=sc[kc % 2][:, 0:256].rearrange("p (h s) -> p h s", s=64)))
                    dvV[kc] = dv.n

            def nps(seg):
                j, hh = divmod(seg, 2)
                return sc[j][0:1, hh * 512:(hh + 1) * 512]

            steps = []
            for hl in range(4):
                steps.append((VEC, dv, lambda ve, hl=hl: ve.tensor_mul(
                    out=sq4[:, :], in0=Qg[:, hl * T:(hl + 1) * T],
                    in1=Qg[:, hl * T:(hl + 1) * T])))
                for seg in range(4):
                    steps.append((TEN, pe, lambda te, seg=seg: te.matmul(
                        nps(seg), oc_b[:, :], sq4[:, seg * 512:(seg + 1) * 512],
                        start=True, stop=True)))
                for seg in range(4):
                    steps.append((VEC, dv, lambda ve, seg=seg: ve.reduce_max(out=t11d[:, :], in_=nps(seg), axis=AX)))
                    if hl == 0 and seg == 0:
                        steps.append((VEC, dv, lambda ve: ve.tensor_copy(out=t11a[:, :], in_=t11d[:, :])))
                    else:
                        steps.append((VEC, dv, lambda ve: ve.tensor_max(out=t11a[:, :], in0=t11a[:, :], in1=t11d[:, :])))
            first = [True]
            for d2cl in range(2):
                for piece in range(2):
                    steps.append((VEC, dv, lambda ve, d2cl=d2cl, piece=piece: ve.tensor_mul(
                        out=sq4[:, :], in0=Kg[:, d2cl * S + piece * T:d2cl * S + (piece + 1) * T],
                        in1=Kg[:, d2cl * S + piece * T:d2cl * S + (piece + 1) * T])))
                    for seg in range(4):
                        steps.append((TEN, pe, lambda te, seg=seg: te.matmul(
                            nps(seg), oc_b[:, :], sq4[:, seg * 512:(seg + 1) * 512],
                            start=True, stop=True)))
                    for seg in range(4):
                        steps.append((VEC, dv, lambda ve, seg=seg: ve.reduce_max(
                            out=t11d[:, :], in_=nps(seg), axis=AX)))
                        if first[0]:
                            steps.append((VEC, dv, lambda ve: ve.tensor_copy(out=t11b[:, :], in_=t11d[:, :])))
                            first[0] = False
                        else:
                            steps.append((VEC, dv, lambda ve: ve.tensor_max(out=t11b[:, :], in0=t11b[:, :], in1=t11d[:, :])))
            steps.append((VEC, dv, lambda ve: ve.tensor_mul(out=t11a[:, :], in0=t11a[:, :], in1=t11b[:, :])))
            steps.append((SCA, ac, lambda se: se.activation(out=t11a[:, :], in_=t11a[:, :], func=AF.Ln)))
            steps.append((SCA, ac, lambda se: se.activation(out=t11a[:, :], in_=t11a[:, :], func=AF.Exp, scale=0.5)))
            steps.append((VEC, dv, lambda ve: ve.tensor_scalar_mul(out=t11c[:, :], in0=t11a[:, :], scalar1=-0.125)))
            steps.append((TEN, pe, lambda te: te.matmul(sc[0][:, 0:1], or_b[:, :], t11c[:, :], start=True, stop=True)))
            steps.append((VEC, dv, lambda ve: ve.tensor_copy(out=negm8[:, :], in_=sc[0][:, 0:1])))
            chain(*steps)

            base_ac, base_sc, base_av = ac.n, pesc.n, peav.n
            base_dv, base_ss = dv.n, ssem.n

            @TEN
            def _(te, g=g, base_ac=base_ac, base_dv=base_dv):
                dv.w(te)

                def spair(p):
                    if p >= 3:
                        te.wait_ge(ac.h, base_ac + p - 2)
                    for u in range(2):
                        m = 2 * p + u
                        hl = m // 128
                        qt = (m // 32) % 4
                        c = m % 32
                        ins = te.matmul(
                            sc[p % 3][:, u * 512:(u + 1) * 512],
                            Kg[:, (hl // 2) * S + c * 128:(hl // 2) * S + (c + 1) * 128],
                            Qg[:, hl * T + qt * 512:hl * T + (qt + 1) * 512],
                            start=True, stop=True)
                    pesc.inc(ins)

                for p in range(3):
                    spair(p)
                for n in range(512):
                    i, c = divmod(n, 32)
                    hl = n // 128
                    if n % 2 == 0 and n // 2 + 3 < 256:
                        spair(n // 2 + 3)
                    if c == 0 and i >= 2:
                        te.wait_ge(dv.h, base_dv + 2 * (i - 1))
                    te.wait_ge(ac.h, base_ac + n // 2 + 1)
                    ins = te.matmul(
                        oacc[i % 2][:, :],
                        Vg[:, (c * HPG + hl) * 65:(c * HPG + hl) * 65 + 128],
                        ring[(n // 2) % 3][:, (n % 2) * 512:(n % 2 + 1) * 512],
                        start=(c == 0), stop=(c == 31))
                    peav.inc(ins)

            @SCA
            def _(se, base_sc=base_sc, base_av=base_av, dvpre=dv.n):
                se.wait_ge(dv.h, dvpre)
                for p in range(256):
                    se.wait_ge(pesc.h, base_sc + p + 1)
                    if p >= 3:
                        se.wait_ge(peav.h, base_av + 2 * p - 4)
                    ac.inc(se.activation(out=ring[p % 3][:, :], in_=sc[p % 3][:, :],
                                         func=AF.Exp, bias=negm8[:, :], scale=0.125))

            @VEC
            def _(ve, g=g, base_av=base_av, base_ss=base_ss):
                for i in range(16):
                    hl = i // 4
                    qt = i % 4
                    h = HPG * g + hl
                    ve.wait_ge(peav.h, base_av + 32 * (i + 1))
                    dv.inc(ve.tensor_copy(
                        out=O_u[(h % 2) * 64:(h % 2) * 64 + 64,
                                (h // 2) * T + qt * 512:(h // 2) * T + (qt + 1) * 512],
                        in_=oacc[i % 2][0:64, :]))
                    if base_ss + 16 * i > 0:
                        ve.wait_ge(ssem.h, base_ss + 16 * i)
                    dv.inc(ve.tensor_copy(out=sring[:, :], in_=oacc[i % 2][64:65, :]))

            @SYN
            def _(sy, g=g, base_dv=base_dv):
                for i in range(16):
                    hl = i // 4
                    qt = i % 4
                    h = HPG * g + hl
                    sy.wait_ge(dv.h, base_dv + 2 * (i + 1))
                    ssem.dma(sy.dma_start(out=S16[h:h + 1, qt * 512:(qt + 1) * 512],
                                          in_=sring[:, :]))

        ctxA2.close()

        ctxB3 = ExitStack()
        E3 = ctxB3.enter_context
        wo_s = E3(nc.sbuf_tensor("wo_s", [128, 8 * 1024], BF16))
        sclb = E3(nc.sbuf_tensor("sclb", [128, 2048], BF16))
        r_att = [E3(nc.sbuf_tensor(f"r_att{j}", [128, D], F32)) for j in range(2)]
        ln2sq = [E3(nc.sbuf_tensor(f"ln2sq{j}", [128, D], F32)) for j in range(2)]
        l2a = [E3(nc.sbuf_tensor(f"l2a{j}", [128, 1], F32)) for j in range(2)]
        l2b = [E3(nc.sbuf_tensor(f"l2b{j}", [128, 1], F32)) for j in range(2)]
        l2c = [E3(nc.sbuf_tensor(f"l2c{j}", [128, 1], F32)) for j in range(2)]
        attnb = [E3(nc.sbuf_tensor(f"attnb3_{j}", [128, D], BF16)) for j in range(2)]
        T16 = E3(nc.sbuf_tensor("T16", [16, T], F32))
        R16 = E3(nc.sbuf_tensor("R16", [16, T], BF16))
        psb = [E3(nc.psum_tensor(f"psb{j}", [128, 1024], F32)) for j in range(2)]

        @GPS
        def _(ge):
            fence(ge)
            ssem.w(ge)
            qG.dma(ge.dma_start(out=wo_s[:, :], in_=wo_h[:, :]))

        @SCA
        def _(se):
            ssem.w(se)
            fence(se)
            ac.inc(se.activation(out=T16[:, :], in_=S16[:, :], func=AF.Ln))
            ac.w(se)
            ac.inc(se.activation(out=R16[:, :], in_=T16[:, :], func=AF.Exp, scale=-1.0))

        peB = {}; dvB = {}
        for dc in range(8):
            for piece in range(2):
                t = dc * 2 + piece

                @TEN
                def _(te, dc=dc, piece=piece, t=t):
                    ac.w(te)
                    if t >= 2:
                        dv.w(te, dvB[t - 2])
                    for hb in range(2):
                        ins = te.matmul(
                            psb[t % 2][:, hb * 512:(hb + 1) * 512],
                            pi_s[:, dc * 128:(dc + 1) * 128],
                            R16[:, piece * 1024 + hb * 512:piece * 1024 + (hb + 1) * 512],
                            start=True, stop=True)
                    pe.inc(ins)
                    peB[t] = pe.n

                @VEC
                def _(ve, dc=dc, piece=piece, t=t):
                    pe.w(ve, peB[t])
                    dv.inc(ve.tensor_copy(out=sclb[:, piece * 1024:(piece + 1) * 1024],
                                          in_=psb[t % 2][:, :]))
                    dv.w(ve)
                    dv.inc(ve.tensor_mul(
                        out=O_u[:, dc * T + piece * 1024:dc * T + (piece + 1) * 1024],
                        in0=O_u[:, dc * T + piece * 1024:dc * T + (piece + 1) * 1024],
                        in1=sclb[:, piece * 1024:(piece + 1) * 1024]))
                    dvB[t] = dv.n

        peY = {}; acR = {}; dvL2 = {}; acL2 = {}
        for tt in range(16):
            @TEN
            def _(te, tt=tt):
                qG.w(te)
                if tt < 2:
                    dv.w(te, dvB[15])
                else:
                    ac.w(te, acR[tt - 2])
                for dc in range(8):
                    for hb in range(2):
                        te.matmul(psb[tt % 2][:, hb * 512:(hb + 1) * 512],
                                  O_u[:, dc * T + tt * 128:dc * T + (tt + 1) * 128],
                                  wo_s[:, dc * 1024 + hb * 512:dc * 1024 + (hb + 1) * 512],
                                  start=(dc == 0), stop=False)
                for hb in range(2):
                    ins = te.matmul(psb[tt % 2][:, hb * 512:(hb + 1) * 512],
                                    or_b[:, :], bo_b[:, hb * 512:(hb + 1) * 512],
                                    start=False, stop=True)
                pe.inc(ins)
                peY[tt] = pe.n

            @SCA
            def _(se, tt=tt):
                pe.w(se, peY[tt])
                if tt >= 2:
                    dv.w(se, dvL2[tt - 2])
                ac.inc(se.activation(out=r_att[tt % 2][:, :], in_=psb[tt % 2][:, :],
                                     func=AF.Relu))
                acR[tt] = ac.n

            jj = tt % 2
            ext0 = [(ac, acR[tt])]
            if tt >= 2:
                ext0 = ext0 + [(dv, dvL2[tt - 2]), (ac, acL2.get(tt - 2, 0))]
            ext9 = [(ascr, 16 * (tt - 1))] if tt >= 2 else []
            cts = pchain([
                (VEC, dv, (lambda ve, jj=jj, tt=tt: ve.reduce_sum(
                    out=l2a[jj][:, :], in_=r_att[jj][:, :], axis=AX)), [], ext0),
                (VEC, dv, (lambda ve, jj=jj: ve.tensor_scalar_mul(
                    out=l2a[jj][:, :], in0=l2a[jj][:, :], scalar1=1.0 / D)), [0], []),
                (VEC, dv, (lambda ve, jj=jj: ve.tensor_mul(
                    out=l2c[jj][:, :], in0=l2a[jj][:, :], in1=l2a[jj][:, :])), [1], []),
                (SCA, ac, (lambda se, jj=jj, tt=tt: se.activation(
                    out=ln2sq[jj][:, :], in_=r_att[jj][:, :], func=AF.Square)), [], ext0),
                (VEC, dv, (lambda ve, jj=jj: ve.reduce_sum(
                    out=l2b[jj][:, :], in_=ln2sq[jj][:, :], axis=AX)), [3], []),
                (VEC, dv, (lambda ve, jj=jj: ve.tensor_scalar_mul(
                    out=l2b[jj][:, :], in0=l2b[jj][:, :], scalar1=1.0 / D)), [4], []),
                (VEC, dv, (lambda ve, jj=jj: ve.tensor_sub(
                    out=l2b[jj][:, :], in0=l2b[jj][:, :], in1=l2c[jj][:, :])), [5, 2], []),
                (SCA, ac, (lambda se, jj=jj: se.activation(
                    out=l2b[jj][:, :], in_=l2b[jj][:, :], func=AF.Ln, bias=epsA[:, :])), [6], []),
                (SCA, ac, (lambda se, jj=jj: se.activation(
                    out=l2b[jj][:, :], in_=l2b[jj][:, :], func=AF.Exp, scale=-0.5)), [7], []),
                (VEC, dv, (lambda ve, jj=jj: ve.tensor_scalar(
                    out=attnb[jj][:, :], in0=r_att[jj][:, :],
                    scalar1=l2a[jj][:, :], scalar2=l2b[jj][:, :],
                    op0=ALU.subtract, op1=ALU.mult)), [8, 1], ext9),
            ])
            acL2[tt] = cts[8][1]
            dvL2[tt] = dv.n

            @SYN
            def _(sy, tt=tt):
                dv.w(sy, dvL2[tt])
                ascr.w(sy, 16 * tt)
                ascr.dma(sy.dma_start(out=attn_scr[tt * 128:(tt + 1) * 128, :],
                                      in_=attnb[tt % 2][:, :]))

        ctxB3.close()
        ctxA.close()

        ctxBB = ctx.enter_context(ExitStack())
        E4 = ctxBB.enter_context
        gaf = E4(nc.sbuf_tensor("gaf", [1, D], F32))
        bef = E4(nc.sbuf_tensor("bef", [1, D], F32))
        b1_s = E4(nc.sbuf_tensor("b1_s", [128, 256], F32))
        bo2_s = E4(nc.sbuf_tensor("bo2_s", [128, 8], F32))
        xe_sb = E4(nc.sbuf_tensor("xe_sb", [128, 16 * 1024], BF16))
        g_e = E4(nc.sbuf_tensor("g_e", [128, 8 * 256], BF16))
        wo2b = E4(nc.sbuf_tensor("wo2b", [128, 4096], BF16))
        ytT = [E4(nc.sbuf_tensor(f"ytT{j}", [128, 256], F32)) for j in range(2)]
        fgt = [E4(nc.sbuf_tensor(f"fgt{j}", [128, D], BF16)) for j in range(2)]
        Gab = E4(nc.sbuf_tensor("Gab", [128, D], BF16))
        Beb = E4(nc.sbuf_tensor("Beb", [128, D], BF16))
        z3 = [E4(nc.sbuf_tensor(f"z3_{j}", [128, D], F32)) for j in range(2)]
        ln3sq = z3
        l3a = [E4(nc.sbuf_tensor(f"l3a{j}", [128, 1], F32)) for j in range(2)]
        l3b = [E4(nc.sbuf_tensor(f"l3b{j}", [128, 1], F32)) for j in range(2)]
        l3c = [E4(nc.sbuf_tensor(f"l3c{j}", [128, 1], F32)) for j in range(2)]
        xtile = [E4(nc.sbuf_tensor(f"xt4_{j}", [128, D], F32)) for j in range(2)]
        outb = [E4(nc.sbuf_tensor("outb0", [128, D], F32))] * 2
        attnb = [E4(nc.sbuf_tensor(f"attnb4_{j}", [128, D], BF16)) for j in range(2)]
        w1b4 = w1b + [E4(nc.sbuf_tensor(f"w1b4_{j}", [128, 8 * 256], BF16))
                      for j in range(2)]
        psz = [E4(nc.psum_tensor(f"psz{j}", [128, 1024], F32)) for j in range(2)]
        ptr = [E4(nc.psum_tensor(f"ptr{j}", [128, 128], F32)) for j in range(2)]
        ptrb = [E4(nc.psum_tensor(f"ptrb{j}", [128, 128], BF16)) for j in range(2)]

    

        @VEC
        def _(ve):
            ascr.w(ve)

        @SCA
        def _(se):
            ascr.w(se)

        @SYN
        def _(sy):
            ascr.w(sy)
            fence(sy)
            for dst, hh in [(gaf, ga_h), (bef, be_h), (b1_s, b1_h), (bo2_s, bo2_h)]:
                qS.dma(sy.dma_start(out=dst[:, :], in_=hh[:, :]))

        @TEN
        def _(te):
            qS.w(te)

        chain(
            (TEN, pe, lambda te: te.matmul(psz[0][:, 0:512], or_f[:, :], gaf[:, 0:512], start=True, stop=True)),
            (TEN, pe, lambda te: te.matmul(psz[0][:, 512:1024], or_f[:, :], gaf[:, 512:1024], start=True, stop=True)),
            (VEC, dv, lambda ve: ve.tensor_copy(out=Gab[:, :], in_=psz[0][:, :])),
            (TEN, pe, lambda te: te.matmul(psz[1][:, 0:512], or_f[:, :], bef[:, 0:512], start=True, stop=True)),
            (TEN, pe, lambda te: te.matmul(psz[1][:, 512:1024], or_f[:, :], bef[:, 512:1024], start=True, stop=True)),
            (VEC, dv, lambda ve: ve.tensor_copy(out=Beb[:, :], in_=psz[1][:, :])),
        )

        for d2c in range(8):
            @GPS
            def _(ge, d2c=d2c):
                if d2c == 0:
                    fence(ge)
                else:
                    wcst.w(ge)
                qG.dma(ge.dma_start(out=wo2b[:, :], in_=wo2_h[d2c]))

            @SYN
            def _(sy, d2c=d2c):
                qG.w(sy)
                wcst.dma(sy.dma_start(out=wo2bf[d2c], in_=wo2b[:, :]))

        xnsc_total = xnsc.n
        isem_scat = isem.n
        for half in range(2):
            ctxH1 = ExitStack()
            xeT = ctxH1.enter_context(nc.sbuf_tensor(f"xeT{half}", [128, 8 * 2048], BF16))

            @SYN
            def _(sy, half=half):
                isem.w(sy, isem_scat)
                fsem.w(sy)
                fsem.dma(sy.dma_start(
                    out=xe_sb[:, :].rearrange("p (c m) -> p c m", c=16),
                    in_=xe_scr[half][0:2048, :].rearrange("(c p) m -> p c m", p=128)))

            peXT = {}
            for sb in range(16):
                for dcb in range(8):
                    b = sb * 8 + dcb

                    @TEN
                    def _(te, sb=sb, dcb=dcb, b=b):
                        if b == 0:
                            fsem.w(te)
                        if b >= 2:
                            dv.w(te, peXT[b - 2])
                        pe.inc(te.transpose(ptrb[b % 2][:, :],
                                            xe_sb[:, sb * 1024 + dcb * 128:sb * 1024 + (dcb + 1) * 128],
                                            idb_s[:, :]))

                    @VEC
                    def _(ve, sb=sb, dcb=dcb, b=b):
                        pe.w(ve)
                        dv.inc(ve.tensor_copy(
                            out=xeT[:, dcb * 2048 + sb * 128:dcb * 2048 + (sb + 1) * 128],
                            in_=ptrb[b % 2][:, :]))
                        peXT[b] = dv.n
            dvXET = dv.n

            base_pez = pe.n
            base_acA = ac.n
            pe_after_z = {}
            cnt = base_pez
            for z in range(256):
                cnt += 1
                pe_after_z[z] = cnt
            ac_after_z = {z: base_acA + z + 1 for z in range(256)}

            @SYN
            def _(sy, half=half, base_pez=base_pez, pe_after_z=pe_after_z):
                if half == 0:
                    for jw in range(4):
                        w1c[jw].w(sy)
                for k in range(128):
                    e, iq4 = divmod(k, 16)
                    if k >= 4:
                        sy.wait_ge(pe.h, pe_after_z[2 * k - 7])
                    elif half == 1:
                        sy.wait_ge(pe.h, base_pez)
                    w1r[k % 4].dma(sy.dma_start(out=w1b4[k % 4][:, :], in_=w1bf[e, iq4]))

            @TEN
            def _(te, half=half, ac_after_z=ac_after_z, dvXET=dvXET):
                te.wait_ge(dv.h, dvXET)
                for e in range(NE):
                    for ic in range(32):
                        z = e * 32 + ic
                        kg = half * 128 + e * 16 + ic // 2
                        te.wait_ge(w1r[kg % 4].h, 16 * (kg // 4 + 1))
                        if z >= 4:
                            te.wait_ge(ac.h, ac_after_z[z - 4])
                        for dc in range(8):
                            ins = te.matmul(
                                psz[z % 2][:, (z // 2 % 2) * 512:(z // 2 % 2) * 512 + 256],
                                w1b4[kg % 4][:, dc * 256 + (ic % 2) * 128:
                                             dc * 256 + (ic % 2 + 1) * 128],
                                xeT[:, dc * 2048 + e * 256:dc * 2048 + (e + 1) * 256],
                                start=(dc == 0), stop=(dc == 7))
                        pe.inc(ins)

            @SCA
            def _(se, half=half, pe_after_z=pe_after_z):
                for e in range(NE):
                    for ic in range(32):
                        z = e * 32 + ic
                        se.wait_ge(pe.h, pe_after_z[z])
                        if ic % 8 == 0 and z >= 8:
                            gsem.w(se, 16 * (half * 32 + (z - 8) // 8 + 1))
                        ac.inc(se.activation(out=g_e[:, (ic % 8) * 256:(ic % 8 + 1) * 256],
                                             in_=psz[z % 2][:, (z // 2 % 2) * 512:(z // 2 % 2) * 512 + 256],
                                             func=AF.Gelu,
                                             bias=b1_s[:, ic * 8 + e:ic * 8 + e + 1]))

            @GPS
            def _(ge, half=half, ac_after_z=ac_after_z):
                for e in range(NE):
                    for icb in range(4):
                        ge.wait_ge(ac.h, ac_after_z[e * 32 + icb * 8 + 7])
                        gsem.dma(ge.dma_start(
                            out=hg_scr[icb * 1024:(icb + 1) * 1024,
                                       e * 256:(e + 1) * 256]
                                .rearrange("(i p) s -> p i s", p=128),
                            in_=g_e[:, :].rearrange("p (i s) -> p i s", i=8)))

            ctxH1.close()

            ctxH2 = ExitStack()
            hgq = [ctxH2.enter_context(nc.sbuf_tensor(f"hgq{half}_{j}", [128, 32 * 256], BF16))
                   for j in range(2)]
            wo2d = [ctxH2.enter_context(nc.sbuf_tensor(f"wo2d{half}_{j}", [128, 4096], BF16))
                    for j in range(2)]
            peB1 = {}; dvS = {}; dvYe = {}
            gsem_all = gsem.n
            for sck in range(8):
                @SYN
                def _(sy, half=half, sck=sck):
                    sy.wait_ge(gsem.h, gsem_all)
                    if sck >= 2:
                        pe.w(sy, peB1[(sck - 2) * 8 + 7])
                    for ih in range(2):
                        hsem[sck % 2].dma(sy.dma_start(
                            out=hgq[sck % 2][:, ih * 4096:(ih + 1) * 4096]
                                .rearrange("p (i s) -> p i s", i=16),
                            in_=hg_scr[ih * 2048:(ih + 1) * 2048,
                                       sck * 256:(sck + 1) * 256]
                                .rearrange("(i p) s -> p i s", p=128)))

                for d2c in range(8):
                    t = sck * 8 + d2c

                    @GPS
                    def _(ge, d2c=d2c, t=t):
                        if t >= 2:
                            pe.w(ge, peB1.get(t - 2, 0))
                        elif t == 0:
                            wcst.w(ge)
                            pe.w(ge)
                        wo2s[t % 2].dma(ge.dma_start(out=wo2d[t % 2][:, :], in_=wo2bf[d2c]))

                    @TEN
                    def _(te, sck=sck, d2c=d2c, t=t, half=half):
                        gsck = half * 8 + sck
                        te.wait_ge(hsem[sck % 2].h, 32 * (gsck // 2 + 1))
                        te.wait_ge(wo2s[t % 2].h, 16 * ((half * 64 + t) // 2 + 1))
                        if t >= 4:
                            dv.w(te, dvS[t - 4])
                        for ic in range(32):
                            ins = te.matmul(
                                psz[t % 2][:, (t // 2 % 2) * 512:(t // 2 % 2) * 512 + 256],
                                wo2d[t % 2][:, ic * 128:(ic + 1) * 128],
                                hgq[sck % 2][:, ic * 256:(ic + 1) * 256],
                                start=(ic == 0), stop=(ic == 31))
                        pe.inc(ins)
                        peB1[t] = pe.n

                    @VEC
                    def _(ve, half=half, sck=sck, d2c=d2c, t=t):
                        pe.w(ve, peB1[t])
                        dv.inc(ve.tensor_scalar_add(
                            out=ytT[t % 2][:, :],
                            in0=psz[t % 2][:, (t // 2 % 2) * 512:(t // 2 % 2) * 512 + 256],
                            scalar1=bo2_s[:, d2c:d2c + 1]))
                        dvS[t] = dv.n

                    for blk in range(2):
                        bg = t * 2 + blk

                        @TEN
                        def _(te, t=t, blk=blk, bg=bg):
                            dv.w(te, dvS[t])
                            if bg >= 2:
                                dv.w(te, dvYe[bg - 2])
                            pe.inc(te.transpose(ptr[bg % 2][:, :],
                                                ytT[t % 2][:, blk * 128:(blk + 1) * 128],
                                                idf_s[:, :]))

                        @VEC
                        def _(ve, sck=sck, d2c=d2c, blk=blk, bg=bg):
                            pe.w(ve)
                            sb = sck * 2 + blk
                            sl = slice(sb * 1024 + d2c * 128, sb * 1024 + (d2c + 1) * 128)
                            dv.inc(ve.tensor_tensor(
                                out=xe_sb[:, sl], in0=ptr[bg % 2][:, :],
                                in1=xe_sb[:, sl], op=ALU.add))
                            dvYe[bg] = dv.n

            ctxH2.close()

            @SYN
            def _(sy):
                dv.w(sy)
                fsem.dma(sy.dma_start(
                    out=fg_scr[:, :].rearrange("(c p) m -> p c m", p=128),
                    in_=xe_sb[:, :].rearrange("p (c m) -> p c m", c=16)))

            if half == 0:
                ln3_state = {"dvTS": {}, "dvCmb": {}, "stC": {}, "isG": {}}
            dvTS = ln3_state["dvTS"]; dvCmb = ln3_state["dvCmb"]
            stC = ln3_state["stC"]; isG = ln3_state["isG"]
            for tb in range(8):
                tt = half * 8 + tb
                gb = half * 8 + tb
                j = tb % 2

                @GPS
                def _(ge, tb=tb, j=j, half=half, gb=gb):
                    fsem.w(ge)
                    isem.w(ge)
                    if gb >= 2:
                        dv.w(ge, dvTS[gb - 2])
                    isem.dma(ge.indirect_dma_start(
                        out=fgt[j][:, :], out_offset=None,
                        in_=fg_scr[:, :],
                        in_offset=bass.IndirectOffsetOnAxis(
                            desti2[:, half * 8 + tb:half * 8 + tb + 1], 0)))
                    isG[gb] = isem.n

                @SYN
                def _(sy, tt=tt, j=j, gb=gb):
                    if gb >= 1:
                        st.w(sy, 16 * gb)
                    if gb >= 2:
                        dv.w(sy, dvCmb[gb - 2])
                    xs[j].dma(sy.dma_start(out=xtile[j][:, :],
                                           in_=x_h[tt * 128:(tt + 1) * 128, :]))
                    asl[j].dma(sy.dma_start(out=attnb[j][:, :],
                                            in_=attn_scr[tt * 128:(tt + 1) * 128, :]))

                extg = [(isem, isG[gb])]
                if gb >= 2:
                    extg = extg + [(dv, dvCmb[gb - 2])]
                cts = pchain([
                    (VEC, dv, (lambda ve, j=j: ve.reduce_sum(
                        out=l3a[j][:, :], in_=fgt[j][:, :], axis=AX)), [], extg),
                    (VEC, dv, (lambda ve, j=j: ve.tensor_scalar_mul(
                        out=l3a[j][:, :], in0=l3a[j][:, :], scalar1=1.0 / D)), [0], []),
                    (VEC, dv, (lambda ve, j=j: ve.tensor_mul(
                        out=l3c[j][:, :], in0=l3a[j][:, :], in1=l3a[j][:, :])), [1], []),
                    (SCA, ac, (lambda se, j=j: se.activation(
                        out=ln3sq[j][:, :], in_=fgt[j][:, :], func=AF.Square)), [], extg),
                    (VEC, dv, (lambda ve, j=j: ve.reduce_sum(
                        out=l3b[j][:, :], in_=ln3sq[j][:, :], axis=AX)), [3], []),
                    (VEC, dv, (lambda ve, j=j: ve.tensor_scalar_mul(
                        out=l3b[j][:, :], in0=l3b[j][:, :], scalar1=1.0 / D)), [4], []),
                    (VEC, dv, (lambda ve, j=j: ve.tensor_sub(
                        out=l3b[j][:, :], in0=l3b[j][:, :], in1=l3c[j][:, :])), [5, 2], []),
                    (SCA, ac, (lambda se, j=j: se.activation(
                        out=l3b[j][:, :], in_=l3b[j][:, :], func=AF.Ln,
                        bias=epsC[:, :])), [6], []),
                    (SCA, ac, (lambda se, j=j: se.activation(
                        out=l3b[j][:, :], in_=l3b[j][:, :], func=AF.Exp,
                        scale=-0.5)), [7], []),
                    (VEC, dv, (lambda ve, j=j: ve.tensor_scalar(
                        out=z3[j][:, :], in0=fgt[j][:, :], scalar1=l3a[j][:, :],
                        scalar2=l3b[j][:, :],
                        op0=ALU.subtract, op1=ALU.mult)), [8, 1], []),
                    (VEC, dv, (lambda ve, j=j: ve.tensor_mul(
                        out=z3[j][:, :], in0=z3[j][:, :], in1=Gab[:, :])), [9], []),
                    (VEC, dv, (lambda ve, j=j: ve.tensor_add(
                        out=z3[j][:, :], in0=z3[j][:, :], in1=Beb[:, :])), [10], []),
                ])
                dvTS[gb] = cts[11][1]

                @VEC
                def _(ve, tt=tt, j=j, gb=gb):
                    xs[j].w(ve)
                    asl[j].w(ve)
                    dv.w(ve, dvTS[gb])
                    if gb >= 1:
                        st.w(ve, 16 * gb)
                    dv.inc(ve.tensor_add(out=outb[j][:, :], in0=z3[j][:, :], in1=xtile[j][:, :]))
                    dv.w(ve)
                    dv.inc(ve.tensor_add(out=outb[j][:, :], in0=outb[j][:, :], in1=attnb[j][:, :]))
                    dvCmb[gb] = dv.n

                @SYN
                def _(sy, tt=tt, j=j, gb=gb):
                    dv.w(sy, dvCmb[gb])
                    st.dma(sy.dma_start(out=y_h[tt * 128:(tt + 1) * 128, :],
                                        in_=outb[j][:, :]))

        @SYN
        def _(sy):
            st.w(sy)

    return nc


_NC_CACHE = {}


def _get_nc():
    if "nc" not in _NC_CACHE:
        _NC_CACHE["nc"] = build()
    return _NC_CACHE["nc"]


def kernel(L0_attn_x, L0_attn_Wq, L0_attn_bq, L0_attn_Wk, L0_attn_bk,
           L0_attn_Wv, L0_attn_bv, L0_attn_Wo, L0_attn_bo,
           L0_ffn_Wr, L0_ffn_W1, L0_ffn_b1, L0_ffn_Wo, L0_ffn_bo,
           L0_ffn_gamma, L0_ffn_beta):
    f = np.asarray
    x = np.ascontiguousarray(f(L0_attn_x, dtype=np.float32))
    Wq = f(L0_attn_Wq, dtype=np.float32); Wk = f(L0_attn_Wk, dtype=np.float32)
    Wv = f(L0_attn_Wv, dtype=np.float32); Wo = f(L0_attn_Wo, dtype=np.float32)
    Wr = f(L0_ffn_Wr, dtype=np.float32); W1 = f(L0_ffn_W1, dtype=np.float32)
    Wo2 = f(L0_ffn_Wo, dtype=np.float32)

    def qkv_lay(W):
        return np.ascontiguousarray(
            W.reshape(8, 128, 4, 256).transpose(2, 1, 0, 3).reshape(4, 128, 8 * 256))

    wq_c = qkv_lay(Wq); wk_c = qkv_lay(Wk); wv_c = qkv_lay(Wv)
    wo_c = np.ascontiguousarray(
        Wo.reshape(8, 128, 1024).transpose(1, 0, 2).reshape(128, 8192))
    wrt_c = np.ascontiguousarray(
        Wr.T.reshape(8, 128, 8).transpose(1, 0, 2).reshape(128, 64))
    w1_c = np.ascontiguousarray(
        W1.reshape(8, 8, 128, 16, 256).transpose(0, 3, 2, 1, 4).reshape(8, 16, 128, 8 * 256))
    wo2_c = np.ascontiguousarray(
        Wo2.reshape(32, 128, 8, 128).transpose(2, 1, 0, 3).reshape(8, 128, 32 * 128))
    bq_t = np.ascontiguousarray(f(L0_attn_bq, dtype=np.float32).reshape(8, 128).T)
    bk_t = np.ascontiguousarray(f(L0_attn_bk, dtype=np.float32).reshape(8, 128).T)
    bv_r = f(L0_attn_bv, dtype=np.float32).reshape(1, 1024)
    bo_r = f(L0_attn_bo, dtype=np.float32).reshape(1, 1024)
    b1_t = np.ascontiguousarray(
        f(L0_ffn_b1, dtype=np.float32).T.reshape(32, 128, 8).transpose(1, 0, 2).reshape(128, 256))
    bo2_t = np.ascontiguousarray(f(L0_ffn_bo, dtype=np.float32).reshape(8, 128).T)
    ga_r = f(L0_ffn_gamma, dtype=np.float32).reshape(1, 1024)
    be_r = f(L0_ffn_beta, dtype=np.float32).reshape(1, 1024)

    shared = dict(wq_c=wq_c, wk_c=wk_c, wv_c=wv_c, wo_c=wo_c, wrt_c=wrt_c,
                  w1_c=w1_c, wo2_c=wo2_c, bq_t=bq_t, bk_t=bk_t, bv_r=bv_r,
                  bo_r=bo_r, b1_t=b1_t, bo2_t=bo2_t, ga_r=ga_r, be_r=be_r)
    in_maps = []
    for c in range(8):
        b, half = divmod(c, 2)
        own = x[b, half * 2048:(half + 1) * 2048]
        par = x[b, (1 - half) * 2048:(2 - half) * 2048]
        m = dict(shared)
        m["xseq"] = np.ascontiguousarray(np.concatenate([own, par], axis=0))
        in_maps.append(m)

    nc = _get_nc()
    res = run_bass_kernel_spmd(nc, in_maps, core_ids=list(range(8)))
    out = np.empty((4, 4096, 1024), dtype=np.float32)
    for c in range(8):
        b, half = divmod(c, 2)
        out[b, half * 2048:(half + 1) * 2048] = res.results[c]["y"]
    return out


if __name__ == "__main__":
    nc = build()
    print("build ok")


# version 22
# speedup vs baseline: 1.0763x; 1.0763x over previous
import numpy as np
import ml_dtypes
from contextlib import ExitStack

import concourse.bass as bass
import concourse.mybir as mybir
from concourse.bass_utils import run_bass_kernel_spmd

F32 = mybir.dt.float32
BF16 = mybir.dt.bfloat16
AF = mybir.ActivationFunctionType
ALU = mybir.AluOpType
AX = mybir.AxisListType.X
BF = ml_dtypes.bfloat16

D = 1024
S = 4096
T = 2048
H = 16
NE = 8
NG = 4
HPG = 4
NI = 4096
EPS1 = 1e-5
EPS3 = 1e-12


class Sem:
    def __init__(self, ctx, nc, name):
        self.h = ctx.enter_context(nc.semaphore(name))
        self.n = 0

    def inc(self, ins, k=1):
        ins.then_inc(self.h, k)
        self.n += k

    def dma(self, ins):
        self.inc(ins, 16)

    def w(self, eng, at=None):
        v = self.n if at is None else at
        if v > 0:
            eng.wait_ge(self.h, v)


def build():
    nc = bass.Bass("TRN2", target_bir_lowering=False, debug=False)

    x_h = nc.dram_tensor("xseq", [S, D], F32, kind="ExternalInput").ap()
    wq_h = nc.dram_tensor("wq_c", [NG, 128, 8 * 256], F32, kind="ExternalInput").ap()
    wk_h = nc.dram_tensor("wk_c", [NG, 128, 8 * 256], F32, kind="ExternalInput").ap()
    wv_h = nc.dram_tensor("wv_c", [NG, 128, 8 * 256], F32, kind="ExternalInput").ap()
    wo_h = nc.dram_tensor("wo_c", [128, 8 * 1024], F32, kind="ExternalInput").ap()
    wrt_h = nc.dram_tensor("wrt_c", [128, 64], F32, kind="ExternalInput").ap()
    w1_h = nc.dram_tensor("w1_c", [NE, 16, 128, 8 * 256], F32, kind="ExternalInput").ap()
    wo2_h = nc.dram_tensor("wo2_c", [8, 128, 32 * 128], F32, kind="ExternalInput").ap()
    bq_h = nc.dram_tensor("bq_t", [128, 8], F32, kind="ExternalInput").ap()
    bk_h = nc.dram_tensor("bk_t", [128, 8], F32, kind="ExternalInput").ap()
    bv_h = nc.dram_tensor("bv_r", [1, D], F32, kind="ExternalInput").ap()
    bo_h = nc.dram_tensor("bo_r", [1, D], F32, kind="ExternalInput").ap()
    b1_h = nc.dram_tensor("b1_t", [128, 32 * 8], F32, kind="ExternalInput").ap()
    bo2_h = nc.dram_tensor("bo2_t", [128, 8], F32, kind="ExternalInput").ap()
    ga_h = nc.dram_tensor("ga_r", [1, D], F32, kind="ExternalInput").ap()
    be_h = nc.dram_tensor("be_r", [1, D], F32, kind="ExternalInput").ap()
    y_h = nc.dram_tensor("y", [T, D], F32, kind="ExternalOutput").ap()
    attn_scr = nc.dram_tensor("attn_scr", [T, D], BF16, kind="Internal").ap()
    xn_scr = nc.dram_tensor("xn_scr", [T, D], BF16, kind="Internal").ap()
    xe_scr = [nc.dram_tensor(f"xe_scr{h}", [2049, D], BF16, kind="Internal").ap()
              for h in range(2)]
    hg_scr = nc.dram_tensor("hg_scr", [NI, 2048], BF16, kind="Internal").ap()
    fg_scr = nc.dram_tensor("fg_scr", [2048, D], BF16, kind="Internal").ap()
    w1bf = nc.dram_tensor("w1bf", [NE, 16, 128, 8 * 256], BF16, kind="Internal").ap()
    wo2bf = nc.dram_tensor("wo2bf", [8, 128, 4096], BF16, kind="Internal").ap()

    idf = nc.inline_tensor(np.eye(128, dtype=np.float32), name="idf").ap()
    idb = nc.inline_tensor(np.eye(128, dtype=BF), name="idb").ap()
    ones_r_b = nc.inline_tensor(np.ones((1, 128), dtype=BF), name="ones_r_b").ap()
    ones_r_f = nc.inline_tensor(np.ones((1, 128), dtype=np.float32), name="ones_r_f").ap()
    ones_c_b = nc.inline_tensor(np.ones((128, 1), dtype=BF), name="ones_c_b").ap()
    ones_c_f = nc.inline_tensor(np.ones((128, 1), dtype=np.float32), name="ones_c_f").ap()
    w8c = nc.inline_tensor(
        np.broadcast_to(8.0 - np.arange(8, dtype=np.float32), (128, 8)).copy(),
        name="w8c").ap()
    pi_np = np.zeros((16, 8 * 128), dtype=BF)
    for dc in range(8):
        pi_np[2 * dc, dc * 128:dc * 128 + 64] = 1
        pi_np[2 * dc + 1, dc * 128 + 64:dc * 128 + 128] = 1
    pairind = nc.inline_tensor(pi_np, name="pairind").ap()
    ecol1 = nc.inline_tensor(
        (np.arange(8, dtype=np.float32) * 256.0 - 1.0).reshape(8, 1), name="ecol1").ap()

    ctx = ExitStack()
    with ctx:
        E = ctx.enter_context

        xnT_own = E(nc.sbuf_tensor("xnT_own", [128, 8 * T], BF16))
        rowf = E(nc.sbuf_tensor("rowf", [1, D], F32))
        bv_b = E(nc.sbuf_tensor("bv_b", [1, D], BF16))
        bo_b = E(nc.sbuf_tensor("bo_b", [1, D], BF16))
        idf_s = E(nc.sbuf_tensor("idf_s", [128, 128], F32))
        idb_s = E(nc.sbuf_tensor("idb_s", [128, 128], BF16))
        or_b = E(nc.sbuf_tensor("or_b", [1, 128], BF16))
        or_f = E(nc.sbuf_tensor("or_f", [1, 128], F32))
        oc_b = E(nc.sbuf_tensor("oc_b", [128, 1], BF16))
        oc_f = E(nc.sbuf_tensor("oc_f", [128, 1], F32))
        w8_s = E(nc.sbuf_tensor("w8_s", [128, 8], F32))
        pi_s = E(nc.sbuf_tensor("pi_s", [16, 8 * 128], BF16))
        ec_s = E(nc.sbuf_tensor("ec_s", [8, 1], F32))
        wrt_s = E(nc.sbuf_tensor("wrt_s", [128, 64], F32))
        bq_s = E(nc.sbuf_tensor("bq_s", [128, 8], F32))
        bk_s = E(nc.sbuf_tensor("bk_s", [128, 8], F32))
        epsA = E(nc.sbuf_tensor("epsA", [128, 1], F32))
        epsC = E(nc.sbuf_tensor("epsC", [128, 1], F32))
        maskT = E(nc.sbuf_tensor("maskT", [8, T], BF16))
        desti2 = E(nc.sbuf_tensor("desti2", [128, 16], mybir.dt.int32))
        w1b = [E(nc.sbuf_tensor(f"w1b{j}", [128, 8 * 256], BF16)) for j in range(2)]

        qS = Sem(ctx, nc, "qS")
        qG = Sem(ctx, nc, "qG")
        xs = [Sem(ctx, nc, f"xs{j}") for j in range(2)]
        asl = [Sem(ctx, nc, f"asl{j}") for j in range(2)]
        w1s = [Sem(ctx, nc, f"w1s{j}") for j in range(2)]
        wo2s = [Sem(ctx, nc, f"wo2s{j}") for j in range(2)]
        pe = Sem(ctx, nc, "pe")
        pesc = Sem(ctx, nc, "pesc")
        peav = Sem(ctx, nc, "peav")
        ac = Sem(ctx, nc, "ac")
        dv = Sem(ctx, nc, "dv")
        ssem = Sem(ctx, nc, "ssem")
        ascr = Sem(ctx, nc, "ascr")
        gp = Sem(ctx, nc, "gp")
        xnsc = Sem(ctx, nc, "xnsc")
        isem = Sem(ctx, nc, "isem")
        gsem = [Sem(ctx, nc, f"gsem{j}") for j in range(2)]
        hsem = [Sem(ctx, nc, f"hsem{j}") for j in range(2)]
        fsem = Sem(ctx, nc, "fsem")
        wcst = Sem(ctx, nc, "wcst")
        gscat = Sem(ctx, nc, "gscat")
        w1c = [Sem(ctx, nc, f"w1c{j}") for j in range(4)]
        w1g = [Sem(ctx, nc, f"w1g{j}") for j in range(4)]
        w1r = [Sem(ctx, nc, f"w1r{j}") for j in range(4)]
        st = Sem(ctx, nc, "st")
        COMPUTE = [pe, pesc, peav, ac, dv]

        block = E(nc.Block())
        TEN, VEC, SCA, GPS, SYN = (block.tensor, block.vector, block.scalar,
                                   block.gpsimd, block.sync)

        def fence(eng):
            for s in COMPUTE:
                s.w(eng)

        def chain(*steps):
            for secwrap, sem, fn in steps:
                def body(eng, fn=fn, sem=sem):
                    fence(eng)
                    sem.inc(fn(eng))
                secwrap(body)

        def pchain(steps):
            counts = []
            for secwrap, sem, fn, deps, extra in steps:
                def body(eng, fn=fn, sem=sem, deps=tuple(deps), extra=tuple(extra)):
                    for s_, v_ in extra:
                        s_.w(eng, v_)
                    for d_ in deps:
                        ds, dval = counts[d_]
                        ds.w(eng, dval)
                    sem.inc(fn(eng))
                secwrap(body)
                counts.append((sem, sem.n))
            return counts

        @SYN
        def _(sy):
            for dst, src in [(idf_s, idf), (idb_s, idb), (or_b, ones_r_b),
                             (or_f, ones_r_f), (oc_b, ones_c_b), (oc_f, ones_c_f), (w8_s, w8c),
                             (pi_s, pairind), (ec_s, ecol1), (wrt_s, wrt_h),
                             (bq_s, bq_h), (bk_s, bk_h),
                             (rowf, bv_h)]:
                qS.dma(sy.dma_start(out=dst[:, :], in_=src[:, :]))

        @VEC
        def _(ve):
            qS.w(ve)
            dv.inc(ve.memset(epsA[:, :], EPS1))
            dv.inc(ve.memset(epsC[:, :], EPS3))
            dv.inc(ve.tensor_copy(out=bv_b[:, :], in_=rowf[:, :]))

        @SYN
        def _(sy):
            dv.w(sy)
            qS.dma(sy.dma_start(out=rowf[:, :], in_=bo_h[:, :]))

        @VEC
        def _(ve):
            qS.w(ve)
            dv.inc(ve.tensor_copy(out=bo_b[:, :], in_=rowf[:, :]))

        ctxA = ctx.enter_context(ExitStack())
        EA = ctxA.enter_context
        xnT_par = EA(nc.sbuf_tensor("xnT_par", [128, 8 * T], BF16))
        scstg = [EA(nc.sbuf_tensor(f"scstg{j}", [128, D], BF16)) for j in range(2)]
        O_u = EA(nc.sbuf_tensor("O_u", [128, 8 * T], BF16))

        ctxA1 = ExitStack()
        EA1 = ctxA1.enter_context
        xtile = [EA1(nc.sbuf_tensor(f"xt1_{j}", [128, D], F32)) for j in range(2)]
        xnb = [EA1(nc.sbuf_tensor(f"xnb{j}", [128, D], BF16)) for j in range(2)]
        sqs = [EA1(nc.sbuf_tensor(f"sqs{j}", [128, D], F32)) for j in range(2)]
        xnf = [EA1(nc.sbuf_tensor(f"xnf{j}", [128, D], F32)) for j in range(2)]
        xtf = EA1(nc.sbuf_tensor("xtf", [128, 256], F32))
        st1 = [EA1(nc.sbuf_tensor(f"st1_{j}", [128, 1], F32)) for j in range(2)]
        st2 = [EA1(nc.sbuf_tensor(f"st2_{j}", [128, 1], F32)) for j in range(2)]
        st3 = [EA1(nc.sbuf_tensor(f"st3_{j}", [128, 1], F32)) for j in range(2)]
        st4 = [EA1(nc.sbuf_tensor(f"st4_{j}", [128, 1], F32)) for j in range(2)]
        lgs = EA1(nc.sbuf_tensor("lgs", [128, 8], F32))
        eqs = EA1(nc.sbuf_tensor("eqs", [128, 8], F32))
        v8s = EA1(nc.sbuf_tensor("v8s", [128, 8], F32))
        mkb = EA1(nc.sbuf_tensor("mkb", [128, 8], BF16))
        psT = [EA1(nc.psum_tensor(f"psT{j}", [128, 128], F32)) for j in range(3)]
        pidx = EA1(nc.psum_tensor("pidx", [1, 1024], F32))
        pslog = [EA1(nc.psum_tensor(f"pslog{j}", [128, 8], F32)) for j in range(2)]
        psmT = EA1(nc.psum_tensor("psmT", [8, 128], BF16))

        NT = S // 128
        acSq = {}; acRstd = {}; dvXnf = {}; dvS2 = {}; dvVar = {}; dvEvac = {}
        dvLog = {}; dvMkb = {}; dvMT = {}; peTr = {}; peLog = {}; peMT = {}

        for tt in range(NT):
            j = tt % 2
            own = tt < 16

            @SYN
            def _(sy, tt=tt, j=j):
                if tt >= 2:
                    ac.w(sy, acSq[tt - 2])
                    dv.w(sy, dvXnf[tt - 2])
                xs[j].dma(sy.dma_start(out=xtile[j][:, :],
                                       in_=x_h[tt * 128:(tt + 1) * 128, :]))

            @SCA
            def _(se, tt=tt, j=j):
                xs[j].w(se)
                if tt >= 2:
                    dv.w(se, dvS2[tt - 2])
                ac.inc(se.activation(out=sqs[j][:, :], in_=xtile[j][:, :], func=AF.Square))
                acSq[tt] = ac.n

            @VEC
            def _(ve, tt=tt, j=j):
                xs[j].w(ve)
                if tt >= 2:
                    dv.w(ve, dvXnf[tt - 2])
                    ac.w(ve, acRstd[tt - 2])
                dv.inc(ve.reduce_sum(out=st1[j][:, :], in_=xtile[j][:, :], axis=AX))
                dv.w(ve)
                dv.inc(ve.tensor_scalar_mul(out=st1[j][:, :], in0=st1[j][:, :], scalar1=1.0 / D))
                dv.w(ve)
                dv.inc(ve.tensor_mul(out=st3[j][:, :], in0=st1[j][:, :], in1=st1[j][:, :]))
                dv.w(ve)
                ac.w(ve, acSq[tt])
                dv.inc(ve.reduce_sum(out=st2[j][:, :], in_=sqs[j][:, :], axis=AX))
                dvS2[tt] = dv.n
                dv.w(ve)
                dv.inc(ve.tensor_scalar_mul(out=st2[j][:, :], in0=st2[j][:, :], scalar1=1.0 / D))
                dv.w(ve)
                dv.inc(ve.tensor_sub(out=st2[j][:, :], in0=st2[j][:, :], in1=st3[j][:, :]))
                dvVar[tt] = dv.n

            @SCA
            def _(se, tt=tt, j=j):
                dv.w(se, dvVar[tt])
                ac.inc(se.activation(out=st4[j][:, :], in_=st2[j][:, :], func=AF.Ln, bias=epsA[:, :]))
                ac.w(se)
                ac.inc(se.activation(out=st4[j][:, :], in_=st4[j][:, :], func=AF.Exp, scale=-0.5))
                acRstd[tt] = ac.n

            @VEC
            def _(ve, tt=tt, j=j, own=own):
                ac.w(ve, acRstd[tt])
                if tt >= 2:
                    pe.w(ve, peTr[(tt - 2, 7)])
                if own:
                    if tt >= 2:
                        xnsc.w(ve, 16 * (tt - 1))
                    dv.inc(ve.tensor_scalar(out=xnb[tt % 2][:, :], in0=xtile[j][:, :],
                                            scalar1=st1[j][:, :], scalar2=st4[j][:, :],
                                            op0=ALU.subtract, op1=ALU.mult))
                dv.inc(ve.tensor_scalar(out=xnf[j][:, :], in0=xtile[j][:, :],
                                        scalar1=st1[j][:, :], scalar2=st4[j][:, :],
                                        op0=ALU.subtract, op1=ALU.mult))
                dvXnf[tt] = dv.n

            if own:
                @SYN
                def _(sy, tt=tt):
                    dv.w(sy, dvXnf[tt])
                    xnsc.w(sy, 16 * tt)
                    xnsc.dma(sy.dma_start(out=xn_scr[tt * 128:(tt + 1) * 128, :],
                                          in_=xnb[tt % 2][:, :]))

            dst = xnT_own if own else xnT_par
            toff = (tt % 16) * 128
            for dc in range(8):
                pj = (tt * 8 + dc) % 3

                @TEN
                def _(te, tt=tt, dc=dc, j=j, pj=pj):
                    dv.w(te, dvXnf[tt])
                    prior = (tt * 8 + dc) - 3
                    if prior >= 0:
                        dv.w(te, dvEvac[(prior // 8, prior % 8)])
                    pe.inc(te.transpose(psT[pj][:, :],
                                        xnf[j][:, dc * 128:(dc + 1) * 128], idf_s[:, :]))
                    peTr[(tt, dc)] = pe.n

                @VEC
                def _(ve, tt=tt, dc=dc, dst=dst, toff=toff, own=own, pj=pj):
                    pe.w(ve, peTr[(tt, dc)])
                    dv.inc(ve.tensor_copy(out=dst[:, dc * T + toff:dc * T + toff + 128],
                                          in_=psT[pj][:, :]))
                    if own:
                        dv.inc(ve.tensor_copy(
                            out=xtf[:, (dc % 2) * 128:(dc % 2) * 128 + 128],
                            in_=psT[pj][:, :]))
                    dvEvac[(tt, dc)] = dv.n

                if own:
                    @TEN
                    def _(te, tt=tt, dc=dc):
                        dv.w(te, dvEvac[(tt, dc)])
                        if dc == 0 and tt >= 2:
                            dv.w(te, dvLog[tt - 2])
                        ins = te.matmul(pslog[tt % 2][:, :],
                                        xtf[:, (dc % 2) * 128:(dc % 2) * 128 + 128],
                                        wrt_s[:, dc * 8:(dc + 1) * 8],
                                        start=(dc == 0), stop=(dc == 7))
                        if dc == 7:
                            pe.inc(ins)
                            peLog[tt] = pe.n

            if own:
                @VEC
                def _(ve, tt=tt):
                    pe.w(ve, peLog[tt])
                    dv.inc(ve.tensor_copy(out=lgs[:, :], in_=pslog[tt % 2][:, :]))
                    dvLog[tt] = dv.n
                    dv.w(ve)
                    dv.inc(ve.reduce_max(out=st1[tt % 2][:, :], in_=lgs[:, :], axis=AX))
                    dv.w(ve)
                    dv.inc(ve.tensor_scalar(out=eqs[:, :], in0=lgs[:, :],
                                            scalar1=st1[tt % 2][:, :], scalar2=None,
                                            op0=ALU.is_equal))
                    dv.w(ve)
                    dv.inc(ve.tensor_mul(out=v8s[:, :], in0=eqs[:, :], in1=w8_s[:, :]))
                    dv.w(ve)
                    dv.inc(ve.reduce_max(out=st3[tt % 2][:, :], in_=v8s[:, :], axis=AX))
                    dv.w(ve)
                    dv.inc(ve.tensor_scalar(out=mkb[:, :], in0=v8s[:, :],
                                            scalar1=st3[tt % 2][:, :], scalar2=None,
                                            op0=ALU.is_equal))
                    dvMkb[tt] = dv.n

                @TEN
                def _(te, tt=tt):
                    dv.w(te, dvMkb[tt])
                    if tt >= 1:
                        dv.w(te, dvMT[tt - 1])
                    pe.inc(te.transpose(psmT[:, :], mkb[:, :], idb_s[:, :]))
                    peMT[tt] = pe.n

                @VEC
                def _(ve, tt=tt, toff=toff):
                    pe.w(ve, peMT[tt])
                    dv.inc(ve.tensor_copy(out=maskT[:, toff:toff + 128], in_=psmT[:, :]))
                    dvMT[tt] = dv.n

        prefA = EA1(nc.sbuf_tensor("prefA", [8, 1024], F32))
        prefB = EA1(nc.sbuf_tensor("prefB", [8, 1024], F32))
        zeros8 = EA1(nc.sbuf_tensor("zeros8", [8, 1024], BF16))
        drow = EA1(nc.sbuf_tensor("drow", [1, 1024], F32))
        vrow = EA1(nc.sbuf_tensor("vrow", [1, 1024], F32))
        destc = EA1(nc.sbuf_tensor("destc", [128, 16], F32))
        stgb = EA1(nc.sbuf_tensor("stgb", [128, D], BF16))

        @VEC
        def _(ve):
            dv.inc(ve.memset(zeros8[:, :], 0.0))
            dv.inc(ve.memset(stgb[:, :], 0.0))

        @SYN
        def _(sy):
            dv.w(sy)
            for h2 in range(2):
                for zb in range(16):
                    fsem.dma(sy.dma_start(out=xe_scr[h2][zb * 128:(zb + 1) * 128, :],
                                          in_=stgb[:, :]))
                fsem.dma(sy.dma_start(out=xe_scr[h2][2048:2049, :], in_=stgb[0:1, :]))

        for half in range(2):
            mTh = maskT[:, half * 1024:(half + 1) * 1024]
            chain(
                (VEC, dv, lambda ve, mTh=mTh: ve.tensor_tensor_scan(
                    out=prefA[:, :], data0=mTh, data1=zeros8[:, :],
                    initial=0.0, op0=ALU.add, op1=ALU.add)),
                (VEC, dv, lambda ve: ve.tensor_scalar(
                    out=prefB[:, :], in0=prefA[:, :], scalar1=256.0, scalar2=None,
                    op0=ALU.is_le)),
                (VEC, dv, lambda ve: ve.tensor_scalar_add(
                    out=prefA[:, :], in0=prefA[:, :], scalar1=ec_s[:, :])),
                (VEC, dv, lambda ve, mTh=mTh: ve.tensor_tensor(
                    out=prefB[:, :], in0=mTh, in1=prefB[:, :], op=ALU.mult)),
                (VEC, dv, lambda ve: ve.tensor_tensor(
                    out=prefA[:, :], in0=prefB[:, :], in1=prefA[:, :], op=ALU.mult)),
                (TEN, pe, lambda te: te.matmul(pidx[0:1, 0:512], oc_f[0:8, :],
                                               prefA[:, 0:512], start=True, stop=True)),
                (TEN, pe, lambda te: te.matmul(pidx[0:1, 512:1024], oc_f[0:8, :],
                                               prefA[:, 512:1024], start=True, stop=True)),
                (VEC, dv, lambda ve: ve.tensor_copy(out=drow[:, :], in_=pidx[0:1, :])),
                (TEN, pe, lambda te: te.matmul(pidx[0:1, 0:512], oc_f[0:8, :],
                                               prefB[:, 0:512], start=True, stop=True)),
                (TEN, pe, lambda te: te.matmul(pidx[0:1, 512:1024], oc_f[0:8, :],
                                               prefB[:, 512:1024], start=True, stop=True)),
                (VEC, dv, lambda ve: ve.tensor_copy(out=vrow[:, :], in_=pidx[0:1, :])),
                (VEC, dv, lambda ve: ve.scalar_tensor_tensor(
                    out=drow[:, :], in0=vrow[:, :], scalar=-2048.0, in1=drow[:, :],
                    op0=ALU.mult, op1=ALU.add)),
                (VEC, dv, lambda ve: ve.tensor_scalar_add(
                    out=drow[:, :], in0=drow[:, :], scalar1=2048.0)),
            )
            for tb in range(8):
                chain(
                    (TEN, pe, lambda te, tb=tb: te.transpose(
                        psT[0][:, 0:1], drow[0:1, tb * 128:(tb + 1) * 128],
                        idf_s[0:1, 0:1])),
                    (VEC, dv, lambda ve, tb=tb, half=half: ve.tensor_copy(
                        out=destc[:, half * 8 + tb:half * 8 + tb + 1],
                        in_=psT[0][0:128, 0:1])),
                )
        chain((VEC, dv, lambda ve: ve.tensor_copy(out=desti2[:, :], in_=destc[:, :])))
        dv_idx2 = dv.n
        fsem_zf = fsem.n

        ctxA1.close()

        ctxA2 = ExitStack()
        E2 = ctxA2.enter_context
        Qg = E2(nc.sbuf_tensor("Qg", [128, 4 * T], BF16))
        Kg = E2(nc.sbuf_tensor("Kg", [128, 2 * S], BF16))
        Vg = E2(nc.sbuf_tensor("Vg", [128, 32 * HPG * 65 + 64], BF16))
        sq4 = E2(nc.sbuf_tensor("sq4", [128, T], BF16))
        wqg = E2(nc.sbuf_tensor("wqg", [128, 2048], BF16))
        wkg = E2(nc.sbuf_tensor("wkg", [128, 2048], BF16))
        ring = [E2(nc.sbuf_tensor(f"ring{r}", [128, 1024], BF16)) for r in range(3)]
        sring = E2(nc.sbuf_tensor("sring", [1, 512], F32))
        S16 = E2(nc.sbuf_tensor("S16", [16, T], F32))
        negm8 = E2(nc.sbuf_tensor("negm8", [128, 1], F32))
        t11a = E2(nc.sbuf_tensor("t11a", [1, 1], F32))
        t11b = E2(nc.sbuf_tensor("t11b", [1, 1], F32))
        t11d = E2(nc.sbuf_tensor("t11d", [1, 1], F32))
        t11c = E2(nc.sbuf_tensor("t11c", [1, 1], BF16))
        sc = [E2(nc.psum_tensor(f"sc{j}", [128, 1024], F32)) for j in range(3)]
        oacc = [E2(nc.psum_tensor(f"oacc{j}", [128, 512], F32)) for j in range(2)]

        @GPS
        def _(ge):
            fence(ge)
            xnsc.w(ge)
            fsem.w(ge)
            gp.inc(ge.memset(Vg[:, :], 0.0))
            gp.w(ge)
            gp.inc(ge.memset(
                Vg[:, 0:32 * HPG * 65].rearrange("p (c s) -> p c s", s=65)[:, :, 64:65], 1.0))
            gp.inc(ge.memset(Qg[:, :], 0.0))
            qG.dma(ge.dma_start(out=wqg[:, :], in_=wq_h[0]))
            qG.dma(ge.dma_start(out=wkg[:, :], in_=wk_h[0]))

        def xnT_at(tok):
            return (xnT_own, tok) if tok < T else (xnT_par, tok - T)

        def w1_precast_batch(g):
            @GPS
            def _(ge, g=g):
                for kk in range(32):
                    k = g * 32 + kk
                    e, iq4 = divmod(k, 16)
                    if k >= 2:
                        w1c[k % 2].w(ge, 16 * (k // 2))
                    w1g[k % 2].dma(ge.dma_start(out=w1b[k % 2][:, :], in_=w1_h[e, iq4]))

            @SYN
            def _(sy, g=g):
                for kk in range(32):
                    k = g * 32 + kk
                    e, iq4 = divmod(k, 16)
                    sy.wait_ge(w1g[k % 2].h, 16 * (k // 2 + 1))
                    w1c[k % 2].dma(sy.dma_start(out=w1bf[e, iq4], in_=w1b[k % 2][:, :]))

        for g in range(NG):
            if g > 0:
                @GPS
                def _(ge, g=g):
                    fence(ge)
                    qG.dma(ge.dma_start(out=wqg[:, :], in_=wq_h[g]))
                    qG.dma(ge.dma_start(out=wkg[:, :], in_=wk_h[g]))

            peP = {}; dvP = {}
            for t in range(8):
                d2cl, qs = divmod(t, 4)

                @TEN
                def _(te, t=t, d2cl=d2cl, qs=qs):
                    qG.w(te)
                    gp.w(te)
                    if t == 0:
                        ac.w(te)
                        dv.w(te)
                    if t >= 2:
                        dv.w(te, dvP[t - 2])
                    for dc in range(8):
                        ins = te.matmul(
                            sc[t % 2][:, 0:512],
                            wqg[:, dc * 256 + d2cl * 128:dc * 256 + (d2cl + 1) * 128],
                            xnT_own[:, dc * T + qs * 512:dc * T + (qs + 1) * 512],
                            start=(dc == 0), stop=(dc == 7))
                    pe.inc(ins)
                    peP[t] = pe.n

                @VEC
                def _(ve, t=t, d2cl=d2cl, qs=qs, g=g):
                    pe.w(ve, peP[t])
                    gp.w(ve)
                    col = 2 * g + d2cl
                    for hh in range(2):
                        hl = 2 * d2cl + hh
                        dv.inc(ve.tensor_scalar_add(
                            out=Qg[hh * 64:hh * 64 + 64,
                                   hl * T + qs * 512:hl * T + (qs + 1) * 512],
                            in0=sc[t % 2][hh * 64:hh * 64 + 64, 0:512],
                            scalar1=bq_s[hh * 64:hh * 64 + 64, col:col + 1]))
                    dvP[t] = dv.n

            peK = {}; dvK = {}
            for t in range(16):
                d2cl, ks = divmod(t, 8)

                @TEN
                def _(te, t=t, d2cl=d2cl, ks=ks):
                    if t < 2:
                        dv.w(te, dvP[6 + t])

                    else:
                        dv.w(te, dvK[t - 2])
                    srcx, base = xnT_at(ks * 512)
                    for dc in range(8):
                        ins = te.matmul(
                            sc[t % 2][:, 0:512],
                            wkg[:, dc * 256 + d2cl * 128:dc * 256 + (d2cl + 1) * 128],
                            srcx[:, dc * T + base:dc * T + base + 512],
                            start=(dc == 0), stop=(dc == 7))
                    pe.inc(ins)
                    peK[t] = pe.n

                @VEC
                def _(ve, t=t, d2cl=d2cl, ks=ks, g=g):
                    pe.w(ve, peK[t])
                    dv.inc(ve.tensor_scalar_add(
                        out=Kg[:, d2cl * S + ks * 512:d2cl * S + (ks + 1) * 512],
                        in0=sc[t % 2][:, 0:512],
                        scalar1=bk_s[:, 2 * g + d2cl:2 * g + d2cl + 1]))
                    dvK[t] = dv.n

            @GPS
            def _(ge, g=g):
                pe.w(ge, peP[7])
                qG.dma(ge.dma_start(out=wqg[:, :], in_=wv_h[g]))

            w1_precast_batch(g)

            if g == 3:
                @GPS
                def _(ge):
                    ge.wait_ge(dv.h, dv_idx2)
                    ge.wait_ge(fsem.h, fsem_zf)
                    for k in range(16):
                        h2, tb = divmod(k, 8)
                        if k >= 2:
                            ge.wait_ge(isem.h, 32 * (k // 2))
                        gscat.dma(ge.dma_start(out=scstg[k % 2][:, :],
                                               in_=xn_scr[h2 * 1024 + tb * 128:
                                                          h2 * 1024 + (tb + 1) * 128, :]))
                        ge.wait_ge(gscat.h, 16 * (k + 1))
                        isem.dma(ge.indirect_dma_start(
                            out=xe_scr[h2][:, :],
                            out_offset=bass.IndirectOffsetOnAxis(
                                desti2[:, h2 * 8 + tb:h2 * 8 + tb + 1], 0),
                            in_=scstg[k % 2][:, :], in_offset=None))

            peV = {}; dvV = {}
            for kc in range(32):
                @TEN
                def _(te, kc=kc, g=g):
                    qG.w(te)
                    if kc < 2:
                        dv.w(te, dvK[14 + kc])
                    else:
                        dv.w(te, dvV[kc - 2])
                    src, base = xnT_at(kc * 128)
                    for dc in range(8):
                        te.matmul(sc[kc % 2][:, 0:256],
                                  src[:, dc * T + base:dc * T + base + 128],
                                  wqg[:, dc * 256:(dc + 1) * 256],
                                  start=(dc == 0), stop=False)
                    ins = te.matmul(sc[kc % 2][:, 0:256], or_b[:, :],
                                    bv_b[:, g * 256:(g + 1) * 256],
                                    start=False, stop=True)
                    pe.inc(ins)
                    peV[kc] = pe.n

                @VEC
                def _(ve, kc=kc):
                    pe.w(ve, peV[kc])
                    dv.inc(ve.tensor_copy(
                        out=Vg[:, kc * HPG * 65:(kc + 1) * HPG * 65]
                            .rearrange("p (h s) -> p h s", s=65)[:, :, 0:64],
                        in_=sc[kc % 2][:, 0:256].rearrange("p (h s) -> p h s", s=64)))
                    dvV[kc] = dv.n

            def nps(seg):
                j, hh = divmod(seg, 2)
                return sc[j][0:1, hh * 512:(hh + 1) * 512]

            steps = []
            for hl in range(4):
                steps.append((VEC, dv, lambda ve, hl=hl: ve.tensor_mul(
                    out=sq4[:, :], in0=Qg[:, hl * T:(hl + 1) * T],
                    in1=Qg[:, hl * T:(hl + 1) * T])))
                for seg in range(4):
                    steps.append((TEN, pe, lambda te, seg=seg: te.matmul(
                        nps(seg), oc_b[:, :], sq4[:, seg * 512:(seg + 1) * 512],
                        start=True, stop=True)))
                for seg in range(4):
                    steps.append((VEC, dv, lambda ve, seg=seg: ve.reduce_max(out=t11d[:, :], in_=nps(seg), axis=AX)))
                    if hl == 0 and seg == 0:
                        steps.append((VEC, dv, lambda ve: ve.tensor_copy(out=t11a[:, :], in_=t11d[:, :])))
                    else:
                        steps.append((VEC, dv, lambda ve: ve.tensor_max(out=t11a[:, :], in0=t11a[:, :], in1=t11d[:, :])))
            first = [True]
            for d2cl in range(2):
                for piece in range(2):
                    steps.append((VEC, dv, lambda ve, d2cl=d2cl, piece=piece: ve.tensor_mul(
                        out=sq4[:, :], in0=Kg[:, d2cl * S + piece * T:d2cl * S + (piece + 1) * T],
                        in1=Kg[:, d2cl * S + piece * T:d2cl * S + (piece + 1) * T])))
                    for seg in range(4):
                        steps.append((TEN, pe, lambda te, seg=seg: te.matmul(
                            nps(seg), oc_b[:, :], sq4[:, seg * 512:(seg + 1) * 512],
                            start=True, stop=True)))
                    for seg in range(4):
                        steps.append((VEC, dv, lambda ve, seg=seg: ve.reduce_max(
                            out=t11d[:, :], in_=nps(seg), axis=AX)))
                        if first[0]:
                            steps.append((VEC, dv, lambda ve: ve.tensor_copy(out=t11b[:, :], in_=t11d[:, :])))
                            first[0] = False
                        else:
                            steps.append((VEC, dv, lambda ve: ve.tensor_max(out=t11b[:, :], in0=t11b[:, :], in1=t11d[:, :])))
            steps.append((VEC, dv, lambda ve: ve.tensor_mul(out=t11a[:, :], in0=t11a[:, :], in1=t11b[:, :])))
            steps.append((SCA, ac, lambda se: se.activation(out=t11a[:, :], in_=t11a[:, :], func=AF.Ln)))
            steps.append((SCA, ac, lambda se: se.activation(out=t11a[:, :], in_=t11a[:, :], func=AF.Exp, scale=0.5)))
            steps.append((VEC, dv, lambda ve: ve.tensor_scalar_mul(out=t11c[:, :], in0=t11a[:, :], scalar1=-0.125)))
            steps.append((TEN, pe, lambda te: te.matmul(sc[0][:, 0:1], or_b[:, :], t11c[:, :], start=True, stop=True)))
            steps.append((VEC, dv, lambda ve: ve.tensor_copy(out=negm8[:, :], in_=sc[0][:, 0:1])))
            chain(*steps)

            base_ac, base_sc, base_av = ac.n, pesc.n, peav.n
            base_dv, base_ss = dv.n, ssem.n

            @TEN
            def _(te, g=g, base_ac=base_ac, base_dv=base_dv):
                dv.w(te)

                def spair(p):
                    if p >= 3:
                        te.wait_ge(ac.h, base_ac + p - 2)
                    for u in range(2):
                        m = 2 * p + u
                        hl = m // 128
                        qt = (m // 32) % 4
                        c = m % 32
                        ins = te.matmul(
                            sc[p % 3][:, u * 512:(u + 1) * 512],
                            Kg[:, (hl // 2) * S + c * 128:(hl // 2) * S + (c + 1) * 128],
                            Qg[:, hl * T + qt * 512:hl * T + (qt + 1) * 512],
                            start=True, stop=True)
                    pesc.inc(ins)

                for p in range(3):
                    spair(p)
                for n in range(512):
                    i, c = divmod(n, 32)
                    hl = n // 128
                    if n % 2 == 0 and n // 2 + 3 < 256:
                        spair(n // 2 + 3)
                    if c == 0 and i >= 2:
                        te.wait_ge(dv.h, base_dv + 2 * (i - 1))
                    te.wait_ge(ac.h, base_ac + n // 2 + 1)
                    ins = te.matmul(
                        oacc[i % 2][:, :],
                        Vg[:, (c * HPG + hl) * 65:(c * HPG + hl) * 65 + 128],
                        ring[(n // 2) % 3][:, (n % 2) * 512:(n % 2 + 1) * 512],
                        start=(c == 0), stop=(c == 31))
                    peav.inc(ins)

            @SCA
            def _(se, base_sc=base_sc, base_av=base_av, dvpre=dv.n):
                se.wait_ge(dv.h, dvpre)
                for p in range(256):
                    se.wait_ge(pesc.h, base_sc + p + 1)
                    if p >= 3:
                        se.wait_ge(peav.h, base_av + 2 * p - 4)
                    ac.inc(se.activation(out=ring[p % 3][:, :], in_=sc[p % 3][:, :],
                                         func=AF.Exp, bias=negm8[:, :], scale=0.125))

            @VEC
            def _(ve, g=g, base_av=base_av, base_ss=base_ss):
                for i in range(16):
                    hl = i // 4
                    qt = i % 4
                    h = HPG * g + hl
                    ve.wait_ge(peav.h, base_av + 32 * (i + 1))
                    dv.inc(ve.tensor_copy(
                        out=O_u[(h % 2) * 64:(h % 2) * 64 + 64,
                                (h // 2) * T + qt * 512:(h // 2) * T + (qt + 1) * 512],
                        in_=oacc[i % 2][0:64, :]))
                    if base_ss + 16 * i > 0:
                        ve.wait_ge(ssem.h, base_ss + 16 * i)
                    dv.inc(ve.tensor_copy(out=sring[:, :], in_=oacc[i % 2][64:65, :]))

            @SYN
            def _(sy, g=g, base_dv=base_dv):
                for i in range(16):
                    hl = i // 4
                    qt = i % 4
                    h = HPG * g + hl
                    sy.wait_ge(dv.h, base_dv + 2 * (i + 1))
                    ssem.dma(sy.dma_start(out=S16[h:h + 1, qt * 512:(qt + 1) * 512],
                                          in_=sring[:, :]))

        ctxA2.close()

        ctxB3 = ExitStack()
        E3 = ctxB3.enter_context
        wo_s = E3(nc.sbuf_tensor("wo_s", [128, 8 * 1024], BF16))
        sclb = E3(nc.sbuf_tensor("sclb", [128, 2048], BF16))
        r_att = [E3(nc.sbuf_tensor(f"r_att{j}", [128, D], F32)) for j in range(2)]
        ln2sq = [E3(nc.sbuf_tensor(f"ln2sq{j}", [128, D], F32)) for j in range(2)]
        l2a = [E3(nc.sbuf_tensor(f"l2a{j}", [128, 1], F32)) for j in range(2)]
        l2b = [E3(nc.sbuf_tensor(f"l2b{j}", [128, 1], F32)) for j in range(2)]
        l2c = [E3(nc.sbuf_tensor(f"l2c{j}", [128, 1], F32)) for j in range(2)]
        attnb = [E3(nc.sbuf_tensor(f"attnb3_{j}", [128, D], BF16)) for j in range(2)]
        T16 = E3(nc.sbuf_tensor("T16", [16, T], F32))
        R16 = E3(nc.sbuf_tensor("R16", [16, T], BF16))
        psb = [E3(nc.psum_tensor(f"psb{j}", [128, 1024], F32)) for j in range(2)]

        @GPS
        def _(ge):
            fence(ge)
            ssem.w(ge)
            qG.dma(ge.dma_start(out=wo_s[:, :], in_=wo_h[:, :]))

        @SCA
        def _(se):
            ssem.w(se)
            fence(se)
            ac.inc(se.activation(out=T16[:, :], in_=S16[:, :], func=AF.Ln))
            ac.w(se)
            ac.inc(se.activation(out=R16[:, :], in_=T16[:, :], func=AF.Exp, scale=-1.0))

        peB = {}; dvB = {}
        for dc in range(8):
            for piece in range(2):
                t = dc * 2 + piece

                @TEN
                def _(te, dc=dc, piece=piece, t=t):
                    ac.w(te)
                    if t >= 2:
                        dv.w(te, dvB[t - 2])
                    for hb in range(2):
                        ins = te.matmul(
                            psb[t % 2][:, hb * 512:(hb + 1) * 512],
                            pi_s[:, dc * 128:(dc + 1) * 128],
                            R16[:, piece * 1024 + hb * 512:piece * 1024 + (hb + 1) * 512],
                            start=True, stop=True)
                    pe.inc(ins)
                    peB[t] = pe.n

                @VEC
                def _(ve, dc=dc, piece=piece, t=t):
                    pe.w(ve, peB[t])
                    dv.inc(ve.tensor_copy(out=sclb[:, piece * 1024:(piece + 1) * 1024],
                                          in_=psb[t % 2][:, :]))
                    dv.w(ve)
                    dv.inc(ve.tensor_mul(
                        out=O_u[:, dc * T + piece * 1024:dc * T + (piece + 1) * 1024],
                        in0=O_u[:, dc * T + piece * 1024:dc * T + (piece + 1) * 1024],
                        in1=sclb[:, piece * 1024:(piece + 1) * 1024]))
                    dvB[t] = dv.n

        peY = {}; acR = {}; dvL2 = {}; acL2 = {}
        for tt in range(16):
            @TEN
            def _(te, tt=tt):
                qG.w(te)
                if tt < 2:
                    dv.w(te, dvB[15])
                else:
                    ac.w(te, acR[tt - 2])
                for dc in range(8):
                    for hb in range(2):
                        te.matmul(psb[tt % 2][:, hb * 512:(hb + 1) * 512],
                                  O_u[:, dc * T + tt * 128:dc * T + (tt + 1) * 128],
                                  wo_s[:, dc * 1024 + hb * 512:dc * 1024 + (hb + 1) * 512],
                                  start=(dc == 0), stop=False)
                for hb in range(2):
                    ins = te.matmul(psb[tt % 2][:, hb * 512:(hb + 1) * 512],
                                    or_b[:, :], bo_b[:, hb * 512:(hb + 1) * 512],
                                    start=False, stop=True)
                pe.inc(ins)
                peY[tt] = pe.n

            @SCA
            def _(se, tt=tt):
                pe.w(se, peY[tt])
                if tt >= 2:
                    dv.w(se, dvL2[tt - 2])
                ac.inc(se.activation(out=r_att[tt % 2][:, :], in_=psb[tt % 2][:, :],
                                     func=AF.Relu))
                acR[tt] = ac.n

            jj = tt % 2
            ext0 = [(ac, acR[tt])]
            if tt >= 2:
                ext0 = ext0 + [(dv, dvL2[tt - 2]), (ac, acL2.get(tt - 2, 0))]
            ext9 = [(ascr, 16 * (tt - 1))] if tt >= 2 else []
            cts = pchain([
                (VEC, dv, (lambda ve, jj=jj, tt=tt: ve.reduce_sum(
                    out=l2a[jj][:, :], in_=r_att[jj][:, :], axis=AX)), [], ext0),
                (VEC, dv, (lambda ve, jj=jj: ve.tensor_scalar_mul(
                    out=l2a[jj][:, :], in0=l2a[jj][:, :], scalar1=1.0 / D)), [0], []),
                (VEC, dv, (lambda ve, jj=jj: ve.tensor_mul(
                    out=l2c[jj][:, :], in0=l2a[jj][:, :], in1=l2a[jj][:, :])), [1], []),
                (SCA, ac, (lambda se, jj=jj, tt=tt: se.activation(
                    out=ln2sq[jj][:, :], in_=r_att[jj][:, :], func=AF.Square)), [], ext0),
                (VEC, dv, (lambda ve, jj=jj: ve.reduce_sum(
                    out=l2b[jj][:, :], in_=ln2sq[jj][:, :], axis=AX)), [3], []),
                (VEC, dv, (lambda ve, jj=jj: ve.tensor_scalar_mul(
                    out=l2b[jj][:, :], in0=l2b[jj][:, :], scalar1=1.0 / D)), [4], []),
                (VEC, dv, (lambda ve, jj=jj: ve.tensor_sub(
                    out=l2b[jj][:, :], in0=l2b[jj][:, :], in1=l2c[jj][:, :])), [5, 2], []),
                (SCA, ac, (lambda se, jj=jj: se.activation(
                    out=l2b[jj][:, :], in_=l2b[jj][:, :], func=AF.Ln, bias=epsA[:, :])), [6], []),
                (SCA, ac, (lambda se, jj=jj: se.activation(
                    out=l2b[jj][:, :], in_=l2b[jj][:, :], func=AF.Exp, scale=-0.5)), [7], []),
                (VEC, dv, (lambda ve, jj=jj: ve.tensor_scalar(
                    out=attnb[jj][:, :], in0=r_att[jj][:, :],
                    scalar1=l2a[jj][:, :], scalar2=l2b[jj][:, :],
                    op0=ALU.subtract, op1=ALU.mult)), [8, 1], ext9),
            ])
            acL2[tt] = cts[8][1]
            dvL2[tt] = dv.n

            @SYN
            def _(sy, tt=tt):
                dv.w(sy, dvL2[tt])
                ascr.w(sy, 16 * tt)
                ascr.dma(sy.dma_start(out=attn_scr[tt * 128:(tt + 1) * 128, :],
                                      in_=attnb[tt % 2][:, :]))

        ctxB3.close()
        ctxA.close()

        ctxBB = ctx.enter_context(ExitStack())
        E4 = ctxBB.enter_context
        gaf = E4(nc.sbuf_tensor("gaf", [1, D], F32))
        bef = E4(nc.sbuf_tensor("bef", [1, D], F32))
        b1_s = E4(nc.sbuf_tensor("b1_s", [128, 256], F32))
        bo2_s = E4(nc.sbuf_tensor("bo2_s", [128, 8], F32))
        xe_sb = E4(nc.sbuf_tensor("xe_sb", [128, 16 * 1024], BF16))
        g_e = [E4(nc.sbuf_tensor(f"g_e{j}", [128, 8 * 256], BF16)) for j in range(2)]
        wo2b = E4(nc.sbuf_tensor("wo2b", [128, 4096], BF16))
        ytT = [E4(nc.sbuf_tensor(f"ytT{j}", [128, 256], F32)) for j in range(2)]
        fgt = [E4(nc.sbuf_tensor(f"fgt{j}", [128, D], BF16)) for j in range(2)]
        Gab = E4(nc.sbuf_tensor("Gab", [128, D], BF16))
        Beb = E4(nc.sbuf_tensor("Beb", [128, D], BF16))
        z3 = [E4(nc.sbuf_tensor(f"z3_{j}", [128, D], F32)) for j in range(2)]
        ln3sq = z3
        l3a = [E4(nc.sbuf_tensor(f"l3a{j}", [128, 1], F32)) for j in range(2)]
        l3b = [E4(nc.sbuf_tensor(f"l3b{j}", [128, 1], F32)) for j in range(2)]
        l3c = [E4(nc.sbuf_tensor(f"l3c{j}", [128, 1], F32)) for j in range(2)]
        xtile = [E4(nc.sbuf_tensor(f"xt4_{j}", [128, D], F32)) for j in range(2)]
        outb = [E4(nc.sbuf_tensor("outb0", [128, D], F32))] * 2
        attnb = [E4(nc.sbuf_tensor(f"attnb4_{j}", [128, D], BF16)) for j in range(2)]
        w1b4 = w1b + [E4(nc.sbuf_tensor(f"w1b4_{j}", [128, 8 * 256], BF16))
                      for j in range(2)]
        psz = [E4(nc.psum_tensor(f"psz{j}", [128, 1024], F32)) for j in range(2)]
        ptr = [E4(nc.psum_tensor(f"ptr{j}", [128, 128], F32)) for j in range(2)]
        ptrb = [E4(nc.psum_tensor(f"ptrb{j}", [128, 128], BF16)) for j in range(2)]

    

        @VEC
        def _(ve):
            ascr.w(ve)

        @SCA
        def _(se):
            ascr.w(se)

        @SYN
        def _(sy):
            ascr.w(sy)
            fence(sy)
            for dst, hh in [(gaf, ga_h), (bef, be_h), (b1_s, b1_h), (bo2_s, bo2_h)]:
                qS.dma(sy.dma_start(out=dst[:, :], in_=hh[:, :]))

        @TEN
        def _(te):
            qS.w(te)

        chain(
            (TEN, pe, lambda te: te.matmul(psz[0][:, 0:512], or_f[:, :], gaf[:, 0:512], start=True, stop=True)),
            (TEN, pe, lambda te: te.matmul(psz[0][:, 512:1024], or_f[:, :], gaf[:, 512:1024], start=True, stop=True)),
            (VEC, dv, lambda ve: ve.tensor_copy(out=Gab[:, :], in_=psz[0][:, :])),
            (TEN, pe, lambda te: te.matmul(psz[1][:, 0:512], or_f[:, :], bef[:, 0:512], start=True, stop=True)),
            (TEN, pe, lambda te: te.matmul(psz[1][:, 512:1024], or_f[:, :], bef[:, 512:1024], start=True, stop=True)),
            (VEC, dv, lambda ve: ve.tensor_copy(out=Beb[:, :], in_=psz[1][:, :])),
        )

        for d2c in range(8):
            @GPS
            def _(ge, d2c=d2c):
                if d2c == 0:
                    fence(ge)
                else:
                    wcst.w(ge)
                qG.dma(ge.dma_start(out=wo2b[:, :], in_=wo2_h[d2c]))

            @SYN
            def _(sy, d2c=d2c):
                qG.w(sy)
                wcst.dma(sy.dma_start(out=wo2bf[d2c], in_=wo2b[:, :]))

        xnsc_total = xnsc.n
        isem_scat = isem.n
        for half in range(2):
            ctxH1 = ExitStack()
            xeT = ctxH1.enter_context(nc.sbuf_tensor(f"xeT{half}", [128, 8 * 2048], BF16))

            @SYN
            def _(sy, half=half):
                isem.w(sy, isem_scat)
                fsem.w(sy)
                fsem.dma(sy.dma_start(
                    out=xe_sb[:, :].rearrange("p (c m) -> p c m", c=16),
                    in_=xe_scr[half][0:2048, :].rearrange("(c p) m -> p c m", p=128)))

            peXT = {}
            for sb in range(16):
                for dcb in range(8):
                    b = sb * 8 + dcb

                    @TEN
                    def _(te, sb=sb, dcb=dcb, b=b):
                        if b == 0:
                            fsem.w(te)
                        if b >= 2:
                            dv.w(te, peXT[b - 2])
                        pe.inc(te.transpose(ptrb[b % 2][:, :],
                                            xe_sb[:, sb * 1024 + dcb * 128:sb * 1024 + (dcb + 1) * 128],
                                            idb_s[:, :]))

                    @VEC
                    def _(ve, sb=sb, dcb=dcb, b=b):
                        pe.w(ve)
                        dv.inc(ve.tensor_copy(
                            out=xeT[:, dcb * 2048 + sb * 128:dcb * 2048 + (sb + 1) * 128],
                            in_=ptrb[b % 2][:, :]))
                        peXT[b] = dv.n
            dvXET = dv.n

            base_pez = pe.n
            base_acA = ac.n
            pe_after_z = {}
            cnt = base_pez
            for z in range(256):
                cnt += 1
                pe_after_z[z] = cnt
            ac_after_z = {z: base_acA + z + 1 for z in range(256)}

            @SYN
            def _(sy, half=half, base_pez=base_pez, pe_after_z=pe_after_z):
                if half == 0:
                    for jw in range(4):
                        w1c[jw].w(sy)
                for k in range(128):
                    e, iq4 = divmod(k, 16)
                    if k >= 4:
                        sy.wait_ge(pe.h, pe_after_z[2 * k - 7])
                    elif half == 1:
                        sy.wait_ge(pe.h, base_pez)
                    w1r[k % 4].dma(sy.dma_start(out=w1b4[k % 4][:, :], in_=w1bf[e, iq4]))

            @TEN
            def _(te, half=half, ac_after_z=ac_after_z, dvXET=dvXET):
                te.wait_ge(dv.h, dvXET)
                for e in range(NE):
                    for ic in range(32):
                        z = e * 32 + ic
                        kg = half * 128 + e * 16 + ic // 2
                        te.wait_ge(w1r[kg % 4].h, 16 * (kg // 4 + 1))
                        if z >= 4:
                            te.wait_ge(ac.h, ac_after_z[z - 4])
                        for dc in range(8):
                            ins = te.matmul(
                                psz[z % 2][:, (z // 2 % 2) * 512:(z // 2 % 2) * 512 + 256],
                                w1b4[kg % 4][:, dc * 256 + (ic % 2) * 128:
                                             dc * 256 + (ic % 2 + 1) * 128],
                                xeT[:, dc * 2048 + e * 256:dc * 2048 + (e + 1) * 256],
                                start=(dc == 0), stop=(dc == 7))
                        pe.inc(ins)

            @SCA
            def _(se, half=half, pe_after_z=pe_after_z):
                for e in range(NE):
                    for ic in range(32):
                        z = e * 32 + ic
                        se.wait_ge(pe.h, pe_after_z[z])
                        if ic % 8 == 0 and z >= 16:
                            bz = half * 32 + z // 8
                            gsem[bz % 2].w(se, 16 * ((bz - 2) // 2 + 1))
                        ac.inc(se.activation(out=g_e[(z // 8) % 2][:, (ic % 8) * 256:(ic % 8 + 1) * 256],
                                             in_=psz[z % 2][:, (z // 2 % 2) * 512:(z // 2 % 2) * 512 + 256],
                                             func=AF.Gelu,
                                             bias=b1_s[:, ic * 8 + e:ic * 8 + e + 1]))

            @GPS
            def _(ge, half=half, ac_after_z=ac_after_z):
                for e in range(NE):
                    for icb in range(4):
                        ge.wait_ge(ac.h, ac_after_z[e * 32 + icb * 8 + 7])
                        gsem[(e * 4 + icb) % 2].dma(ge.dma_start(
                            out=hg_scr[icb * 1024:(icb + 1) * 1024,
                                       e * 256:(e + 1) * 256]
                                .rearrange("(i p) s -> p i s", p=128),
                            in_=g_e[(e * 4 + icb) % 2][:, :]
                                .rearrange("p (i s) -> p i s", i=8)))

            ctxH1.close()

            ctxH2 = ExitStack()
            hgq = [ctxH2.enter_context(nc.sbuf_tensor(f"hgq{half}_{j}", [128, 32 * 256], BF16))
                   for j in range(2)]
            wo2d = [ctxH2.enter_context(nc.sbuf_tensor(f"wo2d{half}_{j}", [128, 4096], BF16))
                    for j in range(2)]
            peB1 = {}; dvS = {}; dvYe = {}
            gsem_all = (gsem[0].n, gsem[1].n)
            for sck in range(8):
                @SYN
                def _(sy, half=half, sck=sck):
                    sy.wait_ge(gsem[0].h, gsem_all[0])
                    sy.wait_ge(gsem[1].h, gsem_all[1])
                    if sck >= 2:
                        pe.w(sy, peB1[(sck - 2) * 8 + 7])
                    for ih in range(2):
                        hsem[sck % 2].dma(sy.dma_start(
                            out=hgq[sck % 2][:, ih * 4096:(ih + 1) * 4096]
                                .rearrange("p (i s) -> p i s", i=16),
                            in_=hg_scr[ih * 2048:(ih + 1) * 2048,
                                       sck * 256:(sck + 1) * 256]
                                .rearrange("(i p) s -> p i s", p=128)))

                for d2c in range(8):
                    t = sck * 8 + d2c

                    @GPS
                    def _(ge, d2c=d2c, t=t):
                        if t >= 2:
                            pe.w(ge, peB1.get(t - 2, 0))
                        elif t == 0:
                            wcst.w(ge)
                            pe.w(ge)
                        wo2s[t % 2].dma(ge.dma_start(out=wo2d[t % 2][:, :], in_=wo2bf[d2c]))

                    @TEN
                    def _(te, sck=sck, d2c=d2c, t=t, half=half):
                        gsck = half * 8 + sck
                        te.wait_ge(hsem[sck % 2].h, 32 * (gsck // 2 + 1))
                        te.wait_ge(wo2s[t % 2].h, 16 * ((half * 64 + t) // 2 + 1))
                        if t >= 4:
                            dv.w(te, dvS[t - 4])
                        for ic in range(32):
                            ins = te.matmul(
                                psz[t % 2][:, (t // 2 % 2) * 512:(t // 2 % 2) * 512 + 256],
                                wo2d[t % 2][:, ic * 128:(ic + 1) * 128],
                                hgq[sck % 2][:, ic * 256:(ic + 1) * 256],
                                start=(ic == 0), stop=(ic == 31))
                        pe.inc(ins)
                        peB1[t] = pe.n

                    @VEC
                    def _(ve, half=half, sck=sck, d2c=d2c, t=t):
                        pe.w(ve, peB1[t])
                        dv.inc(ve.tensor_scalar_add(
                            out=ytT[t % 2][:, :],
                            in0=psz[t % 2][:, (t // 2 % 2) * 512:(t // 2 % 2) * 512 + 256],
                            scalar1=bo2_s[:, d2c:d2c + 1]))
                        dvS[t] = dv.n

                    for blk in range(2):
                        bg = t * 2 + blk

                        @TEN
                        def _(te, t=t, blk=blk, bg=bg):
                            dv.w(te, dvS[t])
                            if bg >= 2:
                                dv.w(te, dvYe[bg - 2])
                            pe.inc(te.transpose(ptr[bg % 2][:, :],
                                                ytT[t % 2][:, blk * 128:(blk + 1) * 128],
                                                idf_s[:, :]))

                        @VEC
                        def _(ve, sck=sck, d2c=d2c, blk=blk, bg=bg):
                            pe.w(ve)
                            sb = sck * 2 + blk
                            sl = slice(sb * 1024 + d2c * 128, sb * 1024 + (d2c + 1) * 128)
                            dv.inc(ve.tensor_tensor(
                                out=xe_sb[:, sl], in0=ptr[bg % 2][:, :],
                                in1=xe_sb[:, sl], op=ALU.add))
                            dvYe[bg] = dv.n

            ctxH2.close()

            @SYN
            def _(sy):
                dv.w(sy)
                fsem.dma(sy.dma_start(
                    out=fg_scr[:, :].rearrange("(c p) m -> p c m", p=128),
                    in_=xe_sb[:, :].rearrange("p (c m) -> p c m", c=16)))

            if half == 0:
                ln3_state = {"dvTS": {}, "dvCmb": {}, "stC": {}, "isG": {}}
            dvTS = ln3_state["dvTS"]; dvCmb = ln3_state["dvCmb"]
            stC = ln3_state["stC"]; isG = ln3_state["isG"]
            for tb in range(8):
                tt = half * 8 + tb
                gb = half * 8 + tb
                j = tb % 2

                @GPS
                def _(ge, tb=tb, j=j, half=half, gb=gb):
                    fsem.w(ge)
                    isem.w(ge)
                    if gb >= 2:
                        dv.w(ge, dvTS[gb - 2])
                    isem.dma(ge.indirect_dma_start(
                        out=fgt[j][:, :], out_offset=None,
                        in_=fg_scr[:, :],
                        in_offset=bass.IndirectOffsetOnAxis(
                            desti2[:, half * 8 + tb:half * 8 + tb + 1], 0)))
                    isG[gb] = isem.n

                @SYN
                def _(sy, tt=tt, j=j, gb=gb):
                    if gb >= 1:
                        st.w(sy, 16 * gb)
                    if gb >= 2:
                        dv.w(sy, dvCmb[gb - 2])
                    xs[j].dma(sy.dma_start(out=xtile[j][:, :],
                                           in_=x_h[tt * 128:(tt + 1) * 128, :]))
                    asl[j].dma(sy.dma_start(out=attnb[j][:, :],
                                            in_=attn_scr[tt * 128:(tt + 1) * 128, :]))

                extg = [(isem, isG[gb])]
                if gb >= 2:
                    extg = extg + [(dv, dvCmb[gb - 2])]
                cts = pchain([
                    (VEC, dv, (lambda ve, j=j: ve.reduce_sum(
                        out=l3a[j][:, :], in_=fgt[j][:, :], axis=AX)), [], extg),
                    (VEC, dv, (lambda ve, j=j: ve.tensor_scalar_mul(
                        out=l3a[j][:, :], in0=l3a[j][:, :], scalar1=1.0 / D)), [0], []),
                    (VEC, dv, (lambda ve, j=j: ve.tensor_mul(
                        out=l3c[j][:, :], in0=l3a[j][:, :], in1=l3a[j][:, :])), [1], []),
                    (SCA, ac, (lambda se, j=j: se.activation(
                        out=ln3sq[j][:, :], in_=fgt[j][:, :], func=AF.Square)), [], extg),
                    (VEC, dv, (lambda ve, j=j: ve.reduce_sum(
                        out=l3b[j][:, :], in_=ln3sq[j][:, :], axis=AX)), [3], []),
                    (VEC, dv, (lambda ve, j=j: ve.tensor_scalar_mul(
                        out=l3b[j][:, :], in0=l3b[j][:, :], scalar1=1.0 / D)), [4], []),
                    (VEC, dv, (lambda ve, j=j: ve.tensor_sub(
                        out=l3b[j][:, :], in0=l3b[j][:, :], in1=l3c[j][:, :])), [5, 2], []),
                    (SCA, ac, (lambda se, j=j: se.activation(
                        out=l3b[j][:, :], in_=l3b[j][:, :], func=AF.Ln,
                        bias=epsC[:, :])), [6], []),
                    (SCA, ac, (lambda se, j=j: se.activation(
                        out=l3b[j][:, :], in_=l3b[j][:, :], func=AF.Exp,
                        scale=-0.5)), [7], []),
                    (VEC, dv, (lambda ve, j=j: ve.tensor_scalar(
                        out=z3[j][:, :], in0=fgt[j][:, :], scalar1=l3a[j][:, :],
                        scalar2=l3b[j][:, :],
                        op0=ALU.subtract, op1=ALU.mult)), [8, 1], []),
                    (VEC, dv, (lambda ve, j=j: ve.tensor_mul(
                        out=z3[j][:, :], in0=z3[j][:, :], in1=Gab[:, :])), [9], []),
                    (VEC, dv, (lambda ve, j=j: ve.tensor_add(
                        out=z3[j][:, :], in0=z3[j][:, :], in1=Beb[:, :])), [10], []),
                ])
                dvTS[gb] = cts[11][1]

                @VEC
                def _(ve, tt=tt, j=j, gb=gb):
                    xs[j].w(ve)
                    asl[j].w(ve)
                    dv.w(ve, dvTS[gb])
                    if gb >= 1:
                        st.w(ve, 16 * gb)
                    dv.inc(ve.tensor_add(out=outb[j][:, :], in0=z3[j][:, :], in1=xtile[j][:, :]))
                    dv.w(ve)
                    dv.inc(ve.tensor_add(out=outb[j][:, :], in0=outb[j][:, :], in1=attnb[j][:, :]))
                    dvCmb[gb] = dv.n

                @SYN
                def _(sy, tt=tt, j=j, gb=gb):
                    dv.w(sy, dvCmb[gb])
                    st.dma(sy.dma_start(out=y_h[tt * 128:(tt + 1) * 128, :],
                                        in_=outb[j][:, :]))

        @SYN
        def _(sy):
            st.w(sy)

    return nc


_NC_CACHE = {}


def _get_nc():
    if "nc" not in _NC_CACHE:
        _NC_CACHE["nc"] = build()
    return _NC_CACHE["nc"]


def kernel(L0_attn_x, L0_attn_Wq, L0_attn_bq, L0_attn_Wk, L0_attn_bk,
           L0_attn_Wv, L0_attn_bv, L0_attn_Wo, L0_attn_bo,
           L0_ffn_Wr, L0_ffn_W1, L0_ffn_b1, L0_ffn_Wo, L0_ffn_bo,
           L0_ffn_gamma, L0_ffn_beta):
    f = np.asarray
    x = np.ascontiguousarray(f(L0_attn_x, dtype=np.float32))
    Wq = f(L0_attn_Wq, dtype=np.float32); Wk = f(L0_attn_Wk, dtype=np.float32)
    Wv = f(L0_attn_Wv, dtype=np.float32); Wo = f(L0_attn_Wo, dtype=np.float32)
    Wr = f(L0_ffn_Wr, dtype=np.float32); W1 = f(L0_ffn_W1, dtype=np.float32)
    Wo2 = f(L0_ffn_Wo, dtype=np.float32)

    def qkv_lay(W):
        return np.ascontiguousarray(
            W.reshape(8, 128, 4, 256).transpose(2, 1, 0, 3).reshape(4, 128, 8 * 256))

    wq_c = qkv_lay(Wq); wk_c = qkv_lay(Wk); wv_c = qkv_lay(Wv)
    wo_c = np.ascontiguousarray(
        Wo.reshape(8, 128, 1024).transpose(1, 0, 2).reshape(128, 8192))
    wrt_c = np.ascontiguousarray(
        Wr.T.reshape(8, 128, 8).transpose(1, 0, 2).reshape(128, 64))
    w1_c = np.ascontiguousarray(
        W1.reshape(8, 8, 128, 16, 256).transpose(0, 3, 2, 1, 4).reshape(8, 16, 128, 8 * 256))
    wo2_c = np.ascontiguousarray(
        Wo2.reshape(32, 128, 8, 128).transpose(2, 1, 0, 3).reshape(8, 128, 32 * 128))
    bq_t = np.ascontiguousarray(f(L0_attn_bq, dtype=np.float32).reshape(8, 128).T)
    bk_t = np.ascontiguousarray(f(L0_attn_bk, dtype=np.float32).reshape(8, 128).T)
    bv_r = f(L0_attn_bv, dtype=np.float32).reshape(1, 1024)
    bo_r = f(L0_attn_bo, dtype=np.float32).reshape(1, 1024)
    b1_t = np.ascontiguousarray(
        f(L0_ffn_b1, dtype=np.float32).T.reshape(32, 128, 8).transpose(1, 0, 2).reshape(128, 256))
    bo2_t = np.ascontiguousarray(f(L0_ffn_bo, dtype=np.float32).reshape(8, 128).T)
    ga_r = f(L0_ffn_gamma, dtype=np.float32).reshape(1, 1024)
    be_r = f(L0_ffn_beta, dtype=np.float32).reshape(1, 1024)

    shared = dict(wq_c=wq_c, wk_c=wk_c, wv_c=wv_c, wo_c=wo_c, wrt_c=wrt_c,
                  w1_c=w1_c, wo2_c=wo2_c, bq_t=bq_t, bk_t=bk_t, bv_r=bv_r,
                  bo_r=bo_r, b1_t=b1_t, bo2_t=bo2_t, ga_r=ga_r, be_r=be_r)
    in_maps = []
    for c in range(8):
        b, half = divmod(c, 2)
        own = x[b, half * 2048:(half + 1) * 2048]
        par = x[b, (1 - half) * 2048:(2 - half) * 2048]
        m = dict(shared)
        m["xseq"] = np.ascontiguousarray(np.concatenate([own, par], axis=0))
        in_maps.append(m)

    nc = _get_nc()
    res = run_bass_kernel_spmd(nc, in_maps, core_ids=list(range(8)))
    out = np.empty((4, 4096, 1024), dtype=np.float32)
    for c in range(8):
        b, half = divmod(c, 2)
        out[b, half * 2048:(half + 1) * 2048] = res.results[c]["y"]
    return out


if __name__ == "__main__":
    nc = build()
    print("build ok")


# version 23
# speedup vs baseline: 1.0958x; 1.0181x over previous
import numpy as np
import ml_dtypes
from contextlib import ExitStack

import concourse.bass as bass
import concourse.mybir as mybir
from concourse.bass_utils import run_bass_kernel_spmd

F32 = mybir.dt.float32
BF16 = mybir.dt.bfloat16
AF = mybir.ActivationFunctionType
ALU = mybir.AluOpType
AX = mybir.AxisListType.X
BF = ml_dtypes.bfloat16

D = 1024
S = 4096
T = 2048
H = 16
NE = 8
NG = 4
HPG = 4
NI = 4096
EPS1 = 1e-5
EPS3 = 1e-12


class Sem:
    def __init__(self, ctx, nc, name):
        self.h = ctx.enter_context(nc.semaphore(name))
        self.n = 0

    def inc(self, ins, k=1):
        ins.then_inc(self.h, k)
        self.n += k

    def dma(self, ins):
        self.inc(ins, 16)

    def w(self, eng, at=None):
        v = self.n if at is None else at
        if v > 0:
            eng.wait_ge(self.h, v)


def build():
    nc = bass.Bass("TRN2", target_bir_lowering=False, debug=False)

    x_h = nc.dram_tensor("xseq", [S, D], F32, kind="ExternalInput").ap()
    wq_h = nc.dram_tensor("wq_c", [NG, 128, 8 * 256], F32, kind="ExternalInput").ap()
    wk_h = nc.dram_tensor("wk_c", [NG, 128, 8 * 256], F32, kind="ExternalInput").ap()
    wv_h = nc.dram_tensor("wv_c", [NG, 128, 8 * 256], F32, kind="ExternalInput").ap()
    wo_h = nc.dram_tensor("wo_c", [128, 8 * 1024], F32, kind="ExternalInput").ap()
    wrt_h = nc.dram_tensor("wrt_c", [128, 64], F32, kind="ExternalInput").ap()
    w1_h = nc.dram_tensor("w1_c", [NE, 16, 128, 8 * 256], F32, kind="ExternalInput").ap()
    wo2_h = nc.dram_tensor("wo2_c", [8, 128, 32 * 128], F32, kind="ExternalInput").ap()
    bq_h = nc.dram_tensor("bq_t", [128, 8], F32, kind="ExternalInput").ap()
    bk_h = nc.dram_tensor("bk_t", [128, 8], F32, kind="ExternalInput").ap()
    bv_h = nc.dram_tensor("bv_r", [1, D], F32, kind="ExternalInput").ap()
    bo_h = nc.dram_tensor("bo_r", [1, D], F32, kind="ExternalInput").ap()
    b1_h = nc.dram_tensor("b1_t", [128, 32 * 8], F32, kind="ExternalInput").ap()
    bo2_h = nc.dram_tensor("bo2_t", [128, 8], F32, kind="ExternalInput").ap()
    ga_h = nc.dram_tensor("ga_r", [1, D], F32, kind="ExternalInput").ap()
    be_h = nc.dram_tensor("be_r", [1, D], F32, kind="ExternalInput").ap()
    y_h = nc.dram_tensor("y", [T, D], F32, kind="ExternalOutput").ap()
    attn_scr = nc.dram_tensor("attn_scr", [T, D], BF16, kind="Internal").ap()
    xn_scr = nc.dram_tensor("xn_scr", [T, D], BF16, kind="Internal").ap()
    xe_scr = [nc.dram_tensor(f"xe_scr{h}", [2049, D], BF16, kind="Internal").ap()
              for h in range(2)]
    hg_scr = nc.dram_tensor("hg_scr", [NI, 2048], BF16, kind="Internal").ap()
    fg_scr = nc.dram_tensor("fg_scr", [2048, D], BF16, kind="Internal").ap()
    w1bf = nc.dram_tensor("w1bf", [NE, 16, 128, 8 * 256], BF16, kind="Internal").ap()
    wo2bf = nc.dram_tensor("wo2bf", [8, 128, 4096], BF16, kind="Internal").ap()

    idf = nc.inline_tensor(np.eye(128, dtype=np.float32), name="idf").ap()
    idb = nc.inline_tensor(np.eye(128, dtype=BF), name="idb").ap()
    ones_r_b = nc.inline_tensor(np.ones((1, 128), dtype=BF), name="ones_r_b").ap()
    ones_r_f = nc.inline_tensor(np.ones((1, 128), dtype=np.float32), name="ones_r_f").ap()
    ones_c_b = nc.inline_tensor(np.ones((128, 1), dtype=BF), name="ones_c_b").ap()
    ones_c_f = nc.inline_tensor(np.ones((128, 1), dtype=np.float32), name="ones_c_f").ap()
    w8c = nc.inline_tensor(
        np.broadcast_to(8.0 - np.arange(8, dtype=np.float32), (128, 8)).copy(),
        name="w8c").ap()
    pi_np = np.zeros((16, 8 * 128), dtype=BF)
    for dc in range(8):
        pi_np[2 * dc, dc * 128:dc * 128 + 64] = 1
        pi_np[2 * dc + 1, dc * 128 + 64:dc * 128 + 128] = 1
    pairind = nc.inline_tensor(pi_np, name="pairind").ap()
    ecol1 = nc.inline_tensor(
        (np.arange(8, dtype=np.float32) * 256.0 - 1.0).reshape(8, 1), name="ecol1").ap()

    ctx = ExitStack()
    with ctx:
        E = ctx.enter_context

        xnT_own = E(nc.sbuf_tensor("xnT_own", [128, 8 * T], BF16))
        rowf = E(nc.sbuf_tensor("rowf", [1, D], F32))
        bv_b = E(nc.sbuf_tensor("bv_b", [1, D], BF16))
        bo_b = E(nc.sbuf_tensor("bo_b", [1, D], BF16))
        idf_s = E(nc.sbuf_tensor("idf_s", [128, 128], F32))
        idb_s = E(nc.sbuf_tensor("idb_s", [128, 128], BF16))
        or_b = E(nc.sbuf_tensor("or_b", [1, 128], BF16))
        or_f = E(nc.sbuf_tensor("or_f", [1, 128], F32))
        oc_b = E(nc.sbuf_tensor("oc_b", [128, 1], BF16))
        oc_f = E(nc.sbuf_tensor("oc_f", [128, 1], F32))
        w8_s = E(nc.sbuf_tensor("w8_s", [128, 8], F32))
        pi_s = E(nc.sbuf_tensor("pi_s", [16, 8 * 128], BF16))
        ec_s = E(nc.sbuf_tensor("ec_s", [8, 1], F32))
        wrt_s = E(nc.sbuf_tensor("wrt_s", [128, 64], F32))
        bq_s = E(nc.sbuf_tensor("bq_s", [128, 8], F32))
        bk_s = E(nc.sbuf_tensor("bk_s", [128, 8], F32))
        epsA = E(nc.sbuf_tensor("epsA", [128, 1], F32))
        epsC = E(nc.sbuf_tensor("epsC", [128, 1], F32))
        maskT = E(nc.sbuf_tensor("maskT", [8, T], BF16))
        desti2 = E(nc.sbuf_tensor("desti2", [128, 16], mybir.dt.int32))
        w1b = [E(nc.sbuf_tensor(f"w1b{j}", [128, 8 * 256], BF16)) for j in range(2)]

        qS = Sem(ctx, nc, "qS")
        qG = Sem(ctx, nc, "qG")
        xs = [Sem(ctx, nc, f"xs{j}") for j in range(2)]
        asl = [Sem(ctx, nc, f"asl{j}") for j in range(2)]
        w1s = [Sem(ctx, nc, f"w1s{j}") for j in range(2)]
        wo2s = [Sem(ctx, nc, f"wo2s{j}") for j in range(2)]
        pe = Sem(ctx, nc, "pe")
        pesc = Sem(ctx, nc, "pesc")
        peav = Sem(ctx, nc, "peav")
        ac = Sem(ctx, nc, "ac")
        dv = Sem(ctx, nc, "dv")
        ssem = Sem(ctx, nc, "ssem")
        ascr = Sem(ctx, nc, "ascr")
        gp = Sem(ctx, nc, "gp")
        xnsc = Sem(ctx, nc, "xnsc")
        isem = Sem(ctx, nc, "isem")
        gsem = [Sem(ctx, nc, f"gsem{j}") for j in range(2)]
        hsem = [Sem(ctx, nc, f"hsem{j}") for j in range(2)]
        fsem = Sem(ctx, nc, "fsem")
        wcst = Sem(ctx, nc, "wcst")
        gscat = Sem(ctx, nc, "gscat")
        w1c = [Sem(ctx, nc, f"w1c{j}") for j in range(4)]
        w1g = [Sem(ctx, nc, f"w1g{j}") for j in range(4)]
        w1r = [Sem(ctx, nc, f"w1r{j}") for j in range(4)]
        st = Sem(ctx, nc, "st")
        COMPUTE = [pe, pesc, peav, ac, dv]

        block = E(nc.Block())
        TEN, VEC, SCA, GPS, SYN = (block.tensor, block.vector, block.scalar,
                                   block.gpsimd, block.sync)

        def fence(eng):
            for s in COMPUTE:
                s.w(eng)

        def chain(*steps):
            for secwrap, sem, fn in steps:
                def body(eng, fn=fn, sem=sem):
                    fence(eng)
                    sem.inc(fn(eng))
                secwrap(body)

        def pchain(steps):
            counts = []
            for secwrap, sem, fn, deps, extra in steps:
                def body(eng, fn=fn, sem=sem, deps=tuple(deps), extra=tuple(extra)):
                    for s_, v_ in extra:
                        s_.w(eng, v_)
                    for d_ in deps:
                        ds, dval = counts[d_]
                        ds.w(eng, dval)
                    sem.inc(fn(eng))
                secwrap(body)
                counts.append((sem, sem.n))
            return counts

        @SYN
        def _(sy):
            for dst, src in [(idf_s, idf), (idb_s, idb), (or_b, ones_r_b),
                             (or_f, ones_r_f), (oc_b, ones_c_b), (oc_f, ones_c_f), (w8_s, w8c),
                             (pi_s, pairind), (ec_s, ecol1), (wrt_s, wrt_h),
                             (bq_s, bq_h), (bk_s, bk_h),
                             (rowf, bv_h)]:
                qS.dma(sy.dma_start(out=dst[:, :], in_=src[:, :]))

        @VEC
        def _(ve):
            qS.w(ve)
            dv.inc(ve.memset(epsA[:, :], EPS1))
            dv.inc(ve.memset(epsC[:, :], EPS3))
            dv.inc(ve.tensor_copy(out=bv_b[:, :], in_=rowf[:, :]))

        @SYN
        def _(sy):
            dv.w(sy)
            qS.dma(sy.dma_start(out=rowf[:, :], in_=bo_h[:, :]))

        @VEC
        def _(ve):
            qS.w(ve)
            dv.inc(ve.tensor_copy(out=bo_b[:, :], in_=rowf[:, :]))

        ctxA = ctx.enter_context(ExitStack())
        EA = ctxA.enter_context
        xnT_par = EA(nc.sbuf_tensor("xnT_par", [128, 8 * T], BF16))
        scstg = [EA(nc.sbuf_tensor(f"scstg{j}", [128, D], BF16)) for j in range(2)]
        O_u = EA(nc.sbuf_tensor("O_u", [128, 8 * T], BF16))

        ctxA1 = ExitStack()
        EA1 = ctxA1.enter_context
        xtile = [EA1(nc.sbuf_tensor(f"xt1_{j}", [128, D], F32)) for j in range(2)]
        xnb = [EA1(nc.sbuf_tensor(f"xnb{j}", [128, D], BF16)) for j in range(2)]
        sqs = [EA1(nc.sbuf_tensor(f"sqs{j}", [128, D], F32)) for j in range(2)]
        xnf = [EA1(nc.sbuf_tensor(f"xnf{j}", [128, D], F32)) for j in range(2)]
        xtf = EA1(nc.sbuf_tensor("xtf", [128, 256], F32))
        st1 = [EA1(nc.sbuf_tensor(f"st1_{j}", [128, 1], F32)) for j in range(2)]
        st2 = [EA1(nc.sbuf_tensor(f"st2_{j}", [128, 1], F32)) for j in range(2)]
        st3 = [EA1(nc.sbuf_tensor(f"st3_{j}", [128, 1], F32)) for j in range(2)]
        st4 = [EA1(nc.sbuf_tensor(f"st4_{j}", [128, 1], F32)) for j in range(2)]
        lgs = EA1(nc.sbuf_tensor("lgs", [128, 8], F32))
        eqs = EA1(nc.sbuf_tensor("eqs", [128, 8], F32))
        v8s = EA1(nc.sbuf_tensor("v8s", [128, 8], F32))
        mkb = EA1(nc.sbuf_tensor("mkb", [128, 8], BF16))
        psT = [EA1(nc.psum_tensor(f"psT{j}", [128, 128], F32)) for j in range(3)]
        pidx = EA1(nc.psum_tensor("pidx", [1, 1024], F32))
        pslog = [EA1(nc.psum_tensor(f"pslog{j}", [128, 8], F32)) for j in range(2)]
        psmT = EA1(nc.psum_tensor("psmT", [8, 128], BF16))

        NT = S // 128
        acSq = {}; acRstd = {}; dvXnf = {}; dvS2 = {}; dvVar = {}; dvEvac = {}
        dvLog = {}; dvMkb = {}; dvMT = {}; peTr = {}; peLog = {}; peMT = {}

        for tt in range(NT):
            j = tt % 2
            own = tt < 16

            @SYN
            def _(sy, tt=tt, j=j):
                if tt >= 2:
                    ac.w(sy, acSq[tt - 2])
                    dv.w(sy, dvXnf[tt - 2])
                xs[j].dma(sy.dma_start(out=xtile[j][:, :],
                                       in_=x_h[tt * 128:(tt + 1) * 128, :]))

            @SCA
            def _(se, tt=tt, j=j):
                xs[j].w(se)
                if tt >= 2:
                    dv.w(se, dvS2[tt - 2])
                ac.inc(se.activation(out=sqs[j][:, :], in_=xtile[j][:, :], func=AF.Square))
                acSq[tt] = ac.n

            @VEC
            def _(ve, tt=tt, j=j):
                xs[j].w(ve)
                if tt >= 2:
                    dv.w(ve, dvXnf[tt - 2])
                    ac.w(ve, acRstd[tt - 2])
                dv.inc(ve.reduce_sum(out=st1[j][:, :], in_=xtile[j][:, :], axis=AX))
                dv.w(ve)
                dv.inc(ve.tensor_scalar_mul(out=st1[j][:, :], in0=st1[j][:, :], scalar1=1.0 / D))
                dv.w(ve)
                dv.inc(ve.tensor_mul(out=st3[j][:, :], in0=st1[j][:, :], in1=st1[j][:, :]))
                dv.w(ve)
                ac.w(ve, acSq[tt])
                dv.inc(ve.reduce_sum(out=st2[j][:, :], in_=sqs[j][:, :], axis=AX))
                dvS2[tt] = dv.n
                dv.w(ve)
                dv.inc(ve.tensor_scalar_mul(out=st2[j][:, :], in0=st2[j][:, :], scalar1=1.0 / D))
                dv.w(ve)
                dv.inc(ve.tensor_sub(out=st2[j][:, :], in0=st2[j][:, :], in1=st3[j][:, :]))
                dvVar[tt] = dv.n

            @SCA
            def _(se, tt=tt, j=j):
                dv.w(se, dvVar[tt])
                ac.inc(se.activation(out=st4[j][:, :], in_=st2[j][:, :], func=AF.Ln, bias=epsA[:, :]))
                ac.w(se)
                ac.inc(se.activation(out=st4[j][:, :], in_=st4[j][:, :], func=AF.Exp, scale=-0.5))
                acRstd[tt] = ac.n

            @VEC
            def _(ve, tt=tt, j=j, own=own):
                ac.w(ve, acRstd[tt])
                if tt >= 2:
                    pe.w(ve, peTr[(tt - 2, 7)])
                if own:
                    if tt >= 2:
                        xnsc.w(ve, 16 * (tt - 1))
                    dv.inc(ve.tensor_scalar(out=xnb[tt % 2][:, :], in0=xtile[j][:, :],
                                            scalar1=st1[j][:, :], scalar2=st4[j][:, :],
                                            op0=ALU.subtract, op1=ALU.mult))
                dv.inc(ve.tensor_scalar(out=xnf[j][:, :], in0=xtile[j][:, :],
                                        scalar1=st1[j][:, :], scalar2=st4[j][:, :],
                                        op0=ALU.subtract, op1=ALU.mult))
                dvXnf[tt] = dv.n

            if own:
                @SYN
                def _(sy, tt=tt):
                    dv.w(sy, dvXnf[tt])
                    xnsc.w(sy, 16 * tt)
                    xnsc.dma(sy.dma_start(out=xn_scr[tt * 128:(tt + 1) * 128, :],
                                          in_=xnb[tt % 2][:, :]))

            dst = xnT_own if own else xnT_par
            toff = (tt % 16) * 128
            for dc in range(8):
                pj = (tt * 8 + dc) % 3

                @TEN
                def _(te, tt=tt, dc=dc, j=j, pj=pj):
                    dv.w(te, dvXnf[tt])
                    prior = (tt * 8 + dc) - 3
                    if prior >= 0:
                        dv.w(te, dvEvac[(prior // 8, prior % 8)])
                    pe.inc(te.transpose(psT[pj][:, :],
                                        xnf[j][:, dc * 128:(dc + 1) * 128], idf_s[:, :]))
                    peTr[(tt, dc)] = pe.n

                @VEC
                def _(ve, tt=tt, dc=dc, dst=dst, toff=toff, own=own, pj=pj):
                    pe.w(ve, peTr[(tt, dc)])
                    dv.inc(ve.tensor_copy(out=dst[:, dc * T + toff:dc * T + toff + 128],
                                          in_=psT[pj][:, :]))
                    if own:
                        dv.inc(ve.tensor_copy(
                            out=xtf[:, (dc % 2) * 128:(dc % 2) * 128 + 128],
                            in_=psT[pj][:, :]))
                    dvEvac[(tt, dc)] = dv.n

                if own:
                    @TEN
                    def _(te, tt=tt, dc=dc):
                        dv.w(te, dvEvac[(tt, dc)])
                        if dc == 0 and tt >= 2:
                            dv.w(te, dvLog[tt - 2])
                        ins = te.matmul(pslog[tt % 2][:, :],
                                        xtf[:, (dc % 2) * 128:(dc % 2) * 128 + 128],
                                        wrt_s[:, dc * 8:(dc + 1) * 8],
                                        start=(dc == 0), stop=(dc == 7))
                        if dc == 7:
                            pe.inc(ins)
                            peLog[tt] = pe.n

            if own:
                @VEC
                def _(ve, tt=tt):
                    pe.w(ve, peLog[tt])
                    dv.inc(ve.tensor_copy(out=lgs[:, :], in_=pslog[tt % 2][:, :]))
                    dvLog[tt] = dv.n
                    dv.w(ve)
                    dv.inc(ve.reduce_max(out=st1[tt % 2][:, :], in_=lgs[:, :], axis=AX))
                    dv.w(ve)
                    dv.inc(ve.tensor_scalar(out=eqs[:, :], in0=lgs[:, :],
                                            scalar1=st1[tt % 2][:, :], scalar2=None,
                                            op0=ALU.is_equal))
                    dv.w(ve)
                    dv.inc(ve.tensor_mul(out=v8s[:, :], in0=eqs[:, :], in1=w8_s[:, :]))
                    dv.w(ve)
                    dv.inc(ve.reduce_max(out=st3[tt % 2][:, :], in_=v8s[:, :], axis=AX))
                    dv.w(ve)
                    dv.inc(ve.tensor_scalar(out=mkb[:, :], in0=v8s[:, :],
                                            scalar1=st3[tt % 2][:, :], scalar2=None,
                                            op0=ALU.is_equal))
                    dvMkb[tt] = dv.n

                @TEN
                def _(te, tt=tt):
                    dv.w(te, dvMkb[tt])
                    if tt >= 1:
                        dv.w(te, dvMT[tt - 1])
                    pe.inc(te.transpose(psmT[:, :], mkb[:, :], idb_s[:, :]))
                    peMT[tt] = pe.n

                @VEC
                def _(ve, tt=tt, toff=toff):
                    pe.w(ve, peMT[tt])
                    dv.inc(ve.tensor_copy(out=maskT[:, toff:toff + 128], in_=psmT[:, :]))
                    dvMT[tt] = dv.n

        prefA = EA1(nc.sbuf_tensor("prefA", [8, 1024], F32))
        prefB = EA1(nc.sbuf_tensor("prefB", [8, 1024], F32))
        zeros8 = EA1(nc.sbuf_tensor("zeros8", [8, 1024], BF16))
        drow = EA1(nc.sbuf_tensor("drow", [1, 1024], F32))
        vrow = EA1(nc.sbuf_tensor("vrow", [1, 1024], F32))
        destc = EA1(nc.sbuf_tensor("destc", [128, 16], F32))
        stgb = EA1(nc.sbuf_tensor("stgb", [128, D], BF16))

        @VEC
        def _(ve):
            dv.inc(ve.memset(zeros8[:, :], 0.0))
            dv.inc(ve.memset(stgb[:, :], 0.0))

        @SYN
        def _(sy):
            dv.w(sy)
            for h2 in range(2):
                for zb in range(16):
                    fsem.dma(sy.dma_start(out=xe_scr[h2][zb * 128:(zb + 1) * 128, :],
                                          in_=stgb[:, :]))
                fsem.dma(sy.dma_start(out=xe_scr[h2][2048:2049, :], in_=stgb[0:1, :]))

        for half in range(2):
            mTh = maskT[:, half * 1024:(half + 1) * 1024]
            chain(
                (VEC, dv, lambda ve, mTh=mTh: ve.tensor_tensor_scan(
                    out=prefA[:, :], data0=mTh, data1=zeros8[:, :],
                    initial=0.0, op0=ALU.add, op1=ALU.add)),
                (VEC, dv, lambda ve: ve.tensor_scalar(
                    out=prefB[:, :], in0=prefA[:, :], scalar1=256.0, scalar2=None,
                    op0=ALU.is_le)),
                (VEC, dv, lambda ve: ve.tensor_scalar_add(
                    out=prefA[:, :], in0=prefA[:, :], scalar1=ec_s[:, :])),
                (VEC, dv, lambda ve, mTh=mTh: ve.tensor_tensor(
                    out=prefB[:, :], in0=mTh, in1=prefB[:, :], op=ALU.mult)),
                (VEC, dv, lambda ve: ve.tensor_tensor(
                    out=prefA[:, :], in0=prefB[:, :], in1=prefA[:, :], op=ALU.mult)),
                (TEN, pe, lambda te: te.matmul(pidx[0:1, 0:512], oc_f[0:8, :],
                                               prefA[:, 0:512], start=True, stop=True)),
                (TEN, pe, lambda te: te.matmul(pidx[0:1, 512:1024], oc_f[0:8, :],
                                               prefA[:, 512:1024], start=True, stop=True)),
                (VEC, dv, lambda ve: ve.tensor_copy(out=drow[:, :], in_=pidx[0:1, :])),
                (TEN, pe, lambda te: te.matmul(pidx[0:1, 0:512], oc_f[0:8, :],
                                               prefB[:, 0:512], start=True, stop=True)),
                (TEN, pe, lambda te: te.matmul(pidx[0:1, 512:1024], oc_f[0:8, :],
                                               prefB[:, 512:1024], start=True, stop=True)),
                (VEC, dv, lambda ve: ve.tensor_copy(out=vrow[:, :], in_=pidx[0:1, :])),
                (VEC, dv, lambda ve: ve.scalar_tensor_tensor(
                    out=drow[:, :], in0=vrow[:, :], scalar=-2048.0, in1=drow[:, :],
                    op0=ALU.mult, op1=ALU.add)),
                (VEC, dv, lambda ve: ve.tensor_scalar_add(
                    out=drow[:, :], in0=drow[:, :], scalar1=2048.0)),
            )
            for tb in range(8):
                chain(
                    (TEN, pe, lambda te, tb=tb: te.transpose(
                        psT[0][:, 0:1], drow[0:1, tb * 128:(tb + 1) * 128],
                        idf_s[0:1, 0:1])),
                    (VEC, dv, lambda ve, tb=tb, half=half: ve.tensor_copy(
                        out=destc[:, half * 8 + tb:half * 8 + tb + 1],
                        in_=psT[0][0:128, 0:1])),
                )
        chain((VEC, dv, lambda ve: ve.tensor_copy(out=desti2[:, :], in_=destc[:, :])))
        dv_idx2 = dv.n
        fsem_zf = fsem.n

        ctxA1.close()

        ctxA2 = ExitStack()
        E2 = ctxA2.enter_context
        Qg = E2(nc.sbuf_tensor("Qg", [128, 4 * T], BF16))
        Kg = E2(nc.sbuf_tensor("Kg", [128, 2 * S], BF16))
        Vg = E2(nc.sbuf_tensor("Vg", [128, 32 * HPG * 65 + 64], BF16))
        sq4 = E2(nc.sbuf_tensor("sq4", [128, T], BF16))
        wqg = E2(nc.sbuf_tensor("wqg", [128, 2048], BF16))
        wkg = E2(nc.sbuf_tensor("wkg", [128, 2048], BF16))
        ring = [E2(nc.sbuf_tensor(f"ring{r}", [128, 1024], BF16)) for r in range(3)]
        sring = E2(nc.sbuf_tensor("sring", [1, 512], F32))
        S16 = E2(nc.sbuf_tensor("S16", [16, T], F32))
        negm8 = E2(nc.sbuf_tensor("negm8", [128, 1], F32))
        t11a = E2(nc.sbuf_tensor("t11a", [1, 1], F32))
        t11b = E2(nc.sbuf_tensor("t11b", [1, 1], F32))
        t11d = E2(nc.sbuf_tensor("t11d", [1, 1], F32))
        t11c = E2(nc.sbuf_tensor("t11c", [1, 1], BF16))
        sc = [E2(nc.psum_tensor(f"sc{j}", [128, 1024], F32)) for j in range(3)]
        oacc = [E2(nc.psum_tensor(f"oacc{j}", [128, 512], F32)) for j in range(2)]

        @GPS
        def _(ge):
            fence(ge)
            xnsc.w(ge)
            fsem.w(ge)
            gp.inc(ge.memset(Vg[:, :], 0.0))
            gp.w(ge)
            gp.inc(ge.memset(
                Vg[:, 0:32 * HPG * 65].rearrange("p (c s) -> p c s", s=65)[:, :, 64:65], 1.0))
            gp.inc(ge.memset(Qg[:, :], 0.0))
            qG.dma(ge.dma_start(out=wqg[:, :], in_=wq_h[0]))
            qG.dma(ge.dma_start(out=wkg[:, :], in_=wk_h[0]))

        def xnT_at(tok):
            return (xnT_own, tok) if tok < T else (xnT_par, tok - T)

        def _precast_pair(k):
            g, kk = divmod(k, 36)
            if kk < 32:
                e, iq4 = divmod(g * 32 + kk, 16)
                return w1_h[e, iq4], w1bf[e, iq4]
            w = g * 4 + (kk - 32)
            d2c, ch = divmod(w, 2)
            return (wo2_h[d2c][:, ch * 2048:(ch + 1) * 2048],
                    wo2bf[d2c][:, ch * 2048:(ch + 1) * 2048])

        def w1_precast_batch(g):
            @GPS
            def _(ge, g=g):
                for kk in range(36):
                    k = g * 36 + kk
                    src, dst = _precast_pair(k)
                    if k >= 2:
                        w1c[k % 2].w(ge, 16 * (k // 2))
                    w1g[k % 2].dma(ge.dma_start(out=w1b[k % 2][:, :], in_=src))

            @SYN
            def _(sy, g=g):
                for kk in range(36):
                    k = g * 36 + kk
                    src, dst = _precast_pair(k)
                    sy.wait_ge(w1g[k % 2].h, 16 * (k // 2 + 1))
                    w1c[k % 2].dma(sy.dma_start(out=dst, in_=w1b[k % 2][:, :]))

        for g in range(NG):
            if g > 0:
                @GPS
                def _(ge, g=g):
                    fence(ge)
                    qG.dma(ge.dma_start(out=wqg[:, :], in_=wq_h[g]))
                    qG.dma(ge.dma_start(out=wkg[:, :], in_=wk_h[g]))

            peP = {}; dvP = {}
            for t in range(8):
                d2cl, qs = divmod(t, 4)

                @TEN
                def _(te, t=t, d2cl=d2cl, qs=qs):
                    qG.w(te)
                    gp.w(te)
                    if t == 0:
                        ac.w(te)
                        dv.w(te)
                    if t >= 2:
                        dv.w(te, dvP[t - 2])
                    for dc in range(8):
                        ins = te.matmul(
                            sc[t % 2][:, 0:512],
                            wqg[:, dc * 256 + d2cl * 128:dc * 256 + (d2cl + 1) * 128],
                            xnT_own[:, dc * T + qs * 512:dc * T + (qs + 1) * 512],
                            start=(dc == 0), stop=(dc == 7))
                    pe.inc(ins)
                    peP[t] = pe.n

                @VEC
                def _(ve, t=t, d2cl=d2cl, qs=qs, g=g):
                    pe.w(ve, peP[t])
                    gp.w(ve)
                    col = 2 * g + d2cl
                    for hh in range(2):
                        hl = 2 * d2cl + hh
                        dv.inc(ve.tensor_scalar_add(
                            out=Qg[hh * 64:hh * 64 + 64,
                                   hl * T + qs * 512:hl * T + (qs + 1) * 512],
                            in0=sc[t % 2][hh * 64:hh * 64 + 64, 0:512],
                            scalar1=bq_s[hh * 64:hh * 64 + 64, col:col + 1]))
                    dvP[t] = dv.n

            peK = {}; dvK = {}
            for t in range(16):
                d2cl, ks = divmod(t, 8)

                @TEN
                def _(te, t=t, d2cl=d2cl, ks=ks):
                    if t < 2:
                        dv.w(te, dvP[6 + t])

                    else:
                        dv.w(te, dvK[t - 2])
                    srcx, base = xnT_at(ks * 512)
                    for dc in range(8):
                        ins = te.matmul(
                            sc[t % 2][:, 0:512],
                            wkg[:, dc * 256 + d2cl * 128:dc * 256 + (d2cl + 1) * 128],
                            srcx[:, dc * T + base:dc * T + base + 512],
                            start=(dc == 0), stop=(dc == 7))
                    pe.inc(ins)
                    peK[t] = pe.n

                @VEC
                def _(ve, t=t, d2cl=d2cl, ks=ks, g=g):
                    pe.w(ve, peK[t])
                    dv.inc(ve.tensor_scalar_add(
                        out=Kg[:, d2cl * S + ks * 512:d2cl * S + (ks + 1) * 512],
                        in0=sc[t % 2][:, 0:512],
                        scalar1=bk_s[:, 2 * g + d2cl:2 * g + d2cl + 1]))
                    dvK[t] = dv.n

            @GPS
            def _(ge, g=g):
                pe.w(ge, peP[7])
                qG.dma(ge.dma_start(out=wqg[:, :], in_=wv_h[g]))

            w1_precast_batch(g)

            if g == 3:
                @GPS
                def _(ge):
                    ge.wait_ge(dv.h, dv_idx2)
                    ge.wait_ge(fsem.h, fsem_zf)
                    for k in range(16):
                        h2, tb = divmod(k, 8)
                        if k >= 2:
                            ge.wait_ge(isem.h, 32 * (k // 2))
                        gscat.dma(ge.dma_start(out=scstg[k % 2][:, :],
                                               in_=xn_scr[h2 * 1024 + tb * 128:
                                                          h2 * 1024 + (tb + 1) * 128, :]))
                        ge.wait_ge(gscat.h, 16 * (k + 1))
                        isem.dma(ge.indirect_dma_start(
                            out=xe_scr[h2][:, :],
                            out_offset=bass.IndirectOffsetOnAxis(
                                desti2[:, h2 * 8 + tb:h2 * 8 + tb + 1], 0),
                            in_=scstg[k % 2][:, :], in_offset=None))

            peV = {}; dvV = {}
            for kc in range(32):
                @TEN
                def _(te, kc=kc, g=g):
                    qG.w(te)
                    if kc < 2:
                        dv.w(te, dvK[14 + kc])
                    else:
                        dv.w(te, dvV[kc - 2])
                    src, base = xnT_at(kc * 128)
                    for dc in range(8):
                        te.matmul(sc[kc % 2][:, 0:256],
                                  src[:, dc * T + base:dc * T + base + 128],
                                  wqg[:, dc * 256:(dc + 1) * 256],
                                  start=(dc == 0), stop=False)
                    ins = te.matmul(sc[kc % 2][:, 0:256], or_b[:, :],
                                    bv_b[:, g * 256:(g + 1) * 256],
                                    start=False, stop=True)
                    pe.inc(ins)
                    peV[kc] = pe.n

                @VEC
                def _(ve, kc=kc):
                    pe.w(ve, peV[kc])
                    dv.inc(ve.tensor_copy(
                        out=Vg[:, kc * HPG * 65:(kc + 1) * HPG * 65]
                            .rearrange("p (h s) -> p h s", s=65)[:, :, 0:64],
                        in_=sc[kc % 2][:, 0:256].rearrange("p (h s) -> p h s", s=64)))
                    dvV[kc] = dv.n

            def nps(seg):
                j, hh = divmod(seg, 2)
                return sc[j][0:1, hh * 512:(hh + 1) * 512]

            steps = []
            for hl in range(4):
                steps.append((VEC, dv, lambda ve, hl=hl: ve.tensor_mul(
                    out=sq4[:, :], in0=Qg[:, hl * T:(hl + 1) * T],
                    in1=Qg[:, hl * T:(hl + 1) * T])))
                for seg in range(4):
                    steps.append((TEN, pe, lambda te, seg=seg: te.matmul(
                        nps(seg), oc_b[:, :], sq4[:, seg * 512:(seg + 1) * 512],
                        start=True, stop=True)))
                for seg in range(4):
                    steps.append((VEC, dv, lambda ve, seg=seg: ve.reduce_max(out=t11d[:, :], in_=nps(seg), axis=AX)))
                    if hl == 0 and seg == 0:
                        steps.append((VEC, dv, lambda ve: ve.tensor_copy(out=t11a[:, :], in_=t11d[:, :])))
                    else:
                        steps.append((VEC, dv, lambda ve: ve.tensor_max(out=t11a[:, :], in0=t11a[:, :], in1=t11d[:, :])))
            first = [True]
            for d2cl in range(2):
                for piece in range(2):
                    steps.append((VEC, dv, lambda ve, d2cl=d2cl, piece=piece: ve.tensor_mul(
                        out=sq4[:, :], in0=Kg[:, d2cl * S + piece * T:d2cl * S + (piece + 1) * T],
                        in1=Kg[:, d2cl * S + piece * T:d2cl * S + (piece + 1) * T])))
                    for seg in range(4):
                        steps.append((TEN, pe, lambda te, seg=seg: te.matmul(
                            nps(seg), oc_b[:, :], sq4[:, seg * 512:(seg + 1) * 512],
                            start=True, stop=True)))
                    for seg in range(4):
                        steps.append((VEC, dv, lambda ve, seg=seg: ve.reduce_max(
                            out=t11d[:, :], in_=nps(seg), axis=AX)))
                        if first[0]:
                            steps.append((VEC, dv, lambda ve: ve.tensor_copy(out=t11b[:, :], in_=t11d[:, :])))
                            first[0] = False
                        else:
                            steps.append((VEC, dv, lambda ve: ve.tensor_max(out=t11b[:, :], in0=t11b[:, :], in1=t11d[:, :])))
            steps.append((VEC, dv, lambda ve: ve.tensor_mul(out=t11a[:, :], in0=t11a[:, :], in1=t11b[:, :])))
            steps.append((SCA, ac, lambda se: se.activation(out=t11a[:, :], in_=t11a[:, :], func=AF.Ln)))
            steps.append((SCA, ac, lambda se: se.activation(out=t11a[:, :], in_=t11a[:, :], func=AF.Exp, scale=0.5)))
            steps.append((VEC, dv, lambda ve: ve.tensor_scalar_mul(out=t11c[:, :], in0=t11a[:, :], scalar1=-0.125)))
            steps.append((TEN, pe, lambda te: te.matmul(sc[0][:, 0:1], or_b[:, :], t11c[:, :], start=True, stop=True)))
            steps.append((VEC, dv, lambda ve: ve.tensor_copy(out=negm8[:, :], in_=sc[0][:, 0:1])))
            chain(*steps)

            base_ac, base_sc, base_av = ac.n, pesc.n, peav.n
            base_dv, base_ss = dv.n, ssem.n

            @TEN
            def _(te, g=g, base_ac=base_ac, base_dv=base_dv):
                dv.w(te)

                def spair(p):
                    if p >= 3:
                        te.wait_ge(ac.h, base_ac + p - 2)
                    for u in range(2):
                        m = 2 * p + u
                        hl = m // 128
                        qt = (m // 32) % 4
                        c = m % 32
                        ins = te.matmul(
                            sc[p % 3][:, u * 512:(u + 1) * 512],
                            Kg[:, (hl // 2) * S + c * 128:(hl // 2) * S + (c + 1) * 128],
                            Qg[:, hl * T + qt * 512:hl * T + (qt + 1) * 512],
                            start=True, stop=True)
                    pesc.inc(ins)

                for p in range(3):
                    spair(p)
                for n in range(512):
                    i, c = divmod(n, 32)
                    hl = n // 128
                    if n % 2 == 0 and n // 2 + 3 < 256:
                        spair(n // 2 + 3)
                    if c == 0 and i >= 2:
                        te.wait_ge(dv.h, base_dv + 2 * (i - 1))
                    te.wait_ge(ac.h, base_ac + n // 2 + 1)
                    ins = te.matmul(
                        oacc[i % 2][:, :],
                        Vg[:, (c * HPG + hl) * 65:(c * HPG + hl) * 65 + 128],
                        ring[(n // 2) % 3][:, (n % 2) * 512:(n % 2 + 1) * 512],
                        start=(c == 0), stop=(c == 31))
                    peav.inc(ins)

            @SCA
            def _(se, base_sc=base_sc, base_av=base_av, dvpre=dv.n):
                se.wait_ge(dv.h, dvpre)
                for p in range(256):
                    se.wait_ge(pesc.h, base_sc + p + 1)
                    if p >= 3:
                        se.wait_ge(peav.h, base_av + 2 * p - 4)
                    ac.inc(se.activation(out=ring[p % 3][:, :], in_=sc[p % 3][:, :],
                                         func=AF.Exp, bias=negm8[:, :], scale=0.125))

            @VEC
            def _(ve, g=g, base_av=base_av, base_ss=base_ss):
                for i in range(16):
                    hl = i // 4
                    qt = i % 4
                    h = HPG * g + hl
                    ve.wait_ge(peav.h, base_av + 32 * (i + 1))
                    dv.inc(ve.tensor_copy(
                        out=O_u[(h % 2) * 64:(h % 2) * 64 + 64,
                                (h // 2) * T + qt * 512:(h // 2) * T + (qt + 1) * 512],
                        in_=oacc[i % 2][0:64, :]))
                    if base_ss + 16 * i > 0:
                        ve.wait_ge(ssem.h, base_ss + 16 * i)
                    dv.inc(ve.tensor_copy(out=sring[:, :], in_=oacc[i % 2][64:65, :]))

            @SYN
            def _(sy, g=g, base_dv=base_dv):
                for i in range(16):
                    hl = i // 4
                    qt = i % 4
                    h = HPG * g + hl
                    sy.wait_ge(dv.h, base_dv + 2 * (i + 1))
                    ssem.dma(sy.dma_start(out=S16[h:h + 1, qt * 512:(qt + 1) * 512],
                                          in_=sring[:, :]))

        ctxA2.close()

        ctxB3 = ExitStack()
        E3 = ctxB3.enter_context
        wo_s = E3(nc.sbuf_tensor("wo_s", [128, 8 * 1024], BF16))
        sclb = E3(nc.sbuf_tensor("sclb", [128, 2048], BF16))
        r_att = [E3(nc.sbuf_tensor(f"r_att{j}", [128, D], F32)) for j in range(2)]
        ln2sq = [E3(nc.sbuf_tensor(f"ln2sq{j}", [128, D], F32)) for j in range(2)]
        l2a = [E3(nc.sbuf_tensor(f"l2a{j}", [128, 1], F32)) for j in range(2)]
        l2b = [E3(nc.sbuf_tensor(f"l2b{j}", [128, 1], F32)) for j in range(2)]
        l2c = [E3(nc.sbuf_tensor(f"l2c{j}", [128, 1], F32)) for j in range(2)]
        attnb = [E3(nc.sbuf_tensor(f"attnb3_{j}", [128, D], BF16)) for j in range(2)]
        T16 = E3(nc.sbuf_tensor("T16", [16, T], F32))
        R16 = E3(nc.sbuf_tensor("R16", [16, T], BF16))
        psb = [E3(nc.psum_tensor(f"psb{j}", [128, 1024], F32)) for j in range(2)]

        @GPS
        def _(ge):
            fence(ge)
            ssem.w(ge)
            qG.dma(ge.dma_start(out=wo_s[:, :], in_=wo_h[:, :]))

        @SCA
        def _(se):
            ssem.w(se)
            fence(se)
            ac.inc(se.activation(out=T16[:, :], in_=S16[:, :], func=AF.Ln))
            ac.w(se)
            ac.inc(se.activation(out=R16[:, :], in_=T16[:, :], func=AF.Exp, scale=-1.0))

        peB = {}; dvB = {}
        for dc in range(8):
            for piece in range(2):
                t = dc * 2 + piece

                @TEN
                def _(te, dc=dc, piece=piece, t=t):
                    ac.w(te)
                    if t >= 2:
                        dv.w(te, dvB[t - 2])
                    for hb in range(2):
                        ins = te.matmul(
                            psb[t % 2][:, hb * 512:(hb + 1) * 512],
                            pi_s[:, dc * 128:(dc + 1) * 128],
                            R16[:, piece * 1024 + hb * 512:piece * 1024 + (hb + 1) * 512],
                            start=True, stop=True)
                    pe.inc(ins)
                    peB[t] = pe.n

                @VEC
                def _(ve, dc=dc, piece=piece, t=t):
                    pe.w(ve, peB[t])
                    dv.inc(ve.tensor_copy(out=sclb[:, piece * 1024:(piece + 1) * 1024],
                                          in_=psb[t % 2][:, :]))
                    dv.w(ve)
                    dv.inc(ve.tensor_mul(
                        out=O_u[:, dc * T + piece * 1024:dc * T + (piece + 1) * 1024],
                        in0=O_u[:, dc * T + piece * 1024:dc * T + (piece + 1) * 1024],
                        in1=sclb[:, piece * 1024:(piece + 1) * 1024]))
                    dvB[t] = dv.n

        peY = {}; acR = {}; dvL2 = {}; acL2 = {}
        for tt in range(16):
            @TEN
            def _(te, tt=tt):
                qG.w(te)
                if tt < 2:
                    dv.w(te, dvB[15])
                else:
                    ac.w(te, acR[tt - 2])
                for dc in range(8):
                    for hb in range(2):
                        te.matmul(psb[tt % 2][:, hb * 512:(hb + 1) * 512],
                                  O_u[:, dc * T + tt * 128:dc * T + (tt + 1) * 128],
                                  wo_s[:, dc * 1024 + hb * 512:dc * 1024 + (hb + 1) * 512],
                                  start=(dc == 0), stop=False)
                for hb in range(2):
                    ins = te.matmul(psb[tt % 2][:, hb * 512:(hb + 1) * 512],
                                    or_b[:, :], bo_b[:, hb * 512:(hb + 1) * 512],
                                    start=False, stop=True)
                pe.inc(ins)
                peY[tt] = pe.n

            @SCA
            def _(se, tt=tt):
                pe.w(se, peY[tt])
                if tt >= 2:
                    dv.w(se, dvL2[tt - 2])
                ac.inc(se.activation(out=r_att[tt % 2][:, :], in_=psb[tt % 2][:, :],
                                     func=AF.Relu))
                acR[tt] = ac.n

            jj = tt % 2
            ext0 = [(ac, acR[tt])]
            if tt >= 2:
                ext0 = ext0 + [(dv, dvL2[tt - 2]), (ac, acL2.get(tt - 2, 0))]
            ext9 = [(ascr, 16 * (tt - 1))] if tt >= 2 else []
            cts = pchain([
                (VEC, dv, (lambda ve, jj=jj, tt=tt: ve.reduce_sum(
                    out=l2a[jj][:, :], in_=r_att[jj][:, :], axis=AX)), [], ext0),
                (VEC, dv, (lambda ve, jj=jj: ve.tensor_scalar_mul(
                    out=l2a[jj][:, :], in0=l2a[jj][:, :], scalar1=1.0 / D)), [0], []),
                (VEC, dv, (lambda ve, jj=jj: ve.tensor_mul(
                    out=l2c[jj][:, :], in0=l2a[jj][:, :], in1=l2a[jj][:, :])), [1], []),
                (SCA, ac, (lambda se, jj=jj, tt=tt: se.activation(
                    out=ln2sq[jj][:, :], in_=r_att[jj][:, :], func=AF.Square)), [], ext0),
                (VEC, dv, (lambda ve, jj=jj: ve.reduce_sum(
                    out=l2b[jj][:, :], in_=ln2sq[jj][:, :], axis=AX)), [3], []),
                (VEC, dv, (lambda ve, jj=jj: ve.tensor_scalar_mul(
                    out=l2b[jj][:, :], in0=l2b[jj][:, :], scalar1=1.0 / D)), [4], []),
                (VEC, dv, (lambda ve, jj=jj: ve.tensor_sub(
                    out=l2b[jj][:, :], in0=l2b[jj][:, :], in1=l2c[jj][:, :])), [5, 2], []),
                (SCA, ac, (lambda se, jj=jj: se.activation(
                    out=l2b[jj][:, :], in_=l2b[jj][:, :], func=AF.Ln, bias=epsA[:, :])), [6], []),
                (SCA, ac, (lambda se, jj=jj: se.activation(
                    out=l2b[jj][:, :], in_=l2b[jj][:, :], func=AF.Exp, scale=-0.5)), [7], []),
                (VEC, dv, (lambda ve, jj=jj: ve.tensor_scalar(
                    out=attnb[jj][:, :], in0=r_att[jj][:, :],
                    scalar1=l2a[jj][:, :], scalar2=l2b[jj][:, :],
                    op0=ALU.subtract, op1=ALU.mult)), [8, 1], ext9),
            ])
            acL2[tt] = cts[8][1]
            dvL2[tt] = dv.n

            @SYN
            def _(sy, tt=tt):
                dv.w(sy, dvL2[tt])
                ascr.w(sy, 16 * tt)
                ascr.dma(sy.dma_start(out=attn_scr[tt * 128:(tt + 1) * 128, :],
                                      in_=attnb[tt % 2][:, :]))

        ctxB3.close()
        ctxA.close()

        ctxBB = ctx.enter_context(ExitStack())
        E4 = ctxBB.enter_context
        gaf = E4(nc.sbuf_tensor("gaf", [1, D], F32))
        bef = E4(nc.sbuf_tensor("bef", [1, D], F32))
        b1_s = E4(nc.sbuf_tensor("b1_s", [128, 256], F32))
        bo2_s = E4(nc.sbuf_tensor("bo2_s", [128, 8], F32))
        xe_sb = E4(nc.sbuf_tensor("xe_sb", [128, 16 * 1024], BF16))
        g_e = [E4(nc.sbuf_tensor(f"g_e{j}", [128, 8 * 256], BF16)) for j in range(2)]
        ytT = [E4(nc.sbuf_tensor(f"ytT{j}", [128, 256], F32)) for j in range(2)]
        fgt = [E4(nc.sbuf_tensor(f"fgt{j}", [128, D], BF16)) for j in range(2)]
        Gab = E4(nc.sbuf_tensor("Gab", [128, D], BF16))
        Beb = E4(nc.sbuf_tensor("Beb", [128, D], BF16))
        z3 = [E4(nc.sbuf_tensor(f"z3_{j}", [128, D], F32)) for j in range(2)]
        ln3sq = z3
        l3a = [E4(nc.sbuf_tensor(f"l3a{j}", [128, 1], F32)) for j in range(2)]
        l3b = [E4(nc.sbuf_tensor(f"l3b{j}", [128, 1], F32)) for j in range(2)]
        l3c = [E4(nc.sbuf_tensor(f"l3c{j}", [128, 1], F32)) for j in range(2)]
        xtile = [E4(nc.sbuf_tensor(f"xt4_{j}", [128, D], F32)) for j in range(2)]
        outb = [E4(nc.sbuf_tensor("outb0", [128, D], F32))] * 2
        attnb = [E4(nc.sbuf_tensor(f"attnb4_{j}", [128, D], BF16)) for j in range(2)]
        w1b4 = w1b + [E4(nc.sbuf_tensor(f"w1b4_{j}", [128, 8 * 256], BF16))
                      for j in range(2)]
        psz = [E4(nc.psum_tensor(f"psz{j}", [128, 1024], F32)) for j in range(2)]
        ptr = [E4(nc.psum_tensor(f"ptr{j}", [128, 128], F32)) for j in range(2)]
        ptrb = [E4(nc.psum_tensor(f"ptrb{j}", [128, 128], BF16)) for j in range(2)]

    

        @VEC
        def _(ve):
            ascr.w(ve)

        @SCA
        def _(se):
            ascr.w(se)

        @SYN
        def _(sy):
            ascr.w(sy)
            fence(sy)
            for dst, hh in [(gaf, ga_h), (bef, be_h), (b1_s, b1_h), (bo2_s, bo2_h)]:
                qS.dma(sy.dma_start(out=dst[:, :], in_=hh[:, :]))

        @TEN
        def _(te):
            qS.w(te)

        chain(
            (TEN, pe, lambda te: te.matmul(psz[0][:, 0:512], or_f[:, :], gaf[:, 0:512], start=True, stop=True)),
            (TEN, pe, lambda te: te.matmul(psz[0][:, 512:1024], or_f[:, :], gaf[:, 512:1024], start=True, stop=True)),
            (VEC, dv, lambda ve: ve.tensor_copy(out=Gab[:, :], in_=psz[0][:, :])),
            (TEN, pe, lambda te: te.matmul(psz[1][:, 0:512], or_f[:, :], bef[:, 0:512], start=True, stop=True)),
            (TEN, pe, lambda te: te.matmul(psz[1][:, 512:1024], or_f[:, :], bef[:, 512:1024], start=True, stop=True)),
            (VEC, dv, lambda ve: ve.tensor_copy(out=Beb[:, :], in_=psz[1][:, :])),
        )

        xnsc_total = xnsc.n
        isem_scat = isem.n
        for half in range(2):
            ctxH1 = ExitStack()
            xeT = ctxH1.enter_context(nc.sbuf_tensor(f"xeT{half}", [128, 8 * 2048], BF16))

            @SYN
            def _(sy, half=half):
                isem.w(sy, isem_scat)
                fsem.w(sy)
                fsem.dma(sy.dma_start(
                    out=xe_sb[:, :].rearrange("p (c m) -> p c m", c=16),
                    in_=xe_scr[half][0:2048, :].rearrange("(c p) m -> p c m", p=128)))

            peXT = {}
            for sb in range(16):
                for dcb in range(8):
                    b = sb * 8 + dcb

                    @TEN
                    def _(te, sb=sb, dcb=dcb, b=b):
                        if b == 0:
                            fsem.w(te)
                        if b >= 2:
                            dv.w(te, peXT[b - 2])
                        pe.inc(te.transpose(ptrb[b % 2][:, :],
                                            xe_sb[:, sb * 1024 + dcb * 128:sb * 1024 + (dcb + 1) * 128],
                                            idb_s[:, :]))

                    @VEC
                    def _(ve, sb=sb, dcb=dcb, b=b):
                        pe.w(ve)
                        dv.inc(ve.tensor_copy(
                            out=xeT[:, dcb * 2048 + sb * 128:dcb * 2048 + (sb + 1) * 128],
                            in_=ptrb[b % 2][:, :]))
                        peXT[b] = dv.n
            dvXET = dv.n

            base_pez = pe.n
            base_acA = ac.n
            pe_after_z = {}
            cnt = base_pez
            for z in range(256):
                cnt += 1
                pe_after_z[z] = cnt
            ac_after_z = {z: base_acA + z + 1 for z in range(256)}

            @SYN
            def _(sy, half=half, base_pez=base_pez, pe_after_z=pe_after_z):
                if half == 0:
                    for jw in range(4):
                        w1c[jw].w(sy)
                for k in range(128):
                    e, iq4 = divmod(k, 16)
                    if k >= 4:
                        sy.wait_ge(pe.h, pe_after_z[2 * k - 7])
                    elif half == 1:
                        sy.wait_ge(pe.h, base_pez)
                    w1r[k % 4].dma(sy.dma_start(out=w1b4[k % 4][:, :], in_=w1bf[e, iq4]))

            @TEN
            def _(te, half=half, ac_after_z=ac_after_z, dvXET=dvXET):
                te.wait_ge(dv.h, dvXET)
                for e in range(NE):
                    for ic in range(32):
                        z = e * 32 + ic
                        kg = half * 128 + e * 16 + ic // 2
                        te.wait_ge(w1r[kg % 4].h, 16 * (kg // 4 + 1))
                        if z >= 4:
                            te.wait_ge(ac.h, ac_after_z[z - 4])
                        for dc in range(8):
                            ins = te.matmul(
                                psz[z % 2][:, (z // 2 % 2) * 512:(z // 2 % 2) * 512 + 256],
                                w1b4[kg % 4][:, dc * 256 + (ic % 2) * 128:
                                             dc * 256 + (ic % 2 + 1) * 128],
                                xeT[:, dc * 2048 + e * 256:dc * 2048 + (e + 1) * 256],
                                start=(dc == 0), stop=(dc == 7))
                        pe.inc(ins)

            @SCA
            def _(se, half=half, pe_after_z=pe_after_z):
                for e in range(NE):
                    for ic in range(32):
                        z = e * 32 + ic
                        se.wait_ge(pe.h, pe_after_z[z])
                        if ic % 8 == 0 and z >= 16:
                            bz = half * 32 + z // 8
                            gsem[bz % 2].w(se, 16 * ((bz - 2) // 2 + 1))
                        ac.inc(se.activation(out=g_e[(z // 8) % 2][:, (ic % 8) * 256:(ic % 8 + 1) * 256],
                                             in_=psz[z % 2][:, (z // 2 % 2) * 512:(z // 2 % 2) * 512 + 256],
                                             func=AF.Gelu,
                                             bias=b1_s[:, ic * 8 + e:ic * 8 + e + 1]))

            @GPS
            def _(ge, half=half, ac_after_z=ac_after_z):
                for e in range(NE):
                    for icb in range(4):
                        ge.wait_ge(ac.h, ac_after_z[e * 32 + icb * 8 + 7])
                        gsem[(e * 4 + icb) % 2].dma(ge.dma_start(
                            out=hg_scr[icb * 1024:(icb + 1) * 1024,
                                       e * 256:(e + 1) * 256]
                                .rearrange("(i p) s -> p i s", p=128),
                            in_=g_e[(e * 4 + icb) % 2][:, :]
                                .rearrange("p (i s) -> p i s", i=8)))

            ctxH1.close()

            ctxH2 = ExitStack()
            hgq = [ctxH2.enter_context(nc.sbuf_tensor(f"hgq{half}_{j}", [128, 32 * 256], BF16))
                   for j in range(2)]
            wo2d = [ctxH2.enter_context(nc.sbuf_tensor(f"wo2d{half}_{j}", [128, 4096], BF16))
                    for j in range(2)]
            peB1 = {}; dvS = {}; dvYe = {}
            gsem_all = (gsem[0].n, gsem[1].n)
            for sck in range(8):
                @SYN
                def _(sy, half=half, sck=sck):
                    sy.wait_ge(gsem[0].h, gsem_all[0])
                    sy.wait_ge(gsem[1].h, gsem_all[1])
                    if sck >= 2:
                        pe.w(sy, peB1[(sck - 2) * 8 + 7])
                    for ih in range(2):
                        hsem[sck % 2].dma(sy.dma_start(
                            out=hgq[sck % 2][:, ih * 4096:(ih + 1) * 4096]
                                .rearrange("p (i s) -> p i s", i=16),
                            in_=hg_scr[ih * 2048:(ih + 1) * 2048,
                                       sck * 256:(sck + 1) * 256]
                                .rearrange("(i p) s -> p i s", p=128)))

                for d2c in range(8):
                    t = sck * 8 + d2c

                    @GPS
                    def _(ge, d2c=d2c, t=t):
                        if t >= 2:
                            pe.w(ge, peB1.get(t - 2, 0))
                        elif t == 0:
                            wcst.w(ge)
                            pe.w(ge)
                        wo2s[t % 2].dma(ge.dma_start(out=wo2d[t % 2][:, :], in_=wo2bf[d2c]))

                    @TEN
                    def _(te, sck=sck, d2c=d2c, t=t, half=half):
                        gsck = half * 8 + sck
                        te.wait_ge(hsem[sck % 2].h, 32 * (gsck // 2 + 1))
                        te.wait_ge(wo2s[t % 2].h, 16 * ((half * 64 + t) // 2 + 1))
                        if t >= 4:
                            dv.w(te, dvS[t - 4])
                        for ic in range(32):
                            ins = te.matmul(
                                psz[t % 2][:, (t // 2 % 2) * 512:(t // 2 % 2) * 512 + 256],
                                wo2d[t % 2][:, ic * 128:(ic + 1) * 128],
                                hgq[sck % 2][:, ic * 256:(ic + 1) * 256],
                                start=(ic == 0), stop=(ic == 31))
                        pe.inc(ins)
                        peB1[t] = pe.n

                    @VEC
                    def _(ve, half=half, sck=sck, d2c=d2c, t=t):
                        pe.w(ve, peB1[t])
                        dv.inc(ve.tensor_scalar_add(
                            out=ytT[t % 2][:, :],
                            in0=psz[t % 2][:, (t // 2 % 2) * 512:(t // 2 % 2) * 512 + 256],
                            scalar1=bo2_s[:, d2c:d2c + 1]))
                        dvS[t] = dv.n

                    for blk in range(2):
                        bg = t * 2 + blk

                        @TEN
                        def _(te, t=t, blk=blk, bg=bg):
                            dv.w(te, dvS[t])
                            if bg >= 2:
                                dv.w(te, dvYe[bg - 2])
                            pe.inc(te.transpose(ptr[bg % 2][:, :],
                                                ytT[t % 2][:, blk * 128:(blk + 1) * 128],
                                                idf_s[:, :]))

                        @VEC
                        def _(ve, sck=sck, d2c=d2c, blk=blk, bg=bg):
                            pe.w(ve)
                            sb = sck * 2 + blk
                            sl = slice(sb * 1024 + d2c * 128, sb * 1024 + (d2c + 1) * 128)
                            dv.inc(ve.tensor_tensor(
                                out=xe_sb[:, sl], in0=ptr[bg % 2][:, :],
                                in1=xe_sb[:, sl], op=ALU.add))
                            dvYe[bg] = dv.n

            ctxH2.close()

            @SYN
            def _(sy):
                dv.w(sy)
                fsem.dma(sy.dma_start(
                    out=fg_scr[:, :].rearrange("(c p) m -> p c m", p=128),
                    in_=xe_sb[:, :].rearrange("p (c m) -> p c m", c=16)))

            if half == 0:
                ln3_state = {"dvTS": {}, "dvCmb": {}, "stC": {}, "isG": {}}
            dvTS = ln3_state["dvTS"]; dvCmb = ln3_state["dvCmb"]
            stC = ln3_state["stC"]; isG = ln3_state["isG"]
            for tb in range(8):
                tt = half * 8 + tb
                gb = half * 8 + tb
                j = tb % 2

                @GPS
                def _(ge, tb=tb, j=j, half=half, gb=gb):
                    fsem.w(ge)
                    isem.w(ge)
                    if gb >= 2:
                        dv.w(ge, dvTS[gb - 2])
                    isem.dma(ge.indirect_dma_start(
                        out=fgt[j][:, :], out_offset=None,
                        in_=fg_scr[:, :],
                        in_offset=bass.IndirectOffsetOnAxis(
                            desti2[:, half * 8 + tb:half * 8 + tb + 1], 0)))
                    isG[gb] = isem.n

                @SYN
                def _(sy, tt=tt, j=j, gb=gb):
                    if gb >= 1:
                        st.w(sy, 16 * gb)
                    if gb >= 2:
                        dv.w(sy, dvCmb[gb - 2])
                    xs[j].dma(sy.dma_start(out=xtile[j][:, :],
                                           in_=x_h[tt * 128:(tt + 1) * 128, :]))
                    asl[j].dma(sy.dma_start(out=attnb[j][:, :],
                                            in_=attn_scr[tt * 128:(tt + 1) * 128, :]))

                extg = [(isem, isG[gb])]
                if gb >= 2:
                    extg = extg + [(dv, dvCmb[gb - 2])]
                cts = pchain([
                    (VEC, dv, (lambda ve, j=j: ve.reduce_sum(
                        out=l3a[j][:, :], in_=fgt[j][:, :], axis=AX)), [], extg),
                    (VEC, dv, (lambda ve, j=j: ve.tensor_scalar_mul(
                        out=l3a[j][:, :], in0=l3a[j][:, :], scalar1=1.0 / D)), [0], []),
                    (VEC, dv, (lambda ve, j=j: ve.tensor_mul(
                        out=l3c[j][:, :], in0=l3a[j][:, :], in1=l3a[j][:, :])), [1], []),
                    (SCA, ac, (lambda se, j=j: se.activation(
                        out=ln3sq[j][:, :], in_=fgt[j][:, :], func=AF.Square)), [], extg),
                    (VEC, dv, (lambda ve, j=j: ve.reduce_sum(
                        out=l3b[j][:, :], in_=ln3sq[j][:, :], axis=AX)), [3], []),
                    (VEC, dv, (lambda ve, j=j: ve.tensor_scalar_mul(
                        out=l3b[j][:, :], in0=l3b[j][:, :], scalar1=1.0 / D)), [4], []),
                    (VEC, dv, (lambda ve, j=j: ve.tensor_sub(
                        out=l3b[j][:, :], in0=l3b[j][:, :], in1=l3c[j][:, :])), [5, 2], []),
                    (SCA, ac, (lambda se, j=j: se.activation(
                        out=l3b[j][:, :], in_=l3b[j][:, :], func=AF.Ln,
                        bias=epsC[:, :])), [6], []),
                    (SCA, ac, (lambda se, j=j: se.activation(
                        out=l3b[j][:, :], in_=l3b[j][:, :], func=AF.Exp,
                        scale=-0.5)), [7], []),
                    (VEC, dv, (lambda ve, j=j: ve.tensor_scalar(
                        out=z3[j][:, :], in0=fgt[j][:, :], scalar1=l3a[j][:, :],
                        scalar2=l3b[j][:, :],
                        op0=ALU.subtract, op1=ALU.mult)), [8, 1], []),
                    (VEC, dv, (lambda ve, j=j: ve.tensor_mul(
                        out=z3[j][:, :], in0=z3[j][:, :], in1=Gab[:, :])), [9], []),
                    (VEC, dv, (lambda ve, j=j: ve.tensor_add(
                        out=z3[j][:, :], in0=z3[j][:, :], in1=Beb[:, :])), [10], []),
                ])
                dvTS[gb] = cts[11][1]

                @VEC
                def _(ve, tt=tt, j=j, gb=gb):
                    xs[j].w(ve)
                    asl[j].w(ve)
                    dv.w(ve, dvTS[gb])
                    if gb >= 1:
                        st.w(ve, 16 * gb)
                    dv.inc(ve.tensor_add(out=outb[j][:, :], in0=z3[j][:, :], in1=xtile[j][:, :]))
                    dv.w(ve)
                    dv.inc(ve.tensor_add(out=outb[j][:, :], in0=outb[j][:, :], in1=attnb[j][:, :]))
                    dvCmb[gb] = dv.n

                @SYN
                def _(sy, tt=tt, j=j, gb=gb):
                    dv.w(sy, dvCmb[gb])
                    st.dma(sy.dma_start(out=y_h[tt * 128:(tt + 1) * 128, :],
                                        in_=outb[j][:, :]))

        @SYN
        def _(sy):
            st.w(sy)

    return nc


_NC_CACHE = {}


def _get_nc():
    if "nc" not in _NC_CACHE:
        _NC_CACHE["nc"] = build()
    return _NC_CACHE["nc"]


def kernel(L0_attn_x, L0_attn_Wq, L0_attn_bq, L0_attn_Wk, L0_attn_bk,
           L0_attn_Wv, L0_attn_bv, L0_attn_Wo, L0_attn_bo,
           L0_ffn_Wr, L0_ffn_W1, L0_ffn_b1, L0_ffn_Wo, L0_ffn_bo,
           L0_ffn_gamma, L0_ffn_beta):
    f = np.asarray
    x = np.ascontiguousarray(f(L0_attn_x, dtype=np.float32))
    Wq = f(L0_attn_Wq, dtype=np.float32); Wk = f(L0_attn_Wk, dtype=np.float32)
    Wv = f(L0_attn_Wv, dtype=np.float32); Wo = f(L0_attn_Wo, dtype=np.float32)
    Wr = f(L0_ffn_Wr, dtype=np.float32); W1 = f(L0_ffn_W1, dtype=np.float32)
    Wo2 = f(L0_ffn_Wo, dtype=np.float32)

    def qkv_lay(W):
        return np.ascontiguousarray(
            W.reshape(8, 128, 4, 256).transpose(2, 1, 0, 3).reshape(4, 128, 8 * 256))

    wq_c = qkv_lay(Wq); wk_c = qkv_lay(Wk); wv_c = qkv_lay(Wv)
    wo_c = np.ascontiguousarray(
        Wo.reshape(8, 128, 1024).transpose(1, 0, 2).reshape(128, 8192))
    wrt_c = np.ascontiguousarray(
        Wr.T.reshape(8, 128, 8).transpose(1, 0, 2).reshape(128, 64))
    w1_c = np.ascontiguousarray(
        W1.reshape(8, 8, 128, 16, 256).transpose(0, 3, 2, 1, 4).reshape(8, 16, 128, 8 * 256))
    wo2_c = np.ascontiguousarray(
        Wo2.reshape(32, 128, 8, 128).transpose(2, 1, 0, 3).reshape(8, 128, 32 * 128))
    bq_t = np.ascontiguousarray(f(L0_attn_bq, dtype=np.float32).reshape(8, 128).T)
    bk_t = np.ascontiguousarray(f(L0_attn_bk, dtype=np.float32).reshape(8, 128).T)
    bv_r = f(L0_attn_bv, dtype=np.float32).reshape(1, 1024)
    bo_r = f(L0_attn_bo, dtype=np.float32).reshape(1, 1024)
    b1_t = np.ascontiguousarray(
        f(L0_ffn_b1, dtype=np.float32).T.reshape(32, 128, 8).transpose(1, 0, 2).reshape(128, 256))
    bo2_t = np.ascontiguousarray(f(L0_ffn_bo, dtype=np.float32).reshape(8, 128).T)
    ga_r = f(L0_ffn_gamma, dtype=np.float32).reshape(1, 1024)
    be_r = f(L0_ffn_beta, dtype=np.float32).reshape(1, 1024)

    shared = dict(wq_c=wq_c, wk_c=wk_c, wv_c=wv_c, wo_c=wo_c, wrt_c=wrt_c,
                  w1_c=w1_c, wo2_c=wo2_c, bq_t=bq_t, bk_t=bk_t, bv_r=bv_r,
                  bo_r=bo_r, b1_t=b1_t, bo2_t=bo2_t, ga_r=ga_r, be_r=be_r)
    in_maps = []
    for c in range(8):
        b, half = divmod(c, 2)
        own = x[b, half * 2048:(half + 1) * 2048]
        par = x[b, (1 - half) * 2048:(2 - half) * 2048]
        m = dict(shared)
        m["xseq"] = np.ascontiguousarray(np.concatenate([own, par], axis=0))
        in_maps.append(m)

    nc = _get_nc()
    res = run_bass_kernel_spmd(nc, in_maps, core_ids=list(range(8)))
    out = np.empty((4, 4096, 1024), dtype=np.float32)
    for c in range(8):
        b, half = divmod(c, 2)
        out[b, half * 2048:(half + 1) * 2048] = res.results[c]["y"]
    return out


if __name__ == "__main__":
    nc = build()
    print("build ok")


# version 24
# speedup vs baseline: 1.1165x; 1.0189x over previous
import numpy as np
import ml_dtypes
from contextlib import ExitStack

import concourse.bass as bass
import concourse.mybir as mybir
from concourse.bass_utils import run_bass_kernel_spmd

F32 = mybir.dt.float32
BF16 = mybir.dt.bfloat16
AF = mybir.ActivationFunctionType
ALU = mybir.AluOpType
AX = mybir.AxisListType.X
BF = ml_dtypes.bfloat16

D = 1024
S = 4096
T = 2048
H = 16
NE = 8
NG = 4
HPG = 4
NI = 4096
EPS1 = 1e-5
EPS3 = 1e-12


class Sem:
    def __init__(self, ctx, nc, name):
        self.h = ctx.enter_context(nc.semaphore(name))
        self.n = 0

    def inc(self, ins, k=1):
        ins.then_inc(self.h, k)
        self.n += k

    def dma(self, ins):
        self.inc(ins, 16)

    def w(self, eng, at=None):
        v = self.n if at is None else at
        if v > 0:
            eng.wait_ge(self.h, v)


def build():
    nc = bass.Bass("TRN2", target_bir_lowering=False, debug=False)

    x_h = nc.dram_tensor("xseq", [S, D], F32, kind="ExternalInput").ap()
    wq_h = nc.dram_tensor("wq_c", [NG, 128, 8 * 256], F32, kind="ExternalInput").ap()
    wk_h = nc.dram_tensor("wk_c", [NG, 128, 8 * 256], F32, kind="ExternalInput").ap()
    wv_h = nc.dram_tensor("wv_c", [NG, 128, 8 * 256], F32, kind="ExternalInput").ap()
    wo_h = nc.dram_tensor("wo_c", [128, 8 * 1024], F32, kind="ExternalInput").ap()
    wrt_h = nc.dram_tensor("wrt_c", [128, 64], F32, kind="ExternalInput").ap()
    w1_h = nc.dram_tensor("w1_c", [NE, 16, 128, 8 * 256], F32, kind="ExternalInput").ap()
    wo2_h = nc.dram_tensor("wo2_c", [8, 128, 32 * 128], F32, kind="ExternalInput").ap()
    bq_h = nc.dram_tensor("bq_t", [128, 8], F32, kind="ExternalInput").ap()
    bk_h = nc.dram_tensor("bk_t", [128, 8], F32, kind="ExternalInput").ap()
    bv_h = nc.dram_tensor("bv_r", [1, D], F32, kind="ExternalInput").ap()
    bo_h = nc.dram_tensor("bo_r", [1, D], F32, kind="ExternalInput").ap()
    b1_h = nc.dram_tensor("b1_t", [128, 32 * 8], F32, kind="ExternalInput").ap()
    bo2_h = nc.dram_tensor("bo2_t", [128, 8], F32, kind="ExternalInput").ap()
    ga_h = nc.dram_tensor("ga_r", [1, D], F32, kind="ExternalInput").ap()
    be_h = nc.dram_tensor("be_r", [1, D], F32, kind="ExternalInput").ap()
    y_h = nc.dram_tensor("y", [T, D], F32, kind="ExternalOutput").ap()
    attn_scr = nc.dram_tensor("attn_scr", [T, D], BF16, kind="Internal").ap()
    xn_scr = nc.dram_tensor("xn_scr", [T, D], BF16, kind="Internal").ap()
    xe_scr = [nc.dram_tensor(f"xe_scr{h}", [2049, D], BF16, kind="Internal").ap()
              for h in range(2)]
    hg_scr = nc.dram_tensor("hg_scr", [NI, 2048], BF16, kind="Internal").ap()
    fg_scr = nc.dram_tensor("fg_scr", [2048, D], BF16, kind="Internal").ap()
    w1bf = nc.dram_tensor("w1bf", [NE, 16, 128, 8 * 256], BF16, kind="Internal").ap()
    wo2bf = nc.dram_tensor("wo2bf", [8, 128, 4096], BF16, kind="Internal").ap()

    idf = nc.inline_tensor(np.eye(128, dtype=np.float32), name="idf").ap()
    idb = nc.inline_tensor(np.eye(128, dtype=BF), name="idb").ap()
    ones_r_b = nc.inline_tensor(np.ones((1, 128), dtype=BF), name="ones_r_b").ap()
    ones_r_f = nc.inline_tensor(np.ones((1, 128), dtype=np.float32), name="ones_r_f").ap()
    ones_c_b = nc.inline_tensor(np.ones((128, 1), dtype=BF), name="ones_c_b").ap()
    ones_c_f = nc.inline_tensor(np.ones((128, 1), dtype=np.float32), name="ones_c_f").ap()
    w8c = nc.inline_tensor(
        np.broadcast_to(8.0 - np.arange(8, dtype=np.float32), (128, 8)).copy(),
        name="w8c").ap()
    pi_np = np.zeros((16, 8 * 128), dtype=BF)
    for dc in range(8):
        pi_np[2 * dc, dc * 128:dc * 128 + 64] = 1
        pi_np[2 * dc + 1, dc * 128 + 64:dc * 128 + 128] = 1
    pairind = nc.inline_tensor(pi_np, name="pairind").ap()
    ecol1 = nc.inline_tensor(
        (np.arange(8, dtype=np.float32) * 256.0 - 1.0).reshape(8, 1), name="ecol1").ap()

    ctx = ExitStack()
    with ctx:
        E = ctx.enter_context

        xnT_own = E(nc.sbuf_tensor("xnT_own", [128, 8 * T], BF16))
        rowf = E(nc.sbuf_tensor("rowf", [1, D], F32))
        bv_b = E(nc.sbuf_tensor("bv_b", [1, D], BF16))
        bo_b = E(nc.sbuf_tensor("bo_b", [1, D], BF16))
        idf_s = E(nc.sbuf_tensor("idf_s", [128, 128], F32))
        idb_s = E(nc.sbuf_tensor("idb_s", [128, 128], BF16))
        or_b = E(nc.sbuf_tensor("or_b", [1, 128], BF16))
        or_f = E(nc.sbuf_tensor("or_f", [1, 128], F32))
        oc_b = E(nc.sbuf_tensor("oc_b", [128, 1], BF16))
        oc_f = E(nc.sbuf_tensor("oc_f", [128, 1], F32))
        w8_s = E(nc.sbuf_tensor("w8_s", [128, 8], F32))
        pi_s = E(nc.sbuf_tensor("pi_s", [16, 8 * 128], BF16))
        ec_s = E(nc.sbuf_tensor("ec_s", [8, 1], F32))
        wrt_s = E(nc.sbuf_tensor("wrt_s", [128, 64], F32))
        bq_s = E(nc.sbuf_tensor("bq_s", [128, 8], F32))
        bk_s = E(nc.sbuf_tensor("bk_s", [128, 8], F32))
        epsA = E(nc.sbuf_tensor("epsA", [128, 1], F32))
        epsC = E(nc.sbuf_tensor("epsC", [128, 1], F32))
        maskT = E(nc.sbuf_tensor("maskT", [8, T], BF16))
        desti2 = E(nc.sbuf_tensor("desti2", [128, 16], mybir.dt.int32))
        w1b = [E(nc.sbuf_tensor(f"w1b{j}", [128, 8 * 256], BF16)) for j in range(2)]

        qS = Sem(ctx, nc, "qS")
        qG = Sem(ctx, nc, "qG")
        xs = [Sem(ctx, nc, f"xs{j}") for j in range(2)]
        xsp = [Sem(ctx, nc, f"xsp{j}") for j in range(4)]
        asl = [Sem(ctx, nc, f"asl{j}") for j in range(2)]
        w1s = [Sem(ctx, nc, f"w1s{j}") for j in range(2)]
        wo2s = [Sem(ctx, nc, f"wo2s{j}") for j in range(2)]
        pe = Sem(ctx, nc, "pe")
        pesc = Sem(ctx, nc, "pesc")
        peav = Sem(ctx, nc, "peav")
        ac = Sem(ctx, nc, "ac")
        dv = Sem(ctx, nc, "dv")
        ssem = Sem(ctx, nc, "ssem")
        ascr = Sem(ctx, nc, "ascr")
        gp = Sem(ctx, nc, "gp")
        xnsc = Sem(ctx, nc, "xnsc")
        isem = Sem(ctx, nc, "isem")
        gsem = [Sem(ctx, nc, f"gsem{j}") for j in range(2)]
        hsem = [Sem(ctx, nc, f"hsem{j}") for j in range(2)]
        fsem = Sem(ctx, nc, "fsem")
        wcst = Sem(ctx, nc, "wcst")
        gscat = Sem(ctx, nc, "gscat")
        w1c = [Sem(ctx, nc, f"w1c{j}") for j in range(4)]
        w1g = [Sem(ctx, nc, f"w1g{j}") for j in range(4)]
        w1r = [Sem(ctx, nc, f"w1r{j}") for j in range(4)]
        st = Sem(ctx, nc, "st")
        COMPUTE = [pe, pesc, peav, ac, dv]

        block = E(nc.Block())
        TEN, VEC, SCA, GPS, SYN = (block.tensor, block.vector, block.scalar,
                                   block.gpsimd, block.sync)

        def fence(eng):
            for s in COMPUTE:
                s.w(eng)

        def chain(*steps):
            for secwrap, sem, fn in steps:
                def body(eng, fn=fn, sem=sem):
                    fence(eng)
                    sem.inc(fn(eng))
                secwrap(body)

        def pchain(steps):
            counts = []
            for secwrap, sem, fn, deps, extra in steps:
                def body(eng, fn=fn, sem=sem, deps=tuple(deps), extra=tuple(extra)):
                    for s_, v_ in extra:
                        s_.w(eng, v_)
                    for d_ in deps:
                        ds, dval = counts[d_]
                        ds.w(eng, dval)
                    sem.inc(fn(eng))
                secwrap(body)
                counts.append((sem, sem.n))
            return counts

        @SYN
        def _(sy):
            for dst, src in [(idf_s, idf), (idb_s, idb), (or_b, ones_r_b),
                             (or_f, ones_r_f), (oc_b, ones_c_b), (oc_f, ones_c_f), (w8_s, w8c),
                             (pi_s, pairind), (ec_s, ecol1), (wrt_s, wrt_h),
                             (bq_s, bq_h), (bk_s, bk_h),
                             (rowf, bv_h)]:
                qS.dma(sy.dma_start(out=dst[:, :], in_=src[:, :]))

        @VEC
        def _(ve):
            qS.w(ve)
            dv.inc(ve.memset(epsA[:, :], EPS1))
            dv.inc(ve.memset(epsC[:, :], EPS3))
            dv.inc(ve.tensor_copy(out=bv_b[:, :], in_=rowf[:, :]))

        @SYN
        def _(sy):
            dv.w(sy)
            qS.dma(sy.dma_start(out=rowf[:, :], in_=bo_h[:, :]))

        @VEC
        def _(ve):
            qS.w(ve)
            dv.inc(ve.tensor_copy(out=bo_b[:, :], in_=rowf[:, :]))

        ctxA = ctx.enter_context(ExitStack())
        EA = ctxA.enter_context
        xnT_par = EA(nc.sbuf_tensor("xnT_par", [128, 8 * T], BF16))
        scstg = [EA(nc.sbuf_tensor(f"scstg{j}", [128, D], BF16)) for j in range(2)]
        O_u = EA(nc.sbuf_tensor("O_u", [128, 8 * T], BF16))

        ctxA1 = ExitStack()
        EA1 = ctxA1.enter_context
        xtile = [EA1(nc.sbuf_tensor(f"xt1_{j}", [128, D], F32)) for j in range(4)]
        sqs = [EA1(nc.sbuf_tensor(f"sqs{j}", [128, D], F32)) for j in range(4)]
        xnf = [EA1(nc.sbuf_tensor(f"xnf{j}", [128, D], F32)) for j in range(4)]
        xtf = EA1(nc.sbuf_tensor("xtf", [128, 256], F32))
        st1 = [EA1(nc.sbuf_tensor(f"st1_{j}", [128, 1], F32)) for j in range(4)]
        st2 = [EA1(nc.sbuf_tensor(f"st2_{j}", [128, 1], F32)) for j in range(4)]
        st3 = [EA1(nc.sbuf_tensor(f"st3_{j}", [128, 1], F32)) for j in range(4)]
        st4 = [EA1(nc.sbuf_tensor(f"st4_{j}", [128, 1], F32)) for j in range(4)]
        mst1 = EA1(nc.sbuf_tensor("mst1", [128, 1], F32))
        mst3 = EA1(nc.sbuf_tensor("mst3", [128, 1], F32))
        lgs = EA1(nc.sbuf_tensor("lgs", [128, 8], F32))
        eqs = EA1(nc.sbuf_tensor("eqs", [128, 8], F32))
        v8s = EA1(nc.sbuf_tensor("v8s", [128, 8], F32))
        mkb = EA1(nc.sbuf_tensor("mkb", [128, 8], BF16))
        psT = [EA1(nc.psum_tensor(f"psT{j}", [128, 128], F32)) for j in range(3)]
        pidx = EA1(nc.psum_tensor("pidx", [1, 1024], F32))
        pslog = [EA1(nc.psum_tensor(f"pslog{j}", [128, 8], F32)) for j in range(2)]
        psmT = EA1(nc.psum_tensor("psmT", [8, 128], BF16))

        NT = S // 128
        acSq = {}; acRstd = {}; dvXnf = {}; dvS2 = {}; dvVar = {}; dvEvac = {}
        dvLog = {}; dvMkb = {}; dvMT = {}; peTr = {}; peLog = {}; peMT = {}

        for tt in range(NT):
            j = tt % 4
            own = tt < 16

            @SYN
            def _(sy, tt=tt, j=j):
                if tt >= 4:
                    ac.w(sy, acSq[tt - 4])
                    dv.w(sy, dvXnf[tt - 4])
                xsp[j].dma(sy.dma_start(out=xtile[j][:, :],
                                        in_=x_h[tt * 128:(tt + 1) * 128, :]))

            @SCA
            def _(se, tt=tt, j=j):
                xsp[j].w(se)
                if tt >= 4:
                    dv.w(se, dvS2[tt - 4])
                ac.inc(se.activation(out=sqs[j][:, :], in_=xtile[j][:, :], func=AF.Square))
                acSq[tt] = ac.n

            @VEC
            def _(ve, tt=tt, j=j):
                xsp[j].w(ve)
                if tt >= 4:
                    dv.w(ve, dvXnf[tt - 4])
                    ac.w(ve, acRstd[tt - 4])
                dv.inc(ve.reduce_sum(out=st1[j][:, :], in_=xtile[j][:, :], axis=AX))
                dv.w(ve)
                dv.inc(ve.tensor_scalar_mul(out=st1[j][:, :], in0=st1[j][:, :], scalar1=1.0 / D))
                dv.w(ve)
                dv.inc(ve.tensor_mul(out=st3[j][:, :], in0=st1[j][:, :], in1=st1[j][:, :]))
                dv.w(ve)
                ac.w(ve, acSq[tt])
                dv.inc(ve.reduce_sum(out=st2[j][:, :], in_=sqs[j][:, :], axis=AX))
                dvS2[tt] = dv.n
                dv.w(ve)
                dv.inc(ve.tensor_scalar_mul(out=st2[j][:, :], in0=st2[j][:, :], scalar1=1.0 / D))
                dv.w(ve)
                dv.inc(ve.tensor_sub(out=st2[j][:, :], in0=st2[j][:, :], in1=st3[j][:, :]))
                dvVar[tt] = dv.n

            @SCA
            def _(se, tt=tt, j=j):
                dv.w(se, dvVar[tt])
                ac.inc(se.activation(out=st4[j][:, :], in_=st2[j][:, :], func=AF.Ln, bias=epsA[:, :]))
                ac.w(se)
                ac.inc(se.activation(out=st4[j][:, :], in_=st4[j][:, :], func=AF.Exp, scale=-0.5))
                acRstd[tt] = ac.n

            @VEC
            def _(ve, tt=tt, j=j, own=own):
                ac.w(ve, acRstd[tt])
                if tt >= 4:
                    pe.w(ve, peTr[(tt - 4, 7)])
                    if tt - 4 < 16:
                        xnsc.w(ve, 16 * (tt - 3))
                dv.inc(ve.tensor_scalar(out=xnf[j][:, :], in0=xtile[j][:, :],
                                        scalar1=st1[j][:, :], scalar2=st4[j][:, :],
                                        op0=ALU.subtract, op1=ALU.mult))
                dvXnf[tt] = dv.n

            if own:
                @GPS
                def _(ge, tt=tt, j=j):
                    dv.w(ge, dvXnf[tt])
                    xnsc.w(ge, 16 * tt)
                    xnsc.dma(ge.dma_start(out=xn_scr[tt * 128:(tt + 1) * 128, :],
                                          in_=xnf[j][:, :]))

            dst = xnT_own if own else xnT_par
            toff = (tt % 16) * 128
            for dc in range(8):
                pj = (tt * 8 + dc) % 3

                @TEN
                def _(te, tt=tt, dc=dc, j=j, pj=pj):
                    dv.w(te, dvXnf[tt])
                    prior = (tt * 8 + dc) - 3
                    if prior >= 0:
                        dv.w(te, dvEvac[(prior // 8, prior % 8)])
                    pe.inc(te.transpose(psT[pj][:, :],
                                        xnf[j][:, dc * 128:(dc + 1) * 128], idf_s[:, :]))
                    peTr[(tt, dc)] = pe.n

                @VEC
                def _(ve, tt=tt, dc=dc, dst=dst, toff=toff, own=own, pj=pj):
                    pe.w(ve, peTr[(tt, dc)])
                    dv.inc(ve.tensor_copy(out=dst[:, dc * T + toff:dc * T + toff + 128],
                                          in_=psT[pj][:, :]))
                    if own:
                        dv.inc(ve.tensor_copy(
                            out=xtf[:, (dc % 2) * 128:(dc % 2) * 128 + 128],
                            in_=psT[pj][:, :]))
                    dvEvac[(tt, dc)] = dv.n

                if own:
                    @TEN
                    def _(te, tt=tt, dc=dc):
                        dv.w(te, dvEvac[(tt, dc)])
                        if dc == 0 and tt >= 2:
                            dv.w(te, dvLog[tt - 2])
                        ins = te.matmul(pslog[tt % 2][:, :],
                                        xtf[:, (dc % 2) * 128:(dc % 2) * 128 + 128],
                                        wrt_s[:, dc * 8:(dc + 1) * 8],
                                        start=(dc == 0), stop=(dc == 7))
                        if dc == 7:
                            pe.inc(ins)
                            peLog[tt] = pe.n

            if own:
                @VEC
                def _(ve, tt=tt):
                    pe.w(ve, peLog[tt])
                    dv.inc(ve.tensor_copy(out=lgs[:, :], in_=pslog[tt % 2][:, :]))
                    dvLog[tt] = dv.n
                    dv.w(ve)
                    dv.inc(ve.reduce_max(out=mst1[:, :], in_=lgs[:, :], axis=AX))
                    dv.w(ve)
                    dv.inc(ve.tensor_scalar(out=eqs[:, :], in0=lgs[:, :],
                                            scalar1=mst1[:, :], scalar2=None,
                                            op0=ALU.is_equal))
                    dv.w(ve)
                    dv.inc(ve.tensor_mul(out=v8s[:, :], in0=eqs[:, :], in1=w8_s[:, :]))
                    dv.w(ve)
                    dv.inc(ve.reduce_max(out=mst3[:, :], in_=v8s[:, :], axis=AX))
                    dv.w(ve)
                    dv.inc(ve.tensor_scalar(out=mkb[:, :], in0=v8s[:, :],
                                            scalar1=mst3[:, :], scalar2=None,
                                            op0=ALU.is_equal))
                    dvMkb[tt] = dv.n

                @TEN
                def _(te, tt=tt):
                    dv.w(te, dvMkb[tt])
                    if tt >= 1:
                        dv.w(te, dvMT[tt - 1])
                    pe.inc(te.transpose(psmT[:, :], mkb[:, :], idb_s[:, :]))
                    peMT[tt] = pe.n

                @VEC
                def _(ve, tt=tt, toff=toff):
                    pe.w(ve, peMT[tt])
                    dv.inc(ve.tensor_copy(out=maskT[:, toff:toff + 128], in_=psmT[:, :]))
                    dvMT[tt] = dv.n

        prefA = EA1(nc.sbuf_tensor("prefA", [8, 1024], F32))
        prefB = EA1(nc.sbuf_tensor("prefB", [8, 1024], F32))
        zeros8 = EA1(nc.sbuf_tensor("zeros8", [8, 1024], BF16))
        drow = EA1(nc.sbuf_tensor("drow", [1, 1024], F32))
        vrow = EA1(nc.sbuf_tensor("vrow", [1, 1024], F32))
        destc = EA1(nc.sbuf_tensor("destc", [128, 16], F32))
        stgb = EA1(nc.sbuf_tensor("stgb", [128, D], BF16))

        @VEC
        def _(ve):
            dv.inc(ve.memset(zeros8[:, :], 0.0))
            dv.inc(ve.memset(stgb[:, :], 0.0))

        @SYN
        def _(sy):
            dv.w(sy)
            for h2 in range(2):
                for zb in range(16):
                    fsem.dma(sy.dma_start(out=xe_scr[h2][zb * 128:(zb + 1) * 128, :],
                                          in_=stgb[:, :]))
                fsem.dma(sy.dma_start(out=xe_scr[h2][2048:2049, :], in_=stgb[0:1, :]))

        for half in range(2):
            mTh = maskT[:, half * 1024:(half + 1) * 1024]
            chain(
                (VEC, dv, lambda ve, mTh=mTh: ve.tensor_tensor_scan(
                    out=prefA[:, :], data0=mTh, data1=zeros8[:, :],
                    initial=0.0, op0=ALU.add, op1=ALU.add)),
                (VEC, dv, lambda ve: ve.tensor_scalar(
                    out=prefB[:, :], in0=prefA[:, :], scalar1=256.0, scalar2=None,
                    op0=ALU.is_le)),
                (VEC, dv, lambda ve: ve.tensor_scalar_add(
                    out=prefA[:, :], in0=prefA[:, :], scalar1=ec_s[:, :])),
                (VEC, dv, lambda ve, mTh=mTh: ve.tensor_tensor(
                    out=prefB[:, :], in0=mTh, in1=prefB[:, :], op=ALU.mult)),
                (VEC, dv, lambda ve: ve.tensor_tensor(
                    out=prefA[:, :], in0=prefB[:, :], in1=prefA[:, :], op=ALU.mult)),
                (TEN, pe, lambda te: te.matmul(pidx[0:1, 0:512], oc_f[0:8, :],
                                               prefA[:, 0:512], start=True, stop=True)),
                (TEN, pe, lambda te: te.matmul(pidx[0:1, 512:1024], oc_f[0:8, :],
                                               prefA[:, 512:1024], start=True, stop=True)),
                (VEC, dv, lambda ve: ve.tensor_copy(out=drow[:, :], in_=pidx[0:1, :])),
                (TEN, pe, lambda te: te.matmul(pidx[0:1, 0:512], oc_f[0:8, :],
                                               prefB[:, 0:512], start=True, stop=True)),
                (TEN, pe, lambda te: te.matmul(pidx[0:1, 512:1024], oc_f[0:8, :],
                                               prefB[:, 512:1024], start=True, stop=True)),
                (VEC, dv, lambda ve: ve.tensor_copy(out=vrow[:, :], in_=pidx[0:1, :])),
                (VEC, dv, lambda ve: ve.scalar_tensor_tensor(
                    out=drow[:, :], in0=vrow[:, :], scalar=-2048.0, in1=drow[:, :],
                    op0=ALU.mult, op1=ALU.add)),
                (VEC, dv, lambda ve: ve.tensor_scalar_add(
                    out=drow[:, :], in0=drow[:, :], scalar1=2048.0)),
            )
            for tb in range(8):
                chain(
                    (TEN, pe, lambda te, tb=tb: te.transpose(
                        psT[0][:, 0:1], drow[0:1, tb * 128:(tb + 1) * 128],
                        idf_s[0:1, 0:1])),
                    (VEC, dv, lambda ve, tb=tb, half=half: ve.tensor_copy(
                        out=destc[:, half * 8 + tb:half * 8 + tb + 1],
                        in_=psT[0][0:128, 0:1])),
                )
        chain((VEC, dv, lambda ve: ve.tensor_copy(out=desti2[:, :], in_=destc[:, :])))
        dv_idx2 = dv.n
        fsem_zf = fsem.n

        ctxA1.close()

        ctxA2 = ExitStack()
        E2 = ctxA2.enter_context
        Qg = E2(nc.sbuf_tensor("Qg", [128, 4 * T], BF16))
        Kg = E2(nc.sbuf_tensor("Kg", [128, 2 * S], BF16))
        Vg = E2(nc.sbuf_tensor("Vg", [128, 32 * HPG * 65 + 64], BF16))
        sq4 = E2(nc.sbuf_tensor("sq4", [128, T], BF16))
        wqg = E2(nc.sbuf_tensor("wqg", [128, 2048], BF16))
        wkg = E2(nc.sbuf_tensor("wkg", [128, 2048], BF16))
        ring = [E2(nc.sbuf_tensor(f"ring{r}", [128, 1024], BF16)) for r in range(3)]
        sring = E2(nc.sbuf_tensor("sring", [1, 512], F32))
        S16 = E2(nc.sbuf_tensor("S16", [16, T], F32))
        negm8 = E2(nc.sbuf_tensor("negm8", [128, 1], F32))
        t11a = E2(nc.sbuf_tensor("t11a", [1, 1], F32))
        t11b = E2(nc.sbuf_tensor("t11b", [1, 1], F32))
        t11d = E2(nc.sbuf_tensor("t11d", [1, 1], F32))
        t11c = E2(nc.sbuf_tensor("t11c", [1, 1], BF16))
        sc = [E2(nc.psum_tensor(f"sc{j}", [128, 1024], F32)) for j in range(3)]
        oacc = [E2(nc.psum_tensor(f"oacc{j}", [128, 512], F32)) for j in range(2)]

        @GPS
        def _(ge):
            fence(ge)
            xnsc.w(ge)
            fsem.w(ge)
            gp.inc(ge.memset(Vg[:, :], 0.0))
            gp.w(ge)
            gp.inc(ge.memset(
                Vg[:, 0:32 * HPG * 65].rearrange("p (c s) -> p c s", s=65)[:, :, 64:65], 1.0))
            gp.inc(ge.memset(Qg[:, :], 0.0))
            qG.dma(ge.dma_start(out=wqg[:, :], in_=wq_h[0]))
            qG.dma(ge.dma_start(out=wkg[:, :], in_=wk_h[0]))

        def xnT_at(tok):
            return (xnT_own, tok) if tok < T else (xnT_par, tok - T)

        def _precast_pair(k):
            g, kk = divmod(k, 36)
            if kk < 32:
                e, iq4 = divmod(g * 32 + kk, 16)
                return w1_h[e, iq4], w1bf[e, iq4]
            w = g * 4 + (kk - 32)
            d2c, ch = divmod(w, 2)
            return (wo2_h[d2c][:, ch * 2048:(ch + 1) * 2048],
                    wo2bf[d2c][:, ch * 2048:(ch + 1) * 2048])

        def w1_precast_batch(g):
            @GPS
            def _(ge, g=g):
                for kk in range(36):
                    k = g * 36 + kk
                    src, dst = _precast_pair(k)
                    if k >= 2:
                        w1c[k % 2].w(ge, 16 * (k // 2))
                    w1g[k % 2].dma(ge.dma_start(out=w1b[k % 2][:, :], in_=src))

            @SYN
            def _(sy, g=g):
                for kk in range(36):
                    k = g * 36 + kk
                    src, dst = _precast_pair(k)
                    sy.wait_ge(w1g[k % 2].h, 16 * (k // 2 + 1))
                    w1c[k % 2].dma(sy.dma_start(out=dst, in_=w1b[k % 2][:, :]))

        for g in range(NG):
            if g > 0:
                @GPS
                def _(ge, g=g):
                    fence(ge)
                    qG.dma(ge.dma_start(out=wqg[:, :], in_=wq_h[g]))
                    qG.dma(ge.dma_start(out=wkg[:, :], in_=wk_h[g]))

            peP = {}; dvP = {}
            for t in range(8):
                d2cl, qs = divmod(t, 4)

                @TEN
                def _(te, t=t, d2cl=d2cl, qs=qs):
                    qG.w(te)
                    gp.w(te)
                    if t == 0:
                        ac.w(te)
                        dv.w(te)
                    if t >= 2:
                        dv.w(te, dvP[t - 2])
                    for dc in range(8):
                        ins = te.matmul(
                            sc[t % 2][:, 0:512],
                            wqg[:, dc * 256 + d2cl * 128:dc * 256 + (d2cl + 1) * 128],
                            xnT_own[:, dc * T + qs * 512:dc * T + (qs + 1) * 512],
                            start=(dc == 0), stop=(dc == 7))
                    pe.inc(ins)
                    peP[t] = pe.n

                @VEC
                def _(ve, t=t, d2cl=d2cl, qs=qs, g=g):
                    pe.w(ve, peP[t])
                    gp.w(ve)
                    col = 2 * g + d2cl
                    for hh in range(2):
                        hl = 2 * d2cl + hh
                        dv.inc(ve.tensor_scalar_add(
                            out=Qg[hh * 64:hh * 64 + 64,
                                   hl * T + qs * 512:hl * T + (qs + 1) * 512],
                            in0=sc[t % 2][hh * 64:hh * 64 + 64, 0:512],
                            scalar1=bq_s[hh * 64:hh * 64 + 64, col:col + 1]))
                    dvP[t] = dv.n

            peK = {}; dvK = {}
            for t in range(16):
                d2cl, ks = divmod(t, 8)

                @TEN
                def _(te, t=t, d2cl=d2cl, ks=ks):
                    if t < 2:
                        dv.w(te, dvP[6 + t])

                    else:
                        dv.w(te, dvK[t - 2])
                    srcx, base = xnT_at(ks * 512)
                    for dc in range(8):
                        ins = te.matmul(
                            sc[t % 2][:, 0:512],
                            wkg[:, dc * 256 + d2cl * 128:dc * 256 + (d2cl + 1) * 128],
                            srcx[:, dc * T + base:dc * T + base + 512],
                            start=(dc == 0), stop=(dc == 7))
                    pe.inc(ins)
                    peK[t] = pe.n

                @VEC
                def _(ve, t=t, d2cl=d2cl, ks=ks, g=g):
                    pe.w(ve, peK[t])
                    dv.inc(ve.tensor_scalar_add(
                        out=Kg[:, d2cl * S + ks * 512:d2cl * S + (ks + 1) * 512],
                        in0=sc[t % 2][:, 0:512],
                        scalar1=bk_s[:, 2 * g + d2cl:2 * g + d2cl + 1]))
                    dvK[t] = dv.n

            @GPS
            def _(ge, g=g):
                pe.w(ge, peP[7])
                qG.dma(ge.dma_start(out=wqg[:, :], in_=wv_h[g]))

            w1_precast_batch(g)

            if g == 3:
                @GPS
                def _(ge):
                    ge.wait_ge(dv.h, dv_idx2)
                    ge.wait_ge(fsem.h, fsem_zf)
                    for k in range(16):
                        h2, tb = divmod(k, 8)
                        if k >= 2:
                            ge.wait_ge(isem.h, 32 * (k // 2))
                        gscat.dma(ge.dma_start(out=scstg[k % 2][:, :],
                                               in_=xn_scr[h2 * 1024 + tb * 128:
                                                          h2 * 1024 + (tb + 1) * 128, :]))
                        ge.wait_ge(gscat.h, 16 * (k + 1))
                        isem.dma(ge.indirect_dma_start(
                            out=xe_scr[h2][:, :],
                            out_offset=bass.IndirectOffsetOnAxis(
                                desti2[:, h2 * 8 + tb:h2 * 8 + tb + 1], 0),
                            in_=scstg[k % 2][:, :], in_offset=None))

            peV = {}; dvV = {}
            for kc in range(32):
                @TEN
                def _(te, kc=kc, g=g):
                    qG.w(te)
                    if kc < 2:
                        dv.w(te, dvK[14 + kc])
                    else:
                        dv.w(te, dvV[kc - 2])
                    src, base = xnT_at(kc * 128)
                    for dc in range(8):
                        te.matmul(sc[kc % 2][:, 0:256],
                                  src[:, dc * T + base:dc * T + base + 128],
                                  wqg[:, dc * 256:(dc + 1) * 256],
                                  start=(dc == 0), stop=False)
                    ins = te.matmul(sc[kc % 2][:, 0:256], or_b[:, :],
                                    bv_b[:, g * 256:(g + 1) * 256],
                                    start=False, stop=True)
                    pe.inc(ins)
                    peV[kc] = pe.n

                @VEC
                def _(ve, kc=kc):
                    pe.w(ve, peV[kc])
                    dv.inc(ve.tensor_copy(
                        out=Vg[:, kc * HPG * 65:(kc + 1) * HPG * 65]
                            .rearrange("p (h s) -> p h s", s=65)[:, :, 0:64],
                        in_=sc[kc % 2][:, 0:256].rearrange("p (h s) -> p h s", s=64)))
                    dvV[kc] = dv.n

            def nps(seg):
                j, hh = divmod(seg, 2)
                return sc[j][0:1, hh * 512:(hh + 1) * 512]

            steps = []
            for hl in range(4):
                steps.append((VEC, dv, lambda ve, hl=hl: ve.tensor_mul(
                    out=sq4[:, :], in0=Qg[:, hl * T:(hl + 1) * T],
                    in1=Qg[:, hl * T:(hl + 1) * T])))
                for seg in range(4):
                    steps.append((TEN, pe, lambda te, seg=seg: te.matmul(
                        nps(seg), oc_b[:, :], sq4[:, seg * 512:(seg + 1) * 512],
                        start=True, stop=True)))
                for seg in range(4):
                    steps.append((VEC, dv, lambda ve, seg=seg: ve.reduce_max(out=t11d[:, :], in_=nps(seg), axis=AX)))
                    if hl == 0 and seg == 0:
                        steps.append((VEC, dv, lambda ve: ve.tensor_copy(out=t11a[:, :], in_=t11d[:, :])))
                    else:
                        steps.append((VEC, dv, lambda ve: ve.tensor_max(out=t11a[:, :], in0=t11a[:, :], in1=t11d[:, :])))
            first = [True]
            for d2cl in range(2):
                for piece in range(2):
                    steps.append((VEC, dv, lambda ve, d2cl=d2cl, piece=piece: ve.tensor_mul(
                        out=sq4[:, :], in0=Kg[:, d2cl * S + piece * T:d2cl * S + (piece + 1) * T],
                        in1=Kg[:, d2cl * S + piece * T:d2cl * S + (piece + 1) * T])))
                    for seg in range(4):
                        steps.append((TEN, pe, lambda te, seg=seg: te.matmul(
                            nps(seg), oc_b[:, :], sq4[:, seg * 512:(seg + 1) * 512],
                            start=True, stop=True)))
                    for seg in range(4):
                        steps.append((VEC, dv, lambda ve, seg=seg: ve.reduce_max(
                            out=t11d[:, :], in_=nps(seg), axis=AX)))
                        if first[0]:
                            steps.append((VEC, dv, lambda ve: ve.tensor_copy(out=t11b[:, :], in_=t11d[:, :])))
                            first[0] = False
                        else:
                            steps.append((VEC, dv, lambda ve: ve.tensor_max(out=t11b[:, :], in0=t11b[:, :], in1=t11d[:, :])))
            steps.append((VEC, dv, lambda ve: ve.tensor_mul(out=t11a[:, :], in0=t11a[:, :], in1=t11b[:, :])))
            steps.append((SCA, ac, lambda se: se.activation(out=t11a[:, :], in_=t11a[:, :], func=AF.Ln)))
            steps.append((SCA, ac, lambda se: se.activation(out=t11a[:, :], in_=t11a[:, :], func=AF.Exp, scale=0.5)))
            steps.append((VEC, dv, lambda ve: ve.tensor_scalar_mul(out=t11c[:, :], in0=t11a[:, :], scalar1=-0.125)))
            steps.append((TEN, pe, lambda te: te.matmul(sc[0][:, 0:1], or_b[:, :], t11c[:, :], start=True, stop=True)))
            steps.append((VEC, dv, lambda ve: ve.tensor_copy(out=negm8[:, :], in_=sc[0][:, 0:1])))
            chain(*steps)

            base_ac, base_sc, base_av = ac.n, pesc.n, peav.n
            base_dv, base_ss = dv.n, ssem.n

            @TEN
            def _(te, g=g, base_ac=base_ac, base_dv=base_dv):
                dv.w(te)

                def spair(p):
                    if p >= 3:
                        te.wait_ge(ac.h, base_ac + p - 2)
                    for u in range(2):
                        m = 2 * p + u
                        hl = m // 128
                        qt = (m // 32) % 4
                        c = m % 32
                        ins = te.matmul(
                            sc[p % 3][:, u * 512:(u + 1) * 512],
                            Kg[:, (hl // 2) * S + c * 128:(hl // 2) * S + (c + 1) * 128],
                            Qg[:, hl * T + qt * 512:hl * T + (qt + 1) * 512],
                            start=True, stop=True)
                    pesc.inc(ins)

                for p in range(3):
                    spair(p)
                for n in range(512):
                    i, c = divmod(n, 32)
                    hl = n // 128
                    if n % 2 == 0 and n // 2 + 3 < 256:
                        spair(n // 2 + 3)
                    if c == 0 and i >= 2:
                        te.wait_ge(dv.h, base_dv + 2 * (i - 1))
                    te.wait_ge(ac.h, base_ac + n // 2 + 1)
                    ins = te.matmul(
                        oacc[i % 2][:, :],
                        Vg[:, (c * HPG + hl) * 65:(c * HPG + hl) * 65 + 128],
                        ring[(n // 2) % 3][:, (n % 2) * 512:(n % 2 + 1) * 512],
                        start=(c == 0), stop=(c == 31))
                    peav.inc(ins)

            @SCA
            def _(se, base_sc=base_sc, base_av=base_av, dvpre=dv.n):
                se.wait_ge(dv.h, dvpre)
                for p in range(256):
                    se.wait_ge(pesc.h, base_sc + p + 1)
                    if p >= 3:
                        se.wait_ge(peav.h, base_av + 2 * p - 4)
                    ac.inc(se.activation(out=ring[p % 3][:, :], in_=sc[p % 3][:, :],
                                         func=AF.Exp, bias=negm8[:, :], scale=0.125))

            @VEC
            def _(ve, g=g, base_av=base_av, base_ss=base_ss):
                for i in range(16):
                    hl = i // 4
                    qt = i % 4
                    h = HPG * g + hl
                    ve.wait_ge(peav.h, base_av + 32 * (i + 1))
                    dv.inc(ve.tensor_copy(
                        out=O_u[(h % 2) * 64:(h % 2) * 64 + 64,
                                (h // 2) * T + qt * 512:(h // 2) * T + (qt + 1) * 512],
                        in_=oacc[i % 2][0:64, :]))
                    if base_ss + 16 * i > 0:
                        ve.wait_ge(ssem.h, base_ss + 16 * i)
                    dv.inc(ve.tensor_copy(out=sring[:, :], in_=oacc[i % 2][64:65, :]))

            @SYN
            def _(sy, g=g, base_dv=base_dv):
                for i in range(16):
                    hl = i // 4
                    qt = i % 4
                    h = HPG * g + hl
                    sy.wait_ge(dv.h, base_dv + 2 * (i + 1))
                    ssem.dma(sy.dma_start(out=S16[h:h + 1, qt * 512:(qt + 1) * 512],
                                          in_=sring[:, :]))

        ctxA2.close()

        ctxB3 = ExitStack()
        E3 = ctxB3.enter_context
        wo_s = E3(nc.sbuf_tensor("wo_s", [128, 8 * 1024], BF16))
        sclb = E3(nc.sbuf_tensor("sclb", [128, 2048], BF16))
        r_att = [E3(nc.sbuf_tensor(f"r_att{j}", [128, D], F32)) for j in range(2)]
        ln2sq = [E3(nc.sbuf_tensor(f"ln2sq{j}", [128, D], F32)) for j in range(2)]
        l2a = [E3(nc.sbuf_tensor(f"l2a{j}", [128, 1], F32)) for j in range(2)]
        l2b = [E3(nc.sbuf_tensor(f"l2b{j}", [128, 1], F32)) for j in range(2)]
        l2c = [E3(nc.sbuf_tensor(f"l2c{j}", [128, 1], F32)) for j in range(2)]
        attnb = [E3(nc.sbuf_tensor(f"attnb3_{j}", [128, D], BF16)) for j in range(2)]
        T16 = E3(nc.sbuf_tensor("T16", [16, T], F32))
        R16 = E3(nc.sbuf_tensor("R16", [16, T], BF16))
        psb = [E3(nc.psum_tensor(f"psb{j}", [128, 1024], F32)) for j in range(2)]

        @GPS
        def _(ge):
            fence(ge)
            ssem.w(ge)
            qG.dma(ge.dma_start(out=wo_s[:, :], in_=wo_h[:, :]))

        @SCA
        def _(se):
            ssem.w(se)
            fence(se)
            ac.inc(se.activation(out=T16[:, :], in_=S16[:, :], func=AF.Ln))
            ac.w(se)
            ac.inc(se.activation(out=R16[:, :], in_=T16[:, :], func=AF.Exp, scale=-1.0))

        peB = {}; dvB = {}
        for dc in range(8):
            for piece in range(2):
                t = dc * 2 + piece

                @TEN
                def _(te, dc=dc, piece=piece, t=t):
                    ac.w(te)
                    if t >= 2:
                        dv.w(te, dvB[t - 2])
                    for hb in range(2):
                        ins = te.matmul(
                            psb[t % 2][:, hb * 512:(hb + 1) * 512],
                            pi_s[:, dc * 128:(dc + 1) * 128],
                            R16[:, piece * 1024 + hb * 512:piece * 1024 + (hb + 1) * 512],
                            start=True, stop=True)
                    pe.inc(ins)
                    peB[t] = pe.n

                @VEC
                def _(ve, dc=dc, piece=piece, t=t):
                    pe.w(ve, peB[t])
                    dv.inc(ve.tensor_copy(out=sclb[:, piece * 1024:(piece + 1) * 1024],
                                          in_=psb[t % 2][:, :]))
                    dv.w(ve)
                    dv.inc(ve.tensor_mul(
                        out=O_u[:, dc * T + piece * 1024:dc * T + (piece + 1) * 1024],
                        in0=O_u[:, dc * T + piece * 1024:dc * T + (piece + 1) * 1024],
                        in1=sclb[:, piece * 1024:(piece + 1) * 1024]))
                    dvB[t] = dv.n

        peY = {}; acR = {}; dvL2 = {}; acL2 = {}
        for tt in range(16):
            @TEN
            def _(te, tt=tt):
                qG.w(te)
                if tt < 2:
                    dv.w(te, dvB[15])
                else:
                    ac.w(te, acR[tt - 2])
                for dc in range(8):
                    for hb in range(2):
                        te.matmul(psb[tt % 2][:, hb * 512:(hb + 1) * 512],
                                  O_u[:, dc * T + tt * 128:dc * T + (tt + 1) * 128],
                                  wo_s[:, dc * 1024 + hb * 512:dc * 1024 + (hb + 1) * 512],
                                  start=(dc == 0), stop=False)
                for hb in range(2):
                    ins = te.matmul(psb[tt % 2][:, hb * 512:(hb + 1) * 512],
                                    or_b[:, :], bo_b[:, hb * 512:(hb + 1) * 512],
                                    start=False, stop=True)
                pe.inc(ins)
                peY[tt] = pe.n

            @SCA
            def _(se, tt=tt):
                pe.w(se, peY[tt])
                if tt >= 2:
                    dv.w(se, dvL2[tt - 2])
                ac.inc(se.activation(out=r_att[tt % 2][:, :], in_=psb[tt % 2][:, :],
                                     func=AF.Relu))
                acR[tt] = ac.n

            jj = tt % 2
            ext0 = [(ac, acR[tt])]
            if tt >= 2:
                ext0 = ext0 + [(dv, dvL2[tt - 2]), (ac, acL2.get(tt - 2, 0))]
            ext9 = [(ascr, 16 * (tt - 1))] if tt >= 2 else []
            cts = pchain([
                (VEC, dv, (lambda ve, jj=jj, tt=tt: ve.reduce_sum(
                    out=l2a[jj][:, :], in_=r_att[jj][:, :], axis=AX)), [], ext0),
                (VEC, dv, (lambda ve, jj=jj: ve.tensor_scalar_mul(
                    out=l2a[jj][:, :], in0=l2a[jj][:, :], scalar1=1.0 / D)), [0], []),
                (VEC, dv, (lambda ve, jj=jj: ve.tensor_mul(
                    out=l2c[jj][:, :], in0=l2a[jj][:, :], in1=l2a[jj][:, :])), [1], []),
                (SCA, ac, (lambda se, jj=jj, tt=tt: se.activation(
                    out=ln2sq[jj][:, :], in_=r_att[jj][:, :], func=AF.Square)), [], ext0),
                (VEC, dv, (lambda ve, jj=jj: ve.reduce_sum(
                    out=l2b[jj][:, :], in_=ln2sq[jj][:, :], axis=AX)), [3], []),
                (VEC, dv, (lambda ve, jj=jj: ve.tensor_scalar_mul(
                    out=l2b[jj][:, :], in0=l2b[jj][:, :], scalar1=1.0 / D)), [4], []),
                (VEC, dv, (lambda ve, jj=jj: ve.tensor_sub(
                    out=l2b[jj][:, :], in0=l2b[jj][:, :], in1=l2c[jj][:, :])), [5, 2], []),
                (SCA, ac, (lambda se, jj=jj: se.activation(
                    out=l2b[jj][:, :], in_=l2b[jj][:, :], func=AF.Ln, bias=epsA[:, :])), [6], []),
                (SCA, ac, (lambda se, jj=jj: se.activation(
                    out=l2b[jj][:, :], in_=l2b[jj][:, :], func=AF.Exp, scale=-0.5)), [7], []),
                (VEC, dv, (lambda ve, jj=jj: ve.tensor_scalar(
                    out=attnb[jj][:, :], in0=r_att[jj][:, :],
                    scalar1=l2a[jj][:, :], scalar2=l2b[jj][:, :],
                    op0=ALU.subtract, op1=ALU.mult)), [8, 1], ext9),
            ])
            acL2[tt] = cts[8][1]
            dvL2[tt] = dv.n

            @SYN
            def _(sy, tt=tt):
                dv.w(sy, dvL2[tt])
                ascr.w(sy, 16 * tt)
                ascr.dma(sy.dma_start(out=attn_scr[tt * 128:(tt + 1) * 128, :],
                                      in_=attnb[tt % 2][:, :]))

        ctxB3.close()
        ctxA.close()

        ctxBB = ctx.enter_context(ExitStack())
        E4 = ctxBB.enter_context
        gaf = E4(nc.sbuf_tensor("gaf", [1, D], F32))
        bef = E4(nc.sbuf_tensor("bef", [1, D], F32))
        b1_s = E4(nc.sbuf_tensor("b1_s", [128, 256], F32))
        bo2_s = E4(nc.sbuf_tensor("bo2_s", [128, 8], F32))
        xe_sb = E4(nc.sbuf_tensor("xe_sb", [128, 16 * 1024], BF16))
        g_e = [E4(nc.sbuf_tensor(f"g_e{j}", [128, 8 * 256], BF16)) for j in range(2)]
        ytT = [E4(nc.sbuf_tensor(f"ytT{j}", [128, 256], F32)) for j in range(2)]
        fgt = [E4(nc.sbuf_tensor(f"fgt{j}", [128, D], BF16)) for j in range(2)]
        Gab = E4(nc.sbuf_tensor("Gab", [128, D], BF16))
        Beb = E4(nc.sbuf_tensor("Beb", [128, D], BF16))
        z3 = [E4(nc.sbuf_tensor(f"z3_{j}", [128, D], F32)) for j in range(2)]
        ln3sq = z3
        l3a = [E4(nc.sbuf_tensor(f"l3a{j}", [128, 1], F32)) for j in range(2)]
        l3b = [E4(nc.sbuf_tensor(f"l3b{j}", [128, 1], F32)) for j in range(2)]
        l3c = [E4(nc.sbuf_tensor(f"l3c{j}", [128, 1], F32)) for j in range(2)]
        xtile = [E4(nc.sbuf_tensor(f"xt4_{j}", [128, D], F32)) for j in range(2)]
        outb = [E4(nc.sbuf_tensor("outb0", [128, D], F32))] * 2
        attnb = [E4(nc.sbuf_tensor(f"attnb4_{j}", [128, D], BF16)) for j in range(2)]
        w1b4 = w1b + [E4(nc.sbuf_tensor(f"w1b4_{j}", [128, 8 * 256], BF16))
                      for j in range(2)]
        psz = [E4(nc.psum_tensor(f"psz{j}", [128, 1024], F32)) for j in range(2)]
        ptr = [E4(nc.psum_tensor(f"ptr{j}", [128, 128], F32)) for j in range(2)]
        ptrb = [E4(nc.psum_tensor(f"ptrb{j}", [128, 128], BF16)) for j in range(2)]

    

        @VEC
        def _(ve):
            ascr.w(ve)

        @SCA
        def _(se):
            ascr.w(se)

        @SYN
        def _(sy):
            ascr.w(sy)
            fence(sy)
            for dst, hh in [(gaf, ga_h), (bef, be_h), (b1_s, b1_h), (bo2_s, bo2_h)]:
                qS.dma(sy.dma_start(out=dst[:, :], in_=hh[:, :]))

        @TEN
        def _(te):
            qS.w(te)

        chain(
            (TEN, pe, lambda te: te.matmul(psz[0][:, 0:512], or_f[:, :], gaf[:, 0:512], start=True, stop=True)),
            (TEN, pe, lambda te: te.matmul(psz[0][:, 512:1024], or_f[:, :], gaf[:, 512:1024], start=True, stop=True)),
            (VEC, dv, lambda ve: ve.tensor_copy(out=Gab[:, :], in_=psz[0][:, :])),
            (TEN, pe, lambda te: te.matmul(psz[1][:, 0:512], or_f[:, :], bef[:, 0:512], start=True, stop=True)),
            (TEN, pe, lambda te: te.matmul(psz[1][:, 512:1024], or_f[:, :], bef[:, 512:1024], start=True, stop=True)),
            (VEC, dv, lambda ve: ve.tensor_copy(out=Beb[:, :], in_=psz[1][:, :])),
        )

        xnsc_total = xnsc.n
        isem_scat = isem.n
        for half in range(2):
            ctxH1 = ExitStack()
            xeT = ctxH1.enter_context(nc.sbuf_tensor(f"xeT{half}", [128, 8 * 2048], BF16))

            @SYN
            def _(sy, half=half):
                isem.w(sy, isem_scat)
                fsem.w(sy)
                fsem.dma(sy.dma_start(
                    out=xe_sb[:, :].rearrange("p (c m) -> p c m", c=16),
                    in_=xe_scr[half][0:2048, :].rearrange("(c p) m -> p c m", p=128)))

            peXT = {}
            for sb in range(16):
                for dcb in range(8):
                    b = sb * 8 + dcb

                    @TEN
                    def _(te, sb=sb, dcb=dcb, b=b):
                        if b == 0:
                            fsem.w(te)
                        if b >= 2:
                            dv.w(te, peXT[b - 2])
                        pe.inc(te.transpose(ptrb[b % 2][:, :],
                                            xe_sb[:, sb * 1024 + dcb * 128:sb * 1024 + (dcb + 1) * 128],
                                            idb_s[:, :]))

                    @VEC
                    def _(ve, sb=sb, dcb=dcb, b=b):
                        pe.w(ve)
                        dv.inc(ve.tensor_copy(
                            out=xeT[:, dcb * 2048 + sb * 128:dcb * 2048 + (sb + 1) * 128],
                            in_=ptrb[b % 2][:, :]))
                        peXT[b] = dv.n
            dvXET = dv.n

            base_pez = pe.n
            base_acA = ac.n
            pe_after_z = {}
            cnt = base_pez
            for z in range(256):
                cnt += 1
                pe_after_z[z] = cnt
            ac_after_z = {z: base_acA + z + 1 for z in range(256)}

            @SYN
            def _(sy, half=half, base_pez=base_pez, pe_after_z=pe_after_z):
                if half == 0:
                    for jw in range(4):
                        w1c[jw].w(sy)
                for k in range(128):
                    e, iq4 = divmod(k, 16)
                    if k >= 4:
                        sy.wait_ge(pe.h, pe_after_z[2 * k - 7])
                    elif half == 1:
                        sy.wait_ge(pe.h, base_pez)
                    w1r[k % 4].dma(sy.dma_start(out=w1b4[k % 4][:, :], in_=w1bf[e, iq4]))

            @TEN
            def _(te, half=half, ac_after_z=ac_after_z, dvXET=dvXET):
                te.wait_ge(dv.h, dvXET)
                for e in range(NE):
                    for ic in range(32):
                        z = e * 32 + ic
                        kg = half * 128 + e * 16 + ic // 2
                        te.wait_ge(w1r[kg % 4].h, 16 * (kg // 4 + 1))
                        if z >= 4:
                            te.wait_ge(ac.h, ac_after_z[z - 4])
                        for dc in range(8):
                            ins = te.matmul(
                                psz[z % 2][:, (z // 2 % 2) * 512:(z // 2 % 2) * 512 + 256],
                                w1b4[kg % 4][:, dc * 256 + (ic % 2) * 128:
                                             dc * 256 + (ic % 2 + 1) * 128],
                                xeT[:, dc * 2048 + e * 256:dc * 2048 + (e + 1) * 256],
                                start=(dc == 0), stop=(dc == 7))
                        pe.inc(ins)

            @SCA
            def _(se, half=half, pe_after_z=pe_after_z):
                for e in range(NE):
                    for ic in range(32):
                        z = e * 32 + ic
                        se.wait_ge(pe.h, pe_after_z[z])
                        if ic % 8 == 0 and z >= 16:
                            bz = half * 32 + z // 8
                            gsem[bz % 2].w(se, 16 * ((bz - 2) // 2 + 1))
                        ac.inc(se.activation(out=g_e[(z // 8) % 2][:, (ic % 8) * 256:(ic % 8 + 1) * 256],
                                             in_=psz[z % 2][:, (z // 2 % 2) * 512:(z // 2 % 2) * 512 + 256],
                                             func=AF.Gelu,
                                             bias=b1_s[:, ic * 8 + e:ic * 8 + e + 1]))

            @GPS
            def _(ge, half=half, ac_after_z=ac_after_z):
                for e in range(NE):
                    for icb in range(4):
                        ge.wait_ge(ac.h, ac_after_z[e * 32 + icb * 8 + 7])
                        gsem[(e * 4 + icb) % 2].dma(ge.dma_start(
                            out=hg_scr[icb * 1024:(icb + 1) * 1024,
                                       e * 256:(e + 1) * 256]
                                .rearrange("(i p) s -> p i s", p=128),
                            in_=g_e[(e * 4 + icb) % 2][:, :]
                                .rearrange("p (i s) -> p i s", i=8)))

            ctxH1.close()

            ctxH2 = ExitStack()
            hgq = [ctxH2.enter_context(nc.sbuf_tensor(f"hgq{half}_{j}", [128, 32 * 256], BF16))
                   for j in range(2)]
            wo2d = [ctxH2.enter_context(nc.sbuf_tensor(f"wo2d{half}_{j}", [128, 4096], BF16))
                    for j in range(2)]
            peB1 = {}; dvS = {}; dvYe = {}
            gsem_all = (gsem[0].n, gsem[1].n)
            for sck in range(8):
                @SYN
                def _(sy, half=half, sck=sck):
                    sy.wait_ge(gsem[0].h, gsem_all[0])
                    sy.wait_ge(gsem[1].h, gsem_all[1])
                    if sck >= 2:
                        pe.w(sy, peB1[(sck - 2) * 8 + 7])
                    for ih in range(2):
                        hsem[sck % 2].dma(sy.dma_start(
                            out=hgq[sck % 2][:, ih * 4096:(ih + 1) * 4096]
                                .rearrange("p (i s) -> p i s", i=16),
                            in_=hg_scr[ih * 2048:(ih + 1) * 2048,
                                       sck * 256:(sck + 1) * 256]
                                .rearrange("(i p) s -> p i s", p=128)))

                for d2c in range(8):
                    t = sck * 8 + d2c

                    @GPS
                    def _(ge, d2c=d2c, t=t):
                        if t >= 2:
                            pe.w(ge, peB1.get(t - 2, 0))
                        elif t == 0:
                            wcst.w(ge)
                            pe.w(ge)
                        wo2s[t % 2].dma(ge.dma_start(out=wo2d[t % 2][:, :], in_=wo2bf[d2c]))

                    @TEN
                    def _(te, sck=sck, d2c=d2c, t=t, half=half):
                        gsck = half * 8 + sck
                        te.wait_ge(hsem[sck % 2].h, 32 * (gsck // 2 + 1))
                        te.wait_ge(wo2s[t % 2].h, 16 * ((half * 64 + t) // 2 + 1))
                        if t >= 4:
                            dv.w(te, dvS[t - 4])
                        for ic in range(32):
                            ins = te.matmul(
                                psz[t % 2][:, (t // 2 % 2) * 512:(t // 2 % 2) * 512 + 256],
                                wo2d[t % 2][:, ic * 128:(ic + 1) * 128],
                                hgq[sck % 2][:, ic * 256:(ic + 1) * 256],
                                start=(ic == 0), stop=(ic == 31))
                        pe.inc(ins)
                        peB1[t] = pe.n

                    @VEC
                    def _(ve, half=half, sck=sck, d2c=d2c, t=t):
                        pe.w(ve, peB1[t])
                        dv.inc(ve.tensor_scalar_add(
                            out=ytT[t % 2][:, :],
                            in0=psz[t % 2][:, (t // 2 % 2) * 512:(t // 2 % 2) * 512 + 256],
                            scalar1=bo2_s[:, d2c:d2c + 1]))
                        dvS[t] = dv.n

                    for blk in range(2):
                        bg = t * 2 + blk

                        @TEN
                        def _(te, t=t, blk=blk, bg=bg):
                            dv.w(te, dvS[t])
                            if bg >= 2:
                                dv.w(te, dvYe[bg - 2])
                            pe.inc(te.transpose(ptr[bg % 2][:, :],
                                                ytT[t % 2][:, blk * 128:(blk + 1) * 128],
                                                idf_s[:, :]))

                        @VEC
                        def _(ve, sck=sck, d2c=d2c, blk=blk, bg=bg):
                            pe.w(ve)
                            sb = sck * 2 + blk
                            sl = slice(sb * 1024 + d2c * 128, sb * 1024 + (d2c + 1) * 128)
                            dv.inc(ve.tensor_tensor(
                                out=xe_sb[:, sl], in0=ptr[bg % 2][:, :],
                                in1=xe_sb[:, sl], op=ALU.add))
                            dvYe[bg] = dv.n

            ctxH2.close()

            @SYN
            def _(sy):
                dv.w(sy)
                fsem.dma(sy.dma_start(
                    out=fg_scr[:, :].rearrange("(c p) m -> p c m", p=128),
                    in_=xe_sb[:, :].rearrange("p (c m) -> p c m", c=16)))

            if half == 0:
                ln3_state = {"dvTS": {}, "dvCmb": {}, "stC": {}, "isG": {}}
            dvTS = ln3_state["dvTS"]; dvCmb = ln3_state["dvCmb"]
            stC = ln3_state["stC"]; isG = ln3_state["isG"]
            for tb in range(8):
                tt = half * 8 + tb
                gb = half * 8 + tb
                j = tb % 2

                @GPS
                def _(ge, tb=tb, j=j, half=half, gb=gb):
                    fsem.w(ge)
                    isem.w(ge)
                    if gb >= 2:
                        dv.w(ge, dvTS[gb - 2])
                    isem.dma(ge.indirect_dma_start(
                        out=fgt[j][:, :], out_offset=None,
                        in_=fg_scr[:, :],
                        in_offset=bass.IndirectOffsetOnAxis(
                            desti2[:, half * 8 + tb:half * 8 + tb + 1], 0)))
                    isG[gb] = isem.n

                @SYN
                def _(sy, tt=tt, j=j, gb=gb):
                    if gb >= 1:
                        st.w(sy, 16 * gb)
                    if gb >= 2:
                        dv.w(sy, dvCmb[gb - 2])
                    xs[j].dma(sy.dma_start(out=xtile[j][:, :],
                                           in_=x_h[tt * 128:(tt + 1) * 128, :]))
                    asl[j].dma(sy.dma_start(out=attnb[j][:, :],
                                            in_=attn_scr[tt * 128:(tt + 1) * 128, :]))

                extg = [(isem, isG[gb])]
                if gb >= 2:
                    extg = extg + [(dv, dvCmb[gb - 2])]
                cts = pchain([
                    (VEC, dv, (lambda ve, j=j: ve.reduce_sum(
                        out=l3a[j][:, :], in_=fgt[j][:, :], axis=AX)), [], extg),
                    (VEC, dv, (lambda ve, j=j: ve.tensor_scalar_mul(
                        out=l3a[j][:, :], in0=l3a[j][:, :], scalar1=1.0 / D)), [0], []),
                    (VEC, dv, (lambda ve, j=j: ve.tensor_mul(
                        out=l3c[j][:, :], in0=l3a[j][:, :], in1=l3a[j][:, :])), [1], []),
                    (SCA, ac, (lambda se, j=j: se.activation(
                        out=ln3sq[j][:, :], in_=fgt[j][:, :], func=AF.Square)), [], extg),
                    (VEC, dv, (lambda ve, j=j: ve.reduce_sum(
                        out=l3b[j][:, :], in_=ln3sq[j][:, :], axis=AX)), [3], []),
                    (VEC, dv, (lambda ve, j=j: ve.tensor_scalar_mul(
                        out=l3b[j][:, :], in0=l3b[j][:, :], scalar1=1.0 / D)), [4], []),
                    (VEC, dv, (lambda ve, j=j: ve.tensor_sub(
                        out=l3b[j][:, :], in0=l3b[j][:, :], in1=l3c[j][:, :])), [5, 2], []),
                    (SCA, ac, (lambda se, j=j: se.activation(
                        out=l3b[j][:, :], in_=l3b[j][:, :], func=AF.Ln,
                        bias=epsC[:, :])), [6], []),
                    (SCA, ac, (lambda se, j=j: se.activation(
                        out=l3b[j][:, :], in_=l3b[j][:, :], func=AF.Exp,
                        scale=-0.5)), [7], []),
                    (VEC, dv, (lambda ve, j=j: ve.tensor_scalar(
                        out=z3[j][:, :], in0=fgt[j][:, :], scalar1=l3a[j][:, :],
                        scalar2=l3b[j][:, :],
                        op0=ALU.subtract, op1=ALU.mult)), [8, 1], []),
                    (VEC, dv, (lambda ve, j=j: ve.tensor_mul(
                        out=z3[j][:, :], in0=z3[j][:, :], in1=Gab[:, :])), [9], []),
                    (VEC, dv, (lambda ve, j=j: ve.tensor_add(
                        out=z3[j][:, :], in0=z3[j][:, :], in1=Beb[:, :])), [10], []),
                ])
                dvTS[gb] = cts[11][1]

                @VEC
                def _(ve, tt=tt, j=j, gb=gb):
                    xs[j].w(ve)
                    asl[j].w(ve)
                    dv.w(ve, dvTS[gb])
                    if gb >= 1:
                        st.w(ve, 16 * gb)
                    dv.inc(ve.tensor_add(out=outb[j][:, :], in0=z3[j][:, :], in1=xtile[j][:, :]))
                    dv.w(ve)
                    dv.inc(ve.tensor_add(out=outb[j][:, :], in0=outb[j][:, :], in1=attnb[j][:, :]))
                    dvCmb[gb] = dv.n

                @SYN
                def _(sy, tt=tt, j=j, gb=gb):
                    dv.w(sy, dvCmb[gb])
                    st.dma(sy.dma_start(out=y_h[tt * 128:(tt + 1) * 128, :],
                                        in_=outb[j][:, :]))

        @SYN
        def _(sy):
            st.w(sy)

    return nc


_NC_CACHE = {}


def _get_nc():
    if "nc" not in _NC_CACHE:
        _NC_CACHE["nc"] = build()
    return _NC_CACHE["nc"]


def kernel(L0_attn_x, L0_attn_Wq, L0_attn_bq, L0_attn_Wk, L0_attn_bk,
           L0_attn_Wv, L0_attn_bv, L0_attn_Wo, L0_attn_bo,
           L0_ffn_Wr, L0_ffn_W1, L0_ffn_b1, L0_ffn_Wo, L0_ffn_bo,
           L0_ffn_gamma, L0_ffn_beta):
    f = np.asarray
    x = np.ascontiguousarray(f(L0_attn_x, dtype=np.float32))
    Wq = f(L0_attn_Wq, dtype=np.float32); Wk = f(L0_attn_Wk, dtype=np.float32)
    Wv = f(L0_attn_Wv, dtype=np.float32); Wo = f(L0_attn_Wo, dtype=np.float32)
    Wr = f(L0_ffn_Wr, dtype=np.float32); W1 = f(L0_ffn_W1, dtype=np.float32)
    Wo2 = f(L0_ffn_Wo, dtype=np.float32)

    def qkv_lay(W):
        return np.ascontiguousarray(
            W.reshape(8, 128, 4, 256).transpose(2, 1, 0, 3).reshape(4, 128, 8 * 256))

    wq_c = qkv_lay(Wq); wk_c = qkv_lay(Wk); wv_c = qkv_lay(Wv)
    wo_c = np.ascontiguousarray(
        Wo.reshape(8, 128, 1024).transpose(1, 0, 2).reshape(128, 8192))
    wrt_c = np.ascontiguousarray(
        Wr.T.reshape(8, 128, 8).transpose(1, 0, 2).reshape(128, 64))
    w1_c = np.ascontiguousarray(
        W1.reshape(8, 8, 128, 16, 256).transpose(0, 3, 2, 1, 4).reshape(8, 16, 128, 8 * 256))
    wo2_c = np.ascontiguousarray(
        Wo2.reshape(32, 128, 8, 128).transpose(2, 1, 0, 3).reshape(8, 128, 32 * 128))
    bq_t = np.ascontiguousarray(f(L0_attn_bq, dtype=np.float32).reshape(8, 128).T)
    bk_t = np.ascontiguousarray(f(L0_attn_bk, dtype=np.float32).reshape(8, 128).T)
    bv_r = f(L0_attn_bv, dtype=np.float32).reshape(1, 1024)
    bo_r = f(L0_attn_bo, dtype=np.float32).reshape(1, 1024)
    b1_t = np.ascontiguousarray(
        f(L0_ffn_b1, dtype=np.float32).T.reshape(32, 128, 8).transpose(1, 0, 2).reshape(128, 256))
    bo2_t = np.ascontiguousarray(f(L0_ffn_bo, dtype=np.float32).reshape(8, 128).T)
    ga_r = f(L0_ffn_gamma, dtype=np.float32).reshape(1, 1024)
    be_r = f(L0_ffn_beta, dtype=np.float32).reshape(1, 1024)

    shared = dict(wq_c=wq_c, wk_c=wk_c, wv_c=wv_c, wo_c=wo_c, wrt_c=wrt_c,
                  w1_c=w1_c, wo2_c=wo2_c, bq_t=bq_t, bk_t=bk_t, bv_r=bv_r,
                  bo_r=bo_r, b1_t=b1_t, bo2_t=bo2_t, ga_r=ga_r, be_r=be_r)
    in_maps = []
    for c in range(8):
        b, half = divmod(c, 2)
        own = x[b, half * 2048:(half + 1) * 2048]
        par = x[b, (1 - half) * 2048:(2 - half) * 2048]
        m = dict(shared)
        m["xseq"] = np.ascontiguousarray(np.concatenate([own, par], axis=0))
        in_maps.append(m)

    nc = _get_nc()
    res = run_bass_kernel_spmd(nc, in_maps, core_ids=list(range(8)))
    out = np.empty((4, 4096, 1024), dtype=np.float32)
    for c in range(8):
        b, half = divmod(c, 2)
        out[b, half * 2048:(half + 1) * 2048] = res.results[c]["y"]
    return out


if __name__ == "__main__":
    nc = build()
    print("build ok")


# version 27
# speedup vs baseline: 1.1582x; 1.0374x over previous
import numpy as np
import ml_dtypes
from contextlib import ExitStack

import concourse.bass as bass
import concourse.mybir as mybir
from concourse.bass_utils import run_bass_kernel_spmd

F32 = mybir.dt.float32
BF16 = mybir.dt.bfloat16
AF = mybir.ActivationFunctionType
ALU = mybir.AluOpType
AX = mybir.AxisListType.X
BF = ml_dtypes.bfloat16

D = 1024
S = 4096
T = 2048
H = 16
NE = 8
NG = 4
HPG = 4
NI = 4096
EPS1 = 1e-5
EPS3 = 1e-12


class Sem:
    def __init__(self, ctx, nc, name):
        self.h = ctx.enter_context(nc.semaphore(name))
        self.n = 0

    def inc(self, ins, k=1):
        ins.then_inc(self.h, k)
        self.n += k

    def dma(self, ins):
        self.inc(ins, 16)

    def w(self, eng, at=None):
        v = self.n if at is None else at
        if v > 0:
            eng.wait_ge(self.h, v)


def build():
    nc = bass.Bass("TRN2", target_bir_lowering=False, debug=False)

    x_h = nc.dram_tensor("xseq", [S, D], F32, kind="ExternalInput").ap()
    wq_h = nc.dram_tensor("wq_c", [NG, 128, 8 * 256], F32, kind="ExternalInput").ap()
    wk_h = nc.dram_tensor("wk_c", [NG, 128, 8 * 256], F32, kind="ExternalInput").ap()
    wv_h = nc.dram_tensor("wv_c", [NG, 128, 8 * 256], F32, kind="ExternalInput").ap()
    wo_h = nc.dram_tensor("wo_c", [128, 8 * 1024], F32, kind="ExternalInput").ap()
    wrt_h = nc.dram_tensor("wrt_c", [128, 64], F32, kind="ExternalInput").ap()
    w1_h = nc.dram_tensor("w1_c", [NE, 16, 128, 8 * 256], F32, kind="ExternalInput").ap()
    wo2_h = nc.dram_tensor("wo2_c", [8, 128, 32 * 128], F32, kind="ExternalInput").ap()
    bq_h = nc.dram_tensor("bq_t", [128, 8], F32, kind="ExternalInput").ap()
    bk_h = nc.dram_tensor("bk_t", [128, 8], F32, kind="ExternalInput").ap()
    bv_h = nc.dram_tensor("bv_r", [1, D], F32, kind="ExternalInput").ap()
    bo_h = nc.dram_tensor("bo_r", [1, D], F32, kind="ExternalInput").ap()
    b1_h = nc.dram_tensor("b1_t", [128, 32 * 8], F32, kind="ExternalInput").ap()
    bo2_h = nc.dram_tensor("bo2_t", [128, 8], F32, kind="ExternalInput").ap()
    ga_h = nc.dram_tensor("ga_r", [1, D], F32, kind="ExternalInput").ap()
    be_h = nc.dram_tensor("be_r", [1, D], F32, kind="ExternalInput").ap()
    y_h = nc.dram_tensor("y", [T, D], F32, kind="ExternalOutput").ap()
    attn_scr = nc.dram_tensor("attn_scr", [T, D], BF16, kind="Internal").ap()
    xn_scr = nc.dram_tensor("xn_scr", [T, D], BF16, kind="Internal").ap()
    xe_scr = [nc.dram_tensor(f"xe_scr{h}", [2049, D], BF16, kind="Internal").ap()
              for h in range(2)]
    hg_scr = nc.dram_tensor("hg_scr", [NI, 2048], BF16, kind="Internal").ap()
    fg_scr = nc.dram_tensor("fg_scr", [2048, D], BF16, kind="Internal").ap()
    w1bf = nc.dram_tensor("w1bf", [NE, 16, 128, 8 * 256], BF16, kind="Internal").ap()
    wo2bf = nc.dram_tensor("wo2bf", [8, 128, 4096], BF16, kind="Internal").ap()

    idf = nc.inline_tensor(np.eye(128, dtype=np.float32), name="idf").ap()
    idb = nc.inline_tensor(np.eye(128, dtype=BF), name="idb").ap()
    ones_r_b = nc.inline_tensor(np.ones((1, 128), dtype=BF), name="ones_r_b").ap()
    ones_r_f = nc.inline_tensor(np.ones((1, 128), dtype=np.float32), name="ones_r_f").ap()
    ones_c_b = nc.inline_tensor(np.ones((128, 1), dtype=BF), name="ones_c_b").ap()
    ones_c_f = nc.inline_tensor(np.ones((128, 1), dtype=np.float32), name="ones_c_f").ap()
    w8c = nc.inline_tensor(
        np.broadcast_to(8.0 - np.arange(8, dtype=np.float32), (128, 8)).copy(),
        name="w8c").ap()
    pi_np = np.zeros((16, 8 * 128), dtype=BF)
    for dc in range(8):
        pi_np[2 * dc, dc * 128:dc * 128 + 64] = 1
        pi_np[2 * dc + 1, dc * 128 + 64:dc * 128 + 128] = 1
    pairind = nc.inline_tensor(pi_np, name="pairind").ap()
    ecol1 = nc.inline_tensor(
        (np.arange(8, dtype=np.float32) * 256.0 - 1.0).reshape(8, 1), name="ecol1").ap()

    ctx = ExitStack()
    with ctx:
        E = ctx.enter_context

        xnT_own = E(nc.sbuf_tensor("xnT_own", [128, 8 * T], BF16))
        rowf = E(nc.sbuf_tensor("rowf", [1, D], F32))
        bv_b = E(nc.sbuf_tensor("bv_b", [1, D], BF16))
        bo_b = E(nc.sbuf_tensor("bo_b", [1, D], BF16))
        idf_s = E(nc.sbuf_tensor("idf_s", [128, 128], F32))
        idb_s = E(nc.sbuf_tensor("idb_s", [128, 128], BF16))
        or_b = E(nc.sbuf_tensor("or_b", [1, 128], BF16))
        or_f = E(nc.sbuf_tensor("or_f", [1, 128], F32))
        oc_b = E(nc.sbuf_tensor("oc_b", [128, 1], BF16))
        oc_f = E(nc.sbuf_tensor("oc_f", [128, 1], F32))
        w8_s = E(nc.sbuf_tensor("w8_s", [128, 8], F32))
        pi_s = E(nc.sbuf_tensor("pi_s", [16, 8 * 128], BF16))
        ec_s = E(nc.sbuf_tensor("ec_s", [8, 1], F32))
        wrt_s = E(nc.sbuf_tensor("wrt_s", [128, 64], F32))
        bq_s = E(nc.sbuf_tensor("bq_s", [128, 8], F32))
        bk_s = E(nc.sbuf_tensor("bk_s", [128, 8], F32))
        epsA = E(nc.sbuf_tensor("epsA", [128, 1], F32))
        epsC = E(nc.sbuf_tensor("epsC", [128, 1], F32))
        maskT = E(nc.sbuf_tensor("maskT", [8, T], BF16))
        desti2 = E(nc.sbuf_tensor("desti2", [128, 16], mybir.dt.int32))
        w1b = [E(nc.sbuf_tensor(f"w1b{j}", [128, 8 * 256], BF16)) for j in range(2)]

        qS = Sem(ctx, nc, "qS")
        qG = Sem(ctx, nc, "qG")
        xs = [Sem(ctx, nc, f"xs{j}") for j in range(2)]
        xsp = [Sem(ctx, nc, f"xsp{j}") for j in range(4)]
        asl = [Sem(ctx, nc, f"asl{j}") for j in range(2)]
        w1s = [Sem(ctx, nc, f"w1s{j}") for j in range(2)]
        wo2s = [Sem(ctx, nc, f"wo2s{j}") for j in range(2)]
        pe = Sem(ctx, nc, "pe")
        pesc = Sem(ctx, nc, "pesc")
        peav = Sem(ctx, nc, "peav")
        ac = Sem(ctx, nc, "ac")
        dv = Sem(ctx, nc, "dv")
        ssem = Sem(ctx, nc, "ssem")
        ascr = Sem(ctx, nc, "ascr")
        gp = Sem(ctx, nc, "gp")
        xnsc = Sem(ctx, nc, "xnsc")
        isem = Sem(ctx, nc, "isem")
        gsem = [Sem(ctx, nc, f"gsem{j}") for j in range(2)]
        hsem = [Sem(ctx, nc, f"hsem{j}") for j in range(2)]
        fsem = Sem(ctx, nc, "fsem")
        wcst = Sem(ctx, nc, "wcst")
        gscat = Sem(ctx, nc, "gscat")
        w1c = [Sem(ctx, nc, f"w1c{j}") for j in range(4)]
        w1g = [Sem(ctx, nc, f"w1g{j}") for j in range(4)]
        w1r = [Sem(ctx, nc, f"w1r{j}") for j in range(4)]
        st = Sem(ctx, nc, "st")
        COMPUTE = [pe, pesc, peav, ac, dv]

        block = E(nc.Block())
        TEN, VEC, SCA, GPS, SYN = (block.tensor, block.vector, block.scalar,
                                   block.gpsimd, block.sync)

        def fence(eng):
            for s in COMPUTE:
                s.w(eng)

        def chain(*steps):
            for secwrap, sem, fn in steps:
                def body(eng, fn=fn, sem=sem):
                    fence(eng)
                    sem.inc(fn(eng))
                secwrap(body)

        def pchain(steps):
            counts = []
            for secwrap, sem, fn, deps, extra in steps:
                def body(eng, fn=fn, sem=sem, deps=tuple(deps), extra=tuple(extra)):
                    for s_, v_ in extra:
                        s_.w(eng, v_)
                    for d_ in deps:
                        ds, dval = counts[d_]
                        ds.w(eng, dval)
                    sem.inc(fn(eng))
                secwrap(body)
                counts.append((sem, sem.n))
            return counts

        @SYN
        def _(sy):
            for dst, src in [(idf_s, idf), (idb_s, idb), (or_b, ones_r_b),
                             (or_f, ones_r_f), (oc_b, ones_c_b), (oc_f, ones_c_f), (w8_s, w8c),
                             (pi_s, pairind), (ec_s, ecol1), (wrt_s, wrt_h),
                             (bq_s, bq_h), (bk_s, bk_h),
                             (rowf, bv_h)]:
                qS.dma(sy.dma_start(out=dst[:, :], in_=src[:, :]))

        @VEC
        def _(ve):
            qS.w(ve)
            dv.inc(ve.memset(epsA[:, :], EPS1))
            dv.inc(ve.memset(epsC[:, :], EPS3))
            dv.inc(ve.tensor_copy(out=bv_b[:, :], in_=rowf[:, :]))

        @SYN
        def _(sy):
            dv.w(sy)
            qS.dma(sy.dma_start(out=rowf[:, :], in_=bo_h[:, :]))

        @VEC
        def _(ve):
            qS.w(ve)
            dv.inc(ve.tensor_copy(out=bo_b[:, :], in_=rowf[:, :]))

        ctxA = ctx.enter_context(ExitStack())
        EA = ctxA.enter_context
        xnT_par = EA(nc.sbuf_tensor("xnT_par", [128, 8 * T], BF16))
        scstg = [EA(nc.sbuf_tensor(f"scstg{j}", [128, D], BF16)) for j in range(2)]
        O_u = EA(nc.sbuf_tensor("O_u", [128, 8 * T], BF16))

        ctxA1 = ExitStack()
        EA1 = ctxA1.enter_context
        xtile = [EA1(nc.sbuf_tensor(f"xt1_{j}", [128, D], F32)) for j in range(4)]
        sqs = EA1(nc.sbuf_tensor("sqs", [128, D], F32))
        xnf = [EA1(nc.sbuf_tensor(f"xnf{j}", [128, D], F32)) for j in range(4)]
        xtf = EA1(nc.sbuf_tensor("xtf", [128, 1024], F32))
        st1 = [EA1(nc.sbuf_tensor(f"st1_{j}", [128, 1], F32)) for j in range(4)]
        st2 = [EA1(nc.sbuf_tensor(f"st2_{j}", [128, 1], F32)) for j in range(4)]
        st3 = [EA1(nc.sbuf_tensor(f"st3_{j}", [128, 1], F32)) for j in range(4)]
        st4 = [EA1(nc.sbuf_tensor(f"st4_{j}", [128, 1], F32)) for j in range(4)]
        mst1 = EA1(nc.sbuf_tensor("mst1", [128, 1], F32))
        mst3 = EA1(nc.sbuf_tensor("mst3", [128, 1], F32))
        lgs = EA1(nc.sbuf_tensor("lgs", [128, 8], F32))
        eqs = EA1(nc.sbuf_tensor("eqs", [128, 8], F32))
        v8s = EA1(nc.sbuf_tensor("v8s", [128, 8], F32))
        mkb = EA1(nc.sbuf_tensor("mkb", [128, 8], BF16))
        psT2 = [EA1(nc.psum_tensor(f"psT2{j}", [128, 512], F32)) for j in range(2)]
        pidx = EA1(nc.psum_tensor("pidx", [1, 1024], F32))
        pslog = [EA1(nc.psum_tensor(f"pslog{j}", [128, 8], F32)) for j in range(2)]
        psmT = EA1(nc.psum_tensor("psmT", [8, 128], BF16))

        NT = S // 128
        acSq = {}; acRstd = {}; dvXnf = {}; dvVar = {}; dvEvacH = {}
        dvLog = {}; dvMkb = {}; dvMT = {}; peTr = {}; peLog = {}; peMT = {}

        for tt in range(NT):
            j = tt % 4
            own = tt < 16

            @SYN
            def _(sy, tt=tt, j=j):
                if tt >= 4:
                    ac.w(sy, acSq[tt - 4])
                    dv.w(sy, dvXnf[tt - 4])
                xsp[j].dma(sy.dma_start(out=xtile[j][:, :],
                                        in_=x_h[tt * 128:(tt + 1) * 128, :]))

            @SCA
            def _(se, tt=tt, j=j):
                xsp[j].w(se)
                if tt >= 4:
                    dv.w(se, dvVar[tt - 4])
                ac.inc(se.activation(out=sqs[:, :], in_=xtile[j][:, :],
                                     func=AF.Square, accum_out=st2[j][:, :]))
                acSq[tt] = ac.n

            @VEC
            def _(ve, tt=tt, j=j):
                xsp[j].w(ve)
                if tt >= 4:
                    dv.w(ve, dvXnf[tt - 4])
                    ac.w(ve, acRstd[tt - 4])
                dv.inc(ve.reduce_sum(out=st1[j][:, :], in_=xtile[j][:, :], axis=AX))
                dv.w(ve)
                dv.inc(ve.tensor_scalar_mul(out=st1[j][:, :], in0=st1[j][:, :],
                                            scalar1=1.0 / D))
                dv.w(ve)
                dv.inc(ve.tensor_scalar(out=st3[j][:, :], in0=st1[j][:, :],
                                        scalar1=st1[j][:, :], scalar2=None,
                                        op0=ALU.mult))
                ac.w(ve, acSq[tt])
                dv.w(ve)
                dv.inc(ve.tensor_scalar(out=st2[j][:, :], in0=st2[j][:, :],
                                        scalar1=1.0 / D, scalar2=st3[j][:, :],
                                        op0=ALU.mult, op1=ALU.subtract))
                dvVar[tt] = dv.n

            @SCA
            def _(se, tt=tt, j=j):
                dv.w(se, dvVar[tt])
                ac.inc(se.activation(out=st4[j][:, :], in_=st2[j][:, :], func=AF.Ln, bias=epsA[:, :]))
                ac.w(se)
                ac.inc(se.activation(out=st4[j][:, :], in_=st4[j][:, :], func=AF.Exp, scale=-0.5))
                acRstd[tt] = ac.n

            @VEC
            def _(ve, tt=tt, j=j, own=own):
                ac.w(ve, acRstd[tt])
                if tt >= 4:
                    pe.w(ve, peTr[(tt - 4, 7)])
                    if tt - 4 < 16:
                        xnsc.w(ve, 16 * (tt - 3))
                dv.inc(ve.tensor_scalar(out=xnf[j][:, :], in0=xtile[j][:, :],
                                        scalar1=st1[j][:, :], scalar2=st4[j][:, :],
                                        op0=ALU.subtract, op1=ALU.mult))
                dvXnf[tt] = dv.n

            if own:
                @GPS
                def _(ge, tt=tt, j=j):
                    dv.w(ge, dvXnf[tt])
                    xnsc.w(ge, 16 * tt)
                    xnsc.dma(ge.dma_start(out=xn_scr[tt * 128:(tt + 1) * 128, :],
                                          in_=xnf[j][:, :]))

            dst = xnT_own if own else xnT_par
            toff = (tt % 16) * 128
            for dc in range(8):
                hb = tt * 2 + dc // 4
                b = hb % 2

                @TEN
                def _(te, tt=tt, dc=dc, j=j, b=b, hb=hb):
                    if dc % 4 == 0:
                        dv.w(te, dvXnf[tt])
                        if hb >= 2:
                            dv.w(te, dvEvacH.get(hb - 2, 0))
                    pe.inc(te.transpose(psT2[b][:, (dc % 4) * 128:(dc % 4 + 1) * 128],
                                        xnf[j][:, dc * 128:(dc + 1) * 128], idf_s[:, :]))
                    peTr[(tt, dc)] = pe.n

                if dc % 4 == 3:
                    @VEC
                    def _(ve, tt=tt, dc=dc, dst=dst, toff=toff, own=own, b=b, hb=hb):
                        pe.w(ve, peTr[(tt, dc)])
                        dv.inc(ve.tensor_copy(
                            out=dst[:, :].rearrange("p (d t) -> p d t", d=8)
                                [:, (hb % 2) * 4:(hb % 2) * 4 + 4, toff:toff + 128],
                            in_=psT2[b][:, :].rearrange("p (d t) -> p d t", d=4)))
                        if own:
                            dv.inc(ve.tensor_copy(
                                out=xtf[:, (hb % 2) * 512:(hb % 2) * 512 + 512],
                                in_=psT2[b][:, :]))
                        dvEvacH[hb] = dv.n

            if own:
                for dc in range(8):
                    @TEN
                    def _(te, tt=tt, dc=dc):
                        if dc == 0:
                            dv.w(te, dvEvacH[tt * 2])
                            if tt >= 2:
                                dv.w(te, dvLog[tt - 2])
                        elif dc == 4:
                            dv.w(te, dvEvacH[tt * 2 + 1])
                        ins = te.matmul(pslog[tt % 2][:, :],
                                        xtf[:, dc * 128:(dc + 1) * 128],
                                        wrt_s[:, dc * 8:(dc + 1) * 8],
                                        start=(dc == 0), stop=(dc == 7))
                        if dc == 7:
                            pe.inc(ins)
                            peLog[tt] = pe.n

                @VEC
                def _(ve, tt=tt):
                    pe.w(ve, peLog[tt])
                    dv.inc(ve.tensor_copy(out=lgs[:, :], in_=pslog[tt % 2][:, :]))
                    dvLog[tt] = dv.n
                    dv.w(ve)
                    dv.inc(ve.reduce_max(out=mst1[:, :], in_=lgs[:, :], axis=AX))
                    dv.w(ve)
                    dv.inc(ve.tensor_scalar(out=eqs[:, :], in0=lgs[:, :],
                                            scalar1=mst1[:, :], scalar2=None,
                                            op0=ALU.is_equal))
                    dv.w(ve)
                    dv.inc(ve.tensor_mul(out=v8s[:, :], in0=eqs[:, :], in1=w8_s[:, :]))
                    dv.w(ve)
                    dv.inc(ve.reduce_max(out=mst3[:, :], in_=v8s[:, :], axis=AX))
                    dv.w(ve)
                    dv.inc(ve.tensor_scalar(out=mkb[:, :], in0=v8s[:, :],
                                            scalar1=mst3[:, :], scalar2=None,
                                            op0=ALU.is_equal))
                    dvMkb[tt] = dv.n

                @TEN
                def _(te, tt=tt):
                    dv.w(te, dvMkb[tt])
                    if tt >= 1:
                        dv.w(te, dvMT[tt - 1])
                    pe.inc(te.transpose(psmT[:, :], mkb[:, :], idb_s[:, :]))
                    peMT[tt] = pe.n

                @VEC
                def _(ve, tt=tt, toff=toff):
                    pe.w(ve, peMT[tt])
                    dv.inc(ve.tensor_copy(out=maskT[:, toff:toff + 128], in_=psmT[:, :]))
                    dvMT[tt] = dv.n

        prefA = EA1(nc.sbuf_tensor("prefA", [8, 1024], F32))
        prefB = EA1(nc.sbuf_tensor("prefB", [8, 1024], F32))
        zeros8 = EA1(nc.sbuf_tensor("zeros8", [8, 1024], BF16))
        drow = EA1(nc.sbuf_tensor("drow", [1, 1024], F32))
        vrow = EA1(nc.sbuf_tensor("vrow", [1, 1024], F32))
        destc = EA1(nc.sbuf_tensor("destc", [128, 16], F32))
        stgb = EA1(nc.sbuf_tensor("stgb", [128, D], BF16))

        @VEC
        def _(ve):
            dv.inc(ve.memset(zeros8[:, :], 0.0))
            dv.inc(ve.memset(stgb[:, :], 0.0))

        @SYN
        def _(sy):
            dv.w(sy)
            for h2 in range(2):
                for zb in range(16):
                    fsem.dma(sy.dma_start(out=xe_scr[h2][zb * 128:(zb + 1) * 128, :],
                                          in_=stgb[:, :]))
                fsem.dma(sy.dma_start(out=xe_scr[h2][2048:2049, :], in_=stgb[0:1, :]))

        for half in range(2):
            mTh = maskT[:, half * 1024:(half + 1) * 1024]
            chain(
                (VEC, dv, lambda ve, mTh=mTh: ve.tensor_tensor_scan(
                    out=prefA[:, :], data0=mTh, data1=zeros8[:, :],
                    initial=0.0, op0=ALU.add, op1=ALU.add)),
                (VEC, dv, lambda ve: ve.tensor_scalar(
                    out=prefB[:, :], in0=prefA[:, :], scalar1=256.0, scalar2=None,
                    op0=ALU.is_le)),
                (VEC, dv, lambda ve: ve.tensor_scalar_add(
                    out=prefA[:, :], in0=prefA[:, :], scalar1=ec_s[:, :])),
                (VEC, dv, lambda ve, mTh=mTh: ve.tensor_tensor(
                    out=prefB[:, :], in0=mTh, in1=prefB[:, :], op=ALU.mult)),
                (VEC, dv, lambda ve: ve.tensor_tensor(
                    out=prefA[:, :], in0=prefB[:, :], in1=prefA[:, :], op=ALU.mult)),
                (TEN, pe, lambda te: te.matmul(pidx[0:1, 0:512], oc_f[0:8, :],
                                               prefA[:, 0:512], start=True, stop=True)),
                (TEN, pe, lambda te: te.matmul(pidx[0:1, 512:1024], oc_f[0:8, :],
                                               prefA[:, 512:1024], start=True, stop=True)),
                (VEC, dv, lambda ve: ve.tensor_copy(out=drow[:, :], in_=pidx[0:1, :])),
                (TEN, pe, lambda te: te.matmul(pidx[0:1, 0:512], oc_f[0:8, :],
                                               prefB[:, 0:512], start=True, stop=True)),
                (TEN, pe, lambda te: te.matmul(pidx[0:1, 512:1024], oc_f[0:8, :],
                                               prefB[:, 512:1024], start=True, stop=True)),
                (VEC, dv, lambda ve: ve.tensor_copy(out=vrow[:, :], in_=pidx[0:1, :])),
                (VEC, dv, lambda ve: ve.scalar_tensor_tensor(
                    out=drow[:, :], in0=vrow[:, :], scalar=-2048.0, in1=drow[:, :],
                    op0=ALU.mult, op1=ALU.add)),
                (VEC, dv, lambda ve: ve.tensor_scalar_add(
                    out=drow[:, :], in0=drow[:, :], scalar1=2048.0)),
            )
            for tb in range(8):
                chain(
                    (TEN, pe, lambda te, tb=tb: te.transpose(
                        psT2[0][:, 0:1], drow[0:1, tb * 128:(tb + 1) * 128],
                        idf_s[0:1, 0:1])),
                    (VEC, dv, lambda ve, tb=tb, half=half: ve.tensor_copy(
                        out=destc[:, half * 8 + tb:half * 8 + tb + 1],
                        in_=psT2[0][0:128, 0:1])),
                )
        chain((VEC, dv, lambda ve: ve.tensor_copy(out=desti2[:, :], in_=destc[:, :])))
        dv_idx2 = dv.n
        fsem_zf = fsem.n

        ctxA1.close()

        ctxA2 = ExitStack()
        E2 = ctxA2.enter_context
        Qg = E2(nc.sbuf_tensor("Qg", [128, 4 * T], BF16))
        Kg = E2(nc.sbuf_tensor("Kg", [128, 2 * S], BF16))
        Vg = E2(nc.sbuf_tensor("Vg", [128, 32 * HPG * 65 + 64], BF16))
        sq4 = E2(nc.sbuf_tensor("sq4", [128, T], BF16))
        wqg = E2(nc.sbuf_tensor("wqg", [128, 2048], BF16))
        wkg = E2(nc.sbuf_tensor("wkg", [128, 2048], BF16))
        ring = [E2(nc.sbuf_tensor(f"ring{r}", [128, 1024], BF16)) for r in range(3)]
        sring = E2(nc.sbuf_tensor("sring", [1, 512], F32))
        S16 = E2(nc.sbuf_tensor("S16", [16, T], F32))
        negm8 = E2(nc.sbuf_tensor("negm8", [128, 1], F32))
        t11a = E2(nc.sbuf_tensor("t11a", [1, 1], F32))
        t11b = E2(nc.sbuf_tensor("t11b", [1, 1], F32))
        t11d = E2(nc.sbuf_tensor("t11d", [1, 1], F32))
        t11c = E2(nc.sbuf_tensor("t11c", [1, 1], BF16))
        sc = [E2(nc.psum_tensor(f"sc{j}", [128, 1024], F32)) for j in range(3)]
        oacc = [E2(nc.psum_tensor(f"oacc{j}", [128, 512], F32)) for j in range(2)]

        @GPS
        def _(ge):
            fence(ge)
            xnsc.w(ge)
            fsem.w(ge)
            gp.inc(ge.memset(Vg[:, :], 0.0))
            gp.w(ge)
            gp.inc(ge.memset(
                Vg[:, 0:32 * HPG * 65].rearrange("p (c s) -> p c s", s=65)[:, :, 64:65], 1.0))
            gp.inc(ge.memset(Qg[:, :], 0.0))
            qG.dma(ge.dma_start(out=wqg[:, :], in_=wq_h[0]))
            qG.dma(ge.dma_start(out=wkg[:, :], in_=wk_h[0]))

        def xnT_at(tok):
            return (xnT_own, tok) if tok < T else (xnT_par, tok - T)

        def _precast_pair(k):
            g, kk = divmod(k, 36)
            if kk < 32:
                e, iq4 = divmod(g * 32 + kk, 16)
                return w1_h[e, iq4], w1bf[e, iq4]
            w = g * 4 + (kk - 32)
            d2c, ch = divmod(w, 2)
            return (wo2_h[d2c][:, ch * 2048:(ch + 1) * 2048],
                    wo2bf[d2c][:, ch * 2048:(ch + 1) * 2048])

        def w1_precast_batch(g):
            @GPS
            def _(ge, g=g):
                for kk in range(36):
                    k = g * 36 + kk
                    src, dst = _precast_pair(k)
                    if k >= 2:
                        w1c[k % 2].w(ge, 16 * (k // 2))
                    w1g[k % 2].dma(ge.dma_start(out=w1b[k % 2][:, :], in_=src))

            @SYN
            def _(sy, g=g):
                for kk in range(36):
                    k = g * 36 + kk
                    src, dst = _precast_pair(k)
                    sy.wait_ge(w1g[k % 2].h, 16 * (k // 2 + 1))
                    w1c[k % 2].dma(sy.dma_start(out=dst, in_=w1b[k % 2][:, :]))

        for g in range(NG):
            if g > 0:
                @GPS
                def _(ge, g=g):
                    fence(ge)
                    qG.dma(ge.dma_start(out=wqg[:, :], in_=wq_h[g]))
                    qG.dma(ge.dma_start(out=wkg[:, :], in_=wk_h[g]))

            peP = {}; dvP = {}
            for t in range(8):
                d2cl, qs = divmod(t, 4)

                @TEN
                def _(te, t=t, d2cl=d2cl, qs=qs):
                    qG.w(te)
                    gp.w(te)
                    if t == 0:
                        ac.w(te)
                        dv.w(te)
                    if t >= 2:
                        dv.w(te, dvP[t - 2])
                    for dc in range(8):
                        ins = te.matmul(
                            sc[t % 2][:, 0:512],
                            wqg[:, dc * 256 + d2cl * 128:dc * 256 + (d2cl + 1) * 128],
                            xnT_own[:, dc * T + qs * 512:dc * T + (qs + 1) * 512],
                            start=(dc == 0), stop=(dc == 7))
                    pe.inc(ins)
                    peP[t] = pe.n

                @VEC
                def _(ve, t=t, d2cl=d2cl, qs=qs, g=g):
                    pe.w(ve, peP[t])
                    gp.w(ve)
                    col = 2 * g + d2cl
                    for hh in range(2):
                        hl = 2 * d2cl + hh
                        dv.inc(ve.tensor_scalar_add(
                            out=Qg[hh * 64:hh * 64 + 64,
                                   hl * T + qs * 512:hl * T + (qs + 1) * 512],
                            in0=sc[t % 2][hh * 64:hh * 64 + 64, 0:512],
                            scalar1=bq_s[hh * 64:hh * 64 + 64, col:col + 1]))
                    dvP[t] = dv.n

            peK = {}; dvK = {}
            for t in range(16):
                d2cl, ks = divmod(t, 8)

                @TEN
                def _(te, t=t, d2cl=d2cl, ks=ks):
                    if t < 2:
                        dv.w(te, dvP[6 + t])

                    else:
                        dv.w(te, dvK[t - 2])
                    srcx, base = xnT_at(ks * 512)
                    for dc in range(8):
                        ins = te.matmul(
                            sc[t % 2][:, 0:512],
                            wkg[:, dc * 256 + d2cl * 128:dc * 256 + (d2cl + 1) * 128],
                            srcx[:, dc * T + base:dc * T + base + 512],
                            start=(dc == 0), stop=(dc == 7))
                    pe.inc(ins)
                    peK[t] = pe.n

                @VEC
                def _(ve, t=t, d2cl=d2cl, ks=ks, g=g):
                    pe.w(ve, peK[t])
                    dv.inc(ve.tensor_scalar_add(
                        out=Kg[:, d2cl * S + ks * 512:d2cl * S + (ks + 1) * 512],
                        in0=sc[t % 2][:, 0:512],
                        scalar1=bk_s[:, 2 * g + d2cl:2 * g + d2cl + 1]))
                    dvK[t] = dv.n

            @GPS
            def _(ge, g=g):
                pe.w(ge, peP[7])
                qG.dma(ge.dma_start(out=wqg[:, :], in_=wv_h[g]))

            w1_precast_batch(g)

            if g == 3:
                @GPS
                def _(ge):
                    ge.wait_ge(dv.h, dv_idx2)
                    ge.wait_ge(fsem.h, fsem_zf)
                    for k in range(16):
                        h2, tb = divmod(k, 8)
                        if k >= 2:
                            ge.wait_ge(isem.h, 32 * (k // 2))
                        gscat.dma(ge.dma_start(out=scstg[k % 2][:, :],
                                               in_=xn_scr[h2 * 1024 + tb * 128:
                                                          h2 * 1024 + (tb + 1) * 128, :]))
                        ge.wait_ge(gscat.h, 16 * (k + 1))
                        isem.dma(ge.indirect_dma_start(
                            out=xe_scr[h2][:, :],
                            out_offset=bass.IndirectOffsetOnAxis(
                                desti2[:, h2 * 8 + tb:h2 * 8 + tb + 1], 0),
                            in_=scstg[k % 2][:, :], in_offset=None))

            peV = {}; dvV = {}
            for kc in range(32):
                @TEN
                def _(te, kc=kc, g=g):
                    qG.w(te)
                    if kc < 2:
                        dv.w(te, dvK[14 + kc])
                    else:
                        dv.w(te, dvV[kc - 2])
                    src, base = xnT_at(kc * 128)
                    for dc in range(8):
                        te.matmul(sc[kc % 2][:, 0:256],
                                  src[:, dc * T + base:dc * T + base + 128],
                                  wqg[:, dc * 256:(dc + 1) * 256],
                                  start=(dc == 0), stop=False)
                    ins = te.matmul(sc[kc % 2][:, 0:256], or_b[:, :],
                                    bv_b[:, g * 256:(g + 1) * 256],
                                    start=False, stop=True)
                    pe.inc(ins)
                    peV[kc] = pe.n

                @VEC
                def _(ve, kc=kc):
                    pe.w(ve, peV[kc])
                    dv.inc(ve.tensor_copy(
                        out=Vg[:, kc * HPG * 65:(kc + 1) * HPG * 65]
                            .rearrange("p (h s) -> p h s", s=65)[:, :, 0:64],
                        in_=sc[kc % 2][:, 0:256].rearrange("p (h s) -> p h s", s=64)))
                    dvV[kc] = dv.n

            def nps(seg):
                j, hh = divmod(seg, 2)
                return sc[j][0:1, hh * 512:(hh + 1) * 512]

            steps = []
            for hl in range(4):
                steps.append((VEC, dv, lambda ve, hl=hl: ve.tensor_mul(
                    out=sq4[:, :], in0=Qg[:, hl * T:(hl + 1) * T],
                    in1=Qg[:, hl * T:(hl + 1) * T])))
                for seg in range(4):
                    steps.append((TEN, pe, lambda te, seg=seg: te.matmul(
                        nps(seg), oc_b[:, :], sq4[:, seg * 512:(seg + 1) * 512],
                        start=True, stop=True)))
                for seg in range(4):
                    steps.append((VEC, dv, lambda ve, seg=seg: ve.reduce_max(out=t11d[:, :], in_=nps(seg), axis=AX)))
                    if hl == 0 and seg == 0:
                        steps.append((VEC, dv, lambda ve: ve.tensor_copy(out=t11a[:, :], in_=t11d[:, :])))
                    else:
                        steps.append((VEC, dv, lambda ve: ve.tensor_max(out=t11a[:, :], in0=t11a[:, :], in1=t11d[:, :])))
            first = [True]
            for d2cl in range(2):
                for piece in range(2):
                    steps.append((VEC, dv, lambda ve, d2cl=d2cl, piece=piece: ve.tensor_mul(
                        out=sq4[:, :], in0=Kg[:, d2cl * S + piece * T:d2cl * S + (piece + 1) * T],
                        in1=Kg[:, d2cl * S + piece * T:d2cl * S + (piece + 1) * T])))
                    for seg in range(4):
                        steps.append((TEN, pe, lambda te, seg=seg: te.matmul(
                            nps(seg), oc_b[:, :], sq4[:, seg * 512:(seg + 1) * 512],
                            start=True, stop=True)))
                    for seg in range(4):
                        steps.append((VEC, dv, lambda ve, seg=seg: ve.reduce_max(
                            out=t11d[:, :], in_=nps(seg), axis=AX)))
                        if first[0]:
                            steps.append((VEC, dv, lambda ve: ve.tensor_copy(out=t11b[:, :], in_=t11d[:, :])))
                            first[0] = False
                        else:
                            steps.append((VEC, dv, lambda ve: ve.tensor_max(out=t11b[:, :], in0=t11b[:, :], in1=t11d[:, :])))
            steps.append((VEC, dv, lambda ve: ve.tensor_mul(out=t11a[:, :], in0=t11a[:, :], in1=t11b[:, :])))
            steps.append((SCA, ac, lambda se: se.activation(out=t11a[:, :], in_=t11a[:, :], func=AF.Ln)))
            steps.append((SCA, ac, lambda se: se.activation(out=t11a[:, :], in_=t11a[:, :], func=AF.Exp, scale=0.5)))
            steps.append((VEC, dv, lambda ve: ve.tensor_scalar_mul(out=t11c[:, :], in0=t11a[:, :], scalar1=-0.125)))
            steps.append((TEN, pe, lambda te: te.matmul(sc[0][:, 0:1], or_b[:, :], t11c[:, :], start=True, stop=True)))
            steps.append((VEC, dv, lambda ve: ve.tensor_copy(out=negm8[:, :], in_=sc[0][:, 0:1])))
            chain(*steps)

            base_ac, base_sc, base_av = ac.n, pesc.n, peav.n
            base_dv, base_ss = dv.n, ssem.n

            @TEN
            def _(te, g=g, base_ac=base_ac, base_dv=base_dv):
                dv.w(te)

                def spair(p):
                    if p >= 3:
                        te.wait_ge(ac.h, base_ac + p - 2)
                    for u in range(2):
                        m = 2 * p + u
                        hl = m // 128
                        qt = (m // 32) % 4
                        c = m % 32
                        ins = te.matmul(
                            sc[p % 3][:, u * 512:(u + 1) * 512],
                            Kg[:, (hl // 2) * S + c * 128:(hl // 2) * S + (c + 1) * 128],
                            Qg[:, hl * T + qt * 512:hl * T + (qt + 1) * 512],
                            start=True, stop=True)
                    pesc.inc(ins)

                for p in range(3):
                    spair(p)
                for n in range(512):
                    i, c = divmod(n, 32)
                    hl = n // 128
                    if n % 2 == 0 and n // 2 + 3 < 256:
                        spair(n // 2 + 3)
                    if c == 0 and i >= 2:
                        te.wait_ge(dv.h, base_dv + 2 * (i - 1))
                    te.wait_ge(ac.h, base_ac + n // 2 + 1)
                    ins = te.matmul(
                        oacc[i % 2][:, :],
                        Vg[:, (c * HPG + hl) * 65:(c * HPG + hl) * 65 + 128],
                        ring[(n // 2) % 3][:, (n % 2) * 512:(n % 2 + 1) * 512],
                        start=(c == 0), stop=(c == 31))
                    peav.inc(ins)

            @SCA
            def _(se, base_sc=base_sc, base_av=base_av, dvpre=dv.n):
                se.wait_ge(dv.h, dvpre)
                for p in range(256):
                    se.wait_ge(pesc.h, base_sc + p + 1)
                    if p >= 3:
                        se.wait_ge(peav.h, base_av + 2 * p - 4)
                    ac.inc(se.activation(out=ring[p % 3][:, :], in_=sc[p % 3][:, :],
                                         func=AF.Exp, bias=negm8[:, :], scale=0.125))

            @VEC
            def _(ve, g=g, base_av=base_av, base_ss=base_ss):
                for i in range(16):
                    hl = i // 4
                    qt = i % 4
                    h = HPG * g + hl
                    ve.wait_ge(peav.h, base_av + 32 * (i + 1))
                    dv.inc(ve.tensor_copy(
                        out=O_u[(h % 2) * 64:(h % 2) * 64 + 64,
                                (h // 2) * T + qt * 512:(h // 2) * T + (qt + 1) * 512],
                        in_=oacc[i % 2][0:64, :]))
                    if base_ss + 16 * i > 0:
                        ve.wait_ge(ssem.h, base_ss + 16 * i)
                    dv.inc(ve.tensor_copy(out=sring[:, :], in_=oacc[i % 2][64:65, :]))

            @SYN
            def _(sy, g=g, base_dv=base_dv):
                for i in range(16):
                    hl = i // 4
                    qt = i % 4
                    h = HPG * g + hl
                    sy.wait_ge(dv.h, base_dv + 2 * (i + 1))
                    ssem.dma(sy.dma_start(out=S16[h:h + 1, qt * 512:(qt + 1) * 512],
                                          in_=sring[:, :]))

        ctxA2.close()

        ctxB3 = ExitStack()
        E3 = ctxB3.enter_context
        wo_s = E3(nc.sbuf_tensor("wo_s", [128, 8 * 1024], BF16))
        sclb = E3(nc.sbuf_tensor("sclb", [128, 2048], BF16))
        r_att = [E3(nc.sbuf_tensor(f"r_att{j}", [128, D], F32)) for j in range(2)]
        ln2sq = [E3(nc.sbuf_tensor(f"ln2sq{j}", [128, D], F32)) for j in range(2)]
        l2a = [E3(nc.sbuf_tensor(f"l2a{j}", [128, 1], F32)) for j in range(2)]
        l2b = [E3(nc.sbuf_tensor(f"l2b{j}", [128, 1], F32)) for j in range(2)]
        l2c = [E3(nc.sbuf_tensor(f"l2c{j}", [128, 1], F32)) for j in range(2)]
        attnb = [E3(nc.sbuf_tensor(f"attnb3_{j}", [128, D], BF16)) for j in range(2)]
        T16 = E3(nc.sbuf_tensor("T16", [16, T], F32))
        R16 = E3(nc.sbuf_tensor("R16", [16, T], BF16))
        psb = [E3(nc.psum_tensor(f"psb{j}", [128, 1024], F32)) for j in range(2)]

        @GPS
        def _(ge):
            fence(ge)
            ssem.w(ge)
            qG.dma(ge.dma_start(out=wo_s[:, :], in_=wo_h[:, :]))

        @SCA
        def _(se):
            ssem.w(se)
            fence(se)
            ac.inc(se.activation(out=T16[:, :], in_=S16[:, :], func=AF.Ln))
            ac.w(se)
            ac.inc(se.activation(out=R16[:, :], in_=T16[:, :], func=AF.Exp, scale=-1.0))

        peB = {}; dvB = {}
        for dc in range(8):
            for piece in range(2):
                t = dc * 2 + piece

                @TEN
                def _(te, dc=dc, piece=piece, t=t):
                    ac.w(te)
                    if t >= 2:
                        dv.w(te, dvB[t - 2])
                    for hb in range(2):
                        ins = te.matmul(
                            psb[t % 2][:, hb * 512:(hb + 1) * 512],
                            pi_s[:, dc * 128:(dc + 1) * 128],
                            R16[:, piece * 1024 + hb * 512:piece * 1024 + (hb + 1) * 512],
                            start=True, stop=True)
                    pe.inc(ins)
                    peB[t] = pe.n

                @VEC
                def _(ve, dc=dc, piece=piece, t=t):
                    pe.w(ve, peB[t])
                    dv.inc(ve.tensor_copy(out=sclb[:, piece * 1024:(piece + 1) * 1024],
                                          in_=psb[t % 2][:, :]))
                    dv.w(ve)
                    dv.inc(ve.tensor_mul(
                        out=O_u[:, dc * T + piece * 1024:dc * T + (piece + 1) * 1024],
                        in0=O_u[:, dc * T + piece * 1024:dc * T + (piece + 1) * 1024],
                        in1=sclb[:, piece * 1024:(piece + 1) * 1024]))
                    dvB[t] = dv.n

        peY = {}; acR = {}; dvL2 = {}; acL2 = {}
        for tt in range(16):
            @TEN
            def _(te, tt=tt):
                qG.w(te)
                if tt < 2:
                    dv.w(te, dvB[15])
                else:
                    ac.w(te, acR[tt - 2])
                for dc in range(8):
                    for hb in range(2):
                        te.matmul(psb[tt % 2][:, hb * 512:(hb + 1) * 512],
                                  O_u[:, dc * T + tt * 128:dc * T + (tt + 1) * 128],
                                  wo_s[:, dc * 1024 + hb * 512:dc * 1024 + (hb + 1) * 512],
                                  start=(dc == 0), stop=False)
                for hb in range(2):
                    ins = te.matmul(psb[tt % 2][:, hb * 512:(hb + 1) * 512],
                                    or_b[:, :], bo_b[:, hb * 512:(hb + 1) * 512],
                                    start=False, stop=True)
                pe.inc(ins)
                peY[tt] = pe.n

            @SCA
            def _(se, tt=tt):
                pe.w(se, peY[tt])
                if tt >= 2:
                    dv.w(se, dvL2[tt - 2])
                ac.inc(se.activation(out=r_att[tt % 2][:, :], in_=psb[tt % 2][:, :],
                                     func=AF.Relu))
                acR[tt] = ac.n

            jj = tt % 2
            ext0 = [(ac, acR[tt])]
            if tt >= 2:
                ext0 = ext0 + [(dv, dvL2[tt - 2]), (ac, acL2.get(tt - 2, 0))]
            ext9 = [(ascr, 16 * (tt - 1))] if tt >= 2 else []
            cts = pchain([
                (VEC, dv, (lambda ve, jj=jj, tt=tt: ve.reduce_sum(
                    out=l2a[jj][:, :], in_=r_att[jj][:, :], axis=AX)), [], ext0),
                (VEC, dv, (lambda ve, jj=jj: ve.tensor_scalar_mul(
                    out=l2a[jj][:, :], in0=l2a[jj][:, :], scalar1=1.0 / D)), [0], []),
                (VEC, dv, (lambda ve, jj=jj: ve.tensor_mul(
                    out=l2c[jj][:, :], in0=l2a[jj][:, :], in1=l2a[jj][:, :])), [1], []),
                (SCA, ac, (lambda se, jj=jj, tt=tt: se.activation(
                    out=ln2sq[jj][:, :], in_=r_att[jj][:, :], func=AF.Square)), [], ext0),
                (VEC, dv, (lambda ve, jj=jj: ve.reduce_sum(
                    out=l2b[jj][:, :], in_=ln2sq[jj][:, :], axis=AX)), [3], []),
                (VEC, dv, (lambda ve, jj=jj: ve.tensor_scalar_mul(
                    out=l2b[jj][:, :], in0=l2b[jj][:, :], scalar1=1.0 / D)), [4], []),
                (VEC, dv, (lambda ve, jj=jj: ve.tensor_sub(
                    out=l2b[jj][:, :], in0=l2b[jj][:, :], in1=l2c[jj][:, :])), [5, 2], []),
                (SCA, ac, (lambda se, jj=jj: se.activation(
                    out=l2b[jj][:, :], in_=l2b[jj][:, :], func=AF.Ln, bias=epsA[:, :])), [6], []),
                (SCA, ac, (lambda se, jj=jj: se.activation(
                    out=l2b[jj][:, :], in_=l2b[jj][:, :], func=AF.Exp, scale=-0.5)), [7], []),
                (VEC, dv, (lambda ve, jj=jj: ve.tensor_scalar(
                    out=attnb[jj][:, :], in0=r_att[jj][:, :],
                    scalar1=l2a[jj][:, :], scalar2=l2b[jj][:, :],
                    op0=ALU.subtract, op1=ALU.mult)), [8, 1], ext9),
            ])
            acL2[tt] = cts[8][1]
            dvL2[tt] = dv.n

            @SYN
            def _(sy, tt=tt):
                dv.w(sy, dvL2[tt])
                ascr.w(sy, 16 * tt)
                ascr.dma(sy.dma_start(out=attn_scr[tt * 128:(tt + 1) * 128, :],
                                      in_=attnb[tt % 2][:, :]))

        ctxB3.close()
        ctxA.close()

        ctxBB = ctx.enter_context(ExitStack())
        E4 = ctxBB.enter_context
        gaf = E4(nc.sbuf_tensor("gaf", [1, D], F32))
        bef = E4(nc.sbuf_tensor("bef", [1, D], F32))
        b1_s = E4(nc.sbuf_tensor("b1_s", [128, 256], F32))
        bo2_s = E4(nc.sbuf_tensor("bo2_s", [128, 8], F32))
        xe_sb = E4(nc.sbuf_tensor("xe_sb", [128, 16 * 1024], BF16))
        g_e = [E4(nc.sbuf_tensor(f"g_e{j}", [128, 8 * 256], BF16)) for j in range(2)]
        ytT = [E4(nc.sbuf_tensor(f"ytT{j}", [128, 256], F32)) for j in range(2)]
        fgt = [E4(nc.sbuf_tensor(f"fgt{j}", [128, D], BF16)) for j in range(2)]
        Gab = E4(nc.sbuf_tensor("Gab", [128, D], BF16))
        Beb = E4(nc.sbuf_tensor("Beb", [128, D], BF16))
        z3 = [E4(nc.sbuf_tensor(f"z3_{j}", [128, D], F32)) for j in range(2)]
        ln3sq = z3
        l3a = [E4(nc.sbuf_tensor(f"l3a{j}", [128, 1], F32)) for j in range(2)]
        l3b = [E4(nc.sbuf_tensor(f"l3b{j}", [128, 1], F32)) for j in range(2)]
        l3c = [E4(nc.sbuf_tensor(f"l3c{j}", [128, 1], F32)) for j in range(2)]
        xtile = [E4(nc.sbuf_tensor(f"xt4_{j}", [128, D], F32)) for j in range(2)]
        outb = [E4(nc.sbuf_tensor("outb0", [128, D], F32))] * 2
        attnb = [E4(nc.sbuf_tensor(f"attnb4_{j}", [128, D], BF16)) for j in range(2)]
        w1b4 = w1b + [E4(nc.sbuf_tensor(f"w1b4_{j}", [128, 8 * 256], BF16))
                      for j in range(2)]
        psz = [E4(nc.psum_tensor(f"psz{j}", [128, 1024], F32)) for j in range(2)]
        ptr = [E4(nc.psum_tensor(f"ptr{j}", [128, 128], F32)) for j in range(2)]
        ptrb = [E4(nc.psum_tensor(f"ptrb{j}", [128, 128], BF16)) for j in range(2)]

    

        @VEC
        def _(ve):
            ascr.w(ve)

        @SCA
        def _(se):
            ascr.w(se)

        @SYN
        def _(sy):
            ascr.w(sy)
            fence(sy)
            for dst, hh in [(gaf, ga_h), (bef, be_h), (b1_s, b1_h), (bo2_s, bo2_h)]:
                qS.dma(sy.dma_start(out=dst[:, :], in_=hh[:, :]))

        @TEN
        def _(te):
            qS.w(te)

        chain(
            (TEN, pe, lambda te: te.matmul(psz[0][:, 0:512], or_f[:, :], gaf[:, 0:512], start=True, stop=True)),
            (TEN, pe, lambda te: te.matmul(psz[0][:, 512:1024], or_f[:, :], gaf[:, 512:1024], start=True, stop=True)),
            (VEC, dv, lambda ve: ve.tensor_copy(out=Gab[:, :], in_=psz[0][:, :])),
            (TEN, pe, lambda te: te.matmul(psz[1][:, 0:512], or_f[:, :], bef[:, 0:512], start=True, stop=True)),
            (TEN, pe, lambda te: te.matmul(psz[1][:, 512:1024], or_f[:, :], bef[:, 512:1024], start=True, stop=True)),
            (VEC, dv, lambda ve: ve.tensor_copy(out=Beb[:, :], in_=psz[1][:, :])),
        )

        xnsc_total = xnsc.n
        isem_scat = isem.n
        for half in range(2):
            ctxH1 = ExitStack()
            xeT = ctxH1.enter_context(nc.sbuf_tensor(f"xeT{half}", [128, 8 * 2048], BF16))

            @SYN
            def _(sy, half=half):
                isem.w(sy, isem_scat)
                fsem.w(sy)
                fsem.dma(sy.dma_start(
                    out=xe_sb[:, :].rearrange("p (c m) -> p c m", c=16),
                    in_=xe_scr[half][0:2048, :].rearrange("(c p) m -> p c m", p=128)))

            peXT = {}
            for sb in range(16):
                for dcb in range(8):
                    b = sb * 8 + dcb

                    @TEN
                    def _(te, sb=sb, dcb=dcb, b=b):
                        if b == 0:
                            fsem.w(te)
                        if b >= 2:
                            dv.w(te, peXT[b - 2])
                        pe.inc(te.transpose(ptrb[b % 2][:, :],
                                            xe_sb[:, sb * 1024 + dcb * 128:sb * 1024 + (dcb + 1) * 128],
                                            idb_s[:, :]))

                    @VEC
                    def _(ve, sb=sb, dcb=dcb, b=b):
                        pe.w(ve)
                        dv.inc(ve.tensor_copy(
                            out=xeT[:, dcb * 2048 + sb * 128:dcb * 2048 + (sb + 1) * 128],
                            in_=ptrb[b % 2][:, :]))
                        peXT[b] = dv.n
            dvXET = dv.n

            base_pez = pe.n
            base_acA = ac.n
            pe_after_z = {}
            cnt = base_pez
            for z in range(256):
                cnt += 1
                pe_after_z[z] = cnt
            ac_after_z = {z: base_acA + z + 1 for z in range(256)}

            @SYN
            def _(sy, half=half, base_pez=base_pez, pe_after_z=pe_after_z):
                if half == 0:
                    for jw in range(4):
                        w1c[jw].w(sy)
                for k in range(128):
                    e, iq4 = divmod(k, 16)
                    if k >= 4:
                        sy.wait_ge(pe.h, pe_after_z[2 * k - 7])
                    elif half == 1:
                        sy.wait_ge(pe.h, base_pez)
                    w1r[k % 4].dma(sy.dma_start(out=w1b4[k % 4][:, :], in_=w1bf[e, iq4]))

            @TEN
            def _(te, half=half, ac_after_z=ac_after_z, dvXET=dvXET):
                te.wait_ge(dv.h, dvXET)
                for e in range(NE):
                    for ic in range(32):
                        z = e * 32 + ic
                        kg = half * 128 + e * 16 + ic // 2
                        te.wait_ge(w1r[kg % 4].h, 16 * (kg // 4 + 1))
                        if z >= 4:
                            te.wait_ge(ac.h, ac_after_z[z - 4])
                        for dc in range(8):
                            ins = te.matmul(
                                psz[z % 2][:, (z // 2 % 2) * 512:(z // 2 % 2) * 512 + 256],
                                w1b4[kg % 4][:, dc * 256 + (ic % 2) * 128:
                                             dc * 256 + (ic % 2 + 1) * 128],
                                xeT[:, dc * 2048 + e * 256:dc * 2048 + (e + 1) * 256],
                                start=(dc == 0), stop=(dc == 7))
                        pe.inc(ins)

            @SCA
            def _(se, half=half, pe_after_z=pe_after_z):
                for e in range(NE):
                    for ic in range(32):
                        z = e * 32 + ic
                        se.wait_ge(pe.h, pe_after_z[z])
                        if ic % 8 == 0 and z >= 16:
                            bz = half * 32 + z // 8
                            gsem[bz % 2].w(se, 16 * ((bz - 2) // 2 + 1))
                        ac.inc(se.activation(out=g_e[(z // 8) % 2][:, (ic % 8) * 256:(ic % 8 + 1) * 256],
                                             in_=psz[z % 2][:, (z // 2 % 2) * 512:(z // 2 % 2) * 512 + 256],
                                             func=AF.Gelu,
                                             bias=b1_s[:, ic * 8 + e:ic * 8 + e + 1]))

            @GPS
            def _(ge, half=half, ac_after_z=ac_after_z):
                for e in range(NE):
                    for icb in range(4):
                        ge.wait_ge(ac.h, ac_after_z[e * 32 + icb * 8 + 7])
                        gsem[(e * 4 + icb) % 2].dma(ge.dma_start(
                            out=hg_scr[icb * 1024:(icb + 1) * 1024,
                                       e * 256:(e + 1) * 256]
                                .rearrange("(i p) s -> p i s", p=128),
                            in_=g_e[(e * 4 + icb) % 2][:, :]
                                .rearrange("p (i s) -> p i s", i=8)))

            ctxH1.close()

            ctxH2 = ExitStack()
            hgq = [ctxH2.enter_context(nc.sbuf_tensor(f"hgq{half}_{j}", [128, 32 * 256], BF16))
                   for j in range(2)]
            wo2d = [ctxH2.enter_context(nc.sbuf_tensor(f"wo2d{half}_{j}", [128, 4096], BF16))
                    for j in range(2)]
            peB1 = {}; dvS = {}; dvYe = {}
            gsem_all = (gsem[0].n, gsem[1].n)
            for sck in range(8):
                @SYN
                def _(sy, half=half, sck=sck):
                    sy.wait_ge(gsem[0].h, gsem_all[0])
                    sy.wait_ge(gsem[1].h, gsem_all[1])
                    if sck >= 2:
                        pe.w(sy, peB1[(sck - 2) * 8 + 7])
                    for ih in range(2):
                        hsem[sck % 2].dma(sy.dma_start(
                            out=hgq[sck % 2][:, ih * 4096:(ih + 1) * 4096]
                                .rearrange("p (i s) -> p i s", i=16),
                            in_=hg_scr[ih * 2048:(ih + 1) * 2048,
                                       sck * 256:(sck + 1) * 256]
                                .rearrange("(i p) s -> p i s", p=128)))

                for d2c in range(8):
                    t = sck * 8 + d2c

                    @GPS
                    def _(ge, d2c=d2c, t=t):
                        if t >= 2:
                            pe.w(ge, peB1.get(t - 2, 0))
                        elif t == 0:
                            wcst.w(ge)
                            pe.w(ge)
                        wo2s[t % 2].dma(ge.dma_start(out=wo2d[t % 2][:, :], in_=wo2bf[d2c]))

                    @TEN
                    def _(te, sck=sck, d2c=d2c, t=t, half=half):
                        gsck = half * 8 + sck
                        te.wait_ge(hsem[sck % 2].h, 32 * (gsck // 2 + 1))
                        te.wait_ge(wo2s[t % 2].h, 16 * ((half * 64 + t) // 2 + 1))
                        if t >= 4:
                            dv.w(te, dvS[t - 4])
                        for ic in range(32):
                            ins = te.matmul(
                                psz[t % 2][:, (t // 2 % 2) * 512:(t // 2 % 2) * 512 + 256],
                                wo2d[t % 2][:, ic * 128:(ic + 1) * 128],
                                hgq[sck % 2][:, ic * 256:(ic + 1) * 256],
                                start=(ic == 0), stop=(ic == 31))
                        pe.inc(ins)
                        peB1[t] = pe.n

                    @VEC
                    def _(ve, half=half, sck=sck, d2c=d2c, t=t):
                        pe.w(ve, peB1[t])
                        dv.inc(ve.tensor_scalar_add(
                            out=ytT[t % 2][:, :],
                            in0=psz[t % 2][:, (t // 2 % 2) * 512:(t // 2 % 2) * 512 + 256],
                            scalar1=bo2_s[:, d2c:d2c + 1]))
                        dvS[t] = dv.n

                    for blk in range(2):
                        bg = t * 2 + blk

                        @TEN
                        def _(te, t=t, blk=blk, bg=bg):
                            dv.w(te, dvS[t])
                            if bg >= 2:
                                dv.w(te, dvYe[bg - 2])
                            pe.inc(te.transpose(ptr[bg % 2][:, :],
                                                ytT[t % 2][:, blk * 128:(blk + 1) * 128],
                                                idf_s[:, :]))

                        @VEC
                        def _(ve, sck=sck, d2c=d2c, blk=blk, bg=bg):
                            pe.w(ve)
                            sb = sck * 2 + blk
                            sl = slice(sb * 1024 + d2c * 128, sb * 1024 + (d2c + 1) * 128)
                            dv.inc(ve.tensor_tensor(
                                out=xe_sb[:, sl], in0=ptr[bg % 2][:, :],
                                in1=xe_sb[:, sl], op=ALU.add))
                            dvYe[bg] = dv.n

            ctxH2.close()

            @SYN
            def _(sy):
                dv.w(sy)
                fsem.dma(sy.dma_start(
                    out=fg_scr[:, :].rearrange("(c p) m -> p c m", p=128),
                    in_=xe_sb[:, :].rearrange("p (c m) -> p c m", c=16)))

            if half == 0:
                ln3_state = {"dvTS": {}, "dvCmb": {}, "stC": {}, "isG": {}}
            dvTS = ln3_state["dvTS"]; dvCmb = ln3_state["dvCmb"]
            stC = ln3_state["stC"]; isG = ln3_state["isG"]
            for tb in range(8):
                tt = half * 8 + tb
                gb = half * 8 + tb
                j = tb % 2

                @GPS
                def _(ge, tb=tb, j=j, half=half, gb=gb):
                    fsem.w(ge)
                    isem.w(ge)
                    if gb >= 2:
                        dv.w(ge, dvTS[gb - 2])
                    isem.dma(ge.indirect_dma_start(
                        out=fgt[j][:, :], out_offset=None,
                        in_=fg_scr[:, :],
                        in_offset=bass.IndirectOffsetOnAxis(
                            desti2[:, half * 8 + tb:half * 8 + tb + 1], 0)))
                    isG[gb] = isem.n

                @SYN
                def _(sy, tt=tt, j=j, gb=gb):
                    if gb >= 1:
                        st.w(sy, 16 * gb)
                    if gb >= 2:
                        dv.w(sy, dvCmb[gb - 2])
                    xs[j].dma(sy.dma_start(out=xtile[j][:, :],
                                           in_=x_h[tt * 128:(tt + 1) * 128, :]))
                    asl[j].dma(sy.dma_start(out=attnb[j][:, :],
                                            in_=attn_scr[tt * 128:(tt + 1) * 128, :]))

                extg = [(isem, isG[gb])]
                if gb >= 2:
                    extg = extg + [(dv, dvCmb[gb - 2])]
                cts = pchain([
                    (VEC, dv, (lambda ve, j=j: ve.reduce_sum(
                        out=l3a[j][:, :], in_=fgt[j][:, :], axis=AX)), [], extg),
                    (VEC, dv, (lambda ve, j=j: ve.tensor_scalar_mul(
                        out=l3a[j][:, :], in0=l3a[j][:, :], scalar1=1.0 / D)), [0], []),
                    (VEC, dv, (lambda ve, j=j: ve.tensor_mul(
                        out=l3c[j][:, :], in0=l3a[j][:, :], in1=l3a[j][:, :])), [1], []),
                    (SCA, ac, (lambda se, j=j: se.activation(
                        out=ln3sq[j][:, :], in_=fgt[j][:, :], func=AF.Square)), [], extg),
                    (VEC, dv, (lambda ve, j=j: ve.reduce_sum(
                        out=l3b[j][:, :], in_=ln3sq[j][:, :], axis=AX)), [3], []),
                    (VEC, dv, (lambda ve, j=j: ve.tensor_scalar_mul(
                        out=l3b[j][:, :], in0=l3b[j][:, :], scalar1=1.0 / D)), [4], []),
                    (VEC, dv, (lambda ve, j=j: ve.tensor_sub(
                        out=l3b[j][:, :], in0=l3b[j][:, :], in1=l3c[j][:, :])), [5, 2], []),
                    (SCA, ac, (lambda se, j=j: se.activation(
                        out=l3b[j][:, :], in_=l3b[j][:, :], func=AF.Ln,
                        bias=epsC[:, :])), [6], []),
                    (SCA, ac, (lambda se, j=j: se.activation(
                        out=l3b[j][:, :], in_=l3b[j][:, :], func=AF.Exp,
                        scale=-0.5)), [7], []),
                    (VEC, dv, (lambda ve, j=j: ve.tensor_scalar(
                        out=z3[j][:, :], in0=fgt[j][:, :], scalar1=l3a[j][:, :],
                        scalar2=l3b[j][:, :],
                        op0=ALU.subtract, op1=ALU.mult)), [8, 1], []),
                    (VEC, dv, (lambda ve, j=j: ve.tensor_mul(
                        out=z3[j][:, :], in0=z3[j][:, :], in1=Gab[:, :])), [9], []),
                    (VEC, dv, (lambda ve, j=j: ve.tensor_add(
                        out=z3[j][:, :], in0=z3[j][:, :], in1=Beb[:, :])), [10], []),
                ])
                dvTS[gb] = cts[11][1]

                @VEC
                def _(ve, tt=tt, j=j, gb=gb):
                    xs[j].w(ve)
                    asl[j].w(ve)
                    dv.w(ve, dvTS[gb])
                    if gb >= 1:
                        st.w(ve, 16 * gb)
                    dv.inc(ve.tensor_add(out=outb[j][:, :], in0=z3[j][:, :], in1=xtile[j][:, :]))
                    dv.w(ve)
                    dv.inc(ve.tensor_add(out=outb[j][:, :], in0=outb[j][:, :], in1=attnb[j][:, :]))
                    dvCmb[gb] = dv.n

                @SYN
                def _(sy, tt=tt, j=j, gb=gb):
                    dv.w(sy, dvCmb[gb])
                    st.dma(sy.dma_start(out=y_h[tt * 128:(tt + 1) * 128, :],
                                        in_=outb[j][:, :]))

        @SYN
        def _(sy):
            st.w(sy)

    return nc


_NC_CACHE = {}


def _get_nc():
    if "nc" not in _NC_CACHE:
        _NC_CACHE["nc"] = build()
    return _NC_CACHE["nc"]


def kernel(L0_attn_x, L0_attn_Wq, L0_attn_bq, L0_attn_Wk, L0_attn_bk,
           L0_attn_Wv, L0_attn_bv, L0_attn_Wo, L0_attn_bo,
           L0_ffn_Wr, L0_ffn_W1, L0_ffn_b1, L0_ffn_Wo, L0_ffn_bo,
           L0_ffn_gamma, L0_ffn_beta):
    f = np.asarray
    x = np.ascontiguousarray(f(L0_attn_x, dtype=np.float32))
    Wq = f(L0_attn_Wq, dtype=np.float32); Wk = f(L0_attn_Wk, dtype=np.float32)
    Wv = f(L0_attn_Wv, dtype=np.float32); Wo = f(L0_attn_Wo, dtype=np.float32)
    Wr = f(L0_ffn_Wr, dtype=np.float32); W1 = f(L0_ffn_W1, dtype=np.float32)
    Wo2 = f(L0_ffn_Wo, dtype=np.float32)

    def qkv_lay(W):
        return np.ascontiguousarray(
            W.reshape(8, 128, 4, 256).transpose(2, 1, 0, 3).reshape(4, 128, 8 * 256))

    wq_c = qkv_lay(Wq); wk_c = qkv_lay(Wk); wv_c = qkv_lay(Wv)
    wo_c = np.ascontiguousarray(
        Wo.reshape(8, 128, 1024).transpose(1, 0, 2).reshape(128, 8192))
    wrt_c = np.ascontiguousarray(
        Wr.T.reshape(8, 128, 8).transpose(1, 0, 2).reshape(128, 64))
    w1_c = np.ascontiguousarray(
        W1.reshape(8, 8, 128, 16, 256).transpose(0, 3, 2, 1, 4).reshape(8, 16, 128, 8 * 256))
    wo2_c = np.ascontiguousarray(
        Wo2.reshape(32, 128, 8, 128).transpose(2, 1, 0, 3).reshape(8, 128, 32 * 128))
    bq_t = np.ascontiguousarray(f(L0_attn_bq, dtype=np.float32).reshape(8, 128).T)
    bk_t = np.ascontiguousarray(f(L0_attn_bk, dtype=np.float32).reshape(8, 128).T)
    bv_r = f(L0_attn_bv, dtype=np.float32).reshape(1, 1024)
    bo_r = f(L0_attn_bo, dtype=np.float32).reshape(1, 1024)
    b1_t = np.ascontiguousarray(
        f(L0_ffn_b1, dtype=np.float32).T.reshape(32, 128, 8).transpose(1, 0, 2).reshape(128, 256))
    bo2_t = np.ascontiguousarray(f(L0_ffn_bo, dtype=np.float32).reshape(8, 128).T)
    ga_r = f(L0_ffn_gamma, dtype=np.float32).reshape(1, 1024)
    be_r = f(L0_ffn_beta, dtype=np.float32).reshape(1, 1024)

    shared = dict(wq_c=wq_c, wk_c=wk_c, wv_c=wv_c, wo_c=wo_c, wrt_c=wrt_c,
                  w1_c=w1_c, wo2_c=wo2_c, bq_t=bq_t, bk_t=bk_t, bv_r=bv_r,
                  bo_r=bo_r, b1_t=b1_t, bo2_t=bo2_t, ga_r=ga_r, be_r=be_r)
    in_maps = []
    for c in range(8):
        b, half = divmod(c, 2)
        own = x[b, half * 2048:(half + 1) * 2048]
        par = x[b, (1 - half) * 2048:(2 - half) * 2048]
        m = dict(shared)
        m["xseq"] = np.ascontiguousarray(np.concatenate([own, par], axis=0))
        in_maps.append(m)

    nc = _get_nc()
    res = run_bass_kernel_spmd(nc, in_maps, core_ids=list(range(8)))
    out = np.empty((4, 4096, 1024), dtype=np.float32)
    for c in range(8):
        b, half = divmod(c, 2)
        out[b, half * 2048:(half + 1) * 2048] = res.results[c]["y"]
    return out


if __name__ == "__main__":
    nc = build()
    print("build ok")


# version 28
# speedup vs baseline: 1.1624x; 1.0036x over previous
import numpy as np
import ml_dtypes
from contextlib import ExitStack

import concourse.bass as bass
import concourse.mybir as mybir
from concourse.bass_utils import run_bass_kernel_spmd

F32 = mybir.dt.float32
BF16 = mybir.dt.bfloat16
AF = mybir.ActivationFunctionType
ALU = mybir.AluOpType
AX = mybir.AxisListType.X
BF = ml_dtypes.bfloat16

D = 1024
S = 4096
T = 2048
H = 16
NE = 8
NG = 4
HPG = 4
NI = 4096
EPS1 = 1e-5
EPS3 = 1e-12


class Sem:
    def __init__(self, ctx, nc, name):
        self.h = ctx.enter_context(nc.semaphore(name))
        self.n = 0

    def inc(self, ins, k=1):
        ins.then_inc(self.h, k)
        self.n += k

    def dma(self, ins):
        self.inc(ins, 16)

    def w(self, eng, at=None):
        v = self.n if at is None else at
        if v > 0:
            eng.wait_ge(self.h, v)


def build():
    nc = bass.Bass("TRN2", target_bir_lowering=False, debug=False)

    x_h = nc.dram_tensor("xseq", [S, D], F32, kind="ExternalInput").ap()
    wq_h = nc.dram_tensor("wq_c", [NG, 128, 8 * 256], F32, kind="ExternalInput").ap()
    wk_h = nc.dram_tensor("wk_c", [NG, 128, 8 * 256], F32, kind="ExternalInput").ap()
    wv_h = nc.dram_tensor("wv_c", [NG, 128, 8 * 256], F32, kind="ExternalInput").ap()
    wo_h = nc.dram_tensor("wo_c", [128, 8 * 1024], F32, kind="ExternalInput").ap()
    wrt_h = nc.dram_tensor("wrt_c", [128, 64], F32, kind="ExternalInput").ap()
    w1_h = nc.dram_tensor("w1_c", [NE, 16, 128, 8 * 256], F32, kind="ExternalInput").ap()
    wo2_h = nc.dram_tensor("wo2_c", [8, 128, 32 * 128], F32, kind="ExternalInput").ap()
    bq_h = nc.dram_tensor("bq_t", [128, 8], F32, kind="ExternalInput").ap()
    bk_h = nc.dram_tensor("bk_t", [128, 8], F32, kind="ExternalInput").ap()
    bv_h = nc.dram_tensor("bv_r", [1, D], F32, kind="ExternalInput").ap()
    bo_h = nc.dram_tensor("bo_r", [1, D], F32, kind="ExternalInput").ap()
    b1_h = nc.dram_tensor("b1_t", [128, 32 * 8], F32, kind="ExternalInput").ap()
    bo2_h = nc.dram_tensor("bo2_t", [128, 8], F32, kind="ExternalInput").ap()
    ga_h = nc.dram_tensor("ga_r", [1, D], F32, kind="ExternalInput").ap()
    be_h = nc.dram_tensor("be_r", [1, D], F32, kind="ExternalInput").ap()
    y_h = nc.dram_tensor("y", [T, D], F32, kind="ExternalOutput").ap()
    attn_scr = nc.dram_tensor("attn_scr", [T, D], BF16, kind="Internal").ap()
    xn_scr = nc.dram_tensor("xn_scr", [T, D], BF16, kind="Internal").ap()
    xe_scr = [nc.dram_tensor(f"xe_scr{h}", [2049, D], BF16, kind="Internal").ap()
              for h in range(2)]
    hg_scr = nc.dram_tensor("hg_scr", [NI, 2048], BF16, kind="Internal").ap()
    fg_scr = nc.dram_tensor("fg_scr", [2048, D], BF16, kind="Internal").ap()
    w1bf = nc.dram_tensor("w1bf", [NE, 16, 128, 8 * 256], BF16, kind="Internal").ap()
    wo2bf = nc.dram_tensor("wo2bf", [8, 128, 4096], BF16, kind="Internal").ap()

    idf = nc.inline_tensor(np.eye(128, dtype=np.float32), name="idf").ap()
    idb = nc.inline_tensor(np.eye(128, dtype=BF), name="idb").ap()
    ones_r_b = nc.inline_tensor(np.ones((1, 128), dtype=BF), name="ones_r_b").ap()
    ones_r_f = nc.inline_tensor(np.ones((1, 128), dtype=np.float32), name="ones_r_f").ap()
    ones_c_b = nc.inline_tensor(np.ones((128, 1), dtype=BF), name="ones_c_b").ap()
    ones_c_f = nc.inline_tensor(np.ones((128, 1), dtype=np.float32), name="ones_c_f").ap()
    w8c = nc.inline_tensor(
        np.broadcast_to(8.0 - np.arange(8, dtype=np.float32), (128, 8)).copy(),
        name="w8c").ap()
    pi_np = np.zeros((16, 8 * 128), dtype=BF)
    for dc in range(8):
        pi_np[2 * dc, dc * 128:dc * 128 + 64] = 1
        pi_np[2 * dc + 1, dc * 128 + 64:dc * 128 + 128] = 1
    pairind = nc.inline_tensor(pi_np, name="pairind").ap()
    ecol1 = nc.inline_tensor(
        (np.arange(8, dtype=np.float32) * 256.0 - 1.0).reshape(8, 1), name="ecol1").ap()

    ctx = ExitStack()
    with ctx:
        E = ctx.enter_context

        xnT_own = E(nc.sbuf_tensor("xnT_own", [128, 8 * T], BF16))
        rowf = E(nc.sbuf_tensor("rowf", [1, D], F32))
        bv_b = E(nc.sbuf_tensor("bv_b", [1, D], BF16))
        bo_b = E(nc.sbuf_tensor("bo_b", [1, D], BF16))
        idf_s = E(nc.sbuf_tensor("idf_s", [128, 128], F32))
        idb_s = E(nc.sbuf_tensor("idb_s", [128, 128], BF16))
        or_b = E(nc.sbuf_tensor("or_b", [1, 128], BF16))
        or_f = E(nc.sbuf_tensor("or_f", [1, 128], F32))
        oc_b = E(nc.sbuf_tensor("oc_b", [128, 1], BF16))
        oc_f = E(nc.sbuf_tensor("oc_f", [128, 1], F32))
        w8_s = E(nc.sbuf_tensor("w8_s", [128, 8], F32))
        pi_s = E(nc.sbuf_tensor("pi_s", [16, 8 * 128], BF16))
        ec_s = E(nc.sbuf_tensor("ec_s", [8, 1], F32))
        wrt_s = E(nc.sbuf_tensor("wrt_s", [128, 64], F32))
        bq_s = E(nc.sbuf_tensor("bq_s", [128, 8], F32))
        bk_s = E(nc.sbuf_tensor("bk_s", [128, 8], F32))
        epsA = E(nc.sbuf_tensor("epsA", [128, 1], F32))
        epsC = E(nc.sbuf_tensor("epsC", [128, 1], F32))
        maskT = E(nc.sbuf_tensor("maskT", [8, T], BF16))
        desti2 = E(nc.sbuf_tensor("desti2", [128, 16], mybir.dt.int32))
        w1b = [E(nc.sbuf_tensor(f"w1b{j}", [128, 8 * 256], BF16)) for j in range(2)]

        qS = Sem(ctx, nc, "qS")
        qG = Sem(ctx, nc, "qG")
        xs = [Sem(ctx, nc, f"xs{j}") for j in range(2)]
        xsp = [Sem(ctx, nc, f"xsp{j}") for j in range(4)]
        asl = [Sem(ctx, nc, f"asl{j}") for j in range(2)]
        w1s = [Sem(ctx, nc, f"w1s{j}") for j in range(2)]
        wo2s = [Sem(ctx, nc, f"wo2s{j}") for j in range(2)]
        pe = Sem(ctx, nc, "pe")
        pesc = Sem(ctx, nc, "pesc")
        peav = Sem(ctx, nc, "peav")
        ac = Sem(ctx, nc, "ac")
        dv = Sem(ctx, nc, "dv")
        ssem = Sem(ctx, nc, "ssem")
        ascr = Sem(ctx, nc, "ascr")
        gp = Sem(ctx, nc, "gp")
        xnsc = Sem(ctx, nc, "xnsc")
        isem = Sem(ctx, nc, "isem")
        gsem = [Sem(ctx, nc, f"gsem{j}") for j in range(2)]
        hsem = [Sem(ctx, nc, f"hsem{j}") for j in range(2)]
        fsem = Sem(ctx, nc, "fsem")
        wcst = Sem(ctx, nc, "wcst")
        gscat = Sem(ctx, nc, "gscat")
        w1c = [Sem(ctx, nc, f"w1c{j}") for j in range(4)]
        w1g = [Sem(ctx, nc, f"w1g{j}") for j in range(4)]
        w1r = [Sem(ctx, nc, f"w1r{j}") for j in range(4)]
        st = Sem(ctx, nc, "st")
        COMPUTE = [pe, pesc, peav, ac, dv]

        block = E(nc.Block())
        TEN, VEC, SCA, GPS, SYN = (block.tensor, block.vector, block.scalar,
                                   block.gpsimd, block.sync)

        def fence(eng):
            for s in COMPUTE:
                s.w(eng)

        def chain(*steps):
            for secwrap, sem, fn in steps:
                def body(eng, fn=fn, sem=sem):
                    fence(eng)
                    sem.inc(fn(eng))
                secwrap(body)

        def pchain(steps):
            counts = []
            for secwrap, sem, fn, deps, extra in steps:
                def body(eng, fn=fn, sem=sem, deps=tuple(deps), extra=tuple(extra)):
                    for s_, v_ in extra:
                        s_.w(eng, v_)
                    for d_ in deps:
                        ds, dval = counts[d_]
                        ds.w(eng, dval)
                    sem.inc(fn(eng))
                secwrap(body)
                counts.append((sem, sem.n))
            return counts

        @SYN
        def _(sy):
            for dst, src in [(idf_s, idf), (idb_s, idb), (or_b, ones_r_b),
                             (or_f, ones_r_f), (oc_b, ones_c_b), (oc_f, ones_c_f), (w8_s, w8c),
                             (pi_s, pairind), (ec_s, ecol1), (wrt_s, wrt_h),
                             (bq_s, bq_h), (bk_s, bk_h),
                             (rowf, bv_h)]:
                qS.dma(sy.dma_start(out=dst[:, :], in_=src[:, :]))

        @VEC
        def _(ve):
            qS.w(ve)
            dv.inc(ve.memset(epsA[:, :], EPS1))
            dv.inc(ve.memset(epsC[:, :], EPS3))
            dv.inc(ve.tensor_copy(out=bv_b[:, :], in_=rowf[:, :]))

        @SYN
        def _(sy):
            dv.w(sy)
            qS.dma(sy.dma_start(out=rowf[:, :], in_=bo_h[:, :]))

        @VEC
        def _(ve):
            qS.w(ve)
            dv.inc(ve.tensor_copy(out=bo_b[:, :], in_=rowf[:, :]))

        ctxA = ctx.enter_context(ExitStack())
        EA = ctxA.enter_context
        xnT_par = EA(nc.sbuf_tensor("xnT_par", [128, 8 * T], BF16))
        scstg = [EA(nc.sbuf_tensor(f"scstg{j}", [128, D], BF16)) for j in range(2)]
        O_u = EA(nc.sbuf_tensor("O_u", [128, 8 * T], BF16))

        ctxA1 = ExitStack()
        EA1 = ctxA1.enter_context
        xtile = [EA1(nc.sbuf_tensor(f"xt1_{j}", [128, D], F32)) for j in range(4)]
        sqs = EA1(nc.sbuf_tensor("sqs", [128, D], F32))
        xnf = [EA1(nc.sbuf_tensor(f"xnf{j}", [128, D], F32)) for j in range(4)]
        xtf = EA1(nc.sbuf_tensor("xtf", [128, 1024], F32))
        st1 = [EA1(nc.sbuf_tensor(f"st1_{j}", [128, 1], F32)) for j in range(4)]
        st2 = [EA1(nc.sbuf_tensor(f"st2_{j}", [128, 1], F32)) for j in range(4)]
        st3 = [EA1(nc.sbuf_tensor(f"st3_{j}", [128, 1], F32)) for j in range(4)]
        st4 = [EA1(nc.sbuf_tensor(f"st4_{j}", [128, 1], F32)) for j in range(4)]
        mst1 = EA1(nc.sbuf_tensor("mst1", [128, 1], F32))
        mst3 = EA1(nc.sbuf_tensor("mst3", [128, 1], F32))
        lgs = EA1(nc.sbuf_tensor("lgs", [128, 8], F32))
        eqs = EA1(nc.sbuf_tensor("eqs", [128, 8], F32))
        v8s = EA1(nc.sbuf_tensor("v8s", [128, 8], F32))
        mkb = EA1(nc.sbuf_tensor("mkb", [128, 8], BF16))
        psT2 = [EA1(nc.psum_tensor(f"psT2{j}", [128, 512], F32)) for j in range(2)]
        pidx = EA1(nc.psum_tensor("pidx", [1, 1024], F32))
        pslog = [EA1(nc.psum_tensor(f"pslog{j}", [128, 8], F32)) for j in range(2)]
        psmT = EA1(nc.psum_tensor("psmT", [8, 128], BF16))

        NT = S // 128
        acSq = {}; acRstd = {}; dvXnf = {}; dvVar = {}; dvEvacH = {}
        dvLog = {}; dvMkb = {}; dvMT = {}; peTr = {}; peLog = {}; peMT = {}

        for tt in range(NT):
            j = tt % 4
            own = tt < 16

            @SYN
            def _(sy, tt=tt, j=j):
                if tt >= 4:
                    ac.w(sy, acSq[tt - 4])
                    dv.w(sy, dvXnf[tt - 4])
                xsp[j].dma(sy.dma_start(out=xtile[j][:, :],
                                        in_=x_h[tt * 128:(tt + 1) * 128, :]))

            @SCA
            def _(se, tt=tt, j=j):
                xsp[j].w(se)
                if tt >= 4:
                    dv.w(se, dvVar[tt - 4])
                ac.inc(se.activation(out=sqs[:, :], in_=xtile[j][:, :],
                                     func=AF.Square, accum_out=st2[j][:, :]))
                acSq[tt] = ac.n

            @VEC
            def _(ve, tt=tt, j=j):
                xsp[j].w(ve)
                if tt >= 4:
                    dv.w(ve, dvXnf[tt - 4])
                    ac.w(ve, acRstd[tt - 4])
                dv.inc(ve.reduce_sum(out=st1[j][:, :], in_=xtile[j][:, :], axis=AX))
                dv.w(ve)
                dv.inc(ve.tensor_scalar_mul(out=st1[j][:, :], in0=st1[j][:, :],
                                            scalar1=1.0 / D))
                dv.w(ve)
                dv.inc(ve.tensor_scalar(out=st3[j][:, :], in0=st1[j][:, :],
                                        scalar1=st1[j][:, :], scalar2=None,
                                        op0=ALU.mult))
                ac.w(ve, acSq[tt])
                dv.w(ve)
                dv.inc(ve.tensor_scalar(out=st2[j][:, :], in0=st2[j][:, :],
                                        scalar1=1.0 / D, scalar2=st3[j][:, :],
                                        op0=ALU.mult, op1=ALU.subtract))
                dvVar[tt] = dv.n

            @SCA
            def _(se, tt=tt, j=j):
                dv.w(se, dvVar[tt])
                ac.inc(se.activation(out=st4[j][:, :], in_=st2[j][:, :], func=AF.Ln, bias=epsA[:, :]))
                ac.w(se)
                ac.inc(se.activation(out=st4[j][:, :], in_=st4[j][:, :], func=AF.Exp, scale=-0.5))
                acRstd[tt] = ac.n

            @VEC
            def _(ve, tt=tt, j=j, own=own):
                ac.w(ve, acRstd[tt])
                if tt >= 4:
                    pe.w(ve, peTr[(tt - 4, 7)])
                    if tt - 4 < 16:
                        xnsc.w(ve, 16 * (tt - 3))
                dv.inc(ve.tensor_scalar(out=xnf[j][:, :], in0=xtile[j][:, :],
                                        scalar1=st1[j][:, :], scalar2=st4[j][:, :],
                                        op0=ALU.subtract, op1=ALU.mult))
                dvXnf[tt] = dv.n

            if own:
                @GPS
                def _(ge, tt=tt, j=j):
                    dv.w(ge, dvXnf[tt])
                    xnsc.w(ge, 16 * tt)
                    xnsc.dma(ge.dma_start(out=xn_scr[tt * 128:(tt + 1) * 128, :],
                                          in_=xnf[j][:, :]))

            dst = xnT_own if own else xnT_par
            toff = (tt % 16) * 128
            for dc in range(8):
                hb = tt * 2 + dc // 4
                b = hb % 2

                @TEN
                def _(te, tt=tt, dc=dc, j=j, b=b, hb=hb):
                    if dc % 4 == 0:
                        dv.w(te, dvXnf[tt])
                        if hb >= 2:
                            dv.w(te, dvEvacH.get(hb - 2, 0))
                    pe.inc(te.transpose(psT2[b][:, (dc % 4) * 128:(dc % 4 + 1) * 128],
                                        xnf[j][:, dc * 128:(dc + 1) * 128], idf_s[:, :]))
                    peTr[(tt, dc)] = pe.n

                if dc % 4 == 3:
                    @VEC
                    def _(ve, tt=tt, dc=dc, dst=dst, toff=toff, own=own, b=b, hb=hb):
                        pe.w(ve, peTr[(tt, dc)])
                        dv.inc(ve.tensor_copy(
                            out=dst[:, :].rearrange("p (d t) -> p d t", d=8)
                                [:, (hb % 2) * 4:(hb % 2) * 4 + 4, toff:toff + 128],
                            in_=psT2[b][:, :].rearrange("p (d t) -> p d t", d=4)))
                        if own:
                            dv.inc(ve.tensor_copy(
                                out=xtf[:, (hb % 2) * 512:(hb % 2) * 512 + 512],
                                in_=psT2[b][:, :]))
                        dvEvacH[hb] = dv.n

            if own:
                for dc in range(8):
                    @TEN
                    def _(te, tt=tt, dc=dc):
                        if dc == 0:
                            dv.w(te, dvEvacH[tt * 2])
                            if tt >= 2:
                                dv.w(te, dvLog[tt - 2])
                        elif dc == 4:
                            dv.w(te, dvEvacH[tt * 2 + 1])
                        ins = te.matmul(pslog[tt % 2][:, :],
                                        xtf[:, dc * 128:(dc + 1) * 128],
                                        wrt_s[:, dc * 8:(dc + 1) * 8],
                                        start=(dc == 0), stop=(dc == 7))
                        if dc == 7:
                            pe.inc(ins)
                            peLog[tt] = pe.n

                @VEC
                def _(ve, tt=tt):
                    pe.w(ve, peLog[tt])
                    dv.inc(ve.tensor_copy(out=lgs[:, :], in_=pslog[tt % 2][:, :]))
                    dvLog[tt] = dv.n
                    dv.w(ve)
                    dv.inc(ve.reduce_max(out=mst1[:, :], in_=lgs[:, :], axis=AX))
                    dv.w(ve)
                    dv.inc(ve.tensor_scalar(out=eqs[:, :], in0=lgs[:, :],
                                            scalar1=mst1[:, :], scalar2=None,
                                            op0=ALU.is_equal))
                    dv.w(ve)
                    dv.inc(ve.tensor_mul(out=v8s[:, :], in0=eqs[:, :], in1=w8_s[:, :]))
                    dv.w(ve)
                    dv.inc(ve.reduce_max(out=mst3[:, :], in_=v8s[:, :], axis=AX))
                    dv.w(ve)
                    dv.inc(ve.tensor_scalar(out=mkb[:, :], in0=v8s[:, :],
                                            scalar1=mst3[:, :], scalar2=None,
                                            op0=ALU.is_equal))
                    dvMkb[tt] = dv.n

                @TEN
                def _(te, tt=tt):
                    dv.w(te, dvMkb[tt])
                    if tt >= 1:
                        dv.w(te, dvMT[tt - 1])
                    pe.inc(te.transpose(psmT[:, :], mkb[:, :], idb_s[:, :]))
                    peMT[tt] = pe.n

                @VEC
                def _(ve, tt=tt, toff=toff):
                    pe.w(ve, peMT[tt])
                    dv.inc(ve.tensor_copy(out=maskT[:, toff:toff + 128], in_=psmT[:, :]))
                    dvMT[tt] = dv.n

        prefA = EA1(nc.sbuf_tensor("prefA", [8, 1024], F32))
        prefB = EA1(nc.sbuf_tensor("prefB", [8, 1024], F32))
        zeros8 = EA1(nc.sbuf_tensor("zeros8", [8, 1024], BF16))
        drow = EA1(nc.sbuf_tensor("drow", [1, 1024], F32))
        vrow = EA1(nc.sbuf_tensor("vrow", [1, 1024], F32))
        destc = EA1(nc.sbuf_tensor("destc", [128, 16], F32))
        stgb = EA1(nc.sbuf_tensor("stgb", [128, D], BF16))

        @VEC
        def _(ve):
            dv.inc(ve.memset(zeros8[:, :], 0.0))
            dv.inc(ve.memset(stgb[:, :], 0.0))

        @SYN
        def _(sy):
            dv.w(sy)
            for h2 in range(2):
                for zb in range(16):
                    fsem.dma(sy.dma_start(out=xe_scr[h2][zb * 128:(zb + 1) * 128, :],
                                          in_=stgb[:, :]))
                fsem.dma(sy.dma_start(out=xe_scr[h2][2048:2049, :], in_=stgb[0:1, :]))

        for half in range(2):
            mTh = maskT[:, half * 1024:(half + 1) * 1024]
            chain(
                (VEC, dv, lambda ve, mTh=mTh: ve.tensor_tensor_scan(
                    out=prefA[:, :], data0=mTh, data1=zeros8[:, :],
                    initial=0.0, op0=ALU.add, op1=ALU.add)),
                (VEC, dv, lambda ve: ve.tensor_scalar(
                    out=prefB[:, :], in0=prefA[:, :], scalar1=256.0, scalar2=None,
                    op0=ALU.is_le)),
                (VEC, dv, lambda ve: ve.tensor_scalar_add(
                    out=prefA[:, :], in0=prefA[:, :], scalar1=ec_s[:, :])),
                (VEC, dv, lambda ve, mTh=mTh: ve.tensor_tensor(
                    out=prefB[:, :], in0=mTh, in1=prefB[:, :], op=ALU.mult)),
                (VEC, dv, lambda ve: ve.tensor_tensor(
                    out=prefA[:, :], in0=prefB[:, :], in1=prefA[:, :], op=ALU.mult)),
                (TEN, pe, lambda te: te.matmul(pidx[0:1, 0:512], oc_f[0:8, :],
                                               prefA[:, 0:512], start=True, stop=True)),
                (TEN, pe, lambda te: te.matmul(pidx[0:1, 512:1024], oc_f[0:8, :],
                                               prefA[:, 512:1024], start=True, stop=True)),
                (VEC, dv, lambda ve: ve.tensor_copy(out=drow[:, :], in_=pidx[0:1, :])),
                (TEN, pe, lambda te: te.matmul(pidx[0:1, 0:512], oc_f[0:8, :],
                                               prefB[:, 0:512], start=True, stop=True)),
                (TEN, pe, lambda te: te.matmul(pidx[0:1, 512:1024], oc_f[0:8, :],
                                               prefB[:, 512:1024], start=True, stop=True)),
                (VEC, dv, lambda ve: ve.tensor_copy(out=vrow[:, :], in_=pidx[0:1, :])),
                (VEC, dv, lambda ve: ve.scalar_tensor_tensor(
                    out=drow[:, :], in0=vrow[:, :], scalar=-2048.0, in1=drow[:, :],
                    op0=ALU.mult, op1=ALU.add)),
                (VEC, dv, lambda ve: ve.tensor_scalar_add(
                    out=drow[:, :], in0=drow[:, :], scalar1=2048.0)),
            )
            for tb in range(8):
                chain(
                    (TEN, pe, lambda te, tb=tb: te.transpose(
                        psT2[0][:, 0:1], drow[0:1, tb * 128:(tb + 1) * 128],
                        idf_s[0:1, 0:1])),
                    (VEC, dv, lambda ve, tb=tb, half=half: ve.tensor_copy(
                        out=destc[:, half * 8 + tb:half * 8 + tb + 1],
                        in_=psT2[0][0:128, 0:1])),
                )
        chain((VEC, dv, lambda ve: ve.tensor_copy(out=desti2[:, :], in_=destc[:, :])))
        dv_idx2 = dv.n
        fsem_zf = fsem.n

        ctxA1.close()

        ctxA2 = ExitStack()
        E2 = ctxA2.enter_context
        Qg = E2(nc.sbuf_tensor("Qg", [128, 4 * T], BF16))
        Kg = E2(nc.sbuf_tensor("Kg", [128, 2 * S], BF16))
        Vg = E2(nc.sbuf_tensor("Vg", [128, 32 * HPG * 65 + 64], BF16))
        sq4 = E2(nc.sbuf_tensor("sq4", [128, T], BF16))
        wqg = E2(nc.sbuf_tensor("wqg", [128, 2048], BF16))
        wkg = E2(nc.sbuf_tensor("wkg", [128, 2048], BF16))
        ring = [E2(nc.sbuf_tensor(f"ring{r}", [128, 1024], BF16)) for r in range(3)]
        sring = E2(nc.sbuf_tensor("sring", [1, 512], F32))
        S16 = E2(nc.sbuf_tensor("S16", [16, T], F32))
        negm8 = E2(nc.sbuf_tensor("negm8", [128, 1], F32))
        t11a = E2(nc.sbuf_tensor("t11a", [1, 1], F32))
        t11b = E2(nc.sbuf_tensor("t11b", [1, 1], F32))
        t11d = E2(nc.sbuf_tensor("t11d", [1, 1], F32))
        t11c = E2(nc.sbuf_tensor("t11c", [1, 1], BF16))
        sc = [E2(nc.psum_tensor(f"sc{j}", [128, 1024], F32)) for j in range(3)]
        oacc = [E2(nc.psum_tensor(f"oacc{j}", [128, 512], F32)) for j in range(2)]

        @GPS
        def _(ge):
            fence(ge)
            xnsc.w(ge)
            fsem.w(ge)
            gp.inc(ge.memset(Vg[:, :], 0.0))
            gp.w(ge)
            gp.inc(ge.memset(
                Vg[:, 0:32 * HPG * 65].rearrange("p (c s) -> p c s", s=65)[:, :, 64:65], 1.0))
            gp.inc(ge.memset(Qg[:, :], 0.0))
            qG.dma(ge.dma_start(out=wqg[:, :], in_=wq_h[0]))
            qG.dma(ge.dma_start(out=wkg[:, :], in_=wk_h[0]))

        def xnT_at(tok):
            return (xnT_own, tok) if tok < T else (xnT_par, tok - T)

        def _precast_pair(k):
            g, kk = divmod(k, 36)
            if kk < 32:
                e, iq4 = divmod(g * 32 + kk, 16)
                return w1_h[e, iq4], w1bf[e, iq4]
            w = g * 4 + (kk - 32)
            d2c, ch = divmod(w, 2)
            return (wo2_h[d2c][:, ch * 2048:(ch + 1) * 2048],
                    wo2bf[d2c][:, ch * 2048:(ch + 1) * 2048])

        def w1_precast_batch(g):
            @GPS
            def _(ge, g=g):
                for kk in range(36):
                    k = g * 36 + kk
                    src, dst = _precast_pair(k)
                    if k >= 2:
                        w1c[k % 2].w(ge, 16 * (k // 2))
                    w1g[k % 2].dma(ge.dma_start(out=w1b[k % 2][:, :], in_=src))

            @SYN
            def _(sy, g=g):
                for kk in range(36):
                    k = g * 36 + kk
                    src, dst = _precast_pair(k)
                    sy.wait_ge(w1g[k % 2].h, 16 * (k // 2 + 1))
                    w1c[k % 2].dma(sy.dma_start(out=dst, in_=w1b[k % 2][:, :]))

        for g in range(NG):
            if g > 0:
                @GPS
                def _(ge, g=g):
                    fence(ge)
                    qG.dma(ge.dma_start(out=wqg[:, :], in_=wq_h[g]))
                    qG.dma(ge.dma_start(out=wkg[:, :], in_=wk_h[g]))

            peP = {}; dvP = {}
            for t in range(8):
                d2cl, qs = divmod(t, 4)

                @TEN
                def _(te, t=t, d2cl=d2cl, qs=qs):
                    qG.w(te)
                    gp.w(te)
                    if t == 0:
                        ac.w(te)
                        dv.w(te)
                    if t >= 2:
                        dv.w(te, dvP[t - 2])
                    for dc in range(8):
                        ins = te.matmul(
                            sc[t % 2][:, 0:512],
                            wqg[:, dc * 256 + d2cl * 128:dc * 256 + (d2cl + 1) * 128],
                            xnT_own[:, dc * T + qs * 512:dc * T + (qs + 1) * 512],
                            start=(dc == 0), stop=(dc == 7))
                    pe.inc(ins)
                    peP[t] = pe.n

                @VEC
                def _(ve, t=t, d2cl=d2cl, qs=qs, g=g):
                    pe.w(ve, peP[t])
                    gp.w(ve)
                    col = 2 * g + d2cl
                    for hh in range(2):
                        hl = 2 * d2cl + hh
                        dv.inc(ve.tensor_scalar_add(
                            out=Qg[hh * 64:hh * 64 + 64,
                                   hl * T + qs * 512:hl * T + (qs + 1) * 512],
                            in0=sc[t % 2][hh * 64:hh * 64 + 64, 0:512],
                            scalar1=bq_s[hh * 64:hh * 64 + 64, col:col + 1]))
                    dvP[t] = dv.n

            peK = {}; dvK = {}
            for t in range(16):
                d2cl, ks = divmod(t, 8)

                @TEN
                def _(te, t=t, d2cl=d2cl, ks=ks):
                    if t < 2:
                        dv.w(te, dvP[6 + t])

                    else:
                        dv.w(te, dvK[t - 2])
                    srcx, base = xnT_at(ks * 512)
                    for dc in range(8):
                        ins = te.matmul(
                            sc[t % 2][:, 0:512],
                            wkg[:, dc * 256 + d2cl * 128:dc * 256 + (d2cl + 1) * 128],
                            srcx[:, dc * T + base:dc * T + base + 512],
                            start=(dc == 0), stop=(dc == 7))
                    pe.inc(ins)
                    peK[t] = pe.n

                @VEC
                def _(ve, t=t, d2cl=d2cl, ks=ks, g=g):
                    pe.w(ve, peK[t])
                    dv.inc(ve.tensor_scalar_add(
                        out=Kg[:, d2cl * S + ks * 512:d2cl * S + (ks + 1) * 512],
                        in0=sc[t % 2][:, 0:512],
                        scalar1=bk_s[:, 2 * g + d2cl:2 * g + d2cl + 1]))
                    dvK[t] = dv.n

            @GPS
            def _(ge, g=g):
                pe.w(ge, peP[7])
                qG.dma(ge.dma_start(out=wqg[:, :], in_=wv_h[g]))

            w1_precast_batch(g)

            if g == 3:
                @GPS
                def _(ge):
                    ge.wait_ge(dv.h, dv_idx2)
                    ge.wait_ge(fsem.h, fsem_zf)
                    for k in range(16):
                        h2, tb = divmod(k, 8)
                        if k >= 2:
                            ge.wait_ge(isem.h, 32 * (k // 2))
                        gscat.dma(ge.dma_start(out=scstg[k % 2][:, :],
                                               in_=xn_scr[h2 * 1024 + tb * 128:
                                                          h2 * 1024 + (tb + 1) * 128, :]))
                        ge.wait_ge(gscat.h, 16 * (k + 1))
                        isem.dma(ge.indirect_dma_start(
                            out=xe_scr[h2][:, :],
                            out_offset=bass.IndirectOffsetOnAxis(
                                desti2[:, h2 * 8 + tb:h2 * 8 + tb + 1], 0),
                            in_=scstg[k % 2][:, :], in_offset=None))

            peV = {}; dvV = {}
            for kc in range(32):
                @TEN
                def _(te, kc=kc, g=g):
                    qG.w(te)
                    if kc < 2:
                        dv.w(te, dvK[14 + kc])
                    else:
                        dv.w(te, dvV[kc - 2])
                    src, base = xnT_at(kc * 128)
                    for dc in range(8):
                        te.matmul(sc[kc % 2][:, 0:256],
                                  src[:, dc * T + base:dc * T + base + 128],
                                  wqg[:, dc * 256:(dc + 1) * 256],
                                  start=(dc == 0), stop=False)
                    ins = te.matmul(sc[kc % 2][:, 0:256], or_b[:, :],
                                    bv_b[:, g * 256:(g + 1) * 256],
                                    start=False, stop=True)
                    pe.inc(ins)
                    peV[kc] = pe.n

                @VEC
                def _(ve, kc=kc):
                    pe.w(ve, peV[kc])
                    dv.inc(ve.tensor_copy(
                        out=Vg[:, kc * HPG * 65:(kc + 1) * HPG * 65]
                            .rearrange("p (h s) -> p h s", s=65)[:, :, 0:64],
                        in_=sc[kc % 2][:, 0:256].rearrange("p (h s) -> p h s", s=64)))
                    dvV[kc] = dv.n

            def nps(seg):
                j, hh = divmod(seg, 2)
                return sc[j][0:1, hh * 512:(hh + 1) * 512]

            steps = []
            for hl in range(4):
                steps.append((VEC, dv, lambda ve, hl=hl: ve.tensor_mul(
                    out=sq4[:, :], in0=Qg[:, hl * T:(hl + 1) * T],
                    in1=Qg[:, hl * T:(hl + 1) * T])))
                for seg in range(4):
                    steps.append((TEN, pe, lambda te, seg=seg: te.matmul(
                        nps(seg), oc_b[:, :], sq4[:, seg * 512:(seg + 1) * 512],
                        start=True, stop=True)))
                for seg in range(4):
                    steps.append((VEC, dv, lambda ve, seg=seg: ve.reduce_max(out=t11d[:, :], in_=nps(seg), axis=AX)))
                    if hl == 0 and seg == 0:
                        steps.append((VEC, dv, lambda ve: ve.tensor_copy(out=t11a[:, :], in_=t11d[:, :])))
                    else:
                        steps.append((VEC, dv, lambda ve: ve.tensor_max(out=t11a[:, :], in0=t11a[:, :], in1=t11d[:, :])))
            first = [True]
            for d2cl in range(2):
                for piece in range(2):
                    steps.append((VEC, dv, lambda ve, d2cl=d2cl, piece=piece: ve.tensor_mul(
                        out=sq4[:, :], in0=Kg[:, d2cl * S + piece * T:d2cl * S + (piece + 1) * T],
                        in1=Kg[:, d2cl * S + piece * T:d2cl * S + (piece + 1) * T])))
                    for seg in range(4):
                        steps.append((TEN, pe, lambda te, seg=seg: te.matmul(
                            nps(seg), oc_b[:, :], sq4[:, seg * 512:(seg + 1) * 512],
                            start=True, stop=True)))
                    for seg in range(4):
                        steps.append((VEC, dv, lambda ve, seg=seg: ve.reduce_max(
                            out=t11d[:, :], in_=nps(seg), axis=AX)))
                        if first[0]:
                            steps.append((VEC, dv, lambda ve: ve.tensor_copy(out=t11b[:, :], in_=t11d[:, :])))
                            first[0] = False
                        else:
                            steps.append((VEC, dv, lambda ve: ve.tensor_max(out=t11b[:, :], in0=t11b[:, :], in1=t11d[:, :])))
            steps.append((VEC, dv, lambda ve: ve.tensor_mul(out=t11a[:, :], in0=t11a[:, :], in1=t11b[:, :])))
            steps.append((SCA, ac, lambda se: se.activation(out=t11a[:, :], in_=t11a[:, :], func=AF.Ln)))
            steps.append((SCA, ac, lambda se: se.activation(out=t11a[:, :], in_=t11a[:, :], func=AF.Exp, scale=0.5)))
            steps.append((VEC, dv, lambda ve: ve.tensor_scalar_mul(out=t11c[:, :], in0=t11a[:, :], scalar1=-0.125)))
            steps.append((TEN, pe, lambda te: te.matmul(sc[0][:, 0:1], or_b[:, :], t11c[:, :], start=True, stop=True)))
            steps.append((VEC, dv, lambda ve: ve.tensor_copy(out=negm8[:, :], in_=sc[0][:, 0:1])))
            chain(*steps)

            base_ac, base_sc, base_av = ac.n, pesc.n, peav.n
            base_dv, base_ss = dv.n, ssem.n

            @TEN
            def _(te, g=g, base_ac=base_ac, base_dv=base_dv):
                dv.w(te)

                def spair(p):
                    if p >= 3:
                        te.wait_ge(ac.h, base_ac + p - 2)
                    for u in range(2):
                        m = 2 * p + u
                        hl = m // 128
                        qt = (m // 32) % 4
                        c = m % 32
                        ins = te.matmul(
                            sc[p % 3][:, u * 512:(u + 1) * 512],
                            Kg[:, (hl // 2) * S + c * 128:(hl // 2) * S + (c + 1) * 128],
                            Qg[:, hl * T + qt * 512:hl * T + (qt + 1) * 512],
                            start=True, stop=True)
                    pesc.inc(ins)

                for p in range(3):
                    spair(p)
                for n in range(512):
                    i, c = divmod(n, 32)
                    hl = n // 128
                    if n % 2 == 0 and n // 2 + 3 < 256:
                        spair(n // 2 + 3)
                    if c == 0 and i >= 2:
                        te.wait_ge(dv.h, base_dv + 2 * (i - 1))
                    te.wait_ge(ac.h, base_ac + n // 2 + 1)
                    ins = te.matmul(
                        oacc[i % 2][:, :],
                        Vg[:, (c * HPG + hl) * 65:(c * HPG + hl) * 65 + 128],
                        ring[(n // 2) % 3][:, (n % 2) * 512:(n % 2 + 1) * 512],
                        start=(c == 0), stop=(c == 31))
                    peav.inc(ins)

            @SCA
            def _(se, base_sc=base_sc, base_av=base_av, dvpre=dv.n):
                se.wait_ge(dv.h, dvpre)
                for p in range(256):
                    se.wait_ge(pesc.h, base_sc + p + 1)
                    if p >= 3:
                        se.wait_ge(peav.h, base_av + 2 * p - 4)
                    ac.inc(se.activation(out=ring[p % 3][:, :], in_=sc[p % 3][:, :],
                                         func=AF.Exp, bias=negm8[:, :], scale=0.125))

            @VEC
            def _(ve, g=g, base_av=base_av, base_ss=base_ss):
                for i in range(16):
                    hl = i // 4
                    qt = i % 4
                    h = HPG * g + hl
                    ve.wait_ge(peav.h, base_av + 32 * (i + 1))
                    dv.inc(ve.tensor_copy(
                        out=O_u[(h % 2) * 64:(h % 2) * 64 + 64,
                                (h // 2) * T + qt * 512:(h // 2) * T + (qt + 1) * 512],
                        in_=oacc[i % 2][0:64, :]))
                    if base_ss + 16 * i > 0:
                        ve.wait_ge(ssem.h, base_ss + 16 * i)
                    dv.inc(ve.tensor_copy(out=sring[:, :], in_=oacc[i % 2][64:65, :]))

            @SYN
            def _(sy, g=g, base_dv=base_dv):
                for i in range(16):
                    hl = i // 4
                    qt = i % 4
                    h = HPG * g + hl
                    sy.wait_ge(dv.h, base_dv + 2 * (i + 1))
                    ssem.dma(sy.dma_start(out=S16[h:h + 1, qt * 512:(qt + 1) * 512],
                                          in_=sring[:, :]))

        ctxA2.close()

        ctxB3 = ExitStack()
        E3 = ctxB3.enter_context
        wo_s = E3(nc.sbuf_tensor("wo_s", [128, 8 * 1024], BF16))
        sclb = E3(nc.sbuf_tensor("sclb", [128, 2048], BF16))
        r_att = [E3(nc.sbuf_tensor(f"r_att{j}", [128, D], F32)) for j in range(2)]
        ln2sq = [E3(nc.sbuf_tensor(f"ln2sq{j}", [128, D], F32)) for j in range(2)]
        l2a = [E3(nc.sbuf_tensor(f"l2a{j}", [128, 1], F32)) for j in range(2)]
        l2b = [E3(nc.sbuf_tensor(f"l2b{j}", [128, 1], F32)) for j in range(2)]
        l2c = [E3(nc.sbuf_tensor(f"l2c{j}", [128, 1], F32)) for j in range(2)]
        attnb = [E3(nc.sbuf_tensor(f"attnb3_{j}", [128, D], BF16)) for j in range(2)]
        T16 = E3(nc.sbuf_tensor("T16", [16, T], F32))
        R16 = E3(nc.sbuf_tensor("R16", [16, T], BF16))
        psb = [E3(nc.psum_tensor(f"psb{j}", [128, 1024], F32)) for j in range(2)]

        @GPS
        def _(ge):
            fence(ge)
            ssem.w(ge)
            qG.dma(ge.dma_start(out=wo_s[:, :], in_=wo_h[:, :]))

        @SCA
        def _(se):
            ssem.w(se)
            fence(se)
            ac.inc(se.activation(out=T16[:, :], in_=S16[:, :], func=AF.Ln))
            ac.w(se)
            ac.inc(se.activation(out=R16[:, :], in_=T16[:, :], func=AF.Exp, scale=-1.0))

        peB = {}; dvB = {}
        for dc in range(8):
            for piece in range(2):
                t = dc * 2 + piece

                @TEN
                def _(te, dc=dc, piece=piece, t=t):
                    ac.w(te)
                    if t >= 2:
                        dv.w(te, dvB[t - 2])
                    for hb in range(2):
                        ins = te.matmul(
                            psb[t % 2][:, hb * 512:(hb + 1) * 512],
                            pi_s[:, dc * 128:(dc + 1) * 128],
                            R16[:, piece * 1024 + hb * 512:piece * 1024 + (hb + 1) * 512],
                            start=True, stop=True)
                    pe.inc(ins)
                    peB[t] = pe.n

                @VEC
                def _(ve, dc=dc, piece=piece, t=t):
                    pe.w(ve, peB[t])
                    dv.inc(ve.tensor_copy(out=sclb[:, piece * 1024:(piece + 1) * 1024],
                                          in_=psb[t % 2][:, :]))
                    dv.w(ve)
                    dv.inc(ve.tensor_mul(
                        out=O_u[:, dc * T + piece * 1024:dc * T + (piece + 1) * 1024],
                        in0=O_u[:, dc * T + piece * 1024:dc * T + (piece + 1) * 1024],
                        in1=sclb[:, piece * 1024:(piece + 1) * 1024]))
                    dvB[t] = dv.n

        peY = {}; acR = {}; dvL2 = {}; acL2 = {}
        for tt in range(16):
            @TEN
            def _(te, tt=tt):
                qG.w(te)
                if tt < 2:
                    dv.w(te, dvB[15])
                else:
                    ac.w(te, acR[tt - 2])
                for dc in range(8):
                    for hb in range(2):
                        te.matmul(psb[tt % 2][:, hb * 512:(hb + 1) * 512],
                                  O_u[:, dc * T + tt * 128:dc * T + (tt + 1) * 128],
                                  wo_s[:, dc * 1024 + hb * 512:dc * 1024 + (hb + 1) * 512],
                                  start=(dc == 0), stop=False)
                for hb in range(2):
                    ins = te.matmul(psb[tt % 2][:, hb * 512:(hb + 1) * 512],
                                    or_b[:, :], bo_b[:, hb * 512:(hb + 1) * 512],
                                    start=False, stop=True)
                pe.inc(ins)
                peY[tt] = pe.n

            @SCA
            def _(se, tt=tt):
                pe.w(se, peY[tt])
                if tt >= 2:
                    dv.w(se, dvL2[tt - 2])
                ac.inc(se.activation(out=r_att[tt % 2][:, :], in_=psb[tt % 2][:, :],
                                     func=AF.Relu))
                acR[tt] = ac.n

            jj = tt % 2
            ext0 = [(ac, acR[tt])]
            if tt >= 2:
                ext0 = ext0 + [(dv, dvL2[tt - 2]), (ac, acL2.get(tt - 2, 0))]
            ext9 = [(ascr, 16 * (tt - 1))] if tt >= 2 else []
            cts = pchain([
                (VEC, dv, (lambda ve, jj=jj, tt=tt: ve.reduce_sum(
                    out=l2a[jj][:, :], in_=r_att[jj][:, :], axis=AX)), [], ext0),
                (VEC, dv, (lambda ve, jj=jj: ve.tensor_scalar_mul(
                    out=l2a[jj][:, :], in0=l2a[jj][:, :], scalar1=1.0 / D)), [0], []),
                (VEC, dv, (lambda ve, jj=jj: ve.tensor_mul(
                    out=l2c[jj][:, :], in0=l2a[jj][:, :], in1=l2a[jj][:, :])), [1], []),
                (SCA, ac, (lambda se, jj=jj, tt=tt: se.activation(
                    out=ln2sq[jj][:, :], in_=r_att[jj][:, :], func=AF.Square)), [], ext0),
                (VEC, dv, (lambda ve, jj=jj: ve.reduce_sum(
                    out=l2b[jj][:, :], in_=ln2sq[jj][:, :], axis=AX)), [3], []),
                (VEC, dv, (lambda ve, jj=jj: ve.tensor_scalar_mul(
                    out=l2b[jj][:, :], in0=l2b[jj][:, :], scalar1=1.0 / D)), [4], []),
                (VEC, dv, (lambda ve, jj=jj: ve.tensor_sub(
                    out=l2b[jj][:, :], in0=l2b[jj][:, :], in1=l2c[jj][:, :])), [5, 2], []),
                (SCA, ac, (lambda se, jj=jj: se.activation(
                    out=l2b[jj][:, :], in_=l2b[jj][:, :], func=AF.Ln, bias=epsA[:, :])), [6], []),
                (SCA, ac, (lambda se, jj=jj: se.activation(
                    out=l2b[jj][:, :], in_=l2b[jj][:, :], func=AF.Exp, scale=-0.5)), [7], []),
                (VEC, dv, (lambda ve, jj=jj: ve.tensor_scalar(
                    out=attnb[jj][:, :], in0=r_att[jj][:, :],
                    scalar1=l2a[jj][:, :], scalar2=l2b[jj][:, :],
                    op0=ALU.subtract, op1=ALU.mult)), [8, 1], ext9),
            ])
            acL2[tt] = cts[8][1]
            dvL2[tt] = dv.n

            @SYN
            def _(sy, tt=tt):
                dv.w(sy, dvL2[tt])
                ascr.w(sy, 16 * tt)
                ascr.dma(sy.dma_start(out=attn_scr[tt * 128:(tt + 1) * 128, :],
                                      in_=attnb[tt % 2][:, :]))

        ctxB3.close()
        ctxA.close()

        ctxBB = ctx.enter_context(ExitStack())
        E4 = ctxBB.enter_context
        gaf = E4(nc.sbuf_tensor("gaf", [1, D], F32))
        bef = E4(nc.sbuf_tensor("bef", [1, D], F32))
        b1_s = E4(nc.sbuf_tensor("b1_s", [128, 256], F32))
        bo2_s = E4(nc.sbuf_tensor("bo2_s", [128, 8], F32))
        xe_sb = E4(nc.sbuf_tensor("xe_sb", [128, 16 * 1024], BF16))
        g_e = [E4(nc.sbuf_tensor(f"g_e{j}", [128, 8 * 256], BF16)) for j in range(2)]
        ytT = [E4(nc.sbuf_tensor(f"ytT{j}", [128, 256], F32)) for j in range(2)]
        fgt = [E4(nc.sbuf_tensor(f"fgt{j}", [128, D], BF16)) for j in range(2)]
        Gab = E4(nc.sbuf_tensor("Gab", [128, D], BF16))
        Beb = E4(nc.sbuf_tensor("Beb", [128, D], BF16))
        z3 = [E4(nc.sbuf_tensor(f"z3_{j}", [128, D], F32)) for j in range(2)]
        ln3sq = z3
        l3a = [E4(nc.sbuf_tensor(f"l3a{j}", [128, 1], F32)) for j in range(2)]
        l3b = [E4(nc.sbuf_tensor(f"l3b{j}", [128, 1], F32)) for j in range(2)]
        l3c = [E4(nc.sbuf_tensor(f"l3c{j}", [128, 1], F32)) for j in range(2)]
        xtile = [E4(nc.sbuf_tensor(f"xt4_{j}", [128, D], F32)) for j in range(2)]
        outb = [E4(nc.sbuf_tensor("outb0", [128, D], F32))] * 2
        attnb = [E4(nc.sbuf_tensor(f"attnb4_{j}", [128, D], BF16)) for j in range(2)]
        w1b4 = w1b + [E4(nc.sbuf_tensor(f"w1b4_{j}", [128, 8 * 256], BF16))
                      for j in range(2)]
        psz = [E4(nc.psum_tensor(f"psz{j}", [128, 1024], F32)) for j in range(2)]
        ptr = [E4(nc.psum_tensor(f"ptr{j}", [128, 128], F32)) for j in range(2)]
        ptrb = [E4(nc.psum_tensor(f"ptrb{j}", [128, 128], BF16)) for j in range(2)]

    

        @VEC
        def _(ve):
            ascr.w(ve)

        @SCA
        def _(se):
            ascr.w(se)

        @SYN
        def _(sy):
            ascr.w(sy)
            fence(sy)
            for dst, hh in [(gaf, ga_h), (bef, be_h), (b1_s, b1_h), (bo2_s, bo2_h)]:
                qS.dma(sy.dma_start(out=dst[:, :], in_=hh[:, :]))

        @TEN
        def _(te):
            qS.w(te)

        chain(
            (TEN, pe, lambda te: te.matmul(psz[0][:, 0:512], or_f[:, :], gaf[:, 0:512], start=True, stop=True)),
            (TEN, pe, lambda te: te.matmul(psz[0][:, 512:1024], or_f[:, :], gaf[:, 512:1024], start=True, stop=True)),
            (VEC, dv, lambda ve: ve.tensor_copy(out=Gab[:, :], in_=psz[0][:, :])),
            (TEN, pe, lambda te: te.matmul(psz[1][:, 0:512], or_f[:, :], bef[:, 0:512], start=True, stop=True)),
            (TEN, pe, lambda te: te.matmul(psz[1][:, 512:1024], or_f[:, :], bef[:, 512:1024], start=True, stop=True)),
            (VEC, dv, lambda ve: ve.tensor_copy(out=Beb[:, :], in_=psz[1][:, :])),
        )

        xnsc_total = xnsc.n
        isem_scat = isem.n
        for half in range(2):
            ctxH1 = ExitStack()
            xeT = ctxH1.enter_context(nc.sbuf_tensor(f"xeT{half}", [128, 8 * 2048], BF16))

            @SYN
            def _(sy, half=half):
                isem.w(sy, isem_scat)
                fsem.w(sy)
                fsem.dma(sy.dma_start(
                    out=xe_sb[:, :].rearrange("p (c m) -> p c m", c=16),
                    in_=xe_scr[half][0:2048, :].rearrange("(c p) m -> p c m", p=128)))

            peXT = {}
            for sb in range(16):
                for dcb in range(8):
                    b = sb * 8 + dcb

                    @TEN
                    def _(te, sb=sb, dcb=dcb, b=b):
                        if b == 0:
                            fsem.w(te)
                        if b >= 2:
                            dv.w(te, peXT[b - 2])
                        pe.inc(te.transpose(ptrb[b % 2][:, :],
                                            xe_sb[:, sb * 1024 + dcb * 128:sb * 1024 + (dcb + 1) * 128],
                                            idb_s[:, :]))

                    @VEC
                    def _(ve, sb=sb, dcb=dcb, b=b):
                        pe.w(ve)
                        dv.inc(ve.tensor_copy(
                            out=xeT[:, dcb * 2048 + sb * 128:dcb * 2048 + (sb + 1) * 128],
                            in_=ptrb[b % 2][:, :]))
                        peXT[b] = dv.n
            dvXET = dv.n

            base_pez = pe.n
            base_acA = ac.n
            pe_after_z = {}
            cnt = base_pez
            for z in range(256):
                cnt += 1
                pe_after_z[z] = cnt
            ac_after_z = {z: base_acA + z + 1 for z in range(256)}

            @SYN
            def _(sy, half=half, base_pez=base_pez, pe_after_z=pe_after_z):
                if half == 0:
                    for jw in range(4):
                        w1c[jw].w(sy)
                for k in range(4 if half == 1 else 0, 128):
                    e, iq4 = divmod(k, 16)
                    if k >= 4:
                        sy.wait_ge(pe.h, pe_after_z[2 * k - 7])
                    w1r[k % 4].dma(sy.dma_start(out=w1b4[k % 4][:, :], in_=w1bf[e, iq4]))
                if half == 0:
                    for k in range(4):
                        e, iq4 = divmod(k, 16)
                        sy.wait_ge(pe.h, pe_after_z[2 * (124 + k) + 1])
                        w1r[k % 4].dma(sy.dma_start(out=w1b4[k % 4][:, :],
                                                    in_=w1bf[e, iq4]))

            @TEN
            def _(te, half=half, ac_after_z=ac_after_z, dvXET=dvXET):
                te.wait_ge(dv.h, dvXET)
                for e in range(NE):
                    for ic in range(32):
                        z = e * 32 + ic
                        kg = half * 128 + e * 16 + ic // 2
                        te.wait_ge(w1r[kg % 4].h, 16 * (kg // 4 + 1))
                        if z >= 4:
                            te.wait_ge(ac.h, ac_after_z[z - 4])
                        for dc in range(8):
                            ins = te.matmul(
                                psz[z % 2][:, (z // 2 % 2) * 512:(z // 2 % 2) * 512 + 256],
                                w1b4[kg % 4][:, dc * 256 + (ic % 2) * 128:
                                             dc * 256 + (ic % 2 + 1) * 128],
                                xeT[:, dc * 2048 + e * 256:dc * 2048 + (e + 1) * 256],
                                start=(dc == 0), stop=(dc == 7))
                        pe.inc(ins)

            @SCA
            def _(se, half=half, pe_after_z=pe_after_z):
                for e in range(NE):
                    for ic in range(32):
                        z = e * 32 + ic
                        se.wait_ge(pe.h, pe_after_z[z])
                        if ic % 8 == 0 and z >= 16:
                            bz = half * 32 + z // 8
                            gsem[bz % 2].w(se, 16 * ((bz - 2) // 2 + 1))
                        ac.inc(se.activation(out=g_e[(z // 8) % 2][:, (ic % 8) * 256:(ic % 8 + 1) * 256],
                                             in_=psz[z % 2][:, (z // 2 % 2) * 512:(z // 2 % 2) * 512 + 256],
                                             func=AF.Gelu,
                                             bias=b1_s[:, ic * 8 + e:ic * 8 + e + 1]))

            @GPS
            def _(ge, half=half, ac_after_z=ac_after_z):
                for e in range(NE):
                    for icb in range(4):
                        ge.wait_ge(ac.h, ac_after_z[e * 32 + icb * 8 + 7])
                        gsem[(e * 4 + icb) % 2].dma(ge.dma_start(
                            out=hg_scr[icb * 1024:(icb + 1) * 1024,
                                       e * 256:(e + 1) * 256]
                                .rearrange("(i p) s -> p i s", p=128),
                            in_=g_e[(e * 4 + icb) % 2][:, :]
                                .rearrange("p (i s) -> p i s", i=8)))

            ctxH1.close()

            ctxH2 = ExitStack()
            hgq = [ctxH2.enter_context(nc.sbuf_tensor(f"hgq{half}_{j}", [128, 32 * 256], BF16))
                   for j in range(2)]
            wo2d = [ctxH2.enter_context(nc.sbuf_tensor(f"wo2d{half}_{j}", [128, 4096], BF16))
                    for j in range(2)]
            peB1 = {}; dvS = {}; dvYe = {}
            gsem_all = (gsem[0].n, gsem[1].n)
            for sck in range(8):
                @SYN
                def _(sy, half=half, sck=sck):
                    sy.wait_ge(gsem[0].h, gsem_all[0])
                    sy.wait_ge(gsem[1].h, gsem_all[1])
                    if sck >= 2:
                        pe.w(sy, peB1[(sck - 2) * 8 + 7])
                    for ih in range(2):
                        hsem[sck % 2].dma(sy.dma_start(
                            out=hgq[sck % 2][:, ih * 4096:(ih + 1) * 4096]
                                .rearrange("p (i s) -> p i s", i=16),
                            in_=hg_scr[ih * 2048:(ih + 1) * 2048,
                                       sck * 256:(sck + 1) * 256]
                                .rearrange("(i p) s -> p i s", p=128)))

                for d2c in range(8):
                    t = sck * 8 + d2c

                    @GPS
                    def _(ge, d2c=d2c, t=t):
                        if t >= 2:
                            pe.w(ge, peB1.get(t - 2, 0))
                        elif t == 0:
                            wcst.w(ge)
                            pe.w(ge)
                        wo2s[t % 2].dma(ge.dma_start(out=wo2d[t % 2][:, :], in_=wo2bf[d2c]))

                    @TEN
                    def _(te, sck=sck, d2c=d2c, t=t, half=half):
                        gsck = half * 8 + sck
                        te.wait_ge(hsem[sck % 2].h, 32 * (gsck // 2 + 1))
                        te.wait_ge(wo2s[t % 2].h, 16 * ((half * 64 + t) // 2 + 1))
                        if t >= 4:
                            dv.w(te, dvS[t - 4])
                        for ic in range(32):
                            ins = te.matmul(
                                psz[t % 2][:, (t // 2 % 2) * 512:(t // 2 % 2) * 512 + 256],
                                wo2d[t % 2][:, ic * 128:(ic + 1) * 128],
                                hgq[sck % 2][:, ic * 256:(ic + 1) * 256],
                                start=(ic == 0), stop=(ic == 31))
                        pe.inc(ins)
                        peB1[t] = pe.n

                    @VEC
                    def _(ve, half=half, sck=sck, d2c=d2c, t=t):
                        pe.w(ve, peB1[t])
                        dv.inc(ve.tensor_scalar_add(
                            out=ytT[t % 2][:, :],
                            in0=psz[t % 2][:, (t // 2 % 2) * 512:(t // 2 % 2) * 512 + 256],
                            scalar1=bo2_s[:, d2c:d2c + 1]))
                        dvS[t] = dv.n

                    for blk in range(2):
                        bg = t * 2 + blk

                        @TEN
                        def _(te, t=t, blk=blk, bg=bg):
                            dv.w(te, dvS[t])
                            if bg >= 2:
                                dv.w(te, dvYe[bg - 2])
                            pe.inc(te.transpose(ptr[bg % 2][:, :],
                                                ytT[t % 2][:, blk * 128:(blk + 1) * 128],
                                                idf_s[:, :]))

                        @VEC
                        def _(ve, sck=sck, d2c=d2c, blk=blk, bg=bg):
                            pe.w(ve)
                            sb = sck * 2 + blk
                            sl = slice(sb * 1024 + d2c * 128, sb * 1024 + (d2c + 1) * 128)
                            dv.inc(ve.tensor_tensor(
                                out=xe_sb[:, sl], in0=ptr[bg % 2][:, :],
                                in1=xe_sb[:, sl], op=ALU.add))
                            dvYe[bg] = dv.n

            ctxH2.close()

            @SYN
            def _(sy):
                dv.w(sy)
                fsem.dma(sy.dma_start(
                    out=fg_scr[:, :].rearrange("(c p) m -> p c m", p=128),
                    in_=xe_sb[:, :].rearrange("p (c m) -> p c m", c=16)))

            if half == 0:
                ln3_state = {"dvTS": {}, "dvCmb": {}, "stC": {}, "isG": {}}
            dvTS = ln3_state["dvTS"]; dvCmb = ln3_state["dvCmb"]
            stC = ln3_state["stC"]; isG = ln3_state["isG"]
            for tb in range(8):
                tt = half * 8 + tb
                gb = half * 8 + tb
                j = tb % 2

                @GPS
                def _(ge, tb=tb, j=j, half=half, gb=gb):
                    fsem.w(ge)
                    isem.w(ge)
                    if gb >= 2:
                        dv.w(ge, dvTS[gb - 2])
                    isem.dma(ge.indirect_dma_start(
                        out=fgt[j][:, :], out_offset=None,
                        in_=fg_scr[:, :],
                        in_offset=bass.IndirectOffsetOnAxis(
                            desti2[:, half * 8 + tb:half * 8 + tb + 1], 0)))
                    isG[gb] = isem.n

                @SYN
                def _(sy, tt=tt, j=j, gb=gb):
                    if gb >= 1:
                        st.w(sy, 16 * gb)
                    if gb >= 2:
                        dv.w(sy, dvCmb[gb - 2])
                    xs[j].dma(sy.dma_start(out=xtile[j][:, :],
                                           in_=x_h[tt * 128:(tt + 1) * 128, :]))
                    asl[j].dma(sy.dma_start(out=attnb[j][:, :],
                                            in_=attn_scr[tt * 128:(tt + 1) * 128, :]))

                extg = [(isem, isG[gb])]
                if gb >= 2:
                    extg = extg + [(dv, dvCmb[gb - 2])]
                cts = pchain([
                    (VEC, dv, (lambda ve, j=j: ve.reduce_sum(
                        out=l3a[j][:, :], in_=fgt[j][:, :], axis=AX)), [], extg),
                    (VEC, dv, (lambda ve, j=j: ve.tensor_scalar_mul(
                        out=l3a[j][:, :], in0=l3a[j][:, :], scalar1=1.0 / D)), [0], []),
                    (VEC, dv, (lambda ve, j=j: ve.tensor_mul(
                        out=l3c[j][:, :], in0=l3a[j][:, :], in1=l3a[j][:, :])), [1], []),
                    (SCA, ac, (lambda se, j=j: se.activation(
                        out=ln3sq[j][:, :], in_=fgt[j][:, :], func=AF.Square,
                        accum_out=l3b[j][:, :])), [], extg),
                    (VEC, dv, (lambda ve, j=j: ve.tensor_scalar(
                        out=l3b[j][:, :], in0=l3b[j][:, :], scalar1=1.0 / D,
                        scalar2=l3c[j][:, :],
                        op0=ALU.mult, op1=ALU.subtract)), [3, 2], []),
                    (SCA, ac, (lambda se, j=j: se.activation(
                        out=l3b[j][:, :], in_=l3b[j][:, :], func=AF.Ln,
                        bias=epsC[:, :])), [4], []),
                    (SCA, ac, (lambda se, j=j: se.activation(
                        out=l3b[j][:, :], in_=l3b[j][:, :], func=AF.Exp,
                        scale=-0.5)), [5], []),
                    (VEC, dv, (lambda ve, j=j: ve.tensor_scalar(
                        out=z3[j][:, :], in0=fgt[j][:, :], scalar1=l3a[j][:, :],
                        scalar2=l3b[j][:, :],
                        op0=ALU.subtract, op1=ALU.mult)), [6, 1], []),
                    (VEC, dv, (lambda ve, j=j: ve.tensor_mul(
                        out=z3[j][:, :], in0=z3[j][:, :], in1=Gab[:, :])), [7], []),
                    (VEC, dv, (lambda ve, j=j: ve.tensor_add(
                        out=z3[j][:, :], in0=z3[j][:, :], in1=Beb[:, :])), [8], []),
                ])
                dvTS[gb] = cts[9][1]

                @VEC
                def _(ve, tt=tt, j=j, gb=gb):
                    xs[j].w(ve)
                    asl[j].w(ve)
                    dv.w(ve, dvTS[gb])
                    if gb >= 1:
                        st.w(ve, 16 * gb)
                    dv.inc(ve.tensor_add(out=outb[j][:, :], in0=z3[j][:, :], in1=xtile[j][:, :]))
                    dv.w(ve)
                    dv.inc(ve.tensor_add(out=outb[j][:, :], in0=outb[j][:, :], in1=attnb[j][:, :]))
                    dvCmb[gb] = dv.n

                @SYN
                def _(sy, tt=tt, j=j, gb=gb):
                    dv.w(sy, dvCmb[gb])
                    st.dma(sy.dma_start(out=y_h[tt * 128:(tt + 1) * 128, :],
                                        in_=outb[j][:, :]))

        @SYN
        def _(sy):
            st.w(sy)

    return nc


_NC_CACHE = {}


def _get_nc():
    if "nc" not in _NC_CACHE:
        _NC_CACHE["nc"] = build()
    return _NC_CACHE["nc"]


def kernel(L0_attn_x, L0_attn_Wq, L0_attn_bq, L0_attn_Wk, L0_attn_bk,
           L0_attn_Wv, L0_attn_bv, L0_attn_Wo, L0_attn_bo,
           L0_ffn_Wr, L0_ffn_W1, L0_ffn_b1, L0_ffn_Wo, L0_ffn_bo,
           L0_ffn_gamma, L0_ffn_beta):
    f = np.asarray
    x = np.ascontiguousarray(f(L0_attn_x, dtype=np.float32))
    Wq = f(L0_attn_Wq, dtype=np.float32); Wk = f(L0_attn_Wk, dtype=np.float32)
    Wv = f(L0_attn_Wv, dtype=np.float32); Wo = f(L0_attn_Wo, dtype=np.float32)
    Wr = f(L0_ffn_Wr, dtype=np.float32); W1 = f(L0_ffn_W1, dtype=np.float32)
    Wo2 = f(L0_ffn_Wo, dtype=np.float32)

    def qkv_lay(W):
        return np.ascontiguousarray(
            W.reshape(8, 128, 4, 256).transpose(2, 1, 0, 3).reshape(4, 128, 8 * 256))

    wq_c = qkv_lay(Wq); wk_c = qkv_lay(Wk); wv_c = qkv_lay(Wv)
    wo_c = np.ascontiguousarray(
        Wo.reshape(8, 128, 1024).transpose(1, 0, 2).reshape(128, 8192))
    wrt_c = np.ascontiguousarray(
        Wr.T.reshape(8, 128, 8).transpose(1, 0, 2).reshape(128, 64))
    w1_c = np.ascontiguousarray(
        W1.reshape(8, 8, 128, 16, 256).transpose(0, 3, 2, 1, 4).reshape(8, 16, 128, 8 * 256))
    wo2_c = np.ascontiguousarray(
        Wo2.reshape(32, 128, 8, 128).transpose(2, 1, 0, 3).reshape(8, 128, 32 * 128))
    bq_t = np.ascontiguousarray(f(L0_attn_bq, dtype=np.float32).reshape(8, 128).T)
    bk_t = np.ascontiguousarray(f(L0_attn_bk, dtype=np.float32).reshape(8, 128).T)
    bv_r = f(L0_attn_bv, dtype=np.float32).reshape(1, 1024)
    bo_r = f(L0_attn_bo, dtype=np.float32).reshape(1, 1024)
    b1_t = np.ascontiguousarray(
        f(L0_ffn_b1, dtype=np.float32).T.reshape(32, 128, 8).transpose(1, 0, 2).reshape(128, 256))
    bo2_t = np.ascontiguousarray(f(L0_ffn_bo, dtype=np.float32).reshape(8, 128).T)
    ga_r = f(L0_ffn_gamma, dtype=np.float32).reshape(1, 1024)
    be_r = f(L0_ffn_beta, dtype=np.float32).reshape(1, 1024)

    shared = dict(wq_c=wq_c, wk_c=wk_c, wv_c=wv_c, wo_c=wo_c, wrt_c=wrt_c,
                  w1_c=w1_c, wo2_c=wo2_c, bq_t=bq_t, bk_t=bk_t, bv_r=bv_r,
                  bo_r=bo_r, b1_t=b1_t, bo2_t=bo2_t, ga_r=ga_r, be_r=be_r)
    in_maps = []
    for c in range(8):
        b, half = divmod(c, 2)
        own = x[b, half * 2048:(half + 1) * 2048]
        par = x[b, (1 - half) * 2048:(2 - half) * 2048]
        m = dict(shared)
        m["xseq"] = np.ascontiguousarray(np.concatenate([own, par], axis=0))
        in_maps.append(m)

    nc = _get_nc()
    res = run_bass_kernel_spmd(nc, in_maps, core_ids=list(range(8)))
    out = np.empty((4, 4096, 1024), dtype=np.float32)
    for c in range(8):
        b, half = divmod(c, 2)
        out[b, half * 2048:(half + 1) * 2048] = res.results[c]["y"]
    return out


if __name__ == "__main__":
    nc = build()
    print("build ok")
